# version 02
# baseline (speedup 1.0000x reference)
import numpy as np
import ml_dtypes
from contextlib import ExitStack

import concourse.bass as bass
import concourse.mybir as mybir
from concourse.bass_utils import run_bass_kernel_spmd

F32 = mybir.dt.float32
BF16 = mybir.dt.bfloat16
AF = mybir.ActivationFunctionType
ALU = mybir.AluOpType
AX = mybir.AxisListType

P = 128
D = 1024
DT = 8
SUP = 4
NSUP = 4
NCH = 16
TOWN = 2048
LR = 0.01
EPS = 1e-5
NE = 8
QUEUES = ("tensor", "vector", "scalar", "gpsimd", "sync")
QSEM = {"vector": "vs", "scalar": "ss", "gpsimd": "gs", "sync": "do", "tensor": "ts"}


class Prog:

    def __init__(self, nc, ctx):
        self.nc = nc
        self.ctx = ctx
        self.q = {k: [] for k in QUEUES}
        self.cnt = {}
        self.hand = {}
        self.lastw = {k: {} for k in QUEUES}

    def sem(self, name):
        if name not in self.hand:
            self.hand[name] = self.ctx.enter_context(self.nc.semaphore(name))
            self.cnt[name] = 0
        return self.hand[name]

    def tok(self, name):
        self.sem(name)
        return (name, self.cnt[name])

    def emit(self, queue, fn, waits=(), inc=None, inc_by=1):
        ws = []
        for w in waits:
            if w is None:
                continue
            s, v = w
            if v <= 0:
                continue
            if self.lastw[queue].get(s, 0) >= v:
                continue
            self.lastw[queue][s] = v
            self.sem(s)
            ws.append((s, v))
        tk = None
        it = None
        if inc is not None:
            self.sem(inc)
            self.cnt[inc] += inc_by
            tk = (inc, self.cnt[inc])
            it = (inc, inc_by)
        self.q[queue].append((ws, fn, it))
        return tk

    def replay(self, queue, eng):
        for ws, fn, it in self.q[queue]:
            for s, v in ws:
                eng.wait_ge(self.hand[s], v)
            ins = fn(eng)
            if it is not None:
                assert ins is not None, f"{queue}: inc {it} but fn returned None"
                ins.then_inc(self.hand[it[0]], it[1])


class Bank:

    def __init__(self, pr, name, dtype=F32):
        width = 512 if dtype == F32 else 1024
        self.pr = pr
        self.t = pr.ctx.enter_context(pr.nc.psum_tensor(name, [P, width], dtype))
        self.w = None
        self.readers = []

    def mm_group(self, mms, extra_waits=()):
        pr = self.pr
        n = len(mms)
        waits = [w for w in list(extra_waits) + self.readers if w is not None]
        tk = None
        for i, (o, l, r) in enumerate(mms):
            tk = pr.emit(
                "tensor",
                (lambda t, o=o, l=l, r=r, st=(i == 0), sp=(i == n - 1):
                 t.matmul(o, l, r, start=st, stop=sp)),
                waits=waits if i == 0 else (),
                inc="ts" if i == n - 1 else None)
        self.w = tk
        self.readers = []
        return tk

    def read(self, queue, fn, extra_waits=()):
        tk = self.pr.emit(queue, fn, waits=[self.w] + list(extra_waits), inc=QSEM[queue])
        self.readers.append(tk)
        return tk


def ap(t, offset, dims):
    return bass.AP(t, offset, dims)


def build_program(debug=False, n_super=2 * NSUP, do_passb=True):
    nc = bass.Bass("TRN2", target_bir_lowering=False)

    d_xm = nc.dram_tensor("xm", [TOWN, D], F32, kind="ExternalInput")
    d_xp = nc.dram_tensor("xp", [TOWN, D], F32, kind="ExternalInput")
    d_gmask = nc.dram_tensor("gmask", [P, 2], F32, kind="ExternalInput")
    d_cvec = nc.dram_tensor("cvec", [D, 1], F32, kind="ExternalInput")
    d_wmod = nc.dram_tensor("wmod", [D, 6 * D], F32, kind="ExternalInput")
    d_thK = nc.dram_tensor("thK", [D, D], F32, kind="ExternalInput")
    d_thV = nc.dram_tensor("thV", [D, D], F32, kind="ExternalInput")
    d_thQ = nc.dram_tensor("thQ", [D, D], F32, kind="ExternalInput")
    d_w0T = nc.dram_tensor("w0T", [D, D], F32, kind="ExternalInput")
    d_b0 = nc.dram_tensor("b0r", [1, D], F32, kind="ExternalInput")
    d_rw = nc.dram_tensor("rw", [D, NE], F32, kind="ExternalInput")
    d_gw = nc.dram_tensor("gw", [NE, D, D], F32, kind="ExternalInput")
    d_uw = nc.dram_tensor("uw", [NE, D, D], F32, kind="ExternalInput")
    d_dw = nc.dram_tensor("dw", [NE, D, D], F32, kind="ExternalInput")
    d_sgw = nc.dram_tensor("sgw", [D, 2 * D], F32, kind="ExternalInput")
    d_suw = nc.dram_tensor("suw", [D, 2 * D], F32, kind="ExternalInput")
    d_sdw = nc.dram_tensor("sdw", [2 * D, D], F32, kind="ExternalInput")
    d_ident = nc.dram_tensor("identb", [P, P], BF16, kind="ExternalInput")
    d_mune = nc.dram_tensor("mune", [P, P], BF16, kind="ExternalInput")
    d_mlne = nc.dram_tensor("mlne", [P, P], BF16, kind="ExternalInput")
    d_muinc = nc.dram_tensor("muinc", [P, P], BF16, kind="ExternalInput")

    d_out = nc.dram_tensor("out", [TOWN, D], F32, kind="ExternalOutput")

    d_x2s = nc.dram_tensor("x2s", [TOWN, D], F32)
    d_h2s = nc.dram_tensor("h2s", [NCH * P, D], BF16)
    d_modscr = nc.dram_tensor("modscr", [1, 6 * D], F32)
    d_thq_scr = nc.dram_tensor("thq_scr", [D, D], BF16)
    d_gw_scr = nc.dram_tensor("gw_scr", [NE, P, DT * D], BF16)
    d_uw_scr = nc.dram_tensor("uw_scr", [NE, P, DT * D], BF16)
    d_dw_scr = nc.dram_tensor("dw_scr", [NE, P, DT * D], BF16)
    d_sg_scr = nc.dram_tensor("sg_scr", [2, P, DT * D], BF16)
    d_su_scr = nc.dram_tensor("su_scr", [2, P, DT * D], BF16)
    d_sd_scr = nc.dram_tensor("sd_scr", [2, P, DT * D], BF16)

    taps = {}

    with ExitStack() as octx:
        pr = Prog(nc, octx)
        E = octx.enter_context

        def tap(name, apv, waits):
            if not debug:
                return
            dt_ = nc.dram_tensor("tap_" + name, list(apv.shape), apv.dtype, kind="ExternalOutput")
            taps[name] = True
            pr.emit("sync", lambda sy, a=apv, d2=dt_: sy.dma_start(d2[...], a),
                    waits=waits, inc="do", inc_by=16)

        s_ident = E(nc.sbuf_tensor("s_ident", [P, P], BF16))
        s_mune = E(nc.sbuf_tensor("s_mune", [P, P], BF16))
        s_mlne = E(nc.sbuf_tensor("s_mlne", [P, P], BF16))
        s_muinc = E(nc.sbuf_tensor("s_muinc", [P, P], BF16))
        ones_r1 = E(nc.sbuf_tensor("ones_r1", [1, P], BF16))
        ones_row = E(nc.sbuf_tensor("ones_row", [1, 512], BF16))
        ones_r1f = E(nc.sbuf_tensor("ones_r1f", [1, P], F32))
        ones_cl = E(nc.sbuf_tensor("ones_cl", [P, 1], BF16))
        onef_col = E(nc.sbuf_tensor("onef_col", [P, 1], F32))
        s_gmask = E(nc.sbuf_tensor("s_gmask", [P, 2], F32))
        gb_bc = E(nc.sbuf_tensor("gb_bc", [P, D], F32))
        ga_bc = E(nc.sbuf_tensor("ga_bc", [P, D], F32))
        comb_all = E(nc.sbuf_tensor("comb_all", [P, NCH * NE], F32))
        b_f32 = E(nc.sbuf_tensor("b_f32", [1, D], F32))
        b_bfA = E(nc.sbuf_tensor("b_bfA", [1, D], BF16))
        b_bfB = E(nc.sbuf_tensor("b_bfB", [1, D], BF16))
        xb = E(nc.sbuf_tensor("xb", [P, 2 * D], F32))
        lnt = E(nc.sbuf_tensor("lnt", [P, 16], F32))
        hbf = E(nc.sbuf_tensor("hbf", [P, D], BF16))
        lg = E(nc.sbuf_tensor("lg", [P, NE], F32))
        zex = E(nc.sbuf_tensor("zex", [P, NE], F32))
        sc8 = E(nc.sbuf_tensor("sc8", [P, NE], F32))
        top8 = E(nc.sbuf_tensor("top8", [P, 8], F32))
        rcol = E(nc.sbuf_tensor("rcol", [P, 4], F32))
        h2stg = E(nc.sbuf_tensor("h2stg", [P, D], BF16))
        ca1c = E(nc.sbuf_tensor("ca1c", [P, DT], F32))
        ca1nc = E(nc.sbuf_tensor("ca1nc", [P, DT], F32))
        cb1c = E(nc.sbuf_tensor("cb1c", [P, DT], F32))
        sa_bf = E(nc.sbuf_tensor("sa_bf", [P, DT], BF16))
        sb_bf = E(nc.sbuf_tensor("sb_bf", [P, DT], BF16))
        rw_bf = E(nc.sbuf_tensor("rw_bf", [P, DT * NE], BF16))
        rsb_r = E(nc.sbuf_tensor("rsb_r", [1, NE], BF16))
        ksa_r = E(nc.sbuf_tensor("ksa_r", [1, D], BF16))
        qsa_r = E(nc.sbuf_tensor("qsa_r", [1, D], BF16))
        vsan_r = E(nc.sbuf_tensor("vsan_r", [1, D], BF16))

        bk = [Bank(pr, f"bk{i}") for i in range(7)]
        bkT = Bank(pr, "bkT", BF16)

        cdb = E(nc.sbuf_tensor("cdb", [P, 4], F32))
        pr.emit("gpsimd", lambda g: g.memset(cdb[:, 0:1], 2 * LR / D), inc="gs")
        pr.emit("gpsimd", lambda g: g.memset(cdb[:, 1:2], 1.0), inc="gs")
        pr.emit("gpsimd", lambda g: g.memset(cdb[:, 2:3], 0.0), inc="gs")
        pr.emit("gpsimd", lambda g: g.memset(cdb[:, 3:4], -1.0), inc="gs")
        nc.const_aps.register(F32, 2 * LR / D, cdb[:, 0:1])
        nc.const_aps.register(F32, 1.0, cdb[:, 1:2])
        nc.const_aps.register(F32, 0.0, cdb[:, 2:3])
        nc.const_aps.register(F32, -1.0, cdb[:, 3:4])

        for t_, d_ in ((s_ident, d_ident), (s_mune, d_mune), (s_mlne, d_mlne), (s_muinc, d_muinc)):
            pr.emit("sync", lambda sy, t2=t_, d2=d_: sy.dma_start(t2[:, :], d2[:, :]),
                    inc="dini", inc_by=16)
        pr.emit("sync", lambda sy: sy.dma_start(s_gmask[:, :], d_gmask[:, :]), inc="dini", inc_by=16)
        pr.emit("sync", lambda sy: sy.dma_start(b_f32[:, :], d_b0[:, :]), inc="dini", inc_by=16)
        pr.emit("gpsimd", lambda g: g.memset(ones_r1[:, :], 1.0), inc="gs")
        pr.emit("gpsimd", lambda g: g.memset(ones_row[:, :], 1.0), inc="gs")
        pr.emit("gpsimd", lambda g: g.memset(ones_cl[:, :], 1.0), inc="gs")
        pr.emit("gpsimd", lambda g: g.memset(ones_r1f[:, :], 1.0), inc="gs")
        pr.emit("gpsimd", lambda g: g.memset(onef_col[:, :], 1.0), inc="gs")
        t_dl0 = pr.tok("dini")
        t_ms0 = pr.tok("gs")

        t_bb0 = pr.emit("scalar", lambda s: s.copy(b_bfA[0:1, :], b_f32[0:1, :]),
                        waits=[t_dl0], inc="ss")

        wprep_toks = {}
        for hh in range(2):
            pr.emit("gpsimd", lambda g, hh=hh: g.dma_start(
                ap(d_thq_scr, hh * 512 * D, [[D, 512], [1, D]]),
                ap(d_thQ, hh * 512 * D, [[D, 512], [1, D]])), inc="dtq", inc_by=16)
        wprep_toks["thq"] = pr.tok("dtq")

        with ExitStack() as actx:
            EA = actx.enter_context

            with ExitStack() as wctx:
                wmb = wctx.enter_context(nc.sbuf_tensor("wmb", [P, 2 * 6 * D], F32))
                silc = wctx.enter_context(nc.sbuf_tensor("silc", [P, DT], F32))
                modr = wctx.enter_context(nc.sbuf_tensor("modr", [1, 6 * D], F32))
                tmpr = wctx.enter_context(nc.sbuf_tensor("tmpr", [1, 3 * D], F32))

                t_cv = pr.emit("sync", lambda sy: sy.dma_start(
                    silc[:, :], ap(d_cvec, 0, [[1, P], [P, DT]]),
                    allow_slow_non_contiguous=True), inc="dcv", inc_by=16)
                t_sil = pr.emit("scalar", lambda s: s.activation(silc[:, :], silc[:, :], AF.Silu),
                                waits=[t_cv], inc="ss")
                prev_mm = [None] * DT
                for pas in range(2):
                    mm_toks = []
                    for dt_i in range(DT):
                        slot = dt_i % 2
                        wl = [mm_toks[dt_i - 2]] if dt_i >= 2 else [prev_mm[DT - 2 + dt_i]]
                        tkd = pr.emit("sync", lambda sy, dt_i=dt_i, slot=slot: sy.dma_start(
                            ap(wmb, slot * 6 * D, [[2 * 6 * D, P], [1, 6 * D]]),
                            ap(d_wmod, dt_i * P * 6 * D, [[6 * D, P], [1, 6 * D]])),
                            waits=wl, inc=f"dwm{slot}", inc_by=16)
                        tk = None
                        for jj in range(6):
                            jb = pas * 6 + jj
                            b2 = bk[jj]
                            o = b2.t[0:1, 0:512]
                            l = silc[:, dt_i:dt_i + 1]
                            r = wmb[:, slot * 6 * D + jb * 512:slot * 6 * D + (jb + 1) * 512]
                            wl2 = [tkd]
                            if dt_i == 0:
                                wl2 += [t_sil] + b2.readers
                            tk = pr.emit("tensor",
                                         (lambda t, o=o, l=l, r=r, st=(dt_i == 0), sp=(dt_i == DT - 1):
                                          t.matmul(o, l, r, start=st, stop=sp)),
                                         waits=wl2, inc="ts" if jj == 5 else None)
                        mm_toks.append(tk)
                        if dt_i == DT - 1:
                            for jj in range(6):
                                bk[jj].w = tk
                                bk[jj].readers = []
                    for jj in range(6):
                        jb = pas * 6 + jj
                        bk[jj].read("vector", lambda v, jj=jj, jb=jb: v.tensor_copy(
                            modr[0:1, jb * 512:(jb + 1) * 512], bk[jj].t[0:1, 0:512]))
                    prev_mm = mm_toks
                t_mod = pr.tok("vs")

                for idx in (2, 5):
                    g_sl = modr[0:1, idx * D:(idx + 1) * D]
                    a_sl = tmpr[0:1, 0:D]
                    bb_sl = tmpr[0:1, D:2 * D]
                    s_sl = tmpr[0:1, 2 * D:3 * D]
                    pr.emit("vector", lambda v, g=g_sl, a=a_sl: v.tensor_scalar_max(a, g, 0.25),
                            waits=[t_mod], inc="vs")
                    pr.emit("vector", lambda v, g=g_sl, b3=bb_sl: v.tensor_scalar_min(b3, g, -0.25), inc="vs")
                    tc0 = pr.emit("vector", lambda v, g=g_sl, s2=s_sl: v.tensor_single_scalar(s2, g, 0.0, op=ALU.is_ge), inc="vs")
                    tc1 = pr.emit("vector", lambda v, a=a_sl, b3=bb_sl: v.tensor_sub(a, a, b3),
                                  waits=[tc0], inc="vs")
                    tc2 = pr.emit("vector", lambda v, a=a_sl, s2=s_sl: v.tensor_mul(a, a, s2),
                                  waits=[tc1], inc="vs")
                    pr.emit("vector", lambda v, g=g_sl, a=a_sl, b3=bb_sl: v.tensor_add(g, b3, a),
                            waits=[tc2], inc="vs")
                t_clamp = pr.tok("vs")

                for row_i, dst in ((2, ga_bc), (5, gb_bc)):
                    for jh in range(2):
                        b2 = bk[6]
                        b2.mm_group(
                            [(b2.t[:, 0:512], ones_r1f[:, :],
                              modr[0:1, row_i * D + jh * 512:row_i * D + (jh + 1) * 512])],
                            extra_waits=[t_clamp, t_ms0])
                        b2.read("vector", lambda v, dst=dst, jh=jh, b2=b2: v.tensor_copy(
                            dst[:, jh * 512:(jh + 1) * 512], b2.t[:, 0:512]))
                t_ms = pr.emit("sync", lambda sy: sy.dma_start(d_modscr[:, :], modr[:, :]),
                               waits=[t_clamp], inc="dms", inc_by=16)
                t_l1 = pr.emit("sync", lambda sy: sy.dma_start(
                    ca1c[:, :], ap(d_modscr, 1 * D, [[1, P], [P, DT]]),
                    allow_slow_non_contiguous=True), waits=[t_ms], inc="dca", inc_by=16)
                t_l2 = pr.emit("sync", lambda sy: sy.dma_start(
                    cb1c[:, :], ap(d_modscr, 4 * D, [[1, P], [P, DT]]),
                    allow_slow_non_contiguous=True), waits=[t_ms], inc="dcb", inc_by=16)
                t_l3 = pr.emit("gpsimd", lambda g: g.dma_start(
                    sa_bf[:, :], ap(d_modscr, 0 * D, [[1, P], [P, DT]]),
                    allow_slow_non_contiguous=True), waits=[t_ms], inc="dsa", inc_by=16)
                t_l4 = pr.emit("gpsimd", lambda g: g.dma_start(
                    sb_bf[:, :], ap(d_modscr, 3 * D, [[1, P], [P, DT]]),
                    allow_slow_non_contiguous=True), waits=[t_ms], inc="dsb2", inc_by=16)
                t_c1 = pr.emit("vector", lambda v: v.tensor_scalar_add(ca1c[:, :], ca1c[:, :], 1.0),
                               waits=[t_l1], inc="vs")
                t_c2 = pr.emit("vector", lambda v: v.tensor_scalar_add(cb1c[:, :], cb1c[:, :], 1.0),
                               waits=[t_l2], inc="vs")
                t_c3 = pr.emit("vector", lambda v: v.tensor_scalar_mul(ca1nc[:, :], ca1c[:, :], -1.0),
                               waits=[t_c1], inc="vs")
                t_wmod_ts = pr.tok("ts")
                t_wmod_vs = pr.tok("vs")

            thK = EA(nc.sbuf_tensor("s_thK", [P, DT * D], BF16))
            thVn = EA(nc.sbuf_tensor("s_thVn", [P, DT * D], BF16))
            thS = EA(nc.sbuf_tensor("s_thS", [P, 2 * D], BF16))
            WT = EA(nc.sbuf_tensor("WT", [P, DT * D], F32))
            Wbf = EA(nc.sbuf_tensor("Wbf", [P, DT * D], BF16))
            hft = EA(nc.sbuf_tensor("hft", [P, DT * 512], BF16))
            Kft = EA(nc.sbuf_tensor("Kft", [P, DT * 512], BF16))
            Ktk = EA(nc.sbuf_tensor("Ktk", [P, SUP * D], BF16))
            Qft = EA(nc.sbuf_tensor("Qft", [P, DT * 512], BF16))
            Gsb = EA(nc.sbuf_tensor("Gsb", [P, SUP * D], BF16))
            Scr = EA(nc.sbuf_tensor("Scr", [P, 6 * P], BF16))
            Sqr = EA(nc.sbuf_tensor("Sqr", [P, 10 * P], BF16))
            Usb = EA(nc.sbuf_tensor("Usb", [P, P], BF16))
            Bp = EA(nc.sbuf_tensor("Bp", [P, 2 * SUP * P], BF16))
            Btp = EA(nc.sbuf_tensor("Btp", [P, 2 * SUP * P], BF16))
            Dsb = EA(nc.sbuf_tensor("Dsb", [P, 2 * P], BF16))
            Xsb = EA(nc.sbuf_tensor("Xsb", [P, SUP * P], BF16))
            RHSs = EA(nc.sbuf_tensor("RHSs", [P, D], BF16))
            x2b = EA(nc.sbuf_tensor("x2b", [P, 2 * D], F32))
            xeb = EA(nc.sbuf_tensor("xeb", [P, 2 * D], F32))

            for dt_i in range(DT):
                pr.emit("gpsimd", lambda g, dt_i=dt_i: g.dma_start(
                    ap(thK, dt_i * D, [[DT * D, P], [1, D]]),
                    ap(d_thK, dt_i * P * D, [[D, P], [1, D]])),
                    waits=[t_wmod_ts, t_wmod_vs], inc="dthk", inc_by=16)
            t_thk_all = pr.tok("dthk")
            thk_toks = [t_thk_all] * DT
            for dt_i in range(DT):
                pr.emit("gpsimd", lambda g, dt_i=dt_i: g.dma_start(
                    ap(thVn, dt_i * D, [[DT * D, P], [1, D]]),
                    ap(d_thV, dt_i * P * D, [[D, P], [1, D]])),
                    waits=[t_wmod_ts, t_wmod_vs], inc="dthv", inc_by=16)
            t_thv_all = pr.tok("dthv")
            thv_toks = [t_thv_all] * DT

            def bias_row(dst_row, src_mat, toks, scale):
                for jh in range(2):
                    b2 = bk[jh]
                    mms = []
                    for dt_i in range(DT):
                        mms.append((b2.t[0:1, 0:512], sa_bf[:, dt_i:dt_i + 1],
                                    src_mat[:, dt_i * D + jh * 512:dt_i * D + (jh + 1) * 512]))
                    b2.mm_group(mms, extra_waits=list(toks) + [t_l3])
                    b2.read("scalar", lambda s, jh=jh, b2=b2, dst_row=dst_row, scale=scale:
                            s.activation(dst_row[0:1, jh * 512:(jh + 1) * 512], b2.t[0:1, 0:512],
                                         AF.Identity, scale=scale))
            bias_row(ksa_r, thK, thk_toks, 1.0)
            t_ksa = pr.tok("ss")
            bias_row(vsan_r, thVn, thv_toks, -1.0)
            t_vsa = pr.tok("ss")
            for dt_i in range(DT):
                pr.emit("vector", lambda v, dt_i=dt_i: v.tensor_scalar_mul(
                    thK[:, dt_i * D:(dt_i + 1) * D], thK[:, dt_i * D:(dt_i + 1) * D],
                    ca1c[:, dt_i:dt_i + 1]),
                    waits=[t_ksa, t_c1, thk_toks[dt_i]], inc="vs")
                pr.emit("vector", lambda v, dt_i=dt_i: v.tensor_scalar_mul(
                    thVn[:, dt_i * D:(dt_i + 1) * D], thVn[:, dt_i * D:(dt_i + 1) * D],
                    ca1nc[:, dt_i:dt_i + 1]),
                    waits=[t_vsa, t_c3, thv_toks[dt_i]], inc="vs")
            t_thprime = pr.tok("vs")

            qcons = {0: None, 1: None}
            for dt_i in range(DT):
                slot = dt_i % 2
                tkd = pr.emit("sync", lambda sy, dt_i=dt_i, slot=slot: sy.dma_start(
                    ap(thS, slot * D, [[2 * D, P], [1, D]]),
                    ap(d_thq_scr, dt_i * P * D, [[D, P], [1, D]])),
                    waits=[wprep_toks["thq"], qcons[slot]], inc=f"dq{slot}", inc_by=16)
                tk = None
                for jh in range(2):
                    b2 = bk[2 + jh]
                    o = b2.t[0:1, 0:512]
                    l = sa_bf[:, dt_i:dt_i + 1]
                    r = thS[:, slot * D + jh * 512:slot * D + (jh + 1) * 512]
                    wl = [tkd, t_l3]
                    if dt_i == 0:
                        wl += b2.readers
                    tk = pr.emit("tensor",
                                 (lambda t, o=o, l=l, r=r, st=(dt_i == 0), sp=(dt_i == DT - 1):
                                  t.matmul(o, l, r, start=st, stop=sp)),
                                 waits=wl, inc="ts" if jh == 1 else None)
                qcons[slot] = tk
            for jh in range(2):
                b2 = bk[2 + jh]
                b2.w = qcons[1]
                b2.readers = []
                b2.read("scalar", lambda s, jh=jh, b2=b2: s.activation(
                    qsa_r[0:1, jh * 512:(jh + 1) * 512], b2.t[0:1, 0:512], AF.Identity))
            t_qsa = pr.tok("ss")

            t_rwl = pr.emit("gpsimd", lambda g: g.dma_start(
                rw_bf[:, :], ap(d_rw, 0, [[NE, P], [P * NE, DT], [1, NE]])),
                waits=[t_wmod_ts, t_wmod_vs], inc="drw", inc_by=16)
            b2 = bk[4]
            mms = [(b2.t[0:1, 0:NE], sb_bf[:, dt_i:dt_i + 1], rw_bf[:, dt_i * NE:(dt_i + 1) * NE])
                   for dt_i in range(DT)]
            b2.mm_group(mms, extra_waits=[t_rwl, t_l4])
            b2.read("scalar", lambda s, b2=b2: s.activation(rsb_r[0:1, :], b2.t[0:1, 0:NE], AF.Identity))
            t_rsb = pr.tok("ss")
            for dt_i in range(DT):
                pr.emit("vector", lambda v, dt_i=dt_i: v.tensor_scalar_mul(
                    rw_bf[:, dt_i * NE:(dt_i + 1) * NE], rw_bf[:, dt_i * NE:(dt_i + 1) * NE],
                    cb1c[:, dt_i:dt_i + 1]),
                    waits=[t_rsb, t_c2], inc="vs")


        def wprep_family(name, pairs):
            for s_ap, d_ap in pairs:
                pr.emit("gpsimd", lambda g, s2=s_ap, d2=d_ap: g.dma_start(d2, s2),
                        inc="dp", inc_by=16)
            wprep_toks[name] = pr.tok("dp")

        def gu_pairs(src, dst):
            out = []
            for e2 in range(NE):
                for ht in range(DT):
                    out.append((ap(src, e2 * D * D + ht * P * D, [[D, P], [1, D]]),
                                ap(dst, e2 * P * DT * D + ht * D, [[DT * D, P], [1, D]])))
            return out
        wprep_family("gw", gu_pairs(d_gw, d_gw_scr))
        wprep_family("uw", gu_pairs(d_uw, d_uw_scr))
        wprep_family("dw", gu_pairs(d_dw, d_dw_scr))

        def sh_pairs(src, dst):
            out = []
            for fh in range(2):
                for ht in range(DT):
                    out.append((ap(src, ht * P * 2 * D + fh * D, [[2 * D, P], [1, D]]),
                                ap(dst, fh * P * DT * D + ht * D, [[DT * D, P], [1, D]])))
            return out
        wprep_family("sgw", sh_pairs(d_sgw, d_sg_scr))
        wprep_family("suw", sh_pairs(d_suw, d_su_scr))
        sd_pairs = []
        for fh in range(2):
            for fj in range(DT):
                sd_pairs.append((ap(d_sdw, (fh * D + fj * P) * D, [[D, P], [1, D]]),
                                 ap(d_sd_scr, fh * P * DT * D + fj * D, [[DT * D, P], [1, D]])))
        wprep_family("sdw", sd_pairs)
        dp_all = pr.tok("dp")
        for k2 in ("gw", "uw", "dw", "sgw", "suw", "sdw"):
            wprep_toks[k2] = dp_all

            for dt_i in range(DT):
                pr.emit("sync", lambda sy, dt_i=dt_i: sy.dma_start(
                    ap(WT, dt_i * D, [[DT * D, P], [1, D]]),
                    ap(d_w0T, dt_i * P * D, [[D, P], [1, D]])),
                    waits=[t_wmod_ts, t_wmod_vs], inc="dw0t", inc_by=16)
            t_w0_all = pr.tok("dw0t")
            for dt_i in range(DT):
                pr.emit("scalar", lambda s, dt_i=dt_i: s.copy(
                    Wbf[:, dt_i * D:(dt_i + 1) * D], WT[:, dt_i * D:(dt_i + 1) * D]),
                    waits=[t_w0_all], inc="ss")
            t_w0 = pr.tok("ss")

            t_prologue = [pr.tok("ts"), pr.tok("vs"), pr.tok("ss"), pr.tok("gs"),
                          t_w0, t_bb0, t_thprime, t_qsa]

            ln_done = {}
            x_dma = {}
            x2_spill_tok = None
            h2_guard = [None, None]

            def x_src(s_abs, ci):
                d_src = d_xp if s_abs < NSUP else d_xm
                c16 = (s_abs % NSUP) * SUP + ci
                return ap(d_src, c16 * P * D, [[D, P], [1, D]])

            def ensure_xdma(cl):
                if cl >= 2 * NSUP * SUP or cl in x_dma:
                    return
                slot = cl % 2
                wl = list(t_prologue) if cl == 0 else []
                if cl - 2 >= 0:
                    wl += list(ln_done[cl - 2])
                x_dma[cl] = pr.emit("sync", lambda sy, cl=cl, slot=slot: sy.dma_start(
                    ap(xb, slot * D, [[2 * D, P], [1, D]]), x_src(cl // SUP, cl % SUP)),
                    waits=wl, inc=f"dx{slot}", inc_by=16)

            def pair_id_s(c, cp):
                return c * (c - 1) // 2 + cp

            def pair_id_q(c, cp):
                return c * (c + 1) // 2 + cp

            def emit_ln(x_ap, wait0, hazard_guard):
                pr.emit("vector", lambda v, x3=x_ap: v.bn_stats(lnt[:, 0:6], x3[:, 0:512]),
                        waits=wait0 + [pr.tok("ss")], inc="vs")
                t_b2 = pr.emit("vector", lambda v, x3=x_ap: v.bn_stats(lnt[:, 6:12], x3[:, 512:1024]), inc="vs")
                t_ag = pr.emit("vector", lambda v: v.bn_aggr(lnt[:, 12:14], lnt[:, 0:12]),
                               waits=[t_b2], inc="vs")
                t_v1 = pr.emit("vector", lambda v: v.tensor_scalar_add(lnt[:, 14:15], lnt[:, 13:14], EPS),
                               waits=[t_ag], inc="vs")
                t_s1 = pr.emit("scalar", lambda s: s.activation(lnt[:, 14:15], lnt[:, 14:15], AF.Sqrt),
                               waits=[t_v1], inc="ss")
                t_rc = pr.emit("vector", lambda v: v.reciprocal(lnt[:, 15:16], lnt[:, 14:15]),
                               waits=[t_s1], inc="vs")
                t_v3 = pr.emit("vector", lambda v: v.scalar_tensor_tensor(
                    lnt[:, 14:15], lnt[:, 12:13], -1.0, lnt[:, 15:16], op0=ALU.mult, op1=ALU.mult),
                    waits=[t_rc], inc="vs")
                t_hn = pr.emit("scalar", lambda s, x3=x_ap: s.activation(
                    hbf[:, :], x3, AF.Identity, bias=lnt[:, 14:15], scale=lnt[:, 15:16]),
                    waits=[t_v3] + hazard_guard, inc="ss")
                return t_v3, t_hn

            def emit_transposes(t_hn, dst_ap_fn, extra=()):
                tk = None
                for i in range(DT):
                    tk = pr.emit("tensor",
                                 (lambda t, i=i: t.transpose(
                                     bkT.t[:, i * P:(i + 1) * P], hbf[:, i * P:(i + 1) * P],
                                     s_ident[:, :])),
                                 waits=([t_hn] + bkT.readers) if i == 0 else (),
                                 inc="ts" if i == DT - 1 else None)
                bkT.w = tk
                bkT.readers = []
                return bkT.read("scalar", dst_ap_fn, extra_waits=list(extra))

            for s_abs in range(n_super):
                is_m = s_abs >= NSUP
                par_buf = b_bfA if s_abs % 2 == 0 else b_bfB
                npar_buf = b_bfB if s_abs % 2 == 0 else b_bfA
                gcol = 1 if is_m else 0
                t_super_ts0 = pr.tok("ts")

                for ci in range(SUP):
                    cl = s_abs * SUP + ci
                    ensure_xdma(cl)
                    ensure_xdma(cl + 1)
                    slot = cl % 2
                    xsl = xb[:, slot * D:(slot + 1) * D]
                    t_v3, t_hn = emit_ln(xsl, [x_dma[cl]], [pr.tok("ts")])
                    ln_done[cl] = (t_v3, t_hn)
                    ensure_xdma(cl + 2)
                    emit_transposes(t_hn, lambda s, ci=ci: s.copy(
                        ap(hft, ci * P, [[DT * 512, P], [512, DT], [1, P]]), bkT.t[:, :]))
                t_hft = pr.tok("ss")

                for gi, (ci, jh) in enumerate([(c, j) for c in range(SUP) for j in range(2)]):
                    b2 = bk[gi % 2]
                    mms = []
                    for dt_i in range(DT):
                        mms.append((b2.t[:, 0:512],
                                    hft[:, dt_i * 512 + ci * P:dt_i * 512 + (ci + 1) * P],
                                    thK[:, dt_i * D + jh * 512:dt_i * D + (jh + 1) * 512]))
                    mms.append((b2.t[:, 0:512], ones_r1[:, :], ksa_r[0:1, jh * 512:(jh + 1) * 512]))
                    b2.mm_group(mms, extra_waits=[t_hft])
                    b2.read("vector", lambda v, ci=ci, jh=jh, b2=b2: v.tensor_copy(
                        Ktk[:, ci * D + jh * 512:ci * D + (jh + 1) * 512], b2.t[:, 0:512]),
                        extra_waits=[t_super_ts0])
                for et in range(DT):
                    b2 = bk[2 + et % 2]
                    mms = []
                    for dt_i in range(DT):
                        mms.append((b2.t[:, 0:512],
                                    thK[:, dt_i * D + et * P:dt_i * D + (et + 1) * P],
                                    hft[:, dt_i * 512:dt_i * 512 + 512]))
                    mms.append((b2.t[:, 0:512], ksa_r[0:1, et * P:(et + 1) * P], ones_row[:, :]))
                    b2.mm_group(mms, extra_waits=[t_hft])
                    b2.read("scalar", lambda s, et=et, b2=b2: s.copy(
                        Kft[:, et * 512:(et + 1) * 512], b2.t[:, 0:512]),
                        extra_waits=[t_super_ts0])
                if is_m:
                    qcons = {0: pr.tok("ts"), 1: pr.tok("ts")}
                    for pas in range(2):
                        for dt_i in range(DT):
                            slot = dt_i % 2
                            tkd = pr.emit("sync", lambda sy, dt_i=dt_i, slot=slot: sy.dma_start(
                                ap(thS, slot * D, [[2 * D, P], [1, D]]),
                                ap(d_thq_scr, dt_i * P * D, [[D, P], [1, D]])),
                                waits=[qcons[slot]], inc=f"dq{slot}", inc_by=16)
                            tkp = pr.emit("vector", lambda v, slot=slot, dt_i=dt_i: v.tensor_scalar_mul(
                                thS[:, slot * D:(slot + 1) * D], thS[:, slot * D:(slot + 1) * D],
                                ca1c[:, dt_i:dt_i + 1]),
                                waits=[tkd], inc="vs")
                            tk = None
                            for ei in range(4):
                                et = pas * 4 + ei
                                b2 = bk[ei]
                                o = b2.t[:, 0:512]
                                l = thS[:, slot * D + et * P:slot * D + (et + 1) * P]
                                r = hft[:, dt_i * 512:dt_i * 512 + 512]
                                wl = [tkp]
                                if dt_i == 0:
                                    wl += [t_hft] + b2.readers
                                tk = pr.emit("tensor",
                                             (lambda t, o=o, l=l, r=r, st=(dt_i == 0):
                                              t.matmul(o, l, r, start=st, stop=False)),
                                             waits=wl, inc="ts" if ei == 3 else None)
                            qcons[slot] = tk
                        for ei in range(4):
                            et = pas * 4 + ei
                            b2 = bk[ei]
                            tk = pr.emit("tensor", lambda t, b2=b2, et=et: t.matmul(
                                b2.t[:, 0:512], qsa_r[0:1, et * P:(et + 1) * P], ones_row[:, :],
                                start=False, stop=True), inc="ts")
                            b2.w = tk
                            b2.readers = []
                            b2.read("scalar", lambda s, et=et, b2=b2: s.copy(
                                Qft[:, et * 512:(et + 1) * 512], b2.t[:, 0:512]),
                                extra_waits=[t_super_ts0])
                t_proj = [pr.tok("ss"), pr.tok("vs")]

                for c in range(SUP):
                    for cp in range(c + 1):
                        b2 = bk[(c + cp) % 2]
                        mms = []
                        for et in range(DT):
                            mms.append((b2.t[:, 0:P],
                                        Kft[:, et * 512 + cp * P:et * 512 + (cp + 1) * P],
                                        Kft[:, et * 512 + c * P:et * 512 + (c + 1) * P]))
                        b2.mm_group(mms, extra_waits=t_proj)
                        if cp == c:
                            t_u = b2.read("scalar", lambda s, b2=b2: s.activation(
                                Usb[:, :], b2.t[:, 0:P], AF.Identity,
                                scale=2 * LR / D, bias=2 * LR / D),
                                extra_waits=[pr.tok("vs")])
                            pr.emit("vector", lambda v, c=c: v.tensor_mul(
                                Bp[:, c * P:(c + 1) * P], Usb[:, :], s_mune[:, :]),
                                waits=[t_u, t_super_ts0], inc="vs")
                            pr.emit("vector", lambda v, c=c: v.tensor_mul(
                                Btp[:, c * P:(c + 1) * P], Usb[:, :], s_mlne[:, :]),
                                waits=[t_u], inc="vs")
                        else:
                            pid = pair_id_s(c, cp)
                            b2.read("scalar", lambda s, b2=b2, pid=pid: s.activation(
                                Scr[:, pid * P:(pid + 1) * P], b2.t[:, 0:P],
                                AF.Identity, scale=1.0, bias=1.0),
                                extra_waits=[t_super_ts0])
                if is_m:
                    for c in range(SUP):
                        for cp in range(c + 1):
                            b2 = bk[2 + (c + cp) % 2]
                            mms = []
                            for et in range(DT):
                                mms.append((b2.t[:, 0:P],
                                            Kft[:, et * 512 + cp * P:et * 512 + (cp + 1) * P],
                                            Qft[:, et * 512 + c * P:et * 512 + (c + 1) * P]))
                            b2.mm_group(mms, extra_waits=t_proj)
                            pid = pair_id_q(c, cp)
                            if cp == c:
                                t_u = b2.read("scalar", lambda s, b2=b2: s.activation(
                                    Usb[:, :], b2.t[:, 0:P], AF.Identity, scale=1.0, bias=1.0),
                                    extra_waits=[pr.tok("vs")])
                                pr.emit("vector", lambda v, pid=pid: v.tensor_mul(
                                    Sqr[:, pid * P:(pid + 1) * P], Usb[:, :], s_muinc[:, :]),
                                    waits=[t_u, t_super_ts0], inc="vs")
                            else:
                                b2.read("scalar", lambda s, b2=b2, pid=pid: s.activation(
                                    Sqr[:, pid * P:(pid + 1) * P], b2.t[:, 0:P],
                                    AF.Identity, scale=1.0, bias=1.0),
                                    extra_waits=[t_super_ts0])
                t_pairs = [pr.tok("ss"), pr.tok("vs")]

                x_tok = {}
                for c in range(SUP):
                    x_tok[c] = pr.emit("vector", lambda v, c=c: v.tensor_add(
                        Xsb[:, c * P:(c + 1) * P], s_ident[:, :], Bp[:, c * P:(c + 1) * P]),
                        waits=t_pairs + [t_super_ts0], inc="vs")
                ds_free = {0: None, 1: None}
                for m in range(1, 7):
                    pslot = (m - 1) % 2
                    cslot = m % 2
                    last = (m == 6)
                    for c in range(SUP):
                        bsl = Bp[:, (pslot * SUP + c) * P:(pslot * SUP + c + 1) * P]
                        btsl = Btp[:, (pslot * SUP + c) * P:(pslot * SUP + c + 1) * P]
                        ew = [x_tok[c]] if m == 1 else []
                        if not last:
                            b2 = bk[c % 2]
                            b2.mm_group([(b2.t[:, 0:P], btsl, bsl)], extra_waits=ew)
                            b2.read("scalar", lambda s, b2=b2, c=c, cslot=cslot: s.copy(
                                Bp[:, (cslot * SUP + c) * P:(cslot * SUP + c + 1) * P], b2.t[:, 0:P]))
                        b3 = bk[2 + c % 2]
                        b3.mm_group([(b3.t[:, 0:P], bsl, btsl)], extra_waits=ew)
                        t_bt = b3.read("scalar", lambda s, b3=b3, c=c, cslot=cslot: s.copy(
                            Btp[:, (cslot * SUP + c) * P:(cslot * SUP + c + 1) * P], b3.t[:, 0:P]))
                        b4 = bk[4 + c % 2]
                        b4.mm_group([(b4.t[:, 0:P],
                                      Btp[:, (cslot * SUP + c) * P:(cslot * SUP + c + 1) * P],
                                      Xsb[:, c * P:(c + 1) * P])],
                                    extra_waits=[t_bt, x_tok[c]])
                        t_d = b4.read("scalar", lambda s, b4=b4, c=c: s.copy(
                            Dsb[:, (c % 2) * P:(c % 2 + 1) * P], b4.t[:, 0:P]),
                            extra_waits=[ds_free[c % 2]])
                        x_tok[c] = pr.emit("vector", lambda v, c=c: v.tensor_add(
                            Xsb[:, c * P:(c + 1) * P], Xsb[:, c * P:(c + 1) * P],
                            Dsb[:, (c % 2) * P:(c % 2 + 1) * P]),
                            waits=[t_d], inc="vs")
                        ds_free[c % 2] = x_tok[c]
                t_newton = pr.tok("vs")

                g_tok = {}
                ep_pend = None

                def emit_epilogue(c, t_xe):
                    nonlocal x2_spill_tok, h2_guard
                    c16 = (s_abs % NSUP) * SUP + c
                    sl = (c % 2) * D
                    x2sl = x2b[:, sl:sl + D]
                    t_x2m = pr.emit("vector", lambda v, sl=sl: v.tensor_add(
                        x2b[:, sl:sl + D], x2b[:, sl:sl + D], xeb[:, sl:sl + D]),
                        waits=[t_xe], inc="vs")
                    x2_spill_tok = pr.emit("sync", lambda sy, c16=c16, x2sl=x2sl: sy.dma_start(
                        ap(d_x2s, c16 * P * D, [[D, P], [1, D]]), x2sl),
                        waits=[t_x2m], inc="dsp", inc_by=16)
                    t_v3, t_hn = emit_ln(x2sl, [t_x2m], [pr.tok("ts")])
                    t_h2 = emit_transposes(t_hn, lambda s: s.copy(h2stg[:, :], bkT.t[:, :]),
                                           extra=[h2_guard[0], h2_guard[1]])
                    h2_spill = pr.emit("sync", lambda sy, c16=c16: sy.dma_start(
                        ap(d_h2s, c16 * P * D, [[D, P], [1, D]]), h2stg[:, :]),
                        waits=[t_h2], inc="dh2s", inc_by=16)
                    b6 = bk[5]
                    mms = []
                    for dt_i in range(DT):
                        mms.append((b6.t[:, 0:NE],
                                    h2stg[:, dt_i * P:(dt_i + 1) * P],
                                    rw_bf[:, dt_i * NE:(dt_i + 1) * NE]))
                    mms.append((b6.t[:, 0:NE], ones_r1[:, :], rsb_r[0:1, :]))
                    b6.mm_group(mms, extra_waits=[t_h2])
                    h2_guard = [h2_spill, pr.tok("ts")]
                    b6.read("vector", lambda v, b6=b6: v.tensor_copy(lg[:, :], b6.t[:, 0:NE]))
                    t_m1 = pr.emit("vector", lambda v: v.reduce_max(rcol[:, 0:1], lg[:, :], axis=AX.X), inc="vs")
                    t_m2 = pr.emit("vector", lambda v: v.tensor_scalar_mul(rcol[:, 1:2], rcol[:, 0:1], -1.0),
                                   waits=[t_m1], inc="vs")
                    t_e1 = pr.emit("scalar", lambda s: s.activation(
                        zex[:, :], lg[:, :], AF.Exp, bias=rcol[:, 1:2]),
                        waits=[t_m2], inc="ss")
                    t_r1 = pr.emit("vector", lambda v: v.reduce_sum(rcol[:, 2:3], zex[:, :], axis=AX.X),
                                   waits=[t_e1], inc="vs")
                    t_r2 = pr.emit("vector", lambda v: v.reciprocal(rcol[:, 3:4], rcol[:, 2:3]),
                                   waits=[t_r1], inc="vs")
                    t_sc = pr.emit("scalar", lambda s: s.activation(
                        sc8[:, :], zex[:, :], AF.Identity, scale=rcol[:, 3:4]),
                        waits=[t_r2], inc="ss")
                    t_t8 = pr.emit("vector", lambda v: v.max(out=top8[:, :], in_=sc8[:, :]),
                                   waits=[t_sc], inc="vs")
                    pr.emit("vector", lambda v, c16=c16: v.scalar_tensor_tensor(
                        comb_all[:, c16 * NE:(c16 + 1) * NE], sc8[:, :], top8[:, 1:2], sc8[:, :],
                        op0=ALU.is_ge, op1=ALU.mult),
                        waits=[t_t8], inc="vs")

                for c in range(SUP):
                    bl, bh = (bk[0], bk[1]) if c % 2 == 0 else (bk[2], bk[3])
                    if is_m:
                        t_xe = pr.emit("sync", lambda sy, s_abs=s_abs, c=c: sy.dma_start(
                            xeb[:, (c % 2) * D:(c % 2 + 1) * D], x_src(s_abs, c)),
                            waits=[pr.tok("vs")], inc="dxe", inc_by=16)
                    for jh, b2 in ((0, bl), (1, bh)):
                        mms = []
                        for et in range(DT):
                            mms.append((b2.t[:, 0:512],
                                        Kft[:, et * 512 + c * P:et * 512 + (c + 1) * P],
                                        Wbf[:, et * D + jh * 512:et * D + (jh + 1) * 512]))
                        for dt_i in range(DT):
                            mms.append((b2.t[:, 0:512],
                                        hft[:, dt_i * 512 + c * P:dt_i * 512 + (c + 1) * P],
                                        thVn[:, dt_i * D + jh * 512:dt_i * D + (jh + 1) * 512]))
                        mms.append((b2.t[:, 0:512], ones_r1[:, :],
                                    par_buf[0:1, jh * 512:(jh + 1) * 512]))
                        mms.append((b2.t[:, 0:512], ones_r1[:, :],
                                    vsan_r[0:1, jh * 512:(jh + 1) * 512]))
                        for cp in range(c):
                            pid = pair_id_s(c, cp)
                            mms.append((b2.t[:, 0:512], Scr[:, pid * P:(pid + 1) * P],
                                        Gsb[:, cp * D + jh * 512:cp * D + (jh + 1) * 512]))
                        ew = [t_newton] + ([g_tok[c - 1]] if c > 0 else [])
                        b2.mm_group(mms, extra_waits=ew)
                        b2.read("scalar", lambda s, b2=b2, jh=jh: s.copy(
                            RHSs[0:P, jh * 512:(jh + 1) * 512], b2.t[:, 0:512]))
                    t_rhs = pr.tok("ss")
                    bg = bk[4]
                    for jh in range(2):
                        bg.mm_group([(bg.t[:, 0:512], Xsb[:, c * P:(c + 1) * P],
                                      RHSs[:, jh * 512:(jh + 1) * 512])],
                                    extra_waits=[t_rhs])
                        g_tok[c] = bg.read("scalar", lambda s, b5=bg, c=c, jh=jh, gcol=gcol: s.activation(
                            Gsb[:, c * D + jh * 512:c * D + (jh + 1) * 512], b5.t[:, 0:512],
                            AF.Identity, scale=s_gmask[:, gcol:gcol + 1]),
                            extra_waits=[t_super_ts0])
                    if is_m:
                        for jh, b2 in ((0, bk[5]), (1, bk[6])):
                            mms = []
                            for et in range(DT):
                                mms.append((b2.t[:, 0:512],
                                            Qft[:, et * 512 + c * P:et * 512 + (c + 1) * P],
                                            Wbf[:, et * D + jh * 512:et * D + (jh + 1) * 512]))
                            mms.append((b2.t[:, 0:512], ones_r1[:, :],
                                        par_buf[0:1, jh * 512:(jh + 1) * 512]))
                            for cp in range(c + 1):
                                pid = pair_id_q(c, cp)
                                mms.append((b2.t[:, 0:512], Sqr[:, pid * P:(pid + 1) * P],
                                            Gsb[:, cp * D + jh * 512:cp * D + (jh + 1) * 512]))
                            b2.mm_group(mms, extra_waits=[g_tok[c]])
                            b2.read("vector", lambda v, b2=b2, jh=jh, c=c: v.tensor_mul(
                                x2b[:, (c % 2) * D + jh * 512:(c % 2) * D + (jh + 1) * 512],
                                b2.t[:, 0:512],
                                ga_bc[:, jh * 512:(jh + 1) * 512]),
                                extra_waits=[pr.tok("ss"), x2_spill_tok])
                        if ep_pend is not None:
                            emit_epilogue(*ep_pend)
                        ep_pend = (c, t_xe)
                if is_m and ep_pend is not None:
                    emit_epilogue(*ep_pend)
                    ep_pend = None
                t_chain = pr.tok("ss")

                t_outs = pr.tok("ts")
                for dt_i in range(DT):
                    for jh in range(2):
                        b2 = bk[(dt_i * 2 + jh) % 2]
                        mms = []
                        for c in range(SUP):
                            mms.append((b2.t[:, 0:512],
                                        Ktk[:, c * D + dt_i * P:c * D + (dt_i + 1) * P],
                                        Gsb[:, c * D + jh * 512:c * D + (jh + 1) * 512]))
                        b2.mm_group(mms, extra_waits=[t_chain])
                        b2.read("vector", lambda v, b2=b2, dt_i=dt_i, jh=jh: v.tensor_add(
                            WT[:, dt_i * D + jh * 512:dt_i * D + (jh + 1) * 512],
                            WT[:, dt_i * D + jh * 512:dt_i * D + (jh + 1) * 512],
                            b2.t[:, 0:512]))
                t_wupd = pr.tok("vs")
                for dt_i in range(DT):
                    pr.emit("scalar", lambda s, dt_i=dt_i: s.copy(
                        Wbf[:, dt_i * D:(dt_i + 1) * D], WT[:, dt_i * D:(dt_i + 1) * D]),
                        waits=[t_wupd, t_outs], inc="ss")
                for jh in range(2):
                    b2 = bk[2 + jh]
                    mms = []
                    for c in range(SUP):
                        mms.append((b2.t[0:1, 0:512], ones_cl[:, :],
                                    Gsb[:, c * D + jh * 512:c * D + (jh + 1) * 512]))
                    b2.mm_group(mms, extra_waits=[t_chain])
                    b2.read("vector", lambda v, b2=b2, jh=jh: v.tensor_add(
                        b_f32[0:1, jh * 512:(jh + 1) * 512],
                        b_f32[0:1, jh * 512:(jh + 1) * 512], b2.t[0:1, 0:512]))
                t_bupd = pr.tok("vs")
                pr.emit("scalar", lambda s, nb=npar_buf: s.copy(nb[0:1, :], b_f32[0:1, :]),
                        waits=[t_bupd, t_outs], inc="ss")

            t_phaseA = [pr.tok("ts"), pr.tok("vs"), pr.tok("ss"), pr.tok("dsp"), pr.tok("dh2s"),
                        pr.tok("dx0"), pr.tok("dx1"), pr.tok("dq0"), pr.tok("dq1"), pr.tok("dxe")]

            if debug:
                tap("WT", WT[:, :], t_phaseA)
                tap("Gsb", Gsb[:, :], t_phaseA)
                tap("bf32", b_f32[:, :], t_phaseA)
                tap("comb", comb_all[:, :], t_phaseA)
                tap("Kft", Kft[:, :], t_phaseA)
                tap("Ktk", Ktk[:, :], t_phaseA)
                tap("Qft", Qft[:, :], t_phaseA)
                tap("Xsb", Xsb[:, :], t_phaseA)
                tap("hft", hft[:, :], t_phaseA)
                tap("ksa", ksa_r[:, :], t_phaseA)
                tap("vsan", vsan_r[:, :], t_phaseA)
                tap("modcols_ca1", ca1c[:, :], t_phaseA)
                tap("h2s0", h2stg[:, :], t_phaseA)
                tap("ga_bc", ga_bc[:, 0:D], t_phaseA)
            if debug:
                t_phaseA = t_phaseA + [pr.tok("do")]

        if do_passb:
            with ExitStack() as bctx:
                EB = bctx.enter_context
                acc = EB(nc.sbuf_tensor("acc", [P, (NCH // 2) * D], F32))
                h2b = EB(nc.sbuf_tensor("h2b", [P, DT * D], BF16))
                act = EB(nc.sbuf_tensor("act", [P, DT * D], BF16))
                wG = EB(nc.sbuf_tensor("wG", [P, 2 * DT * D], BF16))
                wU = EB(nc.sbuf_tensor("wU", [P, 2 * DT * D], BF16))
                wD = EB(nc.sbuf_tensor("wD", [P, DT * D], BF16))
                sg2 = EB(nc.sbuf_tensor("sg2", [P, 512], BF16))
                ub2 = EB(nc.sbuf_tensor("ub2", [P, 512], BF16))
                gsbr = EB(nc.sbuf_tensor("gsbr", [1, D], BF16))
                usbr = EB(nc.sbuf_tensor("usbr", [1, D], BF16))
                obuf = EB(nc.sbuf_tensor("obuf", [P, D], F32))

                pes = [("sh", 0), ("sh", 1)] + [("ex", e2) for e2 in range(NE)]

                def pe_srcs(kind, idx):
                    if kind == "sh":
                        return (ap(d_sg_scr, idx * P * DT * D, [[DT * D, P], [1, DT * D]]),
                                ap(d_su_scr, idx * P * DT * D, [[DT * D, P], [1, DT * D]]),
                                ap(d_sd_scr, idx * P * DT * D, [[DT * D, P], [1, DT * D]]),
                                [wprep_toks["sgw"], wprep_toks["suw"], wprep_toks["sdw"]])
                    return (ap(d_gw_scr, idx * P * DT * D, [[DT * D, P], [1, DT * D]]),
                            ap(d_uw_scr, idx * P * DT * D, [[DT * D, P], [1, DT * D]]),
                            ap(d_dw_scr, idx * P * DT * D, [[DT * D, P], [1, DT * D]]),
                            [wprep_toks["gw"], wprep_toks["uw"], wprep_toks["dw"]])

                wslot_free = {0: list(t_phaseA), 1: list(t_phaseA)}
                wd_free = list(t_phaseA)
                out_toks = []

                for th in range(2):
                    t_ac = pr.emit("gpsimd", lambda g: g.memset(acc[:, :], 0.0),
                                   waits=t_phaseA + [pr.tok("vs")], inc="gs")
                    acc_guard = [t_ac]
                    for ht in range(DT):
                        pr.emit("sync", lambda sy, ht=ht, th=th: sy.dma_start(
                            ap(h2b, ht * D, [[DT * D, P], [P, DT], [1, P]]),
                            ap(d_h2s, th * DT * P * D + P * ht + 0, [[D, P], [P * D, DT], [1, P]])),
                            waits=t_phaseA + [pr.tok("ts")], inc="dh2", inc_by=16)
                    h2_load = [pr.tok("dh2")]
                    for pi, (kind, idx) in enumerate(pes):
                        slot = pi % 2
                        gsrc, usrc, dsrc, wtoks = pe_srcs(kind, idx)
                        t_lg_ = pr.emit("sync", lambda sy, gsrc=gsrc, slot=slot: sy.dma_start(
                            ap(wG, slot * DT * D, [[2 * DT * D, P], [1, DT * D]]), gsrc),
                            waits=wtoks + wslot_free[slot], inc=f"dwg{slot}", inc_by=16)
                        t_lu_ = pr.emit("sync", lambda sy, usrc=usrc, slot=slot: sy.dma_start(
                            ap(wU, slot * DT * D, [[2 * DT * D, P], [1, DT * D]]), usrc),
                            waits=wtoks + wslot_free[slot], inc=f"dwu{slot}", inc_by=16)
                        t_ld_ = pr.emit("sync", lambda sy, dsrc=dsrc: sy.dma_start(wD[:, :], dsrc),
                                        waits=wtoks + wd_free, inc="dwd", inc_by=16)
                        for (wbuf, tld, dstr) in ((wG, t_lg_, gsbr), (wU, t_lu_, usbr)):
                            for jh in range(2):
                                b2 = bk[jh]
                                mms = []
                                for ht in range(DT):
                                    mms.append((b2.t[0:1, 0:512], sb_bf[:, ht:ht + 1],
                                                wbuf[:, slot * DT * D + ht * D + jh * 512:
                                                     slot * DT * D + ht * D + (jh + 1) * 512]))
                                b2.mm_group(mms, extra_waits=[tld])
                                b2.read("scalar", lambda s, b2=b2, dstr=dstr, jh=jh: s.copy(
                                    dstr[0:1, jh * 512:(jh + 1) * 512], b2.t[0:1, 0:512]))
                        t_bias = pr.tok("ss")
                        for (wbuf, tld) in ((wG, t_lg_), (wU, t_lu_)):
                            for ht in range(DT):
                                pr.emit("vector", lambda v, wbuf=wbuf, ht=ht, slot=slot: v.tensor_scalar_mul(
                                    wbuf[:, slot * DT * D + ht * D:slot * DT * D + (ht + 1) * D],
                                    wbuf[:, slot * DT * D + ht * D:slot * DT * D + (ht + 1) * D],
                                    cb1c[:, ht:ht + 1]),
                                    waits=[t_bias, tld], inc="vs")
                        t_prime = pr.tok("vs")
                        for fj in range(DT):
                            for tb in range(2):
                                bA = bk[(fj * 2 + tb) % 2]
                                bB = bk[4 + (fj * 2 + tb) % 2]
                                tok_off = th * 1024 + tb * 512
                                mmsA, mmsB = [], []
                                for ht in range(DT):
                                    rh = h2b[:, ht * D + tb * 512:ht * D + (tb + 1) * 512]
                                    mmsA.append((bA.t[:, 0:512],
                                                 wG[:, slot * DT * D + ht * D + fj * P:
                                                    slot * DT * D + ht * D + (fj + 1) * P], rh))
                                    mmsB.append((bB.t[:, 0:512],
                                                 wU[:, slot * DT * D + ht * D + fj * P:
                                                    slot * DT * D + ht * D + (fj + 1) * P], rh))
                                mmsA.append((bA.t[:, 0:512], gsbr[0:1, fj * P:(fj + 1) * P], ones_row[:, :]))
                                mmsB.append((bB.t[:, 0:512], usbr[0:1, fj * P:(fj + 1) * P], ones_row[:, :]))
                                bA.mm_group(mmsA, extra_waits=[t_prime] + h2_load)
                                bB.mm_group(mmsB, extra_waits=[t_prime] + h2_load)
                                t_sg = bA.read("scalar", lambda s, bA=bA: s.activation(
                                    sg2[:, :], bA.t[:, 0:512], AF.Silu))
                                t_ub = bB.read("scalar", lambda s, bB=bB: s.activation(
                                    ub2[:, :], bB.t[:, 0:512], AF.Identity))
                                pr.emit("vector", lambda v, fj=fj, tb=tb: v.tensor_mul(
                                    act[:, fj * D + tb * 512:fj * D + (tb + 1) * 512],
                                    sg2[:, :], ub2[:, :]),
                                    waits=[t_sg, t_ub], inc="vs")
                        t_act = pr.tok("vs")
                        for tt in range(DT):
                            c16 = th * 8 + tt
                            for hh in range(2):
                                bC = bk[2 + (tt * 2 + hh) % 2]
                                mms = []
                                for fj in range(DT):
                                    mms.append((bC.t[:, 0:512],
                                                ap(act, fj * D + tt * P, [[DT * D, P], [1, P]]),
                                                wD[:, fj * D + hh * 512:fj * D + (hh + 1) * 512]))
                                bC.mm_group(mms, extra_waits=[t_act, t_ld_])
                                if kind == "sh":
                                    csl = onef_col[:, 0:1]
                                else:
                                    csl = comb_all[:, c16 * NE + idx:c16 * NE + idx + 1]
                                bC.read("vector", lambda v, bC=bC, tt=tt, hh=hh, csl=csl: v.scalar_tensor_tensor(
                                    acc[:, tt * D + hh * 512:tt * D + (hh + 1) * 512],
                                    bC.t[:, 0:512], csl,
                                    acc[:, tt * D + hh * 512:tt * D + (hh + 1) * 512],
                                    op0=ALU.mult, op1=ALU.add),
                                    extra_waits=acc_guard)
                        wslot_free[slot] = [pr.tok("ts")]
                        wd_free = [pr.tok("ts")]
                    t_accdone = pr.tok("vs")
                    for cc in range(8):
                        c16 = th * 8 + cc
                        slot = cc % 2
                        t_xr = pr.emit("sync", lambda sy, c16=c16, slot=slot: sy.dma_start(
                            ap(xb, slot * D, [[2 * D, P], [1, D]]),
                            ap(d_x2s, c16 * P * D, [[D, P], [1, D]])),
                            waits=[t_accdone] + ([pr.tok("vs")] if cc >= 2 else []),
                            inc=f"dx{slot}", inc_by=16)
                        pr.emit("vector", lambda v, cc=cc: v.tensor_mul(
                            obuf[:, :], acc[:, cc * D:(cc + 1) * D], gb_bc[:, :]),
                            waits=[t_accdone, pr.tok("do")], inc="vs")
                        t_o2 = pr.emit("vector", lambda v, slot=slot: v.tensor_add(
                            obuf[:, :], obuf[:, :], xb[:, slot * D:(slot + 1) * D]),
                            waits=[t_xr], inc="vs")
                        out_toks.append(pr.emit("sync", lambda sy, c16=c16: sy.dma_start(
                            ap(d_out, c16 * P * D, [[D, P], [1, D]]), obuf[:, :]),
                            waits=[t_o2], inc="do", inc_by=16))

                pr.q["sync"].append(([out_toks[-1]], lambda sy: None, None))
                pr.lastw["sync"][out_toks[-1][0]] = out_toks[-1][1]

        else:
            pr.q["sync"].append(([t_phaseA[0]], lambda sy: None, None))

        block = octx.enter_context(nc.Block())

        @block.tensor
        def _(t):
            pr.replay("tensor", t)

        @block.vector
        def _(v):
            pr.replay("vector", v)

        @block.scalar
        def _(s):
            pr.replay("scalar", s)

        @block.gpsimd
        def _(g):
            pr.replay("gpsimd", g)

        @block.sync
        def _(sy):
            pr.replay("sync", sy)

    return nc, taps


def _mask(fn):
    r = np.arange(P)[:, None]
    c = np.arange(P)[None, :]
    return fn(r, c).astype(np.float32).astype(ml_dtypes.bfloat16)


def make_inputs(L0_attn_in_seq, L0_attn_theta_K, L0_attn_theta_V, L0_attn_theta_Q,
                L0_attn_W0, L0_attn_b0, L0_ffn_router_w, L0_ffn_gate_w, L0_ffn_up_w,
                L0_ffn_down_w, L0_ffn_shared_gate_w, L0_ffn_shared_up_w,
                L0_ffn_shared_down_w, L0_c, L0_wmod):
    x = np.ascontiguousarray(np.asarray(L0_attn_in_seq, dtype=np.float32))
    f32 = lambda a: np.ascontiguousarray(np.asarray(a, dtype=np.float32))
    base = {
        "thK": f32(np.asarray(L0_attn_theta_K).T),
        "thV": f32(np.asarray(L0_attn_theta_V).T),
        "thQ": f32(np.asarray(L0_attn_theta_Q).T),
        "w0T": f32(np.asarray(L0_attn_W0).T),
        "b0r": f32(np.asarray(L0_attn_b0).reshape(1, D)),
        "rw": f32(L0_ffn_router_w),
        "gw": f32(L0_ffn_gate_w), "uw": f32(L0_ffn_up_w), "dw": f32(L0_ffn_down_w),
        "sgw": f32(L0_ffn_shared_gate_w), "suw": f32(L0_ffn_shared_up_w),
        "sdw": f32(L0_ffn_shared_down_w),
        "wmod": f32(L0_wmod),
        "identb": np.eye(P, dtype=np.float32).astype(ml_dtypes.bfloat16),
        "mune": _mask(lambda r, c: np.where(r < c, -1.0, 0.0)),
        "mlne": _mask(lambda r, c: np.where(r > c, -1.0, 0.0)),
        "muinc": _mask(lambda r, c: np.where(r <= c, 1.0, 0.0)),
    }
    cmat = f32(L0_c)
    in_maps = []
    for k in range(8):
        b, h = k >> 1, k & 1
        gm = np.zeros((P, 2), dtype=np.float32)
        gm[:, 0] = -2.0 * LR / D * h
        gm[:, 1] = -2.0 * LR / D
        m = dict(base)
        m["xm"] = np.ascontiguousarray(x[b, h * TOWN:(h + 1) * TOWN])
        m["xp"] = np.ascontiguousarray(x[b, 0:TOWN])
        m["gmask"] = gm
        m["cvec"] = np.ascontiguousarray(cmat[b].reshape(D, 1))
        in_maps.append(m)
    return in_maps


def kernel(**inputs):
    in_maps = make_inputs(**inputs)
    nc, _ = build_program(debug=False)
    res = run_bass_kernel_spmd(nc, in_maps, core_ids=list(range(8)))
    out = np.empty((4, 4096, D), dtype=np.float32)
    for k in range(8):
        b, h = k >> 1, k & 1
        out[b, h * TOWN:(h + 1) * TOWN] = res.results[k]["out"]
    return out


# version 03
# speedup vs baseline: 3.2281x; 3.2281x over previous
import numpy as np
import ml_dtypes
from contextlib import ExitStack

import concourse.bass as bass
import concourse.mybir as mybir
from concourse.bass_utils import run_bass_kernel_spmd

F32 = mybir.dt.float32
BF16 = mybir.dt.bfloat16
AF = mybir.ActivationFunctionType
ALU = mybir.AluOpType
AX = mybir.AxisListType

P = 128
D = 1024
DT = 8
SUP = 4
NSUP = 4
NCH = 16
TOWN = 2048
LR = 0.01
EPS = 1e-5
NE = 8
QUEUES = ("tensor", "vector", "scalar", "gpsimd", "sync")
QSEM = {"vector": "vs", "scalar": "ss", "gpsimd": "gs", "sync": "do", "tensor": "ts"}


class Prog:

    def __init__(self, nc, ctx):
        self.nc = nc
        self.ctx = ctx
        self.q = {k: [] for k in QUEUES}
        self.cnt = {}
        self.hand = {}
        self.lastw = {k: {} for k in QUEUES}

    def sem(self, name):
        if name not in self.hand:
            self.hand[name] = self.ctx.enter_context(self.nc.semaphore(name))
            self.cnt[name] = 0
        return self.hand[name]

    def tok(self, name):
        self.sem(name)
        return (name, self.cnt[name])

    def emit(self, queue, fn, waits=(), inc=None, inc_by=1):
        ws = []
        for w in waits:
            if w is None:
                continue
            s, v = w
            if v <= 0:
                continue
            if self.lastw[queue].get(s, 0) >= v:
                continue
            self.lastw[queue][s] = v
            self.sem(s)
            ws.append((s, v))
        tk = None
        it = None
        if inc is not None:
            self.sem(inc)
            self.cnt[inc] += inc_by
            tk = (inc, self.cnt[inc])
            it = (inc, inc_by)
        self.q[queue].append((ws, fn, it))
        return tk

    def replay(self, queue, eng):
        for ws, fn, it in self.q[queue]:
            for s, v in ws:
                eng.wait_ge(self.hand[s], v)
            ins = fn(eng)
            if it is not None:
                assert ins is not None, f"{queue}: inc {it} but fn returned None"
                ins.then_inc(self.hand[it[0]], it[1])


class Bank:

    def __init__(self, pr, name, dtype=F32):
        width = 512 if dtype == F32 else 1024
        self.pr = pr
        self.t = pr.ctx.enter_context(pr.nc.psum_tensor(name, [P, width], dtype))
        self.w = None
        self.readers = []

    def mm_group(self, mms, extra_waits=()):
        pr = self.pr
        n = len(mms)
        waits = [w for w in list(extra_waits) + self.readers if w is not None]
        tk = None
        for i, (o, l, r) in enumerate(mms):
            tk = pr.emit(
                "tensor",
                (lambda t, o=o, l=l, r=r, st=(i == 0), sp=(i == n - 1):
                 t.matmul(o, l, r, start=st, stop=sp)),
                waits=waits if i == 0 else (),
                inc="ts" if i == n - 1 else None)
        self.w = tk
        self.readers = []
        return tk

    def read(self, queue, fn, extra_waits=()):
        tk = self.pr.emit(queue, fn, waits=[self.w] + list(extra_waits), inc=QSEM[queue])
        self.readers.append(tk)
        return tk


def ap(t, offset, dims):
    return bass.AP(t, offset, dims)


_BUILD_N = [0]


def build_program(debug=False, n_super=2 * NSUP, do_passb=True):
    _BUILD_N[0] += 1
    _tp = "tap_" if _BUILD_N[0] == 1 else f"tap{_BUILD_N[0]}_"
    nc = bass.Bass("TRN2", target_bir_lowering=False)

    d_xm = nc.dram_tensor("xm", [TOWN, D], F32, kind="ExternalInput")
    d_xp = nc.dram_tensor("xp", [TOWN, D], F32, kind="ExternalInput")
    d_gmask = nc.dram_tensor("gmask", [P, 2], F32, kind="ExternalInput")
    d_cvec = nc.dram_tensor("cvec", [D, 1], F32, kind="ExternalInput")
    d_wmod = nc.dram_tensor("wmod", [D, 6 * D], F32, kind="ExternalInput")
    d_thK = nc.dram_tensor("thK", [D, D], F32, kind="ExternalInput")
    d_thV = nc.dram_tensor("thV", [D, D], F32, kind="ExternalInput")
    d_thQ = nc.dram_tensor("thQ", [D, D], F32, kind="ExternalInput")
    d_w0T = nc.dram_tensor("w0T", [D, D], F32, kind="ExternalInput")
    d_b0 = nc.dram_tensor("b0r", [1, D], F32, kind="ExternalInput")
    d_rw = nc.dram_tensor("rw", [D, NE], F32, kind="ExternalInput")
    d_gw = nc.dram_tensor("gw", [NE, D, D], F32, kind="ExternalInput")
    d_uw = nc.dram_tensor("uw", [NE, D, D], F32, kind="ExternalInput")
    d_dw = nc.dram_tensor("dw", [NE, D, D], F32, kind="ExternalInput")
    d_sgw = nc.dram_tensor("sgw", [D, 2 * D], F32, kind="ExternalInput")
    d_suw = nc.dram_tensor("suw", [D, 2 * D], F32, kind="ExternalInput")
    d_sdw = nc.dram_tensor("sdw", [2 * D, D], F32, kind="ExternalInput")
    d_ident = nc.dram_tensor("identb", [P, P], BF16, kind="ExternalInput")
    d_mune = nc.dram_tensor("mune", [P, P], BF16, kind="ExternalInput")
    d_mlne = nc.dram_tensor("mlne", [P, P], BF16, kind="ExternalInput")
    d_muinc = nc.dram_tensor("muinc", [P, P], BF16, kind="ExternalInput")

    d_out = nc.dram_tensor("out", [TOWN, D], F32, kind="ExternalOutput")

    d_x2s = nc.dram_tensor("x2s", [TOWN, D], F32)
    d_h2s = nc.dram_tensor("h2s", [NCH * P, D], BF16)
    d_modscr = nc.dram_tensor("modscr", [1, 6 * D], F32)
    d_thq_scr = nc.dram_tensor("thq_scr", [D, D], BF16)
    d_gw_scr = nc.dram_tensor("gw_scr", [NE, P, DT * D], BF16)
    d_uw_scr = nc.dram_tensor("uw_scr", [NE, P, DT * D], BF16)
    d_dw_scr = nc.dram_tensor("dw_scr", [NE, P, DT * D], BF16)
    d_sg_scr = nc.dram_tensor("sg_scr", [2, P, DT * D], BF16)
    d_su_scr = nc.dram_tensor("su_scr", [2, P, DT * D], BF16)
    d_sd_scr = nc.dram_tensor("sd_scr", [2, P, DT * D], BF16)

    taps = {}

    with ExitStack() as octx:
        pr = Prog(nc, octx)
        E = octx.enter_context

        def tap(name, apv, waits):
            if not debug:
                return
            dt_ = nc.dram_tensor(_tp + name, list(apv.shape), apv.dtype, kind="ExternalOutput")
            taps[name] = True
            pr.emit("sync", lambda sy, a=apv, d2=dt_: sy.dma_start(d2[...], a),
                    waits=waits, inc="do", inc_by=16)

        s_ident = E(nc.sbuf_tensor("s_ident", [P, P], BF16))
        s_mune = E(nc.sbuf_tensor("s_mune", [P, P], BF16))
        s_mlne = E(nc.sbuf_tensor("s_mlne", [P, P], BF16))
        s_muinc = E(nc.sbuf_tensor("s_muinc", [P, P], BF16))
        ones_r1 = E(nc.sbuf_tensor("ones_r1", [1, P], BF16))
        ones_row = E(nc.sbuf_tensor("ones_row", [1, 512], BF16))
        ones_r1f = E(nc.sbuf_tensor("ones_r1f", [1, P], F32))
        ones_cl = E(nc.sbuf_tensor("ones_cl", [P, 1], BF16))
        onef_col = E(nc.sbuf_tensor("onef_col", [P, 1], F32))
        s_gmask = E(nc.sbuf_tensor("s_gmask", [P, 2], F32))
        gb_bc = E(nc.sbuf_tensor("gb_bc", [P, D], F32))
        ga_bc = E(nc.sbuf_tensor("ga_bc", [P, D], F32))
        comb_all = E(nc.sbuf_tensor("comb_all", [P, NCH * NE], F32))
        b_f32 = E(nc.sbuf_tensor("b_f32", [1, D], F32))
        b_bfA = E(nc.sbuf_tensor("b_bfA", [1, D], BF16))
        b_bfB = E(nc.sbuf_tensor("b_bfB", [1, D], BF16))
        xb = E(nc.sbuf_tensor("xb", [P, 2 * D], F32))
        lnt = E(nc.sbuf_tensor("lnt", [P, 16], F32))
        hbf = E(nc.sbuf_tensor("hbf", [P, D], BF16))
        lg = E(nc.sbuf_tensor("lg", [P, NE], F32))
        zex = E(nc.sbuf_tensor("zex", [P, NE], F32))
        sc8 = E(nc.sbuf_tensor("sc8", [P, NE], F32))
        top8 = E(nc.sbuf_tensor("top8", [P, 8], F32))
        rcol = E(nc.sbuf_tensor("rcol", [P, 4], F32))
        h2stg = E(nc.sbuf_tensor("h2stg", [P, D], BF16))
        ca1c = E(nc.sbuf_tensor("ca1c", [P, DT], F32))
        ca1nc = E(nc.sbuf_tensor("ca1nc", [P, DT], F32))
        cb1c = E(nc.sbuf_tensor("cb1c", [P, DT], F32))
        sa_bf = E(nc.sbuf_tensor("sa_bf", [P, DT], BF16))
        sb_bf = E(nc.sbuf_tensor("sb_bf", [P, DT], BF16))
        rw_bf = E(nc.sbuf_tensor("rw_bf", [P, DT * NE], BF16))
        rsb_r = E(nc.sbuf_tensor("rsb_r", [1, NE], BF16))
        ksa_r = E(nc.sbuf_tensor("ksa_r", [1, D], BF16))
        qsa_r = E(nc.sbuf_tensor("qsa_r", [1, D], BF16))
        vsan_r = E(nc.sbuf_tensor("vsan_r", [1, D], BF16))

        bk = [Bank(pr, f"bk{i}") for i in range(7)]
        bkT = Bank(pr, "bkT", BF16)

        cdb = E(nc.sbuf_tensor("cdb", [P, 4], F32))
        pr.emit("gpsimd", lambda g: g.memset(cdb[:, 0:1], 2 * LR / D), inc="gs")
        pr.emit("gpsimd", lambda g: g.memset(cdb[:, 1:2], 1.0), inc="gs")
        pr.emit("gpsimd", lambda g: g.memset(cdb[:, 2:3], 0.0), inc="gs")
        pr.emit("gpsimd", lambda g: g.memset(cdb[:, 3:4], -1.0), inc="gs")
        nc.const_aps.register(F32, 2 * LR / D, cdb[:, 0:1])
        nc.const_aps.register(F32, 1.0, cdb[:, 1:2])
        nc.const_aps.register(F32, 0.0, cdb[:, 2:3])
        nc.const_aps.register(F32, -1.0, cdb[:, 3:4])

        for t_, d_ in ((s_ident, d_ident), (s_mune, d_mune), (s_mlne, d_mlne), (s_muinc, d_muinc)):
            pr.emit("sync", lambda sy, t2=t_, d2=d_: sy.dma_start(t2[:, :], d2[:, :]),
                    inc="dini", inc_by=16)
        pr.emit("sync", lambda sy: sy.dma_start(s_gmask[:, :], d_gmask[:, :]), inc="dini", inc_by=16)
        pr.emit("sync", lambda sy: sy.dma_start(b_f32[:, :], d_b0[:, :]), inc="dini", inc_by=16)
        pr.emit("gpsimd", lambda g: g.memset(ones_r1[:, :], 1.0), inc="gs")
        pr.emit("gpsimd", lambda g: g.memset(ones_row[:, :], 1.0), inc="gs")
        pr.emit("gpsimd", lambda g: g.memset(ones_cl[:, :], 1.0), inc="gs")
        pr.emit("gpsimd", lambda g: g.memset(ones_r1f[:, :], 1.0), inc="gs")
        pr.emit("gpsimd", lambda g: g.memset(onef_col[:, :], 1.0), inc="gs")
        t_dl0 = pr.tok("dini")
        t_ms0 = pr.tok("gs")

        t_bb0 = pr.emit("scalar", lambda s: s.copy(b_bfA[0:1, :], b_f32[0:1, :]),
                        waits=[t_dl0], inc="ss")

        wprep_toks = {}
        for hh in range(2):
            pr.emit("gpsimd", lambda g, hh=hh: g.dma_start(
                ap(d_thq_scr, hh * 512 * D, [[D, 512], [1, D]]),
                ap(d_thQ, hh * 512 * D, [[D, 512], [1, D]])), inc="dtq", inc_by=16)
        wprep_toks["thq"] = pr.tok("dtq")

        with ExitStack() as actx:
            EA = actx.enter_context

            with ExitStack() as wctx:
                wmb = wctx.enter_context(nc.sbuf_tensor("wmb", [P, 2 * 6 * D], F32))
                silc = wctx.enter_context(nc.sbuf_tensor("silc", [P, DT], F32))
                modr = wctx.enter_context(nc.sbuf_tensor("modr", [1, 6 * D], F32))
                tmpr = wctx.enter_context(nc.sbuf_tensor("tmpr", [1, 3 * D], F32))

                t_cv = pr.emit("sync", lambda sy: sy.dma_start(
                    silc[:, :], ap(d_cvec, 0, [[1, P], [P, DT]]),
                    allow_slow_non_contiguous=True), inc="dcv", inc_by=16)
                t_sil = pr.emit("scalar", lambda s: s.activation(silc[:, :], silc[:, :], AF.Silu),
                                waits=[t_cv], inc="ss")
                prev_mm = [None] * DT
                for pas in range(2):
                    mm_toks = []
                    for dt_i in range(DT):
                        slot = dt_i % 2
                        wl = [mm_toks[dt_i - 2]] if dt_i >= 2 else [prev_mm[DT - 2 + dt_i]]
                        tkd = pr.emit("sync", lambda sy, dt_i=dt_i, slot=slot: sy.dma_start(
                            ap(wmb, slot * 6 * D, [[2 * 6 * D, P], [1, 6 * D]]),
                            ap(d_wmod, dt_i * P * 6 * D, [[6 * D, P], [1, 6 * D]])),
                            waits=wl, inc=f"dwm{slot}", inc_by=16)
                        tk = None
                        for jj in range(6):
                            jb = pas * 6 + jj
                            b2 = bk[jj]
                            o = b2.t[0:1, 0:512]
                            l = silc[:, dt_i:dt_i + 1]
                            r = wmb[:, slot * 6 * D + jb * 512:slot * 6 * D + (jb + 1) * 512]
                            wl2 = [tkd]
                            if dt_i == 0:
                                wl2 += [t_sil] + b2.readers
                            tk = pr.emit("tensor",
                                         (lambda t, o=o, l=l, r=r, st=(dt_i == 0), sp=(dt_i == DT - 1):
                                          t.matmul(o, l, r, start=st, stop=sp)),
                                         waits=wl2, inc="ts" if jj == 5 else None)
                        mm_toks.append(tk)
                        if dt_i == DT - 1:
                            for jj in range(6):
                                bk[jj].w = tk
                                bk[jj].readers = []
                    for jj in range(6):
                        jb = pas * 6 + jj
                        bk[jj].read("vector", lambda v, jj=jj, jb=jb: v.tensor_copy(
                            modr[0:1, jb * 512:(jb + 1) * 512], bk[jj].t[0:1, 0:512]))
                    prev_mm = mm_toks
                t_mod = pr.tok("vs")

                for idx in (2, 5):
                    g_sl = modr[0:1, idx * D:(idx + 1) * D]
                    a_sl = tmpr[0:1, 0:D]
                    bb_sl = tmpr[0:1, D:2 * D]
                    s_sl = tmpr[0:1, 2 * D:3 * D]
                    pr.emit("vector", lambda v, g=g_sl, a=a_sl: v.tensor_scalar_max(a, g, 0.25),
                            waits=[t_mod], inc="vs")
                    pr.emit("vector", lambda v, g=g_sl, b3=bb_sl: v.tensor_scalar_min(b3, g, -0.25), inc="vs")
                    tc0 = pr.emit("vector", lambda v, g=g_sl, s2=s_sl: v.tensor_single_scalar(s2, g, 0.0, op=ALU.is_ge), inc="vs")
                    tc1 = pr.emit("vector", lambda v, a=a_sl, b3=bb_sl: v.tensor_sub(a, a, b3),
                                  waits=[tc0], inc="vs")
                    tc2 = pr.emit("vector", lambda v, a=a_sl, s2=s_sl: v.tensor_mul(a, a, s2),
                                  waits=[tc1], inc="vs")
                    pr.emit("vector", lambda v, g=g_sl, a=a_sl, b3=bb_sl: v.tensor_add(g, b3, a),
                            waits=[tc2], inc="vs")
                t_clamp = pr.tok("vs")

                for row_i, dst in ((2, ga_bc), (5, gb_bc)):
                    for jh in range(2):
                        b2 = bk[6]
                        b2.mm_group(
                            [(b2.t[:, 0:512], ones_r1f[:, :],
                              modr[0:1, row_i * D + jh * 512:row_i * D + (jh + 1) * 512])],
                            extra_waits=[t_clamp, t_ms0])
                        b2.read("vector", lambda v, dst=dst, jh=jh, b2=b2: v.tensor_copy(
                            dst[:, jh * 512:(jh + 1) * 512], b2.t[:, 0:512]))
                t_ms = pr.emit("sync", lambda sy: sy.dma_start(d_modscr[:, :], modr[:, :]),
                               waits=[t_clamp], inc="dms", inc_by=16)
                t_l1 = pr.emit("sync", lambda sy: sy.dma_start(
                    ca1c[:, :], ap(d_modscr, 1 * D, [[1, P], [P, DT]]),
                    allow_slow_non_contiguous=True), waits=[t_ms], inc="dca", inc_by=16)
                t_l2 = pr.emit("sync", lambda sy: sy.dma_start(
                    cb1c[:, :], ap(d_modscr, 4 * D, [[1, P], [P, DT]]),
                    allow_slow_non_contiguous=True), waits=[t_ms], inc="dcb", inc_by=16)
                t_l3 = pr.emit("gpsimd", lambda g: g.dma_start(
                    sa_bf[:, :], ap(d_modscr, 0 * D, [[1, P], [P, DT]]),
                    allow_slow_non_contiguous=True), waits=[t_ms], inc="dsa", inc_by=16)
                t_l4 = pr.emit("gpsimd", lambda g: g.dma_start(
                    sb_bf[:, :], ap(d_modscr, 3 * D, [[1, P], [P, DT]]),
                    allow_slow_non_contiguous=True), waits=[t_ms], inc="dsb2", inc_by=16)
                t_c1 = pr.emit("vector", lambda v: v.tensor_scalar_add(ca1c[:, :], ca1c[:, :], 1.0),
                               waits=[t_l1], inc="vs")
                t_c2 = pr.emit("vector", lambda v: v.tensor_scalar_add(cb1c[:, :], cb1c[:, :], 1.0),
                               waits=[t_l2], inc="vs")
                t_c3 = pr.emit("vector", lambda v: v.tensor_scalar_mul(ca1nc[:, :], ca1c[:, :], -1.0),
                               waits=[t_c1], inc="vs")
                t_wmod_ts = pr.tok("ts")
                t_wmod_vs = pr.tok("vs")

            thK = EA(nc.sbuf_tensor("s_thK", [P, DT * D], BF16))
            thVn = EA(nc.sbuf_tensor("s_thVn", [P, DT * D], BF16))
            thS = EA(nc.sbuf_tensor("s_thS", [P, 2 * D], BF16))
            WT = EA(nc.sbuf_tensor("WT", [P, DT * D], F32))
            Wbf = EA(nc.sbuf_tensor("Wbf", [P, DT * D], BF16))
            hft = EA(nc.sbuf_tensor("hft", [P, DT * 512], BF16))
            Kft = EA(nc.sbuf_tensor("Kft", [P, DT * 512], BF16))
            Ktk = EA(nc.sbuf_tensor("Ktk", [P, SUP * D], BF16))
            Qft = EA(nc.sbuf_tensor("Qft", [P, DT * 512], BF16))
            Gsb = EA(nc.sbuf_tensor("Gsb", [P, SUP * D], BF16))
            Scr = EA(nc.sbuf_tensor("Scr", [P, 6 * P], BF16))
            Sqr = EA(nc.sbuf_tensor("Sqr", [P, 10 * P], BF16))
            Usb = EA(nc.sbuf_tensor("Usb", [P, P], BF16))
            Bp = EA(nc.sbuf_tensor("Bp", [P, 2 * SUP * P], BF16))
            Btp = EA(nc.sbuf_tensor("Btp", [P, 2 * SUP * P], BF16))
            Dsb = EA(nc.sbuf_tensor("Dsb", [P, 2 * P], BF16))
            Xsb = EA(nc.sbuf_tensor("Xsb", [P, SUP * P], BF16))
            RHSs = EA(nc.sbuf_tensor("RHSs", [P, D], BF16))
            x2b = EA(nc.sbuf_tensor("x2b", [P, 2 * D], F32))
            xeb = EA(nc.sbuf_tensor("xeb", [P, 2 * D], F32))

            for dt_i in range(DT):
                pr.emit("gpsimd", lambda g, dt_i=dt_i: g.dma_start(
                    ap(thK, dt_i * D, [[DT * D, P], [1, D]]),
                    ap(d_thK, dt_i * P * D, [[D, P], [1, D]])),
                    waits=[t_wmod_ts, t_wmod_vs], inc="dthk", inc_by=16)
            t_thk_all = pr.tok("dthk")
            thk_toks = [t_thk_all] * DT
            for dt_i in range(DT):
                pr.emit("gpsimd", lambda g, dt_i=dt_i: g.dma_start(
                    ap(thVn, dt_i * D, [[DT * D, P], [1, D]]),
                    ap(d_thV, dt_i * P * D, [[D, P], [1, D]])),
                    waits=[t_wmod_ts, t_wmod_vs], inc="dthv", inc_by=16)
            t_thv_all = pr.tok("dthv")
            thv_toks = [t_thv_all] * DT

            def bias_row(dst_row, src_mat, toks, scale):
                for jh in range(2):
                    b2 = bk[jh]
                    mms = []
                    for dt_i in range(DT):
                        mms.append((b2.t[0:1, 0:512], sa_bf[:, dt_i:dt_i + 1],
                                    src_mat[:, dt_i * D + jh * 512:dt_i * D + (jh + 1) * 512]))
                    b2.mm_group(mms, extra_waits=list(toks) + [t_l3])
                    b2.read("scalar", lambda s, jh=jh, b2=b2, dst_row=dst_row, scale=scale:
                            s.activation(dst_row[0:1, jh * 512:(jh + 1) * 512], b2.t[0:1, 0:512],
                                         AF.Identity, scale=scale))
            bias_row(ksa_r, thK, thk_toks, 1.0)
            t_ksa = pr.tok("ss")
            bias_row(vsan_r, thVn, thv_toks, -1.0)
            t_vsa = pr.tok("ss")
            for dt_i in range(DT):
                pr.emit("vector", lambda v, dt_i=dt_i: v.tensor_scalar_mul(
                    thK[:, dt_i * D:(dt_i + 1) * D], thK[:, dt_i * D:(dt_i + 1) * D],
                    ca1c[:, dt_i:dt_i + 1]),
                    waits=[t_ksa, t_c1, thk_toks[dt_i]], inc="vs")
                pr.emit("vector", lambda v, dt_i=dt_i: v.tensor_scalar_mul(
                    thVn[:, dt_i * D:(dt_i + 1) * D], thVn[:, dt_i * D:(dt_i + 1) * D],
                    ca1nc[:, dt_i:dt_i + 1]),
                    waits=[t_vsa, t_c3, thv_toks[dt_i]], inc="vs")
            t_thprime = pr.tok("vs")

            qcons = {0: None, 1: None}
            for dt_i in range(DT):
                slot = dt_i % 2
                tkd = pr.emit("sync", lambda sy, dt_i=dt_i, slot=slot: sy.dma_start(
                    ap(thS, slot * D, [[2 * D, P], [1, D]]),
                    ap(d_thq_scr, dt_i * P * D, [[D, P], [1, D]])),
                    waits=[wprep_toks["thq"], qcons[slot]], inc=f"dq{slot}", inc_by=16)
                tk = None
                for jh in range(2):
                    b2 = bk[2 + jh]
                    o = b2.t[0:1, 0:512]
                    l = sa_bf[:, dt_i:dt_i + 1]
                    r = thS[:, slot * D + jh * 512:slot * D + (jh + 1) * 512]
                    wl = [tkd, t_l3]
                    if dt_i == 0:
                        wl += b2.readers
                    tk = pr.emit("tensor",
                                 (lambda t, o=o, l=l, r=r, st=(dt_i == 0), sp=(dt_i == DT - 1):
                                  t.matmul(o, l, r, start=st, stop=sp)),
                                 waits=wl, inc="ts" if jh == 1 else None)
                qcons[slot] = tk
            for jh in range(2):
                b2 = bk[2 + jh]
                b2.w = qcons[1]
                b2.readers = []
                b2.read("scalar", lambda s, jh=jh, b2=b2: s.activation(
                    qsa_r[0:1, jh * 512:(jh + 1) * 512], b2.t[0:1, 0:512], AF.Identity))
            t_qsa = pr.tok("ss")

            t_rwl = pr.emit("gpsimd", lambda g: g.dma_start(
                rw_bf[:, :], ap(d_rw, 0, [[NE, P], [P * NE, DT], [1, NE]])),
                waits=[t_wmod_ts, t_wmod_vs], inc="drw", inc_by=16)
            b2 = bk[4]
            mms = [(b2.t[0:1, 0:NE], sb_bf[:, dt_i:dt_i + 1], rw_bf[:, dt_i * NE:(dt_i + 1) * NE])
                   for dt_i in range(DT)]
            b2.mm_group(mms, extra_waits=[t_rwl, t_l4])
            b2.read("scalar", lambda s, b2=b2: s.activation(rsb_r[0:1, :], b2.t[0:1, 0:NE], AF.Identity))
            t_rsb = pr.tok("ss")
            for dt_i in range(DT):
                pr.emit("vector", lambda v, dt_i=dt_i: v.tensor_scalar_mul(
                    rw_bf[:, dt_i * NE:(dt_i + 1) * NE], rw_bf[:, dt_i * NE:(dt_i + 1) * NE],
                    cb1c[:, dt_i:dt_i + 1]),
                    waits=[t_rsb, t_c2], inc="vs")


            def wprep_family(name, pairs):
                for s_ap, d_ap in pairs:
                    pr.emit("gpsimd", lambda g, s2=s_ap, d2=d_ap: g.dma_start(d2, s2),
                            inc="dp", inc_by=16)
                wprep_toks[name] = pr.tok("dp")

            def gu_pairs(src, dst):
                out = []
                for e2 in range(NE):
                    for ht in range(DT):
                        out.append((ap(src, e2 * D * D + ht * P * D, [[D, P], [1, D]]),
                                    ap(dst, e2 * P * DT * D + ht * D, [[DT * D, P], [1, D]])))
                return out
            wprep_family("gw", gu_pairs(d_gw, d_gw_scr))
            wprep_family("uw", gu_pairs(d_uw, d_uw_scr))
            wprep_family("dw", gu_pairs(d_dw, d_dw_scr))

            def sh_pairs(src, dst):
                out = []
                for fh in range(2):
                    for ht in range(DT):
                        out.append((ap(src, ht * P * 2 * D + fh * D, [[2 * D, P], [1, D]]),
                                    ap(dst, fh * P * DT * D + ht * D, [[DT * D, P], [1, D]])))
                return out
            wprep_family("sgw", sh_pairs(d_sgw, d_sg_scr))
            wprep_family("suw", sh_pairs(d_suw, d_su_scr))
            sd_pairs = []
            for fh in range(2):
                for fj in range(DT):
                    sd_pairs.append((ap(d_sdw, (fh * D + fj * P) * D, [[D, P], [1, D]]),
                                     ap(d_sd_scr, fh * P * DT * D + fj * D, [[DT * D, P], [1, D]])))
            wprep_family("sdw", sd_pairs)
            dp_all = pr.tok("dp")
            for k2 in ("gw", "uw", "dw", "sgw", "suw", "sdw"):
                wprep_toks[k2] = dp_all

            for dt_i in range(DT):
                pr.emit("sync", lambda sy, dt_i=dt_i: sy.dma_start(
                    ap(WT, dt_i * D, [[DT * D, P], [1, D]]),
                    ap(d_w0T, dt_i * P * D, [[D, P], [1, D]])),
                    waits=[t_wmod_ts, t_wmod_vs], inc="dw0t", inc_by=16)
            t_w0_all = pr.tok("dw0t")
            for dt_i in range(DT):
                pr.emit("scalar", lambda s, dt_i=dt_i: s.copy(
                    Wbf[:, dt_i * D:(dt_i + 1) * D], WT[:, dt_i * D:(dt_i + 1) * D]),
                    waits=[t_w0_all], inc="ss")
            t_w0 = pr.tok("ss")

            t_prologue = [pr.tok("ts"), pr.tok("vs"), pr.tok("ss"), pr.tok("gs"),
                          t_w0, t_bb0, t_thprime, t_qsa]

            ln_done = {}
            x_dma = {}
            x2_spill_tok = None
            h2_guard = [None, None]

            def x_src(s_abs, ci):
                d_src = d_xp if s_abs < NSUP else d_xm
                c16 = (s_abs % NSUP) * SUP + ci
                return ap(d_src, c16 * P * D, [[D, P], [1, D]])

            def ensure_xdma(cl):
                if cl >= 2 * NSUP * SUP or cl in x_dma:
                    return
                slot = cl % 2
                wl = list(t_prologue) if cl == 0 else []
                if cl - 2 >= 0:
                    wl += list(ln_done[cl - 2])
                x_dma[cl] = pr.emit("sync", lambda sy, cl=cl, slot=slot: sy.dma_start(
                    ap(xb, slot * D, [[2 * D, P], [1, D]]), x_src(cl // SUP, cl % SUP)),
                    waits=wl, inc=f"dx{slot}", inc_by=16)

            def pair_id_s(c, cp):
                return c * (c - 1) // 2 + cp

            def pair_id_q(c, cp):
                return c * (c + 1) // 2 + cp

            def emit_ln(x_ap, wait0, hazard_guard):
                pr.emit("vector", lambda v, x3=x_ap: v.bn_stats(lnt[:, 0:6], x3[:, 0:512]),
                        waits=wait0 + [pr.tok("ss")], inc="vs")
                t_b2 = pr.emit("vector", lambda v, x3=x_ap: v.bn_stats(lnt[:, 6:12], x3[:, 512:1024]), inc="vs")
                t_ag = pr.emit("vector", lambda v: v.bn_aggr(lnt[:, 12:14], lnt[:, 0:12]),
                               waits=[t_b2], inc="vs")
                t_v1 = pr.emit("vector", lambda v: v.tensor_scalar_add(lnt[:, 14:15], lnt[:, 13:14], EPS),
                               waits=[t_ag], inc="vs")
                t_s1 = pr.emit("scalar", lambda s: s.activation(lnt[:, 14:15], lnt[:, 14:15], AF.Sqrt),
                               waits=[t_v1], inc="ss")
                t_rc = pr.emit("vector", lambda v: v.reciprocal(lnt[:, 15:16], lnt[:, 14:15]),
                               waits=[t_s1], inc="vs")
                t_v3 = pr.emit("vector", lambda v: v.scalar_tensor_tensor(
                    lnt[:, 14:15], lnt[:, 12:13], -1.0, lnt[:, 15:16], op0=ALU.mult, op1=ALU.mult),
                    waits=[t_rc], inc="vs")
                t_hn = pr.emit("scalar", lambda s, x3=x_ap: s.activation(
                    hbf[:, :], x3, AF.Identity, bias=lnt[:, 14:15], scale=lnt[:, 15:16]),
                    waits=[t_v3] + hazard_guard, inc="ss")
                return t_v3, t_hn

            def emit_transposes(t_hn, dst_ap_fn, extra=()):
                tk = None
                for i in range(DT):
                    tk = pr.emit("tensor",
                                 (lambda t, i=i: t.transpose(
                                     bkT.t[:, i * P:(i + 1) * P], hbf[:, i * P:(i + 1) * P],
                                     s_ident[:, :])),
                                 waits=([t_hn] + bkT.readers) if i == 0 else (),
                                 inc="ts" if i == DT - 1 else None)
                bkT.w = tk
                bkT.readers = []
                return bkT.read("scalar", dst_ap_fn, extra_waits=list(extra))

            for s_abs in range(n_super):
                is_m = s_abs >= NSUP
                par_buf = b_bfA if s_abs % 2 == 0 else b_bfB
                npar_buf = b_bfB if s_abs % 2 == 0 else b_bfA
                gcol = 1 if is_m else 0
                t_super_ts0 = pr.tok("ts")

                for ci in range(SUP):
                    cl = s_abs * SUP + ci
                    ensure_xdma(cl)
                    ensure_xdma(cl + 1)
                    slot = cl % 2
                    xsl = xb[:, slot * D:(slot + 1) * D]
                    t_v3, t_hn = emit_ln(xsl, [x_dma[cl]], [pr.tok("ts")])
                    ln_done[cl] = (t_v3, t_hn)
                    ensure_xdma(cl + 2)
                    emit_transposes(t_hn, lambda s, ci=ci: s.copy(
                        ap(hft, ci * P, [[DT * 512, P], [512, DT], [1, P]]), bkT.t[:, :]))
                t_hft = pr.tok("ss")

                for gi, (ci, jh) in enumerate([(c, j) for c in range(SUP) for j in range(2)]):
                    b2 = bk[gi % 2]
                    mms = []
                    for dt_i in range(DT):
                        mms.append((b2.t[:, 0:512],
                                    hft[:, dt_i * 512 + ci * P:dt_i * 512 + (ci + 1) * P],
                                    thK[:, dt_i * D + jh * 512:dt_i * D + (jh + 1) * 512]))
                    mms.append((b2.t[:, 0:512], ones_r1[:, :], ksa_r[0:1, jh * 512:(jh + 1) * 512]))
                    b2.mm_group(mms, extra_waits=[t_hft])
                    b2.read("vector", lambda v, ci=ci, jh=jh, b2=b2: v.tensor_copy(
                        Ktk[:, ci * D + jh * 512:ci * D + (jh + 1) * 512], b2.t[:, 0:512]),
                        extra_waits=[t_super_ts0])
                for et in range(DT):
                    b2 = bk[2 + et % 2]
                    mms = []
                    for dt_i in range(DT):
                        mms.append((b2.t[:, 0:512],
                                    thK[:, dt_i * D + et * P:dt_i * D + (et + 1) * P],
                                    hft[:, dt_i * 512:dt_i * 512 + 512]))
                    mms.append((b2.t[:, 0:512], ksa_r[0:1, et * P:(et + 1) * P], ones_row[:, :]))
                    b2.mm_group(mms, extra_waits=[t_hft])
                    b2.read("scalar", lambda s, et=et, b2=b2: s.copy(
                        Kft[:, et * 512:(et + 1) * 512], b2.t[:, 0:512]),
                        extra_waits=[t_super_ts0])
                if is_m:
                    qcons = {0: pr.tok("ts"), 1: pr.tok("ts")}
                    for pas in range(2):
                        for dt_i in range(DT):
                            slot = dt_i % 2
                            tkd = pr.emit("sync", lambda sy, dt_i=dt_i, slot=slot: sy.dma_start(
                                ap(thS, slot * D, [[2 * D, P], [1, D]]),
                                ap(d_thq_scr, dt_i * P * D, [[D, P], [1, D]])),
                                waits=[qcons[slot]], inc=f"dq{slot}", inc_by=16)
                            tkp = pr.emit("vector", lambda v, slot=slot, dt_i=dt_i: v.tensor_scalar_mul(
                                thS[:, slot * D:(slot + 1) * D], thS[:, slot * D:(slot + 1) * D],
                                ca1c[:, dt_i:dt_i + 1]),
                                waits=[tkd], inc="vs")
                            tk = None
                            for ei in range(4):
                                et = pas * 4 + ei
                                b2 = bk[ei]
                                o = b2.t[:, 0:512]
                                l = thS[:, slot * D + et * P:slot * D + (et + 1) * P]
                                r = hft[:, dt_i * 512:dt_i * 512 + 512]
                                wl = [tkp]
                                if dt_i == 0:
                                    wl += [t_hft] + b2.readers
                                tk = pr.emit("tensor",
                                             (lambda t, o=o, l=l, r=r, st=(dt_i == 0):
                                              t.matmul(o, l, r, start=st, stop=False)),
                                             waits=wl, inc="ts" if ei == 3 else None)
                            qcons[slot] = tk
                        for ei in range(4):
                            et = pas * 4 + ei
                            b2 = bk[ei]
                            tk = pr.emit("tensor", lambda t, b2=b2, et=et: t.matmul(
                                b2.t[:, 0:512], qsa_r[0:1, et * P:(et + 1) * P], ones_row[:, :],
                                start=False, stop=True), inc="ts")
                            b2.w = tk
                            b2.readers = []
                            b2.read("scalar", lambda s, et=et, b2=b2: s.copy(
                                Qft[:, et * 512:(et + 1) * 512], b2.t[:, 0:512]),
                                extra_waits=[t_super_ts0])
                t_proj = [pr.tok("ss"), pr.tok("vs")]

                for c in range(SUP):
                    for cp in range(c + 1):
                        b2 = bk[(c + cp) % 2]
                        mms = []
                        for et in range(DT):
                            mms.append((b2.t[:, 0:P],
                                        Kft[:, et * 512 + cp * P:et * 512 + (cp + 1) * P],
                                        Kft[:, et * 512 + c * P:et * 512 + (c + 1) * P]))
                        b2.mm_group(mms, extra_waits=t_proj)
                        if cp == c:
                            t_u = b2.read("scalar", lambda s, b2=b2: s.activation(
                                Usb[:, :], b2.t[:, 0:P], AF.Identity,
                                scale=2 * LR / D, bias=2 * LR / D),
                                extra_waits=[pr.tok("vs")])
                            pr.emit("vector", lambda v, c=c: v.tensor_mul(
                                Bp[:, c * P:(c + 1) * P], Usb[:, :], s_mune[:, :]),
                                waits=[t_u, t_super_ts0], inc="vs")
                            pr.emit("vector", lambda v, c=c: v.tensor_mul(
                                Btp[:, c * P:(c + 1) * P], Usb[:, :], s_mlne[:, :]),
                                waits=[t_u], inc="vs")
                        else:
                            pid = pair_id_s(c, cp)
                            b2.read("scalar", lambda s, b2=b2, pid=pid: s.activation(
                                Scr[:, pid * P:(pid + 1) * P], b2.t[:, 0:P],
                                AF.Identity, scale=1.0, bias=1.0),
                                extra_waits=[t_super_ts0])
                if is_m:
                    for c in range(SUP):
                        for cp in range(c + 1):
                            b2 = bk[2 + (c + cp) % 2]
                            mms = []
                            for et in range(DT):
                                mms.append((b2.t[:, 0:P],
                                            Kft[:, et * 512 + cp * P:et * 512 + (cp + 1) * P],
                                            Qft[:, et * 512 + c * P:et * 512 + (c + 1) * P]))
                            b2.mm_group(mms, extra_waits=t_proj)
                            pid = pair_id_q(c, cp)
                            if cp == c:
                                t_u = b2.read("scalar", lambda s, b2=b2: s.activation(
                                    Usb[:, :], b2.t[:, 0:P], AF.Identity, scale=1.0, bias=1.0),
                                    extra_waits=[pr.tok("vs")])
                                pr.emit("vector", lambda v, pid=pid: v.tensor_mul(
                                    Sqr[:, pid * P:(pid + 1) * P], Usb[:, :], s_muinc[:, :]),
                                    waits=[t_u, t_super_ts0], inc="vs")
                            else:
                                b2.read("scalar", lambda s, b2=b2, pid=pid: s.activation(
                                    Sqr[:, pid * P:(pid + 1) * P], b2.t[:, 0:P],
                                    AF.Identity, scale=1.0, bias=1.0),
                                    extra_waits=[t_super_ts0])
                t_pairs = [pr.tok("ss"), pr.tok("vs")]

                x_tok = {}
                for c in range(SUP):
                    x_tok[c] = pr.emit("vector", lambda v, c=c: v.tensor_add(
                        Xsb[:, c * P:(c + 1) * P], s_ident[:, :], Bp[:, c * P:(c + 1) * P]),
                        waits=t_pairs + [t_super_ts0], inc="vs")
                ds_free = {0: None, 1: None}
                for m in range(1, 7):
                    pslot = (m - 1) % 2
                    cslot = m % 2
                    last = (m == 6)
                    for c in range(SUP):
                        bsl = Bp[:, (pslot * SUP + c) * P:(pslot * SUP + c + 1) * P]
                        btsl = Btp[:, (pslot * SUP + c) * P:(pslot * SUP + c + 1) * P]
                        ew = [x_tok[c]] if m == 1 else []
                        if not last:
                            b2 = bk[c % 2]
                            b2.mm_group([(b2.t[:, 0:P], btsl, bsl)], extra_waits=ew)
                            b2.read("scalar", lambda s, b2=b2, c=c, cslot=cslot: s.copy(
                                Bp[:, (cslot * SUP + c) * P:(cslot * SUP + c + 1) * P], b2.t[:, 0:P]))
                        b3 = bk[2 + c % 2]
                        b3.mm_group([(b3.t[:, 0:P], bsl, btsl)], extra_waits=ew)
                        t_bt = b3.read("scalar", lambda s, b3=b3, c=c, cslot=cslot: s.copy(
                            Btp[:, (cslot * SUP + c) * P:(cslot * SUP + c + 1) * P], b3.t[:, 0:P]))
                        b4 = bk[4 + c % 2]
                        b4.mm_group([(b4.t[:, 0:P],
                                      Btp[:, (cslot * SUP + c) * P:(cslot * SUP + c + 1) * P],
                                      Xsb[:, c * P:(c + 1) * P])],
                                    extra_waits=[t_bt, x_tok[c]])
                        t_d = b4.read("scalar", lambda s, b4=b4, c=c: s.copy(
                            Dsb[:, (c % 2) * P:(c % 2 + 1) * P], b4.t[:, 0:P]),
                            extra_waits=[ds_free[c % 2]])
                        x_tok[c] = pr.emit("vector", lambda v, c=c: v.tensor_add(
                            Xsb[:, c * P:(c + 1) * P], Xsb[:, c * P:(c + 1) * P],
                            Dsb[:, (c % 2) * P:(c % 2 + 1) * P]),
                            waits=[t_d], inc="vs")
                        ds_free[c % 2] = x_tok[c]
                t_newton = pr.tok("vs")

                g_tok = {}
                ep_pend = None

                def emit_epilogue(c, t_xe):
                    nonlocal x2_spill_tok, h2_guard
                    c16 = (s_abs % NSUP) * SUP + c
                    sl = (c % 2) * D
                    x2sl = x2b[:, sl:sl + D]
                    t_x2m = pr.emit("vector", lambda v, sl=sl: v.tensor_add(
                        x2b[:, sl:sl + D], x2b[:, sl:sl + D], xeb[:, sl:sl + D]),
                        waits=[t_xe], inc="vs")
                    x2_spill_tok = pr.emit("sync", lambda sy, c16=c16, x2sl=x2sl: sy.dma_start(
                        ap(d_x2s, c16 * P * D, [[D, P], [1, D]]), x2sl),
                        waits=[t_x2m], inc="dsp", inc_by=16)
                    t_v3, t_hn = emit_ln(x2sl, [t_x2m], [pr.tok("ts")])
                    t_h2 = emit_transposes(t_hn, lambda s: s.copy(h2stg[:, :], bkT.t[:, :]),
                                           extra=[h2_guard[0], h2_guard[1]])
                    h2_spill = pr.emit("sync", lambda sy, c16=c16: sy.dma_start(
                        ap(d_h2s, c16 * P * D, [[D, P], [1, D]]), h2stg[:, :]),
                        waits=[t_h2], inc="dh2s", inc_by=16)
                    b6 = bk[5]
                    mms = []
                    for dt_i in range(DT):
                        mms.append((b6.t[:, 0:NE],
                                    h2stg[:, dt_i * P:(dt_i + 1) * P],
                                    rw_bf[:, dt_i * NE:(dt_i + 1) * NE]))
                    mms.append((b6.t[:, 0:NE], ones_r1[:, :], rsb_r[0:1, :]))
                    b6.mm_group(mms, extra_waits=[t_h2])
                    h2_guard = [h2_spill, pr.tok("ts")]
                    b6.read("vector", lambda v, b6=b6: v.tensor_copy(lg[:, :], b6.t[:, 0:NE]))
                    t_m1 = pr.emit("vector", lambda v: v.reduce_max(rcol[:, 0:1], lg[:, :], axis=AX.X), inc="vs")
                    t_m2 = pr.emit("vector", lambda v: v.tensor_scalar_mul(rcol[:, 1:2], rcol[:, 0:1], -1.0),
                                   waits=[t_m1], inc="vs")
                    t_e1 = pr.emit("scalar", lambda s: s.activation(
                        zex[:, :], lg[:, :], AF.Exp, bias=rcol[:, 1:2]),
                        waits=[t_m2], inc="ss")
                    t_r1 = pr.emit("vector", lambda v: v.reduce_sum(rcol[:, 2:3], zex[:, :], axis=AX.X),
                                   waits=[t_e1], inc="vs")
                    t_r2 = pr.emit("vector", lambda v: v.reciprocal(rcol[:, 3:4], rcol[:, 2:3]),
                                   waits=[t_r1], inc="vs")
                    t_sc = pr.emit("scalar", lambda s: s.activation(
                        sc8[:, :], zex[:, :], AF.Identity, scale=rcol[:, 3:4]),
                        waits=[t_r2], inc="ss")
                    t_t8 = pr.emit("vector", lambda v: v.max(out=top8[:, :], in_=sc8[:, :]),
                                   waits=[t_sc], inc="vs")
                    pr.emit("vector", lambda v, c16=c16: v.scalar_tensor_tensor(
                        comb_all[:, c16 * NE:(c16 + 1) * NE], sc8[:, :], top8[:, 1:2], sc8[:, :],
                        op0=ALU.is_ge, op1=ALU.mult),
                        waits=[t_t8], inc="vs")

                for c in range(SUP):
                    bl, bh = (bk[0], bk[1]) if c % 2 == 0 else (bk[2], bk[3])
                    if is_m:
                        t_xe = pr.emit("sync", lambda sy, s_abs=s_abs, c=c: sy.dma_start(
                            xeb[:, (c % 2) * D:(c % 2 + 1) * D], x_src(s_abs, c)),
                            waits=[pr.tok("vs")], inc="dxe", inc_by=16)
                    for jh, b2 in ((0, bl), (1, bh)):
                        mms = []
                        for et in range(DT):
                            mms.append((b2.t[:, 0:512],
                                        Kft[:, et * 512 + c * P:et * 512 + (c + 1) * P],
                                        Wbf[:, et * D + jh * 512:et * D + (jh + 1) * 512]))
                        for dt_i in range(DT):
                            mms.append((b2.t[:, 0:512],
                                        hft[:, dt_i * 512 + c * P:dt_i * 512 + (c + 1) * P],
                                        thVn[:, dt_i * D + jh * 512:dt_i * D + (jh + 1) * 512]))
                        mms.append((b2.t[:, 0:512], ones_r1[:, :],
                                    par_buf[0:1, jh * 512:(jh + 1) * 512]))
                        mms.append((b2.t[:, 0:512], ones_r1[:, :],
                                    vsan_r[0:1, jh * 512:(jh + 1) * 512]))
                        for cp in range(c):
                            pid = pair_id_s(c, cp)
                            mms.append((b2.t[:, 0:512], Scr[:, pid * P:(pid + 1) * P],
                                        Gsb[:, cp * D + jh * 512:cp * D + (jh + 1) * 512]))
                        ew = [t_newton] + ([g_tok[c - 1]] if c > 0 else [])
                        b2.mm_group(mms, extra_waits=ew)
                        b2.read("scalar", lambda s, b2=b2, jh=jh: s.copy(
                            RHSs[0:P, jh * 512:(jh + 1) * 512], b2.t[:, 0:512]))
                    t_rhs = pr.tok("ss")
                    bg = bk[4]
                    for jh in range(2):
                        bg.mm_group([(bg.t[:, 0:512], Xsb[:, c * P:(c + 1) * P],
                                      RHSs[:, jh * 512:(jh + 1) * 512])],
                                    extra_waits=[t_rhs])
                        g_tok[c] = bg.read("scalar", lambda s, b5=bg, c=c, jh=jh, gcol=gcol: s.activation(
                            Gsb[:, c * D + jh * 512:c * D + (jh + 1) * 512], b5.t[:, 0:512],
                            AF.Identity, scale=s_gmask[:, gcol:gcol + 1]),
                            extra_waits=[t_super_ts0])
                    if is_m:
                        for jh, b2 in ((0, bk[5]), (1, bk[6])):
                            mms = []
                            for et in range(DT):
                                mms.append((b2.t[:, 0:512],
                                            Qft[:, et * 512 + c * P:et * 512 + (c + 1) * P],
                                            Wbf[:, et * D + jh * 512:et * D + (jh + 1) * 512]))
                            mms.append((b2.t[:, 0:512], ones_r1[:, :],
                                        par_buf[0:1, jh * 512:(jh + 1) * 512]))
                            for cp in range(c + 1):
                                pid = pair_id_q(c, cp)
                                mms.append((b2.t[:, 0:512], Sqr[:, pid * P:(pid + 1) * P],
                                            Gsb[:, cp * D + jh * 512:cp * D + (jh + 1) * 512]))
                            b2.mm_group(mms, extra_waits=[g_tok[c]])
                            b2.read("vector", lambda v, b2=b2, jh=jh, c=c: v.tensor_mul(
                                x2b[:, (c % 2) * D + jh * 512:(c % 2) * D + (jh + 1) * 512],
                                b2.t[:, 0:512],
                                ga_bc[:, jh * 512:(jh + 1) * 512]),
                                extra_waits=[pr.tok("ss"), x2_spill_tok])
                        if ep_pend is not None:
                            emit_epilogue(*ep_pend)
                        ep_pend = (c, t_xe)
                if is_m and ep_pend is not None:
                    emit_epilogue(*ep_pend)
                    ep_pend = None
                t_chain = pr.tok("ss")

                t_outs = pr.tok("ts")
                for dt_i in range(DT):
                    for jh in range(2):
                        b2 = bk[(dt_i * 2 + jh) % 2]
                        mms = []
                        for c in range(SUP):
                            mms.append((b2.t[:, 0:512],
                                        Ktk[:, c * D + dt_i * P:c * D + (dt_i + 1) * P],
                                        Gsb[:, c * D + jh * 512:c * D + (jh + 1) * 512]))
                        b2.mm_group(mms, extra_waits=[t_chain])
                        b2.read("vector", lambda v, b2=b2, dt_i=dt_i, jh=jh: v.tensor_add(
                            WT[:, dt_i * D + jh * 512:dt_i * D + (jh + 1) * 512],
                            WT[:, dt_i * D + jh * 512:dt_i * D + (jh + 1) * 512],
                            b2.t[:, 0:512]))
                t_wupd = pr.tok("vs")
                for dt_i in range(DT):
                    pr.emit("scalar", lambda s, dt_i=dt_i: s.copy(
                        Wbf[:, dt_i * D:(dt_i + 1) * D], WT[:, dt_i * D:(dt_i + 1) * D]),
                        waits=[t_wupd, t_outs], inc="ss")
                for jh in range(2):
                    b2 = bk[2 + jh]
                    mms = []
                    for c in range(SUP):
                        mms.append((b2.t[0:1, 0:512], ones_cl[:, :],
                                    Gsb[:, c * D + jh * 512:c * D + (jh + 1) * 512]))
                    b2.mm_group(mms, extra_waits=[t_chain])
                    b2.read("vector", lambda v, b2=b2, jh=jh: v.tensor_add(
                        b_f32[0:1, jh * 512:(jh + 1) * 512],
                        b_f32[0:1, jh * 512:(jh + 1) * 512], b2.t[0:1, 0:512]))
                t_bupd = pr.tok("vs")
                pr.emit("scalar", lambda s, nb=npar_buf: s.copy(nb[0:1, :], b_f32[0:1, :]),
                        waits=[t_bupd, t_outs], inc="ss")

            t_phaseA = [pr.tok("ts"), pr.tok("vs"), pr.tok("ss"), pr.tok("dsp"), pr.tok("dh2s"),
                        pr.tok("dx0"), pr.tok("dx1"), pr.tok("dq0"), pr.tok("dq1"), pr.tok("dxe")]

            if debug:
                tap("WT", WT[:, :], t_phaseA)
                tap("Gsb", Gsb[:, :], t_phaseA)
                tap("bf32", b_f32[:, :], t_phaseA)
                tap("comb", comb_all[:, :], t_phaseA)
                tap("Kft", Kft[:, :], t_phaseA)
                tap("Ktk", Ktk[:, :], t_phaseA)
                tap("Qft", Qft[:, :], t_phaseA)
                tap("Xsb", Xsb[:, :], t_phaseA)
                tap("hft", hft[:, :], t_phaseA)
                tap("ksa", ksa_r[:, :], t_phaseA)
                tap("vsan", vsan_r[:, :], t_phaseA)
                tap("modcols_ca1", ca1c[:, :], t_phaseA)
                tap("h2s0", h2stg[:, :], t_phaseA)
                tap("ga_bc", ga_bc[:, 0:D], t_phaseA)
            if debug:
                t_phaseA = t_phaseA + [pr.tok("do")]

        if do_passb:
            with ExitStack() as bctx:
                EB = bctx.enter_context
                acc = EB(nc.sbuf_tensor("acc", [P, (NCH // 2) * D], F32))
                h2b = EB(nc.sbuf_tensor("h2b", [P, DT * D], BF16))
                act = EB(nc.sbuf_tensor("act", [P, DT * D], BF16))
                wG = EB(nc.sbuf_tensor("wG", [P, 2 * DT * D], BF16))
                wU = EB(nc.sbuf_tensor("wU", [P, 2 * DT * D], BF16))
                wD = EB(nc.sbuf_tensor("wD", [P, DT * D], BF16))
                sg2 = EB(nc.sbuf_tensor("sg2", [P, 512], BF16))
                ub2 = EB(nc.sbuf_tensor("ub2", [P, 512], BF16))
                gsbr = EB(nc.sbuf_tensor("gsbr", [1, D], BF16))
                usbr = EB(nc.sbuf_tensor("usbr", [1, D], BF16))
                obuf = EB(nc.sbuf_tensor("obuf", [P, D], F32))

                pes = [("sh", 0), ("sh", 1)] + [("ex", e2) for e2 in range(NE)]

                def pe_srcs(kind, idx):
                    if kind == "sh":
                        return (ap(d_sg_scr, idx * P * DT * D, [[DT * D, P], [1, DT * D]]),
                                ap(d_su_scr, idx * P * DT * D, [[DT * D, P], [1, DT * D]]),
                                ap(d_sd_scr, idx * P * DT * D, [[DT * D, P], [1, DT * D]]),
                                [wprep_toks["sgw"], wprep_toks["suw"], wprep_toks["sdw"]])
                    return (ap(d_gw_scr, idx * P * DT * D, [[DT * D, P], [1, DT * D]]),
                            ap(d_uw_scr, idx * P * DT * D, [[DT * D, P], [1, DT * D]]),
                            ap(d_dw_scr, idx * P * DT * D, [[DT * D, P], [1, DT * D]]),
                            [wprep_toks["gw"], wprep_toks["uw"], wprep_toks["dw"]])

                wslot_free = {0: list(t_phaseA), 1: list(t_phaseA)}
                wd_free = list(t_phaseA)
                out_toks = []

                for th in range(2):
                    t_ac = pr.emit("gpsimd", lambda g: g.memset(acc[:, :], 0.0),
                                   waits=t_phaseA + [pr.tok("vs")], inc="gs")
                    acc_guard = [t_ac]
                    for ht in range(DT):
                        pr.emit("sync", lambda sy, ht=ht, th=th: sy.dma_start(
                            ap(h2b, ht * D, [[DT * D, P], [P, DT], [1, P]]),
                            ap(d_h2s, th * DT * P * D + P * ht + 0, [[D, P], [P * D, DT], [1, P]])),
                            waits=t_phaseA + [pr.tok("ts")], inc="dh2", inc_by=16)
                    h2_load = [pr.tok("dh2")]
                    for pi, (kind, idx) in enumerate(pes):
                        slot = pi % 2
                        gsrc, usrc, dsrc, wtoks = pe_srcs(kind, idx)
                        t_lg_ = pr.emit("sync", lambda sy, gsrc=gsrc, slot=slot: sy.dma_start(
                            ap(wG, slot * DT * D, [[2 * DT * D, P], [1, DT * D]]), gsrc),
                            waits=wtoks + wslot_free[slot], inc=f"dwg{slot}", inc_by=16)
                        t_lu_ = pr.emit("sync", lambda sy, usrc=usrc, slot=slot: sy.dma_start(
                            ap(wU, slot * DT * D, [[2 * DT * D, P], [1, DT * D]]), usrc),
                            waits=wtoks + wslot_free[slot], inc=f"dwu{slot}", inc_by=16)
                        t_ld_ = pr.emit("sync", lambda sy, dsrc=dsrc: sy.dma_start(wD[:, :], dsrc),
                                        waits=wtoks + wd_free, inc="dwd", inc_by=16)
                        for (wbuf, tld, dstr) in ((wG, t_lg_, gsbr), (wU, t_lu_, usbr)):
                            for jh in range(2):
                                b2 = bk[jh]
                                mms = []
                                for ht in range(DT):
                                    mms.append((b2.t[0:1, 0:512], sb_bf[:, ht:ht + 1],
                                                wbuf[:, slot * DT * D + ht * D + jh * 512:
                                                     slot * DT * D + ht * D + (jh + 1) * 512]))
                                b2.mm_group(mms, extra_waits=[tld])
                                b2.read("scalar", lambda s, b2=b2, dstr=dstr, jh=jh: s.copy(
                                    dstr[0:1, jh * 512:(jh + 1) * 512], b2.t[0:1, 0:512]))
                        t_bias = pr.tok("ss")
                        for (wbuf, tld) in ((wG, t_lg_), (wU, t_lu_)):
                            for ht in range(DT):
                                pr.emit("vector", lambda v, wbuf=wbuf, ht=ht, slot=slot: v.tensor_scalar_mul(
                                    wbuf[:, slot * DT * D + ht * D:slot * DT * D + (ht + 1) * D],
                                    wbuf[:, slot * DT * D + ht * D:slot * DT * D + (ht + 1) * D],
                                    cb1c[:, ht:ht + 1]),
                                    waits=[t_bias, tld], inc="vs")
                        t_prime = pr.tok("vs")
                        for fj in range(DT):
                            for tb in range(2):
                                bA = bk[(fj * 2 + tb) % 2]
                                bB = bk[4 + (fj * 2 + tb) % 2]
                                tok_off = th * 1024 + tb * 512
                                mmsA, mmsB = [], []
                                for ht in range(DT):
                                    rh = h2b[:, ht * D + tb * 512:ht * D + (tb + 1) * 512]
                                    mmsA.append((bA.t[:, 0:512],
                                                 wG[:, slot * DT * D + ht * D + fj * P:
                                                    slot * DT * D + ht * D + (fj + 1) * P], rh))
                                    mmsB.append((bB.t[:, 0:512],
                                                 wU[:, slot * DT * D + ht * D + fj * P:
                                                    slot * DT * D + ht * D + (fj + 1) * P], rh))
                                mmsA.append((bA.t[:, 0:512], gsbr[0:1, fj * P:(fj + 1) * P], ones_row[:, :]))
                                mmsB.append((bB.t[:, 0:512], usbr[0:1, fj * P:(fj + 1) * P], ones_row[:, :]))
                                bA.mm_group(mmsA, extra_waits=[t_prime] + h2_load)
                                bB.mm_group(mmsB, extra_waits=[t_prime] + h2_load)
                                t_sg = bA.read("scalar", lambda s, bA=bA: s.activation(
                                    sg2[:, :], bA.t[:, 0:512], AF.Silu))
                                t_ub = bB.read("scalar", lambda s, bB=bB: s.activation(
                                    ub2[:, :], bB.t[:, 0:512], AF.Identity))
                                pr.emit("vector", lambda v, fj=fj, tb=tb: v.tensor_mul(
                                    act[:, fj * D + tb * 512:fj * D + (tb + 1) * 512],
                                    sg2[:, :], ub2[:, :]),
                                    waits=[t_sg, t_ub], inc="vs")
                        t_act = pr.tok("vs")
                        for tt in range(DT):
                            c16 = th * 8 + tt
                            for hh in range(2):
                                bC = bk[2 + (tt * 2 + hh) % 2]
                                mms = []
                                for fj in range(DT):
                                    mms.append((bC.t[:, 0:512],
                                                ap(act, fj * D + tt * P, [[DT * D, P], [1, P]]),
                                                wD[:, fj * D + hh * 512:fj * D + (hh + 1) * 512]))
                                bC.mm_group(mms, extra_waits=[t_act, t_ld_])
                                if kind == "sh":
                                    csl = onef_col[:, 0:1]
                                else:
                                    csl = comb_all[:, c16 * NE + idx:c16 * NE + idx + 1]
                                bC.read("vector", lambda v, bC=bC, tt=tt, hh=hh, csl=csl: v.scalar_tensor_tensor(
                                    acc[:, tt * D + hh * 512:tt * D + (hh + 1) * 512],
                                    bC.t[:, 0:512], csl,
                                    acc[:, tt * D + hh * 512:tt * D + (hh + 1) * 512],
                                    op0=ALU.mult, op1=ALU.add),
                                    extra_waits=acc_guard)
                        wslot_free[slot] = [pr.tok("ts")]
                        wd_free = [pr.tok("ts")]
                    t_accdone = pr.tok("vs")
                    for cc in range(8):
                        c16 = th * 8 + cc
                        slot = cc % 2
                        t_xr = pr.emit("sync", lambda sy, c16=c16, slot=slot: sy.dma_start(
                            ap(xb, slot * D, [[2 * D, P], [1, D]]),
                            ap(d_x2s, c16 * P * D, [[D, P], [1, D]])),
                            waits=[t_accdone] + ([pr.tok("vs")] if cc >= 2 else []),
                            inc=f"dx{slot}", inc_by=16)
                        pr.emit("vector", lambda v, cc=cc: v.tensor_mul(
                            obuf[:, :], acc[:, cc * D:(cc + 1) * D], gb_bc[:, :]),
                            waits=[t_accdone, pr.tok("do")], inc="vs")
                        t_o2 = pr.emit("vector", lambda v, slot=slot: v.tensor_add(
                            obuf[:, :], obuf[:, :], xb[:, slot * D:(slot + 1) * D]),
                            waits=[t_xr], inc="vs")
                        out_toks.append(pr.emit("sync", lambda sy, c16=c16: sy.dma_start(
                            ap(d_out, c16 * P * D, [[D, P], [1, D]]), obuf[:, :]),
                            waits=[t_o2], inc="do", inc_by=16))

                pr.q["sync"].append(([out_toks[-1]], lambda sy: None, None))
                pr.lastw["sync"][out_toks[-1][0]] = out_toks[-1][1]

        else:
            pr.q["sync"].append(([t_phaseA[0]], lambda sy: None, None))

        block = octx.enter_context(nc.Block())

        @block.tensor
        def _(t):
            pr.replay("tensor", t)

        @block.vector
        def _(v):
            pr.replay("vector", v)

        @block.scalar
        def _(s):
            pr.replay("scalar", s)

        @block.gpsimd
        def _(g):
            pr.replay("gpsimd", g)

        @block.sync
        def _(sy):
            pr.replay("sync", sy)

    return nc, taps


def _mask(fn):
    r = np.arange(P)[:, None]
    c = np.arange(P)[None, :]
    return fn(r, c).astype(np.float32).astype(ml_dtypes.bfloat16)


def make_inputs(L0_attn_in_seq, L0_attn_theta_K, L0_attn_theta_V, L0_attn_theta_Q,
                L0_attn_W0, L0_attn_b0, L0_ffn_router_w, L0_ffn_gate_w, L0_ffn_up_w,
                L0_ffn_down_w, L0_ffn_shared_gate_w, L0_ffn_shared_up_w,
                L0_ffn_shared_down_w, L0_c, L0_wmod):
    x = np.ascontiguousarray(np.asarray(L0_attn_in_seq, dtype=np.float32))
    f32 = lambda a: np.ascontiguousarray(np.asarray(a, dtype=np.float32))
    base = {
        "thK": f32(np.asarray(L0_attn_theta_K).T),
        "thV": f32(np.asarray(L0_attn_theta_V).T),
        "thQ": f32(np.asarray(L0_attn_theta_Q).T),
        "w0T": f32(np.asarray(L0_attn_W0).T),
        "b0r": f32(np.asarray(L0_attn_b0).reshape(1, D)),
        "rw": f32(L0_ffn_router_w),
        "gw": f32(L0_ffn_gate_w), "uw": f32(L0_ffn_up_w), "dw": f32(L0_ffn_down_w),
        "sgw": f32(L0_ffn_shared_gate_w), "suw": f32(L0_ffn_shared_up_w),
        "sdw": f32(L0_ffn_shared_down_w),
        "wmod": f32(L0_wmod),
        "identb": np.eye(P, dtype=np.float32).astype(ml_dtypes.bfloat16),
        "mune": _mask(lambda r, c: np.where(r < c, -1.0, 0.0)),
        "mlne": _mask(lambda r, c: np.where(r > c, -1.0, 0.0)),
        "muinc": _mask(lambda r, c: np.where(r <= c, 1.0, 0.0)),
    }
    cmat = f32(L0_c)
    in_maps = []
    for k in range(8):
        b, h = k >> 1, k & 1
        gm = np.zeros((P, 2), dtype=np.float32)
        gm[:, 0] = -2.0 * LR / D * h
        gm[:, 1] = -2.0 * LR / D
        m = dict(base)
        m["xm"] = np.ascontiguousarray(x[b, h * TOWN:(h + 1) * TOWN])
        m["xp"] = np.ascontiguousarray(x[b, 0:TOWN])
        m["gmask"] = gm
        m["cvec"] = np.ascontiguousarray(cmat[b].reshape(D, 1))
        in_maps.append(m)
    return in_maps


def kernel(**inputs):
    in_maps = make_inputs(**inputs)
    nc, _ = build_program(debug=False)
    res = run_bass_kernel_spmd(nc, in_maps, core_ids=list(range(8)))
    out = np.empty((4, 4096, D), dtype=np.float32)
    for k in range(8):
        b, h = k >> 1, k & 1
        out[b, h * TOWN:(h + 1) * TOWN] = res.results[k]["out"]
    return out


# version 05
# speedup vs baseline: 3.5473x; 1.0989x over previous
import numpy as np
import ml_dtypes
from contextlib import ExitStack

import concourse.bass as bass
import concourse.mybir as mybir
from concourse.bass_utils import run_bass_kernel_spmd

F32 = mybir.dt.float32
BF16 = mybir.dt.bfloat16
AF = mybir.ActivationFunctionType
ALU = mybir.AluOpType
AX = mybir.AxisListType

P = 128
D = 1024
DT = 8
SUP = 4
NSUP = 4
NCH = 16
TOWN = 2048
LR = 0.01
EPS = 1e-5
NE = 8
QUEUES = ("tensor", "vector", "scalar", "gpsimd", "sync")
QSEM = {"vector": "vs", "scalar": "ss", "gpsimd": "gs", "sync": "do", "tensor": "ts"}


class Prog:

    def __init__(self, nc, ctx):
        self.nc = nc
        self.ctx = ctx
        self.q = {k: [] for k in QUEUES}
        self.cnt = {}
        self.hand = {}
        self.lastw = {k: {} for k in QUEUES}

    def sem(self, name):
        if name not in self.hand:
            self.hand[name] = self.ctx.enter_context(self.nc.semaphore(name))
            self.cnt[name] = 0
        return self.hand[name]

    def tok(self, name):
        self.sem(name)
        return (name, self.cnt[name])

    def emit(self, queue, fn, waits=(), inc=None, inc_by=1):
        ws = []
        for w in waits:
            if w is None:
                continue
            s, v = w
            if v <= 0:
                continue
            if self.lastw[queue].get(s, 0) >= v:
                continue
            self.lastw[queue][s] = v
            self.sem(s)
            ws.append((s, v))
        tk = None
        it = None
        if inc is not None:
            self.sem(inc)
            self.cnt[inc] += inc_by
            tk = (inc, self.cnt[inc])
            it = (inc, inc_by)
        self.q[queue].append((ws, fn, it))
        return tk

    def replay(self, queue, eng):
        for ws, fn, it in self.q[queue]:
            for s, v in ws:
                eng.wait_ge(self.hand[s], v)
            ins = fn(eng)
            if it is not None:
                assert ins is not None, f"{queue}: inc {it} but fn returned None"
                ins.then_inc(self.hand[it[0]], it[1])


class Bank:

    def __init__(self, pr, name, dtype=F32):
        width = 512 if dtype == F32 else 1024
        self.pr = pr
        self.t = pr.ctx.enter_context(pr.nc.psum_tensor(name, [P, width], dtype))
        self.w = None
        self.readers = []

    def mm_group(self, mms, extra_waits=(), cont=False):
        pr = self.pr
        n = len(mms)
        waits = [w for w in list(extra_waits) + ([] if cont else self.readers) if w is not None]
        tk = None
        for i, (o, l, r) in enumerate(mms):
            tk = pr.emit(
                "tensor",
                (lambda t, o=o, l=l, r=r, st=(i == 0 and not cont), sp=(i == n - 1):
                 t.matmul(o, l, r, start=st, stop=sp)),
                waits=waits if i == 0 else (),
                inc="ts" if i == n - 1 else None)
        self.w = tk
        if not cont:
            self.readers = []
        return tk

    def read(self, queue, fn, extra_waits=()):
        tk = self.pr.emit(queue, fn, waits=[self.w] + list(extra_waits), inc=QSEM[queue])
        self.readers.append(tk)
        return tk


def ap(t, offset, dims):
    return bass.AP(t, offset, dims)


_BUILD_N = [0]


def build_program(debug=False, n_super=2 * NSUP, do_passb=True):
    _BUILD_N[0] += 1
    _tp = "tap_" if _BUILD_N[0] == 1 else f"tap{_BUILD_N[0]}_"
    nc = bass.Bass("TRN2", target_bir_lowering=False)

    d_xm = nc.dram_tensor("xm", [TOWN, D], F32, kind="ExternalInput")
    d_xp = nc.dram_tensor("xp", [TOWN, D], F32, kind="ExternalInput")
    d_gmask = nc.dram_tensor("gmask", [P, 2], F32, kind="ExternalInput")
    d_cvec = nc.dram_tensor("cvec", [D, 1], F32, kind="ExternalInput")
    d_wmod = nc.dram_tensor("wmod", [D, 6 * D], F32, kind="ExternalInput")
    d_thK = nc.dram_tensor("thK", [D, D], F32, kind="ExternalInput")
    d_thV = nc.dram_tensor("thV", [D, D], F32, kind="ExternalInput")
    d_thQ = nc.dram_tensor("thQ", [D, D], F32, kind="ExternalInput")
    d_w0T = nc.dram_tensor("w0T", [D, D], F32, kind="ExternalInput")
    d_b0 = nc.dram_tensor("b0r", [1, D], F32, kind="ExternalInput")
    d_rw = nc.dram_tensor("rw", [D, NE], F32, kind="ExternalInput")
    d_gw = nc.dram_tensor("gw", [NE, D, D], F32, kind="ExternalInput")
    d_uw = nc.dram_tensor("uw", [NE, D, D], F32, kind="ExternalInput")
    d_dw = nc.dram_tensor("dw", [NE, D, D], F32, kind="ExternalInput")
    d_sgw = nc.dram_tensor("sgw", [D, 2 * D], F32, kind="ExternalInput")
    d_suw = nc.dram_tensor("suw", [D, 2 * D], F32, kind="ExternalInput")
    d_sdw = nc.dram_tensor("sdw", [2 * D, D], F32, kind="ExternalInput")
    d_ident = nc.dram_tensor("identb", [P, P], BF16, kind="ExternalInput")
    d_mune = nc.dram_tensor("mune", [P, P], BF16, kind="ExternalInput")
    d_mlne = nc.dram_tensor("mlne", [P, P], BF16, kind="ExternalInput")
    d_muinc = nc.dram_tensor("muinc", [P, P], BF16, kind="ExternalInput")

    d_out = nc.dram_tensor("out", [TOWN, D], F32, kind="ExternalOutput")

    d_x2s = nc.dram_tensor("x2s", [TOWN, D], F32)
    d_h2s = nc.dram_tensor("h2s", [NCH * P, D], BF16)
    d_modscr = nc.dram_tensor("modscr", [1, 6 * D], F32)
    d_thq_scr = nc.dram_tensor("thq_scr", [D, D], BF16)
    d_gw_scr = nc.dram_tensor("gw_scr", [NE, P, DT * D], BF16)
    d_uw_scr = nc.dram_tensor("uw_scr", [NE, P, DT * D], BF16)
    d_dw_scr = nc.dram_tensor("dw_scr", [NE, P, DT * D], BF16)
    d_sg_scr = nc.dram_tensor("sg_scr", [2, P, DT * D], BF16)
    d_su_scr = nc.dram_tensor("su_scr", [2, P, DT * D], BF16)
    d_sd_scr = nc.dram_tensor("sd_scr", [2, P, DT * D], BF16)

    taps = {}

    with ExitStack() as octx:
        pr = Prog(nc, octx)
        E = octx.enter_context

        def tap(name, apv, waits):
            if not debug:
                return
            dt_ = nc.dram_tensor(_tp + name, list(apv.shape), apv.dtype, kind="ExternalOutput")
            taps[name] = True
            pr.emit("sync", lambda sy, a=apv, d2=dt_: sy.dma_start(d2[...], a),
                    waits=waits, inc="do", inc_by=16)

        s_ident = E(nc.sbuf_tensor("s_ident", [P, P], BF16))
        s_mune = E(nc.sbuf_tensor("s_mune", [P, P], BF16))
        s_mlne = E(nc.sbuf_tensor("s_mlne", [P, P], BF16))
        s_muinc = E(nc.sbuf_tensor("s_muinc", [P, P], BF16))
        ones_r1 = E(nc.sbuf_tensor("ones_r1", [1, P], BF16))
        ones_row = E(nc.sbuf_tensor("ones_row", [1, 512], BF16))
        ones_r1f = E(nc.sbuf_tensor("ones_r1f", [1, P], F32))
        ones_cl = E(nc.sbuf_tensor("ones_cl", [P, 1], BF16))
        onef_col = E(nc.sbuf_tensor("onef_col", [P, 1], F32))
        s_gmask = E(nc.sbuf_tensor("s_gmask", [P, 2], F32))
        gb_bc = E(nc.sbuf_tensor("gb_bc", [P, D], F32))
        ga_bc = E(nc.sbuf_tensor("ga_bc", [P, D], F32))
        comb_all = E(nc.sbuf_tensor("comb_all", [P, NCH * NE], F32))
        b_f32 = E(nc.sbuf_tensor("b_f32", [1, D], F32))
        b_bfA = E(nc.sbuf_tensor("b_bfA", [1, D], BF16))
        b_bfB = E(nc.sbuf_tensor("b_bfB", [1, D], BF16))
        xb = E(nc.sbuf_tensor("xb", [P, 2 * D], F32))
        lnt = E(nc.sbuf_tensor("lnt", [P, 16], F32))
        hbf = E(nc.sbuf_tensor("hbf", [P, D], BF16))
        lg = E(nc.sbuf_tensor("lg", [P, NE], F32))
        zex = E(nc.sbuf_tensor("zex", [P, NE], F32))
        sc8 = E(nc.sbuf_tensor("sc8", [P, NE], F32))
        top8 = E(nc.sbuf_tensor("top8", [P, 8], F32))
        rcol = E(nc.sbuf_tensor("rcol", [P, 4], F32))
        h2stg = E(nc.sbuf_tensor("h2stg", [P, D], BF16))
        ca1c = E(nc.sbuf_tensor("ca1c", [P, DT], F32))
        ca1nc = E(nc.sbuf_tensor("ca1nc", [P, DT], F32))
        cb1c = E(nc.sbuf_tensor("cb1c", [P, DT], F32))
        sa_bf = E(nc.sbuf_tensor("sa_bf", [P, DT], BF16))
        sb_bf = E(nc.sbuf_tensor("sb_bf", [P, DT], BF16))
        rw_bf = E(nc.sbuf_tensor("rw_bf", [P, DT * NE], BF16))
        rsb_r = E(nc.sbuf_tensor("rsb_r", [1, NE], BF16))
        ksa_r = E(nc.sbuf_tensor("ksa_r", [1, D], BF16))
        qsa_r = E(nc.sbuf_tensor("qsa_r", [1, D], BF16))
        vsan_r = E(nc.sbuf_tensor("vsan_r", [1, D], BF16))

        bk = [Bank(pr, f"bk{i}") for i in range(7)]
        bkT = Bank(pr, "bkT", BF16)

        cdb = E(nc.sbuf_tensor("cdb", [P, 4], F32))
        pr.emit("gpsimd", lambda g: g.memset(cdb[:, 0:1], 2 * LR / D), inc="gs")
        pr.emit("gpsimd", lambda g: g.memset(cdb[:, 1:2], 1.0), inc="gs")
        pr.emit("gpsimd", lambda g: g.memset(cdb[:, 2:3], 0.0), inc="gs")
        pr.emit("gpsimd", lambda g: g.memset(cdb[:, 3:4], -1.0), inc="gs")
        nc.const_aps.register(F32, 2 * LR / D, cdb[:, 0:1])
        nc.const_aps.register(F32, 1.0, cdb[:, 1:2])
        nc.const_aps.register(F32, 0.0, cdb[:, 2:3])
        nc.const_aps.register(F32, -1.0, cdb[:, 3:4])

        for t_, d_ in ((s_ident, d_ident), (s_mune, d_mune), (s_mlne, d_mlne), (s_muinc, d_muinc)):
            pr.emit("sync", lambda sy, t2=t_, d2=d_: sy.dma_start(t2[:, :], d2[:, :]),
                    inc="dini", inc_by=16)
        pr.emit("sync", lambda sy: sy.dma_start(s_gmask[:, :], d_gmask[:, :]), inc="dini", inc_by=16)
        pr.emit("sync", lambda sy: sy.dma_start(b_f32[:, :], d_b0[:, :]), inc="dini", inc_by=16)
        pr.emit("gpsimd", lambda g: g.memset(ones_r1[:, :], 1.0), inc="gs")
        pr.emit("gpsimd", lambda g: g.memset(ones_row[:, :], 1.0), inc="gs")
        pr.emit("gpsimd", lambda g: g.memset(ones_cl[:, :], 1.0), inc="gs")
        pr.emit("gpsimd", lambda g: g.memset(ones_r1f[:, :], 1.0), inc="gs")
        pr.emit("gpsimd", lambda g: g.memset(onef_col[:, :], 1.0), inc="gs")
        t_dl0 = pr.tok("dini")
        t_ms0 = pr.tok("gs")

        t_bb0 = pr.emit("scalar", lambda s: s.copy(b_bfA[0:1, :], b_f32[0:1, :]),
                        waits=[t_dl0], inc="ss")

        wprep_toks = {}


        with ExitStack() as actx:
            EA = actx.enter_context

            with ExitStack() as wctx:
                wmb = wctx.enter_context(nc.sbuf_tensor("wmb", [P, 2 * 6 * D], F32))
                silc = wctx.enter_context(nc.sbuf_tensor("silc", [P, DT], F32))
                modr = wctx.enter_context(nc.sbuf_tensor("modr", [1, 6 * D], F32))
                tmpr = wctx.enter_context(nc.sbuf_tensor("tmpr", [1, 3 * D], F32))

                t_cv = pr.emit("sync", lambda sy: sy.dma_start(
                    silc[:, :], ap(d_cvec, 0, [[1, P], [P, DT]]),
                    allow_slow_non_contiguous=True), inc="dcv", inc_by=16)
                t_sil = pr.emit("scalar", lambda s: s.activation(silc[:, :], silc[:, :], AF.Silu),
                                waits=[t_cv], inc="ss")
                prev_mm = [None] * DT
                for pas in range(2):
                    mm_toks = []
                    for dt_i in range(DT):
                        slot = dt_i % 2
                        wl = [mm_toks[dt_i - 2]] if dt_i >= 2 else [prev_mm[DT - 2 + dt_i]]
                        tkd = pr.emit("sync", lambda sy, dt_i=dt_i, slot=slot: sy.dma_start(
                            ap(wmb, slot * 6 * D, [[2 * 6 * D, P], [1, 6 * D]]),
                            ap(d_wmod, dt_i * P * 6 * D, [[6 * D, P], [1, 6 * D]])),
                            waits=wl, inc=f"dwm{slot}", inc_by=16)
                        tk = None
                        for jj in range(6):
                            jb = pas * 6 + jj
                            b2 = bk[jj]
                            o = b2.t[0:1, 0:512]
                            l = silc[:, dt_i:dt_i + 1]
                            r = wmb[:, slot * 6 * D + jb * 512:slot * 6 * D + (jb + 1) * 512]
                            wl2 = [tkd]
                            if dt_i == 0:
                                wl2 += [t_sil] + b2.readers
                            tk = pr.emit("tensor",
                                         (lambda t, o=o, l=l, r=r, st=(dt_i == 0), sp=(dt_i == DT - 1):
                                          t.matmul(o, l, r, start=st, stop=sp)),
                                         waits=wl2, inc="ts" if jj == 5 else None)
                        mm_toks.append(tk)
                        if dt_i == DT - 1:
                            for jj in range(6):
                                bk[jj].w = tk
                                bk[jj].readers = []
                    for jj in range(6):
                        jb = pas * 6 + jj
                        bk[jj].read("vector", lambda v, jj=jj, jb=jb: v.tensor_copy(
                            modr[0:1, jb * 512:(jb + 1) * 512], bk[jj].t[0:1, 0:512]))
                    prev_mm = mm_toks
                t_mod = pr.tok("vs")

                for idx in (2, 5):
                    g_sl = modr[0:1, idx * D:(idx + 1) * D]
                    a_sl = tmpr[0:1, 0:D]
                    bb_sl = tmpr[0:1, D:2 * D]
                    s_sl = tmpr[0:1, 2 * D:3 * D]
                    pr.emit("vector", lambda v, g=g_sl, a=a_sl: v.tensor_scalar_max(a, g, 0.25),
                            waits=[t_mod], inc="vs")
                    pr.emit("vector", lambda v, g=g_sl, b3=bb_sl: v.tensor_scalar_min(b3, g, -0.25), inc="vs")
                    tc0 = pr.emit("vector", lambda v, g=g_sl, s2=s_sl: v.tensor_single_scalar(s2, g, 0.0, op=ALU.is_ge), inc="vs")
                    tc1 = pr.emit("vector", lambda v, a=a_sl, b3=bb_sl: v.tensor_sub(a, a, b3),
                                  waits=[tc0], inc="vs")
                    tc2 = pr.emit("vector", lambda v, a=a_sl, s2=s_sl: v.tensor_mul(a, a, s2),
                                  waits=[tc1], inc="vs")
                    pr.emit("vector", lambda v, g=g_sl, a=a_sl, b3=bb_sl: v.tensor_add(g, b3, a),
                            waits=[tc2], inc="vs")
                t_clamp = pr.tok("vs")

                for row_i, dst in ((2, ga_bc), (5, gb_bc)):
                    for jh in range(2):
                        b2 = bk[6]
                        b2.mm_group(
                            [(b2.t[:, 0:512], ones_r1f[:, :],
                              modr[0:1, row_i * D + jh * 512:row_i * D + (jh + 1) * 512])],
                            extra_waits=[t_clamp, t_ms0])
                        b2.read("vector", lambda v, dst=dst, jh=jh, b2=b2: v.tensor_copy(
                            dst[:, jh * 512:(jh + 1) * 512], b2.t[:, 0:512]))
                t_ms = pr.emit("sync", lambda sy: sy.dma_start(d_modscr[:, :], modr[:, :]),
                               waits=[t_clamp], inc="dms", inc_by=16)
                t_l1 = pr.emit("sync", lambda sy: sy.dma_start(
                    ca1c[:, :], ap(d_modscr, 1 * D, [[1, P], [P, DT]]),
                    allow_slow_non_contiguous=True), waits=[t_ms], inc="dca", inc_by=16)
                t_l2 = pr.emit("sync", lambda sy: sy.dma_start(
                    cb1c[:, :], ap(d_modscr, 4 * D, [[1, P], [P, DT]]),
                    allow_slow_non_contiguous=True), waits=[t_ms], inc="dcb", inc_by=16)
                t_l3 = pr.emit("gpsimd", lambda g: g.dma_start(
                    sa_bf[:, :], ap(d_modscr, 0 * D, [[1, P], [P, DT]]),
                    allow_slow_non_contiguous=True), waits=[t_ms], inc="dsa", inc_by=16)
                t_l4 = pr.emit("gpsimd", lambda g: g.dma_start(
                    sb_bf[:, :], ap(d_modscr, 3 * D, [[1, P], [P, DT]]),
                    allow_slow_non_contiguous=True), waits=[t_ms], inc="dsb2", inc_by=16)
                t_c1 = pr.emit("vector", lambda v: v.tensor_scalar_add(ca1c[:, :], ca1c[:, :], 1.0),
                               waits=[t_l1], inc="vs")
                t_c2 = pr.emit("vector", lambda v: v.tensor_scalar_add(cb1c[:, :], cb1c[:, :], 1.0),
                               waits=[t_l2], inc="vs")
                t_c3 = pr.emit("vector", lambda v: v.tensor_scalar_mul(ca1nc[:, :], ca1c[:, :], -1.0),
                               waits=[t_c1], inc="vs")
                t_wmod_ts = pr.tok("ts")
                t_wmod_vs = pr.tok("vs")

            thK = EA(nc.sbuf_tensor("s_thK", [P, DT * D], BF16))
            thVn = EA(nc.sbuf_tensor("s_thVn", [P, DT * D], BF16))
            thQp = EA(nc.sbuf_tensor("s_thQp", [P, DT * D], BF16))
            WT = EA(nc.sbuf_tensor("WT", [P, DT * D], F32))
            Wbf = EA(nc.sbuf_tensor("Wbf", [P, DT * D], BF16))
            hft = EA(nc.sbuf_tensor("hft", [P, DT * 512], BF16))
            Kft = EA(nc.sbuf_tensor("Kft", [P, DT * 512], BF16))
            Ktk = EA(nc.sbuf_tensor("Ktk", [P, SUP * D], BF16))
            Qft = EA(nc.sbuf_tensor("Qft", [P, DT * 512], BF16))
            Gsb = EA(nc.sbuf_tensor("Gsb", [P, SUP * D], BF16))
            Scr = EA(nc.sbuf_tensor("Scr", [P, 6 * P], BF16))
            Sqr = EA(nc.sbuf_tensor("Sqr", [P, 10 * P], BF16))
            Usb = EA(nc.sbuf_tensor("Usb", [P, P], BF16))
            Bp = EA(nc.sbuf_tensor("Bp", [P, 2 * SUP * P], BF16))
            Btp = EA(nc.sbuf_tensor("Btp", [P, 2 * SUP * P], BF16))
            Dsb = EA(nc.sbuf_tensor("Dsb", [P, SUP * P], BF16))
            Xsb = EA(nc.sbuf_tensor("Xsb", [P, SUP * P], BF16))
            RHSs = EA(nc.sbuf_tensor("RHSs", [P, D], BF16))
            x2b = EA(nc.sbuf_tensor("x2b", [P, 2 * D], F32))
            xeb = EA(nc.sbuf_tensor("xeb", [P, 2 * D], F32))

            for dt_i in range(DT):
                pr.emit("gpsimd", lambda g, dt_i=dt_i: g.dma_start(
                    ap(thK, dt_i * D, [[DT * D, P], [1, D]]),
                    ap(d_thK, dt_i * P * D, [[D, P], [1, D]])),
                    waits=[t_wmod_ts, t_wmod_vs], inc="dthk", inc_by=16)
            t_thk_all = pr.tok("dthk")
            thk_toks = [t_thk_all] * DT
            for dt_i in range(DT):
                pr.emit("gpsimd", lambda g, dt_i=dt_i: g.dma_start(
                    ap(thVn, dt_i * D, [[DT * D, P], [1, D]]),
                    ap(d_thV, dt_i * P * D, [[D, P], [1, D]])),
                    waits=[t_wmod_ts, t_wmod_vs], inc="dthv", inc_by=16)
            t_thv_all = pr.tok("dthv")
            thv_toks = [t_thv_all] * DT
            for dt_i in range(DT):
                pr.emit("gpsimd", lambda g, dt_i=dt_i: g.dma_start(
                    ap(thQp, dt_i * D, [[DT * D, P], [1, D]]),
                    ap(d_thQ, dt_i * P * D, [[D, P], [1, D]])),
                    waits=[t_wmod_ts, t_wmod_vs], inc="dthq", inc_by=16)
            t_thq_all = pr.tok("dthq")
            thq_toks = [t_thq_all] * DT

            def bias_row(dst_row, src_mat, toks, scale):
                for jh in range(2):
                    b2 = bk[jh]
                    mms = []
                    for dt_i in range(DT):
                        mms.append((b2.t[0:1, 0:512], sa_bf[:, dt_i:dt_i + 1],
                                    src_mat[:, dt_i * D + jh * 512:dt_i * D + (jh + 1) * 512]))
                    b2.mm_group(mms, extra_waits=list(toks) + [t_l3])
                    b2.read("scalar", lambda s, jh=jh, b2=b2, dst_row=dst_row, scale=scale:
                            s.activation(dst_row[0:1, jh * 512:(jh + 1) * 512], b2.t[0:1, 0:512],
                                         AF.Identity, scale=scale))
            bias_row(ksa_r, thK, thk_toks, 1.0)
            t_ksa = pr.tok("ss")
            bias_row(vsan_r, thVn, thv_toks, -1.0)
            t_vsa = pr.tok("ss")
            for dt_i in range(DT):
                pr.emit("vector", lambda v, dt_i=dt_i: v.tensor_scalar_mul(
                    thK[:, dt_i * D:(dt_i + 1) * D], thK[:, dt_i * D:(dt_i + 1) * D],
                    ca1c[:, dt_i:dt_i + 1]),
                    waits=[t_ksa, t_c1, thk_toks[dt_i]], inc="vs")
                pr.emit("vector", lambda v, dt_i=dt_i: v.tensor_scalar_mul(
                    thVn[:, dt_i * D:(dt_i + 1) * D], thVn[:, dt_i * D:(dt_i + 1) * D],
                    ca1nc[:, dt_i:dt_i + 1]),
                    waits=[t_vsa, t_c3, thv_toks[dt_i]], inc="vs")
            t_thprime = pr.tok("vs")

            bias_row(qsa_r, thQp, thq_toks, 1.0)
            t_qsa = pr.tok("ss")
            for dt_i in range(DT):
                pr.emit("vector", lambda v, dt_i=dt_i: v.tensor_scalar_mul(
                    thQp[:, dt_i * D:(dt_i + 1) * D], thQp[:, dt_i * D:(dt_i + 1) * D],
                    ca1c[:, dt_i:dt_i + 1]),
                    waits=[t_qsa, t_c1, thq_toks[dt_i]], inc="vs")
            t_qprime = pr.tok("vs")

            t_rwl = pr.emit("gpsimd", lambda g: g.dma_start(
                rw_bf[:, :], ap(d_rw, 0, [[NE, P], [P * NE, DT], [1, NE]])),
                waits=[t_wmod_ts, t_wmod_vs], inc="drw", inc_by=16)
            b2 = bk[4]
            mms = [(b2.t[0:1, 0:NE], sb_bf[:, dt_i:dt_i + 1], rw_bf[:, dt_i * NE:(dt_i + 1) * NE])
                   for dt_i in range(DT)]
            b2.mm_group(mms, extra_waits=[t_rwl, t_l4])
            b2.read("scalar", lambda s, b2=b2: s.activation(rsb_r[0:1, :], b2.t[0:1, 0:NE], AF.Identity))
            t_rsb = pr.tok("ss")
            for dt_i in range(DT):
                pr.emit("vector", lambda v, dt_i=dt_i: v.tensor_scalar_mul(
                    rw_bf[:, dt_i * NE:(dt_i + 1) * NE], rw_bf[:, dt_i * NE:(dt_i + 1) * NE],
                    cb1c[:, dt_i:dt_i + 1]),
                    waits=[t_rsb, t_c2], inc="vs")


            def wprep_family(name, pairs):
                for s_ap, d_ap in pairs:
                    pr.emit("gpsimd", lambda g, s2=s_ap, d2=d_ap: g.dma_start(d2, s2),
                            inc="dp", inc_by=16)
                wprep_toks[name] = pr.tok("dp")

            def gu_pairs(src, dst):
                out = []
                for e2 in range(NE):
                    for ht in range(DT):
                        out.append((ap(src, e2 * D * D + ht * P * D, [[D, P], [1, D]]),
                                    ap(dst, e2 * P * DT * D + ht * D, [[DT * D, P], [1, D]])))
                return out
            wprep_family("gw", gu_pairs(d_gw, d_gw_scr))
            wprep_family("uw", gu_pairs(d_uw, d_uw_scr))
            wprep_family("dw", gu_pairs(d_dw, d_dw_scr))

            def sh_pairs(src, dst):
                out = []
                for fh in range(2):
                    for ht in range(DT):
                        out.append((ap(src, ht * P * 2 * D + fh * D, [[2 * D, P], [1, D]]),
                                    ap(dst, fh * P * DT * D + ht * D, [[DT * D, P], [1, D]])))
                return out
            wprep_family("sgw", sh_pairs(d_sgw, d_sg_scr))
            wprep_family("suw", sh_pairs(d_suw, d_su_scr))
            sd_pairs = []
            for fh in range(2):
                for fj in range(DT):
                    sd_pairs.append((ap(d_sdw, (fh * D + fj * P) * D, [[D, P], [1, D]]),
                                     ap(d_sd_scr, fh * P * DT * D + fj * D, [[DT * D, P], [1, D]])))
            wprep_family("sdw", sd_pairs)
            dp_all = pr.tok("dp")
            for k2 in ("gw", "uw", "dw", "sgw", "suw", "sdw"):
                wprep_toks[k2] = dp_all

            for dt_i in range(DT):
                pr.emit("sync", lambda sy, dt_i=dt_i: sy.dma_start(
                    ap(WT, dt_i * D, [[DT * D, P], [1, D]]),
                    ap(d_w0T, dt_i * P * D, [[D, P], [1, D]])),
                    waits=[t_wmod_ts, t_wmod_vs], inc="dw0t", inc_by=16)
            t_w0_all = pr.tok("dw0t")
            for dt_i in range(DT):
                pr.emit("scalar", lambda s, dt_i=dt_i: s.copy(
                    Wbf[:, dt_i * D:(dt_i + 1) * D], WT[:, dt_i * D:(dt_i + 1) * D]),
                    waits=[t_w0_all], inc="ss")
            t_w0 = pr.tok("ss")

            t_prologue = [pr.tok("ts"), pr.tok("vs"), pr.tok("ss"), pr.tok("gs"),
                          t_w0, t_bb0, t_thprime, t_qprime]

            ln_done = {}
            x_dma = {}
            x2_spill_tok = None
            h2_guard = [None, None]
            last_hn = [None]
            last_tr = [None]

            def x_src(s_abs, ci):
                d_src = d_xp if s_abs < NSUP else d_xm
                c16 = (s_abs % NSUP) * SUP + ci
                return ap(d_src, c16 * P * D, [[D, P], [1, D]])

            def ensure_xdma(cl):
                if cl >= 2 * NSUP * SUP or cl in x_dma:
                    return
                slot = cl % 2
                wl = list(t_prologue) if cl == 0 else []
                if cl - 2 >= 0:
                    wl += list(ln_done[cl - 2])
                x_dma[cl] = pr.emit("sync", lambda sy, cl=cl, slot=slot: sy.dma_start(
                    ap(xb, slot * D, [[2 * D, P], [1, D]]), x_src(cl // SUP, cl % SUP)),
                    waits=wl, inc=f"dx{slot}", inc_by=16)

            def pair_id_s(c, cp):
                return c * (c - 1) // 2 + cp

            def pair_id_q(c, cp):
                return c * (c + 1) // 2 + cp

            def emit_ln(x_ap, wait0, hazard_guard):
                pr.emit("vector", lambda v, x3=x_ap: v.bn_stats(lnt[:, 0:6], x3[:, 0:512]),
                        waits=wait0 + [last_hn[0]], inc="vs")
                t_b2 = pr.emit("vector", lambda v, x3=x_ap: v.bn_stats(lnt[:, 6:12], x3[:, 512:1024]), inc="vs")
                t_ag = pr.emit("vector", lambda v: v.bn_aggr(lnt[:, 12:14], lnt[:, 0:12]),
                               waits=[t_b2], inc="vs")
                t_v1 = pr.emit("vector", lambda v: v.tensor_scalar_add(lnt[:, 14:15], lnt[:, 13:14], EPS),
                               waits=[t_ag], inc="vs")
                t_s1 = pr.emit("scalar", lambda s: s.activation(lnt[:, 14:15], lnt[:, 14:15], AF.Sqrt),
                               waits=[t_v1], inc="ss")
                t_rc = pr.emit("vector", lambda v: v.reciprocal(lnt[:, 15:16], lnt[:, 14:15]),
                               waits=[t_s1], inc="vs")
                t_v3 = pr.emit("vector", lambda v: v.scalar_tensor_tensor(
                    lnt[:, 14:15], lnt[:, 12:13], -1.0, lnt[:, 15:16], op0=ALU.mult, op1=ALU.mult),
                    waits=[t_rc], inc="vs")
                t_hn = pr.emit("scalar", lambda s, x3=x_ap: s.activation(
                    hbf[:, :], x3, AF.Identity, bias=lnt[:, 14:15], scale=lnt[:, 15:16]),
                    waits=[t_v3] + hazard_guard, inc="ss")
                last_hn[0] = t_hn
                return t_v3, t_hn

            def emit_transposes(t_hn, dst_ap_fn, extra=()):
                tk = None
                for i in range(DT):
                    tk = pr.emit("tensor",
                                 (lambda t, i=i: t.transpose(
                                     bkT.t[:, i * P:(i + 1) * P], hbf[:, i * P:(i + 1) * P],
                                     s_ident[:, :])),
                                 waits=([t_hn] + bkT.readers) if i == 0 else (),
                                 inc="ts" if i == DT - 1 else None)
                bkT.w = tk
                bkT.readers = []
                last_tr[0] = tk
                return bkT.read("scalar", dst_ap_fn, extra_waits=list(extra))

            def emit_state_update(t_chain, t_outs, npar_buf):
                for dt_i in range(DT):
                    for jh in range(2):
                        b2 = bk[(dt_i * 2 + jh) % 2]
                        mms = []
                        for c in range(SUP):
                            mms.append((b2.t[:, 0:512],
                                        Ktk[:, c * D + dt_i * P:c * D + (dt_i + 1) * P],
                                        Gsb[:, c * D + jh * 512:c * D + (jh + 1) * 512]))
                        b2.mm_group(mms, extra_waits=[t_chain])
                        b2.read("vector", lambda v, b2=b2, dt_i=dt_i, jh=jh: v.tensor_add(
                            WT[:, dt_i * D + jh * 512:dt_i * D + (jh + 1) * 512],
                            WT[:, dt_i * D + jh * 512:dt_i * D + (jh + 1) * 512],
                            b2.t[:, 0:512]))
                t_wupd = pr.tok("vs")
                for dt_i in range(DT):
                    pr.emit("scalar", lambda s, dt_i=dt_i: s.copy(
                        Wbf[:, dt_i * D:(dt_i + 1) * D], WT[:, dt_i * D:(dt_i + 1) * D]),
                        waits=[t_wupd, t_outs], inc="ss")
                for jh in range(2):
                    b2 = bk[2 + jh]
                    mms = []
                    for c in range(SUP):
                        mms.append((b2.t[0:1, 0:512], ones_cl[:, :],
                                    Gsb[:, c * D + jh * 512:c * D + (jh + 1) * 512]))
                    b2.mm_group(mms, extra_waits=[t_chain])
                    b2.read("vector", lambda v, b2=b2, jh=jh: v.tensor_add(
                        b_f32[0:1, jh * 512:(jh + 1) * 512],
                        b_f32[0:1, jh * 512:(jh + 1) * 512], b2.t[0:1, 0:512]))
                t_bupd = pr.tok("vs")
                pr.emit("scalar", lambda s, nb=npar_buf: s.copy(nb[0:1, :], b_f32[0:1, :]),
                        waits=[t_bupd, t_outs], inc="ss")

            pending_state = None

            for s_abs in range(n_super):
                is_m = s_abs >= NSUP
                par_buf = b_bfA if s_abs % 2 == 0 else b_bfB
                npar_buf = b_bfB if s_abs % 2 == 0 else b_bfA
                gcol = 1 if is_m else 0
                t_guard_a = pr.tok("ts")
                if pending_state is not None:
                    emit_state_update(*pending_state)
                    pending_state = None
                t_super_ts0 = pr.tok("ts")

                for ci in range(SUP):
                    cl = s_abs * SUP + ci
                    ensure_xdma(cl)
                    ensure_xdma(cl + 1)
                    slot = cl % 2
                    xsl = xb[:, slot * D:(slot + 1) * D]
                    t_v3, t_hn = emit_ln(xsl, [x_dma[cl]], [last_tr[0]])
                    ln_done[cl] = (t_v3, t_hn)
                    ensure_xdma(cl + 2)
                    emit_transposes(t_hn, lambda s, ci=ci: s.copy(
                        ap(hft, ci * P, [[DT * 512, P], [512, DT], [1, P]]), bkT.t[:, :]),
                        extra=[t_guard_a])
                t_hft = pr.tok("ss")

                for gi, (ci, jh) in enumerate([(c, j) for c in range(SUP) for j in range(2)]):
                    b2 = bk[gi % 2]
                    mms = []
                    for dt_i in range(DT):
                        mms.append((b2.t[:, 0:512],
                                    hft[:, dt_i * 512 + ci * P:dt_i * 512 + (ci + 1) * P],
                                    thK[:, dt_i * D + jh * 512:dt_i * D + (jh + 1) * 512]))
                    mms.append((b2.t[:, 0:512], ones_r1[:, :], ksa_r[0:1, jh * 512:(jh + 1) * 512]))
                    b2.mm_group(mms, extra_waits=[t_hft])
                    b2.read("vector", lambda v, ci=ci, jh=jh, b2=b2: v.tensor_copy(
                        Ktk[:, ci * D + jh * 512:ci * D + (jh + 1) * 512], b2.t[:, 0:512]),
                        extra_waits=[t_super_ts0])
                for et in range(DT):
                    b2 = bk[2 + et % 2]
                    mms = []
                    for dt_i in range(DT):
                        mms.append((b2.t[:, 0:512],
                                    thK[:, dt_i * D + et * P:dt_i * D + (et + 1) * P],
                                    hft[:, dt_i * 512:dt_i * 512 + 512]))
                    mms.append((b2.t[:, 0:512], ksa_r[0:1, et * P:(et + 1) * P], ones_row[:, :]))
                    b2.mm_group(mms, extra_waits=[t_hft])
                    b2.read("scalar", lambda s, et=et, b2=b2: s.copy(
                        Kft[:, et * 512:(et + 1) * 512], b2.t[:, 0:512]),
                        extra_waits=[t_super_ts0])
                if is_m:
                    for et in range(DT):
                        b2 = bk[4 + et % 2]
                        mms = []
                        for dt_i in range(DT):
                            mms.append((b2.t[:, 0:512],
                                        thQp[:, dt_i * D + et * P:dt_i * D + (et + 1) * P],
                                        hft[:, dt_i * 512:dt_i * 512 + 512]))
                        mms.append((b2.t[:, 0:512], qsa_r[0:1, et * P:(et + 1) * P], ones_row[:, :]))
                        b2.mm_group(mms, extra_waits=[t_hft])
                        b2.read("scalar", lambda s, et=et, b2=b2: s.copy(
                            Qft[:, et * 512:(et + 1) * 512], b2.t[:, 0:512]),
                            extra_waits=[t_super_ts0])
                t_proj = [pr.tok("ss"), pr.tok("vs")]

                for cp in range(SUP):
                    b2 = bk[cp % 2]
                    mms = []
                    for et in range(DT):
                        mms.append((b2.t[:, 0:512],
                                    Kft[:, et * 512 + cp * P:et * 512 + (cp + 1) * P],
                                    Kft[:, et * 512:et * 512 + 512]))
                    b2.mm_group(mms, extra_waits=t_proj)
                    t_u = b2.read("scalar", lambda s, b2=b2, cp=cp: s.activation(
                        Usb[:, :], b2.t[:, cp * P:(cp + 1) * P], AF.Identity,
                        scale=2 * LR / D, bias=2 * LR / D),
                        extra_waits=[pr.tok("vs")])
                    pr.emit("vector", lambda v, cp=cp: v.tensor_mul(
                        Bp[:, cp * P:(cp + 1) * P], Usb[:, :], s_mune[:, :]),
                        waits=[t_u, t_super_ts0], inc="vs")
                    pr.emit("vector", lambda v, cp=cp: v.tensor_mul(
                        Btp[:, cp * P:(cp + 1) * P], Usb[:, :], s_mlne[:, :]),
                        waits=[t_u], inc="vs")
                    for c in range(cp + 1, SUP):
                        pid = pair_id_s(c, cp)
                        b2.read("scalar", lambda s, b2=b2, pid=pid, c=c: s.activation(
                            Scr[:, pid * P:(pid + 1) * P], b2.t[:, c * P:(c + 1) * P],
                            AF.Identity, scale=1.0, bias=1.0),
                            extra_waits=[t_super_ts0])
                if is_m:
                    for cp in range(SUP):
                        b2 = bk[2 + cp % 2]
                        mms = []
                        for et in range(DT):
                            mms.append((b2.t[:, 0:512],
                                        Kft[:, et * 512 + cp * P:et * 512 + (cp + 1) * P],
                                        Qft[:, et * 512:et * 512 + 512]))
                        b2.mm_group(mms, extra_waits=t_proj)
                        pid = pair_id_q(cp, cp)
                        t_u = b2.read("scalar", lambda s, b2=b2, cp=cp: s.activation(
                            Usb[:, :], b2.t[:, cp * P:(cp + 1) * P], AF.Identity, scale=1.0, bias=1.0),
                            extra_waits=[pr.tok("vs")])
                        pr.emit("vector", lambda v, pid=pid: v.tensor_mul(
                            Sqr[:, pid * P:(pid + 1) * P], Usb[:, :], s_muinc[:, :]),
                            waits=[t_u, t_super_ts0], inc="vs")
                        for c in range(cp + 1, SUP):
                            pid = pair_id_q(c, cp)
                            b2.read("scalar", lambda s, b2=b2, pid=pid, c=c: s.activation(
                                Sqr[:, pid * P:(pid + 1) * P], b2.t[:, c * P:(c + 1) * P],
                                AF.Identity, scale=1.0, bias=1.0),
                                extra_waits=[t_super_ts0])
                t_pairs = [pr.tok("ss"), pr.tok("vs")]

                x_tok = {}
                for c in range(SUP):
                    x_tok[c] = pr.emit("vector", lambda v, c=c: v.tensor_add(
                        Xsb[:, c * P:(c + 1) * P], s_ident[:, :], Bp[:, c * P:(c + 1) * P]),
                        waits=t_pairs + [t_super_ts0], inc="vs")
                ds_free = {c: None for c in range(SUP)}
                for m in range(1, 7):
                    pslot = (m - 1) % 2
                    cslot = m % 2
                    last = (m == 6)
                    bt_tok = {}
                    if not last:
                        for c in range(SUP):
                            bsl = Bp[:, (pslot * SUP + c) * P:(pslot * SUP + c + 1) * P]
                            btsl = Btp[:, (pslot * SUP + c) * P:(pslot * SUP + c + 1) * P]
                            b2 = bk[c % 2]
                            b2.mm_group([(b2.t[:, 0:P], btsl, bsl)],
                                        extra_waits=[x_tok[c]] if m == 1 else [])
                            b2.read("scalar", lambda s, b2=b2, c=c, cslot=cslot: s.copy(
                                Bp[:, (cslot * SUP + c) * P:(cslot * SUP + c + 1) * P], b2.t[:, 0:P]))
                    for c in range(SUP):
                        bsl = Bp[:, (pslot * SUP + c) * P:(pslot * SUP + c + 1) * P]
                        btsl = Btp[:, (pslot * SUP + c) * P:(pslot * SUP + c + 1) * P]
                        b3 = bk[2 + c % 2]
                        b3.mm_group([(b3.t[:, 0:P], bsl, btsl)],
                                    extra_waits=[x_tok[c]] if m == 1 else [])
                        bt_tok[c] = b3.read("scalar", lambda s, b3=b3, c=c, cslot=cslot: s.copy(
                            Btp[:, (cslot * SUP + c) * P:(cslot * SUP + c + 1) * P], b3.t[:, 0:P]))
                    for c in range(SUP):
                        b4 = bk[4 + c % 2]
                        b4.mm_group([(b4.t[:, 0:P],
                                      Btp[:, (cslot * SUP + c) * P:(cslot * SUP + c + 1) * P],
                                      Xsb[:, c * P:(c + 1) * P])],
                                    extra_waits=[bt_tok[c], x_tok[c]])
                        t_d = b4.read("scalar", lambda s, b4=b4, c=c: s.copy(
                            Dsb[:, c * P:(c + 1) * P], b4.t[:, 0:P]),
                            extra_waits=[ds_free[c]])
                        x_tok[c] = pr.emit("vector", lambda v, c=c: v.tensor_add(
                            Xsb[:, c * P:(c + 1) * P], Xsb[:, c * P:(c + 1) * P],
                            Dsb[:, c * P:(c + 1) * P]),
                            waits=[t_d], inc="vs")
                        ds_free[c] = x_tok[c]
                t_newton = pr.tok("vs")

                g_tok = {}
                ep_pend = None

                def emit_epilogue(c, t_xe):
                    nonlocal x2_spill_tok, h2_guard
                    c16 = (s_abs % NSUP) * SUP + c
                    sl = (c % 2) * D
                    x2sl = x2b[:, sl:sl + D]
                    t_x2m = pr.emit("vector", lambda v, sl=sl: v.tensor_add(
                        x2b[:, sl:sl + D], x2b[:, sl:sl + D], xeb[:, sl:sl + D]),
                        waits=[t_xe], inc="vs")
                    x2_spill_tok = pr.emit("sync", lambda sy, c16=c16, x2sl=x2sl: sy.dma_start(
                        ap(d_x2s, c16 * P * D, [[D, P], [1, D]]), x2sl),
                        waits=[t_x2m], inc="dsp", inc_by=16)
                    t_v3, t_hn = emit_ln(x2sl, [t_x2m], [last_tr[0]])
                    t_h2 = emit_transposes(t_hn, lambda s: s.copy(h2stg[:, :], bkT.t[:, :]),
                                           extra=[h2_guard[0], h2_guard[1]])
                    h2_spill = pr.emit("sync", lambda sy, c16=c16: sy.dma_start(
                        ap(d_h2s, c16 * P * D, [[D, P], [1, D]]), h2stg[:, :]),
                        waits=[t_h2], inc="dh2s", inc_by=16)
                    b6 = bk[4]
                    mms = []
                    for dt_i in range(DT):
                        mms.append((b6.t[:, 0:NE],
                                    h2stg[:, dt_i * P:(dt_i + 1) * P],
                                    rw_bf[:, dt_i * NE:(dt_i + 1) * NE]))
                    mms.append((b6.t[:, 0:NE], ones_r1[:, :], rsb_r[0:1, :]))
                    b6.mm_group(mms, extra_waits=[t_h2])
                    h2_guard = [h2_spill, pr.tok("ts")]
                    b6.read("vector", lambda v, b6=b6: v.tensor_copy(lg[:, :], b6.t[:, 0:NE]))
                    t_m1 = pr.emit("vector", lambda v: v.reduce_max(rcol[:, 0:1], lg[:, :], axis=AX.X), inc="vs")
                    t_m2 = pr.emit("vector", lambda v: v.tensor_scalar_mul(rcol[:, 1:2], rcol[:, 0:1], -1.0),
                                   waits=[t_m1], inc="vs")
                    t_e1 = pr.emit("scalar", lambda s: s.activation(
                        zex[:, :], lg[:, :], AF.Exp, bias=rcol[:, 1:2]),
                        waits=[t_m2], inc="ss")
                    t_r1 = pr.emit("vector", lambda v: v.reduce_sum(rcol[:, 2:3], zex[:, :], axis=AX.X),
                                   waits=[t_e1], inc="vs")
                    t_r2 = pr.emit("vector", lambda v: v.reciprocal(rcol[:, 3:4], rcol[:, 2:3]),
                                   waits=[t_r1], inc="vs")
                    t_sc = pr.emit("scalar", lambda s: s.activation(
                        sc8[:, :], zex[:, :], AF.Identity, scale=rcol[:, 3:4]),
                        waits=[t_r2], inc="ss")
                    t_t8 = pr.emit("vector", lambda v: v.max(out=top8[:, :], in_=sc8[:, :]),
                                   waits=[t_sc], inc="vs")
                    pr.emit("vector", lambda v, c16=c16: v.scalar_tensor_tensor(
                        comb_all[:, c16 * NE:(c16 + 1) * NE], sc8[:, :], top8[:, 1:2], sc8[:, :],
                        op0=ALU.is_ge, op1=ALU.mult),
                        waits=[t_t8], inc="vs")

                for c in range(SUP):
                    bl, bh = (bk[0], bk[1]) if c % 2 == 0 else (bk[2], bk[3])
                    if is_m:
                        t_xe = pr.emit("sync", lambda sy, s_abs=s_abs, c=c: sy.dma_start(
                            xeb[:, (c % 2) * D:(c % 2 + 1) * D], x_src(s_abs, c)),
                            waits=[pr.tok("vs")], inc="dxe", inc_by=16)
                    for jh, b2 in ((0, bl), (1, bh)):
                        mms = []
                        for et in range(DT):
                            mms.append((b2.t[:, 0:512],
                                        Kft[:, et * 512 + c * P:et * 512 + (c + 1) * P],
                                        Wbf[:, et * D + jh * 512:et * D + (jh + 1) * 512]))
                        for dt_i in range(DT):
                            mms.append((b2.t[:, 0:512],
                                        hft[:, dt_i * 512 + c * P:dt_i * 512 + (c + 1) * P],
                                        thVn[:, dt_i * D + jh * 512:dt_i * D + (jh + 1) * 512]))
                        mms.append((b2.t[:, 0:512], ones_r1[:, :],
                                    par_buf[0:1, jh * 512:(jh + 1) * 512]))
                        mms.append((b2.t[:, 0:512], ones_r1[:, :],
                                    vsan_r[0:1, jh * 512:(jh + 1) * 512]))
                        b2.mm_group(mms, extra_waits=[t_newton])
                    if is_m:
                        for jh, b2 in ((0, bk[5]), (1, bk[6])):
                            mms = []
                            for et in range(DT):
                                mms.append((b2.t[:, 0:512],
                                            Qft[:, et * 512 + c * P:et * 512 + (c + 1) * P],
                                            Wbf[:, et * D + jh * 512:et * D + (jh + 1) * 512]))
                            mms.append((b2.t[:, 0:512], ones_r1[:, :],
                                        par_buf[0:1, jh * 512:(jh + 1) * 512]))
                            b2.mm_group(mms, extra_waits=[t_newton])
                    for jh, b2 in ((0, bl), (1, bh)):
                        mms = []
                        for cp in range(c):
                            pid = pair_id_s(c, cp)
                            mms.append((b2.t[:, 0:512], Scr[:, pid * P:(pid + 1) * P],
                                        Gsb[:, cp * D + jh * 512:cp * D + (jh + 1) * 512]))
                        if mms:
                            b2.mm_group(mms, extra_waits=[g_tok[c - 1]], cont=True)
                        b2.read("scalar", lambda s, b2=b2, jh=jh: s.copy(
                            RHSs[0:P, jh * 512:(jh + 1) * 512], b2.t[:, 0:512]))
                    t_rhs = pr.tok("ss")
                    if is_m and ep_pend is not None:
                        emit_epilogue(*ep_pend)
                        ep_pend = None
                    bg = bk[4]
                    for jh in range(2):
                        bg.mm_group([(bg.t[:, 0:512], Xsb[:, c * P:(c + 1) * P],
                                      RHSs[:, jh * 512:(jh + 1) * 512])],
                                    extra_waits=[t_rhs])
                        g_tok[c] = bg.read("scalar", lambda s, b5=bg, c=c, jh=jh, gcol=gcol: s.activation(
                            Gsb[:, c * D + jh * 512:c * D + (jh + 1) * 512], b5.t[:, 0:512],
                            AF.Identity, scale=s_gmask[:, gcol:gcol + 1]),
                            extra_waits=[t_super_ts0])
                    if is_m:
                        for jh, b2 in ((0, bk[5]), (1, bk[6])):
                            mms = []
                            for cp in range(c + 1):
                                pid = pair_id_q(c, cp)
                                mms.append((b2.t[:, 0:512], Sqr[:, pid * P:(pid + 1) * P],
                                            Gsb[:, cp * D + jh * 512:cp * D + (jh + 1) * 512]))
                            b2.mm_group(mms, extra_waits=[g_tok[c]], cont=True)
                            b2.read("vector", lambda v, b2=b2, jh=jh, c=c: v.tensor_mul(
                                x2b[:, (c % 2) * D + jh * 512:(c % 2) * D + (jh + 1) * 512],
                                b2.t[:, 0:512],
                                ga_bc[:, jh * 512:(jh + 1) * 512]),
                                extra_waits=[pr.tok("ss"), x2_spill_tok])
                        ep_pend = (c, t_xe)
                if is_m and ep_pend is not None:
                    emit_epilogue(*ep_pend)
                    ep_pend = None
                t_chain = pr.tok("ss")

                t_outs = pr.tok("ts")
                pending_state = (t_chain, t_outs, npar_buf)

            if pending_state is not None:
                emit_state_update(*pending_state)
                pending_state = None

            t_phaseA = [pr.tok("ts"), pr.tok("vs"), pr.tok("ss"), pr.tok("dsp"), pr.tok("dh2s"),
                        pr.tok("dx0"), pr.tok("dx1"), pr.tok("dxe")]

            if debug:
                tap("WT", WT[:, :], t_phaseA)
                tap("Gsb", Gsb[:, :], t_phaseA)
                tap("bf32", b_f32[:, :], t_phaseA)
                tap("comb", comb_all[:, :], t_phaseA)
                tap("Kft", Kft[:, :], t_phaseA)
                tap("Ktk", Ktk[:, :], t_phaseA)
                tap("Qft", Qft[:, :], t_phaseA)
                tap("Xsb", Xsb[:, :], t_phaseA)
                tap("hft", hft[:, :], t_phaseA)
                tap("ksa", ksa_r[:, :], t_phaseA)
                tap("vsan", vsan_r[:, :], t_phaseA)
                tap("modcols_ca1", ca1c[:, :], t_phaseA)
                tap("h2s0", h2stg[:, :], t_phaseA)
                tap("ga_bc", ga_bc[:, 0:D], t_phaseA)
            if debug:
                t_phaseA = t_phaseA + [pr.tok("do")]

        if do_passb:
            with ExitStack() as bctx:
                EB = bctx.enter_context
                acc = EB(nc.sbuf_tensor("acc", [P, (NCH // 2) * D], F32))
                h2b = EB(nc.sbuf_tensor("h2b", [P, DT * D], BF16))
                act = EB(nc.sbuf_tensor("act", [P, DT * D], BF16))
                wG = EB(nc.sbuf_tensor("wG", [P, 2 * DT * D], BF16))
                wU = EB(nc.sbuf_tensor("wU", [P, 2 * DT * D], BF16))
                wD = EB(nc.sbuf_tensor("wD", [P, DT * D], BF16))
                sg2 = EB(nc.sbuf_tensor("sg2", [P, 512], BF16))
                ub2 = EB(nc.sbuf_tensor("ub2", [P, 512], BF16))
                gsbr = EB(nc.sbuf_tensor("gsbr", [1, D], BF16))
                usbr = EB(nc.sbuf_tensor("usbr", [1, D], BF16))
                obuf = EB(nc.sbuf_tensor("obuf", [P, D], F32))

                pes = [("sh", 0), ("sh", 1)] + [("ex", e2) for e2 in range(NE)]

                def pe_srcs(kind, idx):
                    if kind == "sh":
                        return (ap(d_sg_scr, idx * P * DT * D, [[DT * D, P], [1, DT * D]]),
                                ap(d_su_scr, idx * P * DT * D, [[DT * D, P], [1, DT * D]]),
                                ap(d_sd_scr, idx * P * DT * D, [[DT * D, P], [1, DT * D]]),
                                [wprep_toks["sgw"], wprep_toks["suw"], wprep_toks["sdw"]])
                    return (ap(d_gw_scr, idx * P * DT * D, [[DT * D, P], [1, DT * D]]),
                            ap(d_uw_scr, idx * P * DT * D, [[DT * D, P], [1, DT * D]]),
                            ap(d_dw_scr, idx * P * DT * D, [[DT * D, P], [1, DT * D]]),
                            [wprep_toks["gw"], wprep_toks["uw"], wprep_toks["dw"]])

                wslot_free = {0: list(t_phaseA), 1: list(t_phaseA)}
                wd_free = list(t_phaseA)
                out_toks = []

                for th in range(2):
                    t_ac = pr.emit("gpsimd", lambda g: g.memset(acc[:, :], 0.0),
                                   waits=t_phaseA + [pr.tok("vs")], inc="gs")
                    acc_guard = [t_ac]
                    for ht in range(DT):
                        pr.emit("sync", lambda sy, ht=ht, th=th: sy.dma_start(
                            ap(h2b, ht * D, [[DT * D, P], [P, DT], [1, P]]),
                            ap(d_h2s, th * DT * P * D + P * ht + 0, [[D, P], [P * D, DT], [1, P]])),
                            waits=t_phaseA + [pr.tok("ts")], inc="dh2", inc_by=16)
                    h2_load = [pr.tok("dh2")]
                    for pi, (kind, idx) in enumerate(pes):
                        slot = pi % 2
                        gsrc, usrc, dsrc, wtoks = pe_srcs(kind, idx)
                        t_lg_ = pr.emit("sync", lambda sy, gsrc=gsrc, slot=slot: sy.dma_start(
                            ap(wG, slot * DT * D, [[2 * DT * D, P], [1, DT * D]]), gsrc),
                            waits=wtoks + wslot_free[slot], inc=f"dwg{slot}", inc_by=16)
                        t_lu_ = pr.emit("sync", lambda sy, usrc=usrc, slot=slot: sy.dma_start(
                            ap(wU, slot * DT * D, [[2 * DT * D, P], [1, DT * D]]), usrc),
                            waits=wtoks + wslot_free[slot], inc=f"dwu{slot}", inc_by=16)
                        t_ld_ = pr.emit("sync", lambda sy, dsrc=dsrc: sy.dma_start(wD[:, :], dsrc),
                                        waits=wtoks + wd_free, inc="dwd", inc_by=16)
                        for (wbuf, tld, dstr) in ((wG, t_lg_, gsbr), (wU, t_lu_, usbr)):
                            for jh in range(2):
                                b2 = bk[jh]
                                mms = []
                                for ht in range(DT):
                                    mms.append((b2.t[0:1, 0:512], sb_bf[:, ht:ht + 1],
                                                wbuf[:, slot * DT * D + ht * D + jh * 512:
                                                     slot * DT * D + ht * D + (jh + 1) * 512]))
                                b2.mm_group(mms, extra_waits=[tld])
                                b2.read("scalar", lambda s, b2=b2, dstr=dstr, jh=jh: s.copy(
                                    dstr[0:1, jh * 512:(jh + 1) * 512], b2.t[0:1, 0:512]))
                        t_bias = pr.tok("ss")
                        for (wbuf, tld) in ((wG, t_lg_), (wU, t_lu_)):
                            for ht in range(DT):
                                pr.emit("vector", lambda v, wbuf=wbuf, ht=ht, slot=slot: v.tensor_scalar_mul(
                                    wbuf[:, slot * DT * D + ht * D:slot * DT * D + (ht + 1) * D],
                                    wbuf[:, slot * DT * D + ht * D:slot * DT * D + (ht + 1) * D],
                                    cb1c[:, ht:ht + 1]),
                                    waits=[t_bias, tld], inc="vs")
                        t_prime = pr.tok("vs")
                        for fj in range(DT):
                            for tb in range(2):
                                bA = bk[(fj * 2 + tb) % 2]
                                bB = bk[4 + (fj * 2 + tb) % 2]
                                tok_off = th * 1024 + tb * 512
                                mmsA, mmsB = [], []
                                for ht in range(DT):
                                    rh = h2b[:, ht * D + tb * 512:ht * D + (tb + 1) * 512]
                                    mmsA.append((bA.t[:, 0:512],
                                                 wG[:, slot * DT * D + ht * D + fj * P:
                                                    slot * DT * D + ht * D + (fj + 1) * P], rh))
                                    mmsB.append((bB.t[:, 0:512],
                                                 wU[:, slot * DT * D + ht * D + fj * P:
                                                    slot * DT * D + ht * D + (fj + 1) * P], rh))
                                mmsA.append((bA.t[:, 0:512], gsbr[0:1, fj * P:(fj + 1) * P], ones_row[:, :]))
                                mmsB.append((bB.t[:, 0:512], usbr[0:1, fj * P:(fj + 1) * P], ones_row[:, :]))
                                bA.mm_group(mmsA, extra_waits=[t_prime] + h2_load)
                                bB.mm_group(mmsB, extra_waits=[t_prime] + h2_load)
                                t_sg = bA.read("scalar", lambda s, bA=bA: s.activation(
                                    sg2[:, :], bA.t[:, 0:512], AF.Silu))
                                t_ub = bB.read("scalar", lambda s, bB=bB: s.activation(
                                    ub2[:, :], bB.t[:, 0:512], AF.Identity))
                                pr.emit("vector", lambda v, fj=fj, tb=tb: v.tensor_mul(
                                    act[:, fj * D + tb * 512:fj * D + (tb + 1) * 512],
                                    sg2[:, :], ub2[:, :]),
                                    waits=[t_sg, t_ub], inc="vs")
                        t_act = pr.tok("vs")
                        for tt in range(DT):
                            c16 = th * 8 + tt
                            for hh in range(2):
                                bC = bk[2 + (tt * 2 + hh) % 2]
                                mms = []
                                for fj in range(DT):
                                    mms.append((bC.t[:, 0:512],
                                                ap(act, fj * D + tt * P, [[DT * D, P], [1, P]]),
                                                wD[:, fj * D + hh * 512:fj * D + (hh + 1) * 512]))
                                bC.mm_group(mms, extra_waits=[t_act, t_ld_])
                                if kind == "sh":
                                    csl = onef_col[:, 0:1]
                                else:
                                    csl = comb_all[:, c16 * NE + idx:c16 * NE + idx + 1]
                                bC.read("vector", lambda v, bC=bC, tt=tt, hh=hh, csl=csl: v.scalar_tensor_tensor(
                                    acc[:, tt * D + hh * 512:tt * D + (hh + 1) * 512],
                                    bC.t[:, 0:512], csl,
                                    acc[:, tt * D + hh * 512:tt * D + (hh + 1) * 512],
                                    op0=ALU.mult, op1=ALU.add),
                                    extra_waits=acc_guard)
                        wslot_free[slot] = [pr.tok("ts")]
                        wd_free = [pr.tok("ts")]
                    t_accdone = pr.tok("vs")
                    for cc in range(8):
                        c16 = th * 8 + cc
                        slot = cc % 2
                        t_xr = pr.emit("sync", lambda sy, c16=c16, slot=slot: sy.dma_start(
                            ap(xb, slot * D, [[2 * D, P], [1, D]]),
                            ap(d_x2s, c16 * P * D, [[D, P], [1, D]])),
                            waits=[t_accdone] + ([pr.tok("vs")] if cc >= 2 else []),
                            inc=f"dx{slot}", inc_by=16)
                        pr.emit("vector", lambda v, cc=cc: v.tensor_mul(
                            obuf[:, :], acc[:, cc * D:(cc + 1) * D], gb_bc[:, :]),
                            waits=[t_accdone, pr.tok("do")], inc="vs")
                        t_o2 = pr.emit("vector", lambda v, slot=slot: v.tensor_add(
                            obuf[:, :], obuf[:, :], xb[:, slot * D:(slot + 1) * D]),
                            waits=[t_xr], inc="vs")
                        out_toks.append(pr.emit("sync", lambda sy, c16=c16: sy.dma_start(
                            ap(d_out, c16 * P * D, [[D, P], [1, D]]), obuf[:, :]),
                            waits=[t_o2], inc="do", inc_by=16))

                pr.q["sync"].append(([out_toks[-1]], lambda sy: None, None))
                pr.lastw["sync"][out_toks[-1][0]] = out_toks[-1][1]

        else:
            pr.q["sync"].append(([t_phaseA[0]], lambda sy: None, None))

        block = octx.enter_context(nc.Block())

        @block.tensor
        def _(t):
            pr.replay("tensor", t)

        @block.vector
        def _(v):
            pr.replay("vector", v)

        @block.scalar
        def _(s):
            pr.replay("scalar", s)

        @block.gpsimd
        def _(g):
            pr.replay("gpsimd", g)

        @block.sync
        def _(sy):
            pr.replay("sync", sy)

    return nc, taps


def _mask(fn):
    r = np.arange(P)[:, None]
    c = np.arange(P)[None, :]
    return fn(r, c).astype(np.float32).astype(ml_dtypes.bfloat16)


def make_inputs(L0_attn_in_seq, L0_attn_theta_K, L0_attn_theta_V, L0_attn_theta_Q,
                L0_attn_W0, L0_attn_b0, L0_ffn_router_w, L0_ffn_gate_w, L0_ffn_up_w,
                L0_ffn_down_w, L0_ffn_shared_gate_w, L0_ffn_shared_up_w,
                L0_ffn_shared_down_w, L0_c, L0_wmod):
    x = np.ascontiguousarray(np.asarray(L0_attn_in_seq, dtype=np.float32))
    f32 = lambda a: np.ascontiguousarray(np.asarray(a, dtype=np.float32))
    base = {
        "thK": f32(np.asarray(L0_attn_theta_K).T),
        "thV": f32(np.asarray(L0_attn_theta_V).T),
        "thQ": f32(np.asarray(L0_attn_theta_Q).T),
        "w0T": f32(np.asarray(L0_attn_W0).T),
        "b0r": f32(np.asarray(L0_attn_b0).reshape(1, D)),
        "rw": f32(L0_ffn_router_w),
        "gw": f32(L0_ffn_gate_w), "uw": f32(L0_ffn_up_w), "dw": f32(L0_ffn_down_w),
        "sgw": f32(L0_ffn_shared_gate_w), "suw": f32(L0_ffn_shared_up_w),
        "sdw": f32(L0_ffn_shared_down_w),
        "wmod": f32(L0_wmod),
        "identb": np.eye(P, dtype=np.float32).astype(ml_dtypes.bfloat16),
        "mune": _mask(lambda r, c: np.where(r < c, -1.0, 0.0)),
        "mlne": _mask(lambda r, c: np.where(r > c, -1.0, 0.0)),
        "muinc": _mask(lambda r, c: np.where(r <= c, 1.0, 0.0)),
    }
    cmat = f32(L0_c)
    in_maps = []
    for k in range(8):
        b, h = k >> 1, k & 1
        gm = np.zeros((P, 2), dtype=np.float32)
        gm[:, 0] = -2.0 * LR / D * h
        gm[:, 1] = -2.0 * LR / D
        m = dict(base)
        m["xm"] = np.ascontiguousarray(x[b, h * TOWN:(h + 1) * TOWN])
        m["xp"] = np.ascontiguousarray(x[b, 0:TOWN])
        m["gmask"] = gm
        m["cvec"] = np.ascontiguousarray(cmat[b].reshape(D, 1))
        in_maps.append(m)
    return in_maps


def kernel(**inputs):
    in_maps = make_inputs(**inputs)
    nc, _ = build_program(debug=False)
    res = run_bass_kernel_spmd(nc, in_maps, core_ids=list(range(8)))
    out = np.empty((4, 4096, D), dtype=np.float32)
    for k in range(8):
        b, h = k >> 1, k & 1
        out[b, h * TOWN:(h + 1) * TOWN] = res.results[k]["out"]
    return out


# version 07
# speedup vs baseline: 4.1794x; 1.1782x over previous
import numpy as np
import ml_dtypes
from contextlib import ExitStack

import concourse.bass as bass
import concourse.mybir as mybir
from concourse.bass_utils import run_bass_kernel_spmd

F32 = mybir.dt.float32
BF16 = mybir.dt.bfloat16
AF = mybir.ActivationFunctionType
ALU = mybir.AluOpType
AX = mybir.AxisListType

P = 128
D = 1024
DT = 8
SUP = 4
NSUP = 4
NCH = 16
TOWN = 2048
LR = 0.01
EPS = 1e-5
NE = 8
QUEUES = ("tensor", "vector", "scalar", "gpsimd", "sync")
QSEM = {"vector": "vs", "scalar": "ss", "gpsimd": "gs", "sync": "do", "tensor": "ts"}


class Prog:

    def __init__(self, nc, ctx):
        self.nc = nc
        self.ctx = ctx
        self.q = {k: [] for k in QUEUES}
        self.cnt = {}
        self.hand = {}
        self.lastw = {k: {} for k in QUEUES}

    def sem(self, name):
        if name not in self.hand:
            self.hand[name] = self.ctx.enter_context(self.nc.semaphore(name))
            self.cnt[name] = 0
        return self.hand[name]

    def tok(self, name):
        self.sem(name)
        return (name, self.cnt[name])

    def emit(self, queue, fn, waits=(), inc=None, inc_by=1):
        ws = []
        for w in waits:
            if w is None:
                continue
            s, v = w
            if v <= 0:
                continue
            if self.lastw[queue].get(s, 0) >= v:
                continue
            self.lastw[queue][s] = v
            self.sem(s)
            ws.append((s, v))
        tk = None
        it = None
        if inc is not None:
            self.sem(inc)
            self.cnt[inc] += inc_by
            tk = (inc, self.cnt[inc])
            it = (inc, inc_by)
        self.q[queue].append((ws, fn, it))
        return tk

    def replay(self, queue, eng):
        for ws, fn, it in self.q[queue]:
            for s, v in ws:
                eng.wait_ge(self.hand[s], v)
            ins = fn(eng)
            if it is not None:
                assert ins is not None, f"{queue}: inc {it} but fn returned None"
                ins.then_inc(self.hand[it[0]], it[1])


class Bank:

    def __init__(self, pr, name, dtype=F32):
        width = 512 if dtype == F32 else 1024
        self.pr = pr
        self.t = pr.ctx.enter_context(pr.nc.psum_tensor(name, [P, width], dtype))
        self.w = None
        self.readers = []

    def mm_group(self, mms, extra_waits=(), cont=False):
        pr = self.pr
        n = len(mms)
        waits = [w for w in list(extra_waits) + ([] if cont else self.readers) if w is not None]
        tk = None
        for i, (o, l, r) in enumerate(mms):
            tk = pr.emit(
                "tensor",
                (lambda t, o=o, l=l, r=r, st=(i == 0 and not cont), sp=(i == n - 1):
                 t.matmul(o, l, r, start=st, stop=sp)),
                waits=waits if i == 0 else (),
                inc="ts" if i == n - 1 else None)
        self.w = tk
        if not cont:
            self.readers = []
        return tk

    def read(self, queue, fn, extra_waits=()):
        tk = self.pr.emit(queue, fn, waits=[self.w] + list(extra_waits), inc=QSEM[queue])
        self.readers.append(tk)
        return tk


def ap(t, offset, dims):
    return bass.AP(t, offset, dims)


_BUILD_N = [0]


def build_program(debug=False, n_super=2 * NSUP, do_passb=True):
    _BUILD_N[0] += 1
    _tp = "tap_" if _BUILD_N[0] == 1 else f"tap{_BUILD_N[0]}_"
    nc = bass.Bass("TRN2", target_bir_lowering=False)

    d_xm = nc.dram_tensor("xm", [TOWN, D], F32, kind="ExternalInput")
    d_xp = nc.dram_tensor("xp", [TOWN, D], F32, kind="ExternalInput")
    d_gmask = nc.dram_tensor("gmask", [P, 2], F32, kind="ExternalInput")
    d_cvec = nc.dram_tensor("cvec", [D, 1], F32, kind="ExternalInput")
    d_wmod = nc.dram_tensor("wmod", [D, 6 * D], F32, kind="ExternalInput")
    d_thK = nc.dram_tensor("thK", [D, D], F32, kind="ExternalInput")
    d_thV = nc.dram_tensor("thV", [D, D], F32, kind="ExternalInput")
    d_thQ = nc.dram_tensor("thQ", [D, D], F32, kind="ExternalInput")
    d_w0T = nc.dram_tensor("w0T", [D, D], F32, kind="ExternalInput")
    d_b0 = nc.dram_tensor("b0r", [1, D], F32, kind="ExternalInput")
    d_rw = nc.dram_tensor("rw", [D, NE], F32, kind="ExternalInput")
    d_gw = nc.dram_tensor("gw", [NE, D, D], F32, kind="ExternalInput")
    d_uw = nc.dram_tensor("uw", [NE, D, D], F32, kind="ExternalInput")
    d_dw = nc.dram_tensor("dw", [NE, D, D], F32, kind="ExternalInput")
    d_sgw = nc.dram_tensor("sgw", [D, 2 * D], F32, kind="ExternalInput")
    d_suw = nc.dram_tensor("suw", [D, 2 * D], F32, kind="ExternalInput")
    d_sdw = nc.dram_tensor("sdw", [2 * D, D], F32, kind="ExternalInput")
    d_ident = nc.dram_tensor("identb", [P, P], BF16, kind="ExternalInput")
    d_mune = nc.dram_tensor("mune", [P, P], BF16, kind="ExternalInput")
    d_mlne = nc.dram_tensor("mlne", [P, P], BF16, kind="ExternalInput")
    d_muinc = nc.dram_tensor("muinc", [P, P], BF16, kind="ExternalInput")

    d_out = nc.dram_tensor("out", [TOWN, D], F32, kind="ExternalOutput")

    d_x2s = nc.dram_tensor("x2s", [TOWN, D], F32)
    d_h2s = nc.dram_tensor("h2s", [NCH * P, D], BF16)
    d_modscr = nc.dram_tensor("modscr", [1, 6 * D], F32)
    d_thq_scr = nc.dram_tensor("thq_scr", [D, D], BF16)
    d_gw_scr = nc.dram_tensor("gw_scr", [NE, P, DT * D], BF16)
    d_uw_scr = nc.dram_tensor("uw_scr", [NE, P, DT * D], BF16)
    d_dw_scr = nc.dram_tensor("dw_scr", [NE, P, DT * D], BF16)
    d_sg_scr = nc.dram_tensor("sg_scr", [2, P, DT * D], BF16)
    d_su_scr = nc.dram_tensor("su_scr", [2, P, DT * D], BF16)
    d_sd_scr = nc.dram_tensor("sd_scr", [2, P, DT * D], BF16)

    taps = {}

    with ExitStack() as octx:
        pr = Prog(nc, octx)
        E = octx.enter_context

        def tap(name, apv, waits):
            if not debug:
                return
            dt_ = nc.dram_tensor(_tp + name, list(apv.shape), apv.dtype, kind="ExternalOutput")
            taps[name] = True
            pr.emit("sync", lambda sy, a=apv, d2=dt_: sy.dma_start(d2[...], a),
                    waits=waits, inc="do", inc_by=16)

        s_ident = E(nc.sbuf_tensor("s_ident", [P, P], BF16))
        s_mune = E(nc.sbuf_tensor("s_mune", [P, P], BF16))
        s_mlne = E(nc.sbuf_tensor("s_mlne", [P, P], BF16))
        s_muinc = E(nc.sbuf_tensor("s_muinc", [P, P], BF16))
        ones_r1 = E(nc.sbuf_tensor("ones_r1", [1, P], BF16))
        ones_row = E(nc.sbuf_tensor("ones_row", [1, 512], BF16))
        ones_r1f = E(nc.sbuf_tensor("ones_r1f", [1, P], F32))
        ones_cl = E(nc.sbuf_tensor("ones_cl", [P, 1], BF16))
        onef_col = E(nc.sbuf_tensor("onef_col", [P, 1], F32))
        s_gmask = E(nc.sbuf_tensor("s_gmask", [P, 2], F32))
        gb_bc = E(nc.sbuf_tensor("gb_bc", [P, D], F32))
        ga_bc = E(nc.sbuf_tensor("ga_bc", [P, D], F32))
        comb_all = E(nc.sbuf_tensor("comb_all", [P, NCH * NE], F32))
        b_f32 = E(nc.sbuf_tensor("b_f32", [1, D], F32))
        b_bfA = E(nc.sbuf_tensor("b_bfA", [1, D], BF16))
        b_bfB = E(nc.sbuf_tensor("b_bfB", [1, D], BF16))
        xb = E(nc.sbuf_tensor("xb", [P, 2 * D], F32))
        lnt = E(nc.sbuf_tensor("lnt", [P, 16], F32))
        hbf = E(nc.sbuf_tensor("hbf", [P, D], BF16))
        lg = E(nc.sbuf_tensor("lg", [P, NE], F32))
        zex = E(nc.sbuf_tensor("zex", [P, NE], F32))
        sc8 = E(nc.sbuf_tensor("sc8", [P, NE], F32))
        top8 = E(nc.sbuf_tensor("top8", [P, 8], F32))
        rcol = E(nc.sbuf_tensor("rcol", [P, 4], F32))
        h2stg = E(nc.sbuf_tensor("h2stg", [P, D], BF16))
        ca1c = E(nc.sbuf_tensor("ca1c", [P, DT], F32))
        cb1c = E(nc.sbuf_tensor("cb1c", [P, DT], F32))
        sa_f = E(nc.sbuf_tensor("sa_f", [P, DT], F32))
        sb_f = E(nc.sbuf_tensor("sb_f", [P, DT], F32))
        rw_bf = E(nc.sbuf_tensor("rw_bf", [P, DT * NE], BF16))

        bk = [Bank(pr, f"bk{i}") for i in range(7)]
        bkT = Bank(pr, "bkT", BF16)

        cdb = E(nc.sbuf_tensor("cdb", [P, 4], F32))
        pr.emit("gpsimd", lambda g: g.memset(cdb[:, 0:1], 2 * LR / D), inc="gs")
        pr.emit("gpsimd", lambda g: g.memset(cdb[:, 1:2], 1.0), inc="gs")
        pr.emit("gpsimd", lambda g: g.memset(cdb[:, 2:3], 0.0), inc="gs")
        pr.emit("gpsimd", lambda g: g.memset(cdb[:, 3:4], -1.0), inc="gs")
        nc.const_aps.register(F32, 2 * LR / D, cdb[:, 0:1])
        nc.const_aps.register(F32, 1.0, cdb[:, 1:2])
        nc.const_aps.register(F32, 0.0, cdb[:, 2:3])
        nc.const_aps.register(F32, -1.0, cdb[:, 3:4])

        for t_, d_ in ((s_ident, d_ident), (s_mune, d_mune), (s_mlne, d_mlne), (s_muinc, d_muinc)):
            pr.emit("sync", lambda sy, t2=t_, d2=d_: sy.dma_start(t2[:, :], d2[:, :]),
                    inc="dini", inc_by=16)
        pr.emit("sync", lambda sy: sy.dma_start(s_gmask[:, :], d_gmask[:, :]), inc="dini", inc_by=16)
        pr.emit("sync", lambda sy: sy.dma_start(b_f32[:, :], d_b0[:, :]), inc="dini", inc_by=16)
        pr.emit("gpsimd", lambda g: g.memset(ones_r1[:, :], 1.0), inc="gs")
        pr.emit("gpsimd", lambda g: g.memset(ones_row[:, :], 1.0), inc="gs")
        pr.emit("gpsimd", lambda g: g.memset(ones_cl[:, :], 1.0), inc="gs")
        pr.emit("gpsimd", lambda g: g.memset(ones_r1f[:, :], 1.0), inc="gs")
        pr.emit("gpsimd", lambda g: g.memset(onef_col[:, :], 1.0), inc="gs")
        t_dl0 = pr.tok("dini")
        t_ms0 = pr.tok("gs")

        t_bb0 = pr.emit("scalar", lambda s: s.copy(b_bfA[0:1, :], b_f32[0:1, :]),
                        waits=[t_dl0], inc="ss")

        wprep_toks = {}


        with ExitStack() as actx:
            EA = actx.enter_context

            with ExitStack() as wctx:
                wmb = wctx.enter_context(nc.sbuf_tensor("wmb", [P, 2 * 6 * D], F32))
                silc = wctx.enter_context(nc.sbuf_tensor("silc", [P, DT], F32))
                modr = wctx.enter_context(nc.sbuf_tensor("modr", [1, 6 * D], F32))
                tmpr = wctx.enter_context(nc.sbuf_tensor("tmpr", [1, 3 * D], F32))

                t_cv = pr.emit("sync", lambda sy: sy.dma_start(
                    silc[:, :], ap(d_cvec, 0, [[1, P], [P, DT]]),
                    allow_slow_non_contiguous=True), inc="dcv", inc_by=16)
                t_sil = pr.emit("scalar", lambda s: s.activation(silc[:, :], silc[:, :], AF.Silu),
                                waits=[t_cv], inc="ss")
                prev_mm = [None] * DT
                for pas in range(2):
                    mm_toks = []
                    for dt_i in range(DT):
                        slot = dt_i % 2
                        wl = [mm_toks[dt_i - 2]] if dt_i >= 2 else [prev_mm[DT - 2 + dt_i]]
                        tkd = pr.emit("sync", lambda sy, dt_i=dt_i, slot=slot: sy.dma_start(
                            ap(wmb, slot * 6 * D, [[2 * 6 * D, P], [1, 6 * D]]),
                            ap(d_wmod, dt_i * P * 6 * D, [[6 * D, P], [1, 6 * D]])),
                            waits=wl, inc=f"dwm{slot}", inc_by=16)
                        tk = None
                        for jj in range(6):
                            jb = pas * 6 + jj
                            b2 = bk[jj]
                            o = b2.t[0:1, 0:512]
                            l = silc[:, dt_i:dt_i + 1]
                            r = wmb[:, slot * 6 * D + jb * 512:slot * 6 * D + (jb + 1) * 512]
                            wl2 = [tkd]
                            if dt_i == 0:
                                wl2 += [t_sil] + b2.readers
                            tk = pr.emit("tensor",
                                         (lambda t, o=o, l=l, r=r, st=(dt_i == 0), sp=(dt_i == DT - 1):
                                          t.matmul(o, l, r, start=st, stop=sp)),
                                         waits=wl2, inc="ts" if jj == 5 else None)
                        mm_toks.append(tk)
                        if dt_i == DT - 1:
                            for jj in range(6):
                                bk[jj].w = tk
                                bk[jj].readers = []
                    for jj in range(6):
                        jb = pas * 6 + jj
                        bk[jj].read("vector", lambda v, jj=jj, jb=jb: v.tensor_copy(
                            modr[0:1, jb * 512:(jb + 1) * 512], bk[jj].t[0:1, 0:512]))
                    prev_mm = mm_toks
                t_mod = pr.tok("vs")

                for idx in (2, 5):
                    g_sl = modr[0:1, idx * D:(idx + 1) * D]
                    a_sl = tmpr[0:1, 0:D]
                    bb_sl = tmpr[0:1, D:2 * D]
                    s_sl = tmpr[0:1, 2 * D:3 * D]
                    pr.emit("vector", lambda v, g=g_sl, a=a_sl: v.tensor_scalar_max(a, g, 0.25),
                            waits=[t_mod], inc="vs")
                    pr.emit("vector", lambda v, g=g_sl, b3=bb_sl: v.tensor_scalar_min(b3, g, -0.25), inc="vs")
                    tc0 = pr.emit("vector", lambda v, g=g_sl, s2=s_sl: v.tensor_single_scalar(s2, g, 0.0, op=ALU.is_ge), inc="vs")
                    tc1 = pr.emit("vector", lambda v, a=a_sl, b3=bb_sl: v.tensor_sub(a, a, b3),
                                  waits=[tc0], inc="vs")
                    tc2 = pr.emit("vector", lambda v, a=a_sl, s2=s_sl: v.tensor_mul(a, a, s2),
                                  waits=[tc1], inc="vs")
                    pr.emit("vector", lambda v, g=g_sl, a=a_sl, b3=bb_sl: v.tensor_add(g, b3, a),
                            waits=[tc2], inc="vs")
                t_clamp = pr.tok("vs")

                for row_i, dst in ((2, ga_bc), (5, gb_bc)):
                    for jh in range(2):
                        b2 = bk[6]
                        b2.mm_group(
                            [(b2.t[:, 0:512], ones_r1f[:, :],
                              modr[0:1, row_i * D + jh * 512:row_i * D + (jh + 1) * 512])],
                            extra_waits=[t_clamp, t_ms0])
                        b2.read("vector", lambda v, dst=dst, jh=jh, b2=b2: v.tensor_copy(
                            dst[:, jh * 512:(jh + 1) * 512], b2.t[:, 0:512]))
                t_ms = pr.emit("sync", lambda sy: sy.dma_start(d_modscr[:, :], modr[:, :]),
                               waits=[t_clamp], inc="dms", inc_by=16)
                t_l1 = pr.emit("sync", lambda sy: sy.dma_start(
                    ca1c[:, :], ap(d_modscr, 1 * D, [[1, P], [P, DT]]),
                    allow_slow_non_contiguous=True), waits=[t_ms], inc="dca", inc_by=16)
                t_l2 = pr.emit("sync", lambda sy: sy.dma_start(
                    cb1c[:, :], ap(d_modscr, 4 * D, [[1, P], [P, DT]]),
                    allow_slow_non_contiguous=True), waits=[t_ms], inc="dcb", inc_by=16)
                t_l3 = pr.emit("gpsimd", lambda g: g.dma_start(
                    sa_f[:, :], ap(d_modscr, 0 * D, [[1, P], [P, DT]]),
                    allow_slow_non_contiguous=True), waits=[t_ms], inc="dsa", inc_by=16)
                t_l4 = pr.emit("gpsimd", lambda g: g.dma_start(
                    sb_f[:, :], ap(d_modscr, 3 * D, [[1, P], [P, DT]]),
                    allow_slow_non_contiguous=True), waits=[t_ms], inc="dsb2", inc_by=16)
                t_c1 = pr.emit("vector", lambda v: v.tensor_scalar_add(ca1c[:, :], ca1c[:, :], 1.0),
                               waits=[t_l1], inc="vs")
                t_c2 = pr.emit("vector", lambda v: v.tensor_scalar_add(cb1c[:, :], cb1c[:, :], 1.0),
                               waits=[t_l2], inc="vs")
                t_wmod_ts = pr.tok("ts")
                t_wmod_vs = pr.tok("vs")

            thK = EA(nc.sbuf_tensor("s_thK", [P, DT * D], BF16))
            thVn = EA(nc.sbuf_tensor("s_thVn", [P, DT * D], BF16))
            thQp = EA(nc.sbuf_tensor("s_thQp", [P, DT * D], BF16))
            WT = EA(nc.sbuf_tensor("WT", [P, DT * D], F32))
            Wbf = EA(nc.sbuf_tensor("Wbf", [P, DT * D], BF16))
            hft = EA(nc.sbuf_tensor("hft", [P, DT * 512], BF16))
            Kft = EA(nc.sbuf_tensor("Kft", [P, DT * 512], BF16))
            Ktk = EA(nc.sbuf_tensor("Ktk", [P, SUP * D], BF16))
            Qft = EA(nc.sbuf_tensor("Qft", [P, DT * 512], BF16))
            Gsb = EA(nc.sbuf_tensor("Gsb", [P, SUP * D], BF16))
            Scr = EA(nc.sbuf_tensor("Scr", [P, 6 * P], BF16))
            Sqr = EA(nc.sbuf_tensor("Sqr", [P, 10 * P], BF16))
            Usb = EA(nc.sbuf_tensor("Usb", [P, P], BF16))
            Bp = EA(nc.sbuf_tensor("Bp", [P, 2 * SUP * P], BF16))
            Btp = EA(nc.sbuf_tensor("Btp", [P, 2 * SUP * P], BF16))
            Dsb = EA(nc.sbuf_tensor("Dsb", [P, SUP * P], BF16))
            Xsb = EA(nc.sbuf_tensor("Xsb", [P, SUP * P], BF16))
            RHSs = EA(nc.sbuf_tensor("RHSs", [P, D], BF16))
            x2b = EA(nc.sbuf_tensor("x2b", [P, 2 * D], F32))
            xeb = EA(nc.sbuf_tensor("xeb", [P, 2 * D], F32))

            for dt_i in range(DT):
                pr.emit("gpsimd", lambda g, dt_i=dt_i: g.dma_start(
                    ap(thK, dt_i * D, [[DT * D, P], [1, D]]),
                    ap(d_thK, dt_i * P * D, [[D, P], [1, D]])),
                    waits=[t_wmod_ts, t_wmod_vs], inc="dthk", inc_by=16)
            t_thk_all = pr.tok("dthk")
            thk_toks = [t_thk_all] * DT
            for dt_i in range(DT):
                pr.emit("gpsimd", lambda g, dt_i=dt_i: g.dma_start(
                    ap(thVn, dt_i * D, [[DT * D, P], [1, D]]),
                    ap(d_thV, dt_i * P * D, [[D, P], [1, D]])),
                    waits=[t_wmod_ts, t_wmod_vs], inc="dthv", inc_by=16)
            t_thv_all = pr.tok("dthv")
            thv_toks = [t_thv_all] * DT
            for dt_i in range(DT):
                pr.emit("gpsimd", lambda g, dt_i=dt_i: g.dma_start(
                    ap(thQp, dt_i * D, [[DT * D, P], [1, D]]),
                    ap(d_thQ, dt_i * P * D, [[D, P], [1, D]])),
                    waits=[t_wmod_ts, t_wmod_vs], inc="dthq", inc_by=16)
            t_thq_all = pr.tok("dthq")
            thq_toks = [t_thq_all] * DT

            for dt_i in range(DT):
                pr.emit("scalar", lambda s, dt_i=dt_i: s.activation(
                    thVn[:, dt_i * D:(dt_i + 1) * D], thVn[:, dt_i * D:(dt_i + 1) * D],
                    AF.Identity, scale=-1.0),
                    waits=[thv_toks[dt_i]], inc="ss")
            t_thprime = pr.tok("ss")
            t_qprime = t_thq_all

            t_rwl = pr.emit("gpsimd", lambda g: g.dma_start(
                rw_bf[:, :], ap(d_rw, 0, [[NE, P], [P * NE, DT], [1, NE]])),
                waits=[t_wmod_ts, t_wmod_vs], inc="drw", inc_by=16)


            def wprep_family(name, pairs):
                for s_ap, d_ap in pairs:
                    pr.emit("gpsimd", lambda g, s2=s_ap, d2=d_ap: g.dma_start(d2, s2),
                            inc="dp", inc_by=16)
                wprep_toks[name] = pr.tok("dp")

            def gu_pairs(src, dst):
                out = []
                for e2 in range(NE):
                    for ht in range(DT):
                        out.append((ap(src, e2 * D * D + ht * P * D, [[D, P], [1, D]]),
                                    ap(dst, e2 * P * DT * D + ht * D, [[DT * D, P], [1, D]])))
                return out
            wprep_family("gw", gu_pairs(d_gw, d_gw_scr))
            wprep_family("uw", gu_pairs(d_uw, d_uw_scr))
            wprep_family("dw", gu_pairs(d_dw, d_dw_scr))

            def sh_pairs(src, dst):
                out = []
                for fh in range(2):
                    for ht in range(DT):
                        out.append((ap(src, ht * P * 2 * D + fh * D, [[2 * D, P], [1, D]]),
                                    ap(dst, fh * P * DT * D + ht * D, [[DT * D, P], [1, D]])))
                return out
            wprep_family("sgw", sh_pairs(d_sgw, d_sg_scr))
            wprep_family("suw", sh_pairs(d_suw, d_su_scr))
            sd_pairs = []
            for fh in range(2):
                for fj in range(DT):
                    sd_pairs.append((ap(d_sdw, (fh * D + fj * P) * D, [[D, P], [1, D]]),
                                     ap(d_sd_scr, fh * P * DT * D + fj * D, [[DT * D, P], [1, D]])))
            wprep_family("sdw", sd_pairs)
            dp_all = pr.tok("dp")
            for k2 in ("gw", "uw", "dw", "sgw", "suw", "sdw"):
                wprep_toks[k2] = dp_all

            for dt_i in range(DT):
                pr.emit("sync", lambda sy, dt_i=dt_i: sy.dma_start(
                    ap(WT, dt_i * D, [[DT * D, P], [1, D]]),
                    ap(d_w0T, dt_i * P * D, [[D, P], [1, D]])),
                    waits=[t_wmod_ts, t_wmod_vs], inc="dw0t", inc_by=16)
            t_w0_all = pr.tok("dw0t")
            for dt_i in range(DT):
                pr.emit("scalar", lambda s, dt_i=dt_i: s.copy(
                    Wbf[:, dt_i * D:(dt_i + 1) * D], WT[:, dt_i * D:(dt_i + 1) * D]),
                    waits=[t_w0_all], inc="ss")
            t_w0 = pr.tok("ss")

            t_prologue = [pr.tok("ts"), pr.tok("vs"), pr.tok("ss"), pr.tok("gs"),
                          t_w0, t_bb0, t_thprime, t_qprime,
                          t_thk_all, t_thq_all, t_rwl, t_l3, t_l4]

            ln_done = {}
            x_dma = {}
            x2_spill_tok = None
            h2_guard = [None, None]
            last_hn = [None]
            last_tr = [None]

            def x_src(s_abs, ci):
                d_src = d_xp if s_abs < NSUP else d_xm
                c16 = (s_abs % NSUP) * SUP + ci
                return ap(d_src, c16 * P * D, [[D, P], [1, D]])

            def ensure_xdma(cl):
                if cl >= 2 * NSUP * SUP or cl in x_dma:
                    return
                slot = cl % 2
                wl = list(t_prologue) if cl == 0 else []
                if cl - 2 >= 0:
                    wl += list(ln_done[cl - 2])
                x_dma[cl] = pr.emit("sync", lambda sy, cl=cl, slot=slot: sy.dma_start(
                    ap(xb, slot * D, [[2 * D, P], [1, D]]), x_src(cl // SUP, cl % SUP)),
                    waits=wl, inc=f"dx{slot}", inc_by=16)

            def pair_id_s(c, cp):
                return c * (c - 1) // 2 + cp

            def pair_id_q(c, cp):
                return c * (c + 1) // 2 + cp

            def emit_ln(x_ap, wait0, hazard_guard):
                pr.emit("vector", lambda v, x3=x_ap: v.bn_stats(lnt[:, 0:6], x3[:, 0:512]),
                        waits=wait0 + [last_hn[0]], inc="vs")
                t_b2 = pr.emit("vector", lambda v, x3=x_ap: v.bn_stats(lnt[:, 6:12], x3[:, 512:1024]), inc="vs")
                t_ag = pr.emit("vector", lambda v: v.bn_aggr(lnt[:, 12:14], lnt[:, 0:12]),
                               waits=[t_b2], inc="vs")
                t_v1 = pr.emit("vector", lambda v: v.tensor_scalar_add(lnt[:, 14:15], lnt[:, 13:14], EPS),
                               waits=[t_ag], inc="vs")
                t_s1 = pr.emit("scalar", lambda s: s.activation(lnt[:, 14:15], lnt[:, 14:15], AF.Sqrt),
                               waits=[t_v1], inc="ss")
                t_rc = pr.emit("vector", lambda v: v.reciprocal(lnt[:, 15:16], lnt[:, 14:15]),
                               waits=[t_s1], inc="vs")
                t_v3 = pr.emit("vector", lambda v: v.scalar_tensor_tensor(
                    lnt[:, 14:15], lnt[:, 12:13], -1.0, lnt[:, 15:16], op0=ALU.mult, op1=ALU.mult),
                    waits=[t_rc], inc="vs")
                t_hn = pr.emit("scalar", lambda s, x3=x_ap: s.activation(
                    hbf[:, :], x3, AF.Identity, bias=lnt[:, 14:15], scale=lnt[:, 15:16]),
                    waits=[t_v3] + hazard_guard, inc="ss")
                last_hn[0] = t_hn
                return t_v3, t_hn

            def emit_transposes(t_hn, dst_fns, extra=()):
                tk = None
                for i in range(DT):
                    tk = pr.emit("tensor",
                                 (lambda t, i=i: t.transpose(
                                     bkT.t[:, i * P:(i + 1) * P], hbf[:, i * P:(i + 1) * P],
                                     s_ident[:, :])),
                                 waits=([t_hn] + bkT.readers) if i == 0 else (),
                                 inc="ts" if i == DT - 1 else None)
                bkT.w = tk
                bkT.readers = []
                last_tr[0] = tk
                tk = None
                for fi, fn in enumerate(dst_fns):
                    tk = bkT.read("scalar", fn, extra_waits=list(extra) if fi == 0 else ())
                return tk

            def emit_state_update(t_chain, t_outs, npar_buf):
                for dt_i in range(DT):
                    for jh in range(2):
                        b2 = bk[(dt_i * 2 + jh) % 2]
                        mms = []
                        for c in range(SUP):
                            mms.append((b2.t[:, 0:512],
                                        Ktk[:, c * D + dt_i * P:c * D + (dt_i + 1) * P],
                                        Gsb[:, c * D + jh * 512:c * D + (jh + 1) * 512]))
                        b2.mm_group(mms, extra_waits=[t_chain])
                        b2.read("vector", lambda v, b2=b2, dt_i=dt_i, jh=jh: v.tensor_add(
                            WT[:, dt_i * D + jh * 512:dt_i * D + (jh + 1) * 512],
                            WT[:, dt_i * D + jh * 512:dt_i * D + (jh + 1) * 512],
                            b2.t[:, 0:512]))
                t_wupd = pr.tok("vs")
                for dt_i in range(DT):
                    pr.emit("scalar", lambda s, dt_i=dt_i: s.copy(
                        Wbf[:, dt_i * D:(dt_i + 1) * D], WT[:, dt_i * D:(dt_i + 1) * D]),
                        waits=[t_wupd, t_outs], inc="ss")
                for jh in range(2):
                    b2 = bk[2 + jh]
                    mms = []
                    for c in range(SUP):
                        mms.append((b2.t[0:1, 0:512], ones_cl[:, :],
                                    Gsb[:, c * D + jh * 512:c * D + (jh + 1) * 512]))
                    b2.mm_group(mms, extra_waits=[t_chain])
                    b2.read("vector", lambda v, b2=b2, jh=jh: v.tensor_add(
                        b_f32[0:1, jh * 512:(jh + 1) * 512],
                        b_f32[0:1, jh * 512:(jh + 1) * 512], b2.t[0:1, 0:512]))
                t_bupd = pr.tok("vs")
                pr.emit("scalar", lambda s, nb=npar_buf: s.copy(nb[0:1, :], b_f32[0:1, :]),
                        waits=[t_bupd, t_outs], inc="ss")

            pending_state = None

            for s_abs in range(n_super):
                is_m = s_abs >= NSUP
                par_buf = b_bfA if s_abs % 2 == 0 else b_bfB
                npar_buf = b_bfB if s_abs % 2 == 0 else b_bfA
                gcol = 1 if is_m else 0
                t_guard_a = pr.tok("ts")
                if pending_state is not None:
                    emit_state_update(*pending_state)
                    pending_state = None
                t_super_ts0 = pr.tok("ts")

                for ci in range(SUP):
                    cl = s_abs * SUP + ci
                    ensure_xdma(cl)
                    ensure_xdma(cl + 1)
                    slot = cl % 2
                    xsl = xb[:, slot * D:(slot + 1) * D]
                    t_v3, t_hn = emit_ln(xsl, [x_dma[cl]], [last_tr[0]])
                    ln_done[cl] = (t_v3, t_hn)
                    ensure_xdma(cl + 2)
                    emit_transposes(t_hn, [
                        (lambda s, ci=ci, dt=dt: s.activation(
                            hft[:, dt * 512 + ci * P:dt * 512 + ci * P + P],
                            bkT.t[:, dt * P:(dt + 1) * P], AF.Identity,
                            scale=ca1c[:, dt:dt + 1], bias=sa_f[:, dt:dt + 1]))
                        for dt in range(DT)],
                        extra=[t_guard_a])
                t_hft = pr.tok("ss")

                for et in range(DT):
                    b2 = bk[2 + et % 2]
                    mms = []
                    for dt_i in range(DT):
                        mms.append((b2.t[:, 0:512],
                                    thK[:, dt_i * D + et * P:dt_i * D + (et + 1) * P],
                                    hft[:, dt_i * 512:dt_i * 512 + 512]))
                    b2.mm_group(mms, extra_waits=[t_hft])
                    b2.read("scalar", lambda s, et=et, b2=b2: s.copy(
                        Kft[:, et * 512:(et + 1) * 512], b2.t[:, 0:512]),
                        extra_waits=[t_super_ts0])
                if is_m:
                    for et in range(DT):
                        b2 = bk[4 + et % 2]
                        mms = []
                        for dt_i in range(DT):
                            mms.append((b2.t[:, 0:512],
                                        thQp[:, dt_i * D + et * P:dt_i * D + (et + 1) * P],
                                        hft[:, dt_i * 512:dt_i * 512 + 512]))
                        b2.mm_group(mms, extra_waits=[t_hft])
                        b2.read("scalar", lambda s, et=et, b2=b2: s.copy(
                            Qft[:, et * 512:(et + 1) * 512], b2.t[:, 0:512]),
                            extra_waits=[t_super_ts0])
                t_proj = [pr.tok("ss"), pr.tok("vs")]

                for cp in range(SUP):
                    b2 = bk[cp % 2]
                    mms = []
                    for et in range(DT):
                        mms.append((b2.t[:, 0:512],
                                    Kft[:, et * 512 + cp * P:et * 512 + (cp + 1) * P],
                                    Kft[:, et * 512:et * 512 + 512]))
                    b2.mm_group(mms, extra_waits=t_proj)
                    t_u = b2.read("scalar", lambda s, b2=b2, cp=cp: s.activation(
                        Usb[:, :], b2.t[:, cp * P:(cp + 1) * P], AF.Identity,
                        scale=2 * LR / D, bias=2 * LR / D),
                        extra_waits=[pr.tok("vs")])
                    pr.emit("vector", lambda v, cp=cp: v.tensor_mul(
                        Bp[:, cp * P:(cp + 1) * P], Usb[:, :], s_mune[:, :]),
                        waits=[t_u, t_super_ts0], inc="vs")
                    pr.emit("vector", lambda v, cp=cp: v.tensor_mul(
                        Btp[:, cp * P:(cp + 1) * P], Usb[:, :], s_mlne[:, :]),
                        waits=[t_u], inc="vs")
                    for c in range(cp + 1, SUP):
                        pid = pair_id_s(c, cp)
                        b2.read("scalar", lambda s, b2=b2, pid=pid, c=c: s.activation(
                            Scr[:, pid * P:(pid + 1) * P], b2.t[:, c * P:(c + 1) * P],
                            AF.Identity, scale=1.0, bias=1.0),
                            extra_waits=[t_super_ts0])
                if is_m:
                    for cp in range(SUP):
                        b2 = bk[2 + cp % 2]
                        mms = []
                        for et in range(DT):
                            mms.append((b2.t[:, 0:512],
                                        Kft[:, et * 512 + cp * P:et * 512 + (cp + 1) * P],
                                        Qft[:, et * 512:et * 512 + 512]))
                        b2.mm_group(mms, extra_waits=t_proj)
                        pid = pair_id_q(cp, cp)
                        t_u = b2.read("scalar", lambda s, b2=b2, cp=cp: s.activation(
                            Usb[:, :], b2.t[:, cp * P:(cp + 1) * P], AF.Identity, scale=1.0, bias=1.0),
                            extra_waits=[pr.tok("vs")])
                        pr.emit("vector", lambda v, pid=pid: v.tensor_mul(
                            Sqr[:, pid * P:(pid + 1) * P], Usb[:, :], s_muinc[:, :]),
                            waits=[t_u, t_super_ts0], inc="vs")
                        for c in range(cp + 1, SUP):
                            pid = pair_id_q(c, cp)
                            b2.read("scalar", lambda s, b2=b2, pid=pid, c=c: s.activation(
                                Sqr[:, pid * P:(pid + 1) * P], b2.t[:, c * P:(c + 1) * P],
                                AF.Identity, scale=1.0, bias=1.0),
                                extra_waits=[t_super_ts0])
                t_pairs = [pr.tok("ss"), pr.tok("vs")]

                for c in range(SUP):
                    tk = None
                    for et in range(DT):
                        tk = pr.emit("tensor",
                                     (lambda t, c=c, et=et: t.transpose(
                                         bkT.t[:, et * P:(et + 1) * P],
                                         Kft[:, et * 512 + c * P:et * 512 + c * P + P],
                                         s_ident[:, :])),
                                     waits=(t_proj + bkT.readers) if et == 0 else (),
                                     inc="ts" if et == DT - 1 else None)
                    bkT.w = tk
                    bkT.readers = []
                    bkT.read("scalar", lambda s, c=c: s.copy(
                        Ktk[:, c * D:(c + 1) * D], bkT.t[:, :]))

                x_tok = {}
                for c in range(SUP):
                    x_tok[c] = pr.emit("vector", lambda v, c=c: v.tensor_add(
                        Xsb[:, c * P:(c + 1) * P], s_ident[:, :], Bp[:, c * P:(c + 1) * P]),
                        waits=t_pairs + [t_super_ts0], inc="vs")
                ds_free = {c: None for c in range(SUP)}
                for m in range(1, 7):
                    pslot = (m - 1) % 2
                    cslot = m % 2
                    last = (m == 6)
                    bt_tok = {}
                    if not last:
                        for c in range(SUP):
                            bsl = Bp[:, (pslot * SUP + c) * P:(pslot * SUP + c + 1) * P]
                            btsl = Btp[:, (pslot * SUP + c) * P:(pslot * SUP + c + 1) * P]
                            b2 = bk[c % 2]
                            b2.mm_group([(b2.t[:, 0:P], btsl, bsl)],
                                        extra_waits=[x_tok[c]] if m == 1 else [])
                            b2.read("scalar", lambda s, b2=b2, c=c, cslot=cslot: s.copy(
                                Bp[:, (cslot * SUP + c) * P:(cslot * SUP + c + 1) * P], b2.t[:, 0:P]))
                    for c in range(SUP):
                        bsl = Bp[:, (pslot * SUP + c) * P:(pslot * SUP + c + 1) * P]
                        btsl = Btp[:, (pslot * SUP + c) * P:(pslot * SUP + c + 1) * P]
                        b3 = bk[2 + c % 2]
                        b3.mm_group([(b3.t[:, 0:P], bsl, btsl)],
                                    extra_waits=[x_tok[c]] if m == 1 else [])
                        bt_tok[c] = b3.read("scalar", lambda s, b3=b3, c=c, cslot=cslot: s.copy(
                            Btp[:, (cslot * SUP + c) * P:(cslot * SUP + c + 1) * P], b3.t[:, 0:P]))
                    for c in range(SUP):
                        b4 = bk[4 + c % 2]
                        b4.mm_group([(b4.t[:, 0:P],
                                      Btp[:, (cslot * SUP + c) * P:(cslot * SUP + c + 1) * P],
                                      Xsb[:, c * P:(c + 1) * P])],
                                    extra_waits=[bt_tok[c], x_tok[c]])
                        t_d = b4.read("scalar", lambda s, b4=b4, c=c: s.copy(
                            Dsb[:, c * P:(c + 1) * P], b4.t[:, 0:P]),
                            extra_waits=[ds_free[c]])
                        x_tok[c] = pr.emit("vector", lambda v, c=c: v.tensor_add(
                            Xsb[:, c * P:(c + 1) * P], Xsb[:, c * P:(c + 1) * P],
                            Dsb[:, c * P:(c + 1) * P]),
                            waits=[t_d], inc="vs")
                        ds_free[c] = x_tok[c]
                t_newton = pr.tok("vs")

                g_tok = {}
                ep_pend = None

                def emit_epilogue(c, t_xe):
                    nonlocal x2_spill_tok, h2_guard
                    c16 = (s_abs % NSUP) * SUP + c
                    sl = (c % 2) * D
                    x2sl = x2b[:, sl:sl + D]
                    t_x2m = pr.emit("vector", lambda v, sl=sl: v.tensor_add(
                        x2b[:, sl:sl + D], x2b[:, sl:sl + D], xeb[:, sl:sl + D]),
                        waits=[t_xe], inc="vs")
                    x2_spill_tok = pr.emit("sync", lambda sy, c16=c16, x2sl=x2sl: sy.dma_start(
                        ap(d_x2s, c16 * P * D, [[D, P], [1, D]]), x2sl),
                        waits=[t_x2m], inc="dsp", inc_by=16)
                    t_v3, t_hn = emit_ln(x2sl, [t_x2m], [last_tr[0]])
                    t_h2 = emit_transposes(t_hn, [
                        (lambda s, dt=dt: s.activation(
                            h2stg[:, dt * P:(dt + 1) * P],
                            bkT.t[:, dt * P:(dt + 1) * P], AF.Identity,
                            scale=cb1c[:, dt:dt + 1], bias=sb_f[:, dt:dt + 1]))
                        for dt in range(DT)],
                        extra=[h2_guard[0], h2_guard[1]])
                    h2_spill = pr.emit("sync", lambda sy, c16=c16: sy.dma_start(
                        ap(d_h2s, c16 * P * D, [[D, P], [1, D]]), h2stg[:, :]),
                        waits=[t_h2], inc="dh2s", inc_by=16)
                    b6 = bk[4]
                    mms = []
                    for dt_i in range(DT):
                        mms.append((b6.t[:, 0:NE],
                                    h2stg[:, dt_i * P:(dt_i + 1) * P],
                                    rw_bf[:, dt_i * NE:(dt_i + 1) * NE]))
                    b6.mm_group(mms, extra_waits=[t_h2])
                    h2_guard = [h2_spill, pr.tok("ts")]
                    b6.read("vector", lambda v, b6=b6: v.tensor_copy(lg[:, :], b6.t[:, 0:NE]))
                    t_m1 = pr.emit("vector", lambda v: v.reduce_max(rcol[:, 0:1], lg[:, :], axis=AX.X), inc="vs")
                    t_m2 = pr.emit("vector", lambda v: v.tensor_scalar_mul(rcol[:, 1:2], rcol[:, 0:1], -1.0),
                                   waits=[t_m1], inc="vs")
                    t_e1 = pr.emit("scalar", lambda s: s.activation(
                        zex[:, :], lg[:, :], AF.Exp, bias=rcol[:, 1:2]),
                        waits=[t_m2], inc="ss")
                    t_r1 = pr.emit("vector", lambda v: v.reduce_sum(rcol[:, 2:3], zex[:, :], axis=AX.X),
                                   waits=[t_e1], inc="vs")
                    t_r2 = pr.emit("vector", lambda v: v.reciprocal(rcol[:, 3:4], rcol[:, 2:3]),
                                   waits=[t_r1], inc="vs")
                    t_sc = pr.emit("scalar", lambda s: s.activation(
                        sc8[:, :], zex[:, :], AF.Identity, scale=rcol[:, 3:4]),
                        waits=[t_r2], inc="ss")
                    t_t8 = pr.emit("vector", lambda v: v.max(out=top8[:, :], in_=sc8[:, :]),
                                   waits=[t_sc], inc="vs")
                    pr.emit("vector", lambda v, c16=c16: v.scalar_tensor_tensor(
                        comb_all[:, c16 * NE:(c16 + 1) * NE], sc8[:, :], top8[:, 1:2], sc8[:, :],
                        op0=ALU.is_ge, op1=ALU.mult),
                        waits=[t_t8], inc="vs")

                for c in range(SUP):
                    bl, bh = (bk[0], bk[1]) if c % 2 == 0 else (bk[2], bk[3])
                    if is_m:
                        t_xe = pr.emit("sync", lambda sy, s_abs=s_abs, c=c: sy.dma_start(
                            xeb[:, (c % 2) * D:(c % 2 + 1) * D], x_src(s_abs, c)),
                            waits=[pr.tok("vs")], inc="dxe", inc_by=16)
                    for jh, b2 in ((0, bl), (1, bh)):
                        mms = []
                        for et in range(DT):
                            mms.append((b2.t[:, 0:512],
                                        Kft[:, et * 512 + c * P:et * 512 + (c + 1) * P],
                                        Wbf[:, et * D + jh * 512:et * D + (jh + 1) * 512]))
                        for dt_i in range(DT):
                            mms.append((b2.t[:, 0:512],
                                        hft[:, dt_i * 512 + c * P:dt_i * 512 + (c + 1) * P],
                                        thVn[:, dt_i * D + jh * 512:dt_i * D + (jh + 1) * 512]))
                        mms.append((b2.t[:, 0:512], ones_r1[:, :],
                                    par_buf[0:1, jh * 512:(jh + 1) * 512]))
                        b2.mm_group(mms, extra_waits=[t_newton])
                    if is_m:
                        for jh, b2 in ((0, bk[5]), (1, bk[6])):
                            mms = []
                            for et in range(DT):
                                mms.append((b2.t[:, 0:512],
                                            Qft[:, et * 512 + c * P:et * 512 + (c + 1) * P],
                                            Wbf[:, et * D + jh * 512:et * D + (jh + 1) * 512]))
                            mms.append((b2.t[:, 0:512], ones_r1[:, :],
                                        par_buf[0:1, jh * 512:(jh + 1) * 512]))
                            b2.mm_group(mms, extra_waits=[t_newton])
                    for jh, b2 in ((0, bl), (1, bh)):
                        mms = []
                        for cp in range(c):
                            pid = pair_id_s(c, cp)
                            mms.append((b2.t[:, 0:512], Scr[:, pid * P:(pid + 1) * P],
                                        Gsb[:, cp * D + jh * 512:cp * D + (jh + 1) * 512]))
                        if mms:
                            b2.mm_group(mms, extra_waits=[g_tok[c - 1]], cont=True)
                        b2.read("scalar", lambda s, b2=b2, jh=jh: s.copy(
                            RHSs[0:P, jh * 512:(jh + 1) * 512], b2.t[:, 0:512]))
                    t_rhs = pr.tok("ss")
                    if is_m and ep_pend is not None:
                        emit_epilogue(*ep_pend)
                        ep_pend = None
                    bg = bk[4]
                    for jh in range(2):
                        bg.mm_group([(bg.t[:, 0:512], Xsb[:, c * P:(c + 1) * P],
                                      RHSs[:, jh * 512:(jh + 1) * 512])],
                                    extra_waits=[t_rhs])
                        g_tok[c] = bg.read("scalar", lambda s, b5=bg, c=c, jh=jh, gcol=gcol: s.activation(
                            Gsb[:, c * D + jh * 512:c * D + (jh + 1) * 512], b5.t[:, 0:512],
                            AF.Identity, scale=s_gmask[:, gcol:gcol + 1]),
                            extra_waits=[t_super_ts0])
                    if is_m:
                        for jh, b2 in ((0, bk[5]), (1, bk[6])):
                            mms = []
                            for cp in range(c + 1):
                                pid = pair_id_q(c, cp)
                                mms.append((b2.t[:, 0:512], Sqr[:, pid * P:(pid + 1) * P],
                                            Gsb[:, cp * D + jh * 512:cp * D + (jh + 1) * 512]))
                            b2.mm_group(mms, extra_waits=[g_tok[c]], cont=True)
                            b2.read("vector", lambda v, b2=b2, jh=jh, c=c: v.tensor_mul(
                                x2b[:, (c % 2) * D + jh * 512:(c % 2) * D + (jh + 1) * 512],
                                b2.t[:, 0:512],
                                ga_bc[:, jh * 512:(jh + 1) * 512]),
                                extra_waits=[pr.tok("ss"), x2_spill_tok])
                        ep_pend = (c, t_xe)
                if is_m and ep_pend is not None:
                    emit_epilogue(*ep_pend)
                    ep_pend = None
                t_chain = pr.tok("ss")

                t_outs = pr.tok("ts")
                pending_state = (t_chain, t_outs, npar_buf)

            if pending_state is not None:
                emit_state_update(*pending_state)
                pending_state = None

            t_phaseA = [pr.tok("ts"), pr.tok("vs"), pr.tok("ss"), pr.tok("dsp"), pr.tok("dh2s"),
                        pr.tok("dx0"), pr.tok("dx1"), pr.tok("dxe")]

            if debug:
                tap("WT", WT[:, :], t_phaseA)
                tap("Gsb", Gsb[:, :], t_phaseA)
                tap("bf32", b_f32[:, :], t_phaseA)
                tap("comb", comb_all[:, :], t_phaseA)
                tap("Kft", Kft[:, :], t_phaseA)
                tap("Ktk", Ktk[:, :], t_phaseA)
                tap("Qft", Qft[:, :], t_phaseA)
                tap("Xsb", Xsb[:, :], t_phaseA)
                tap("hft", hft[:, :], t_phaseA)
                tap("modcols_ca1", ca1c[:, :], t_phaseA)
                tap("h2s0", h2stg[:, :], t_phaseA)
                tap("ga_bc", ga_bc[:, 0:D], t_phaseA)
            if debug:
                t_phaseA = t_phaseA + [pr.tok("do")]

        if do_passb:
            with ExitStack() as bctx:
                EB = bctx.enter_context
                acc = EB(nc.sbuf_tensor("acc", [P, (NCH // 2) * D], F32))
                h2b = EB(nc.sbuf_tensor("h2b", [P, DT * D], BF16))
                act = EB(nc.sbuf_tensor("act", [P, DT * D], BF16))
                wG = EB(nc.sbuf_tensor("wG", [P, 2 * DT * D], BF16))
                wU = EB(nc.sbuf_tensor("wU", [P, 2 * DT * D], BF16))
                wD = EB(nc.sbuf_tensor("wD", [P, DT * D], BF16))
                sg2 = EB(nc.sbuf_tensor("sg2", [P, 512], BF16))
                ub2 = EB(nc.sbuf_tensor("ub2", [P, 512], BF16))
                obuf = EB(nc.sbuf_tensor("obuf", [P, D], F32))

                pes = [("sh", 0), ("sh", 1)] + [("ex", e2) for e2 in range(NE)]

                def pe_srcs(kind, idx):
                    if kind == "sh":
                        return (ap(d_sg_scr, idx * P * DT * D, [[DT * D, P], [1, DT * D]]),
                                ap(d_su_scr, idx * P * DT * D, [[DT * D, P], [1, DT * D]]),
                                ap(d_sd_scr, idx * P * DT * D, [[DT * D, P], [1, DT * D]]),
                                [wprep_toks["sgw"], wprep_toks["suw"], wprep_toks["sdw"]])
                    return (ap(d_gw_scr, idx * P * DT * D, [[DT * D, P], [1, DT * D]]),
                            ap(d_uw_scr, idx * P * DT * D, [[DT * D, P], [1, DT * D]]),
                            ap(d_dw_scr, idx * P * DT * D, [[DT * D, P], [1, DT * D]]),
                            [wprep_toks["gw"], wprep_toks["uw"], wprep_toks["dw"]])

                wslot_free = {0: list(t_phaseA), 1: list(t_phaseA)}
                wd_free = list(t_phaseA)
                out_toks = []

                for th in range(2):
                    t_ac = pr.emit("gpsimd", lambda g: g.memset(acc[:, :], 0.0),
                                   waits=t_phaseA + [pr.tok("vs")], inc="gs")
                    acc_guard = [t_ac]
                    for ht in range(DT):
                        pr.emit("sync", lambda sy, ht=ht, th=th: sy.dma_start(
                            ap(h2b, ht * D, [[DT * D, P], [P, DT], [1, P]]),
                            ap(d_h2s, th * DT * P * D + P * ht + 0, [[D, P], [P * D, DT], [1, P]])),
                            waits=t_phaseA + [pr.tok("ts")], inc="dh2", inc_by=16)
                    h2_load = [pr.tok("dh2")]
                    for pi, (kind, idx) in enumerate(pes):
                        slot = pi % 2
                        gsrc, usrc, dsrc, wtoks = pe_srcs(kind, idx)
                        t_lg_ = pr.emit("sync", lambda sy, gsrc=gsrc, slot=slot: sy.dma_start(
                            ap(wG, slot * DT * D, [[2 * DT * D, P], [1, DT * D]]), gsrc),
                            waits=wtoks + wslot_free[slot], inc=f"dwg{slot}", inc_by=16)
                        t_lu_ = pr.emit("sync", lambda sy, usrc=usrc, slot=slot: sy.dma_start(
                            ap(wU, slot * DT * D, [[2 * DT * D, P], [1, DT * D]]), usrc),
                            waits=wtoks + wslot_free[slot], inc=f"dwu{slot}", inc_by=16)
                        t_ld_ = pr.emit("sync", lambda sy, dsrc=dsrc: sy.dma_start(wD[:, :], dsrc),
                                        waits=wtoks + wd_free, inc="dwd", inc_by=16)
                        for fj in range(DT):
                            for tb in range(2):
                                bA = bk[(fj * 2 + tb) % 2]
                                bB = bk[4 + (fj * 2 + tb) % 2]
                                mmsA, mmsB = [], []
                                for ht in range(DT):
                                    rh = h2b[:, ht * D + tb * 512:ht * D + (tb + 1) * 512]
                                    mmsA.append((bA.t[:, 0:512],
                                                 wG[:, slot * DT * D + ht * D + fj * P:
                                                    slot * DT * D + ht * D + (fj + 1) * P], rh))
                                    mmsB.append((bB.t[:, 0:512],
                                                 wU[:, slot * DT * D + ht * D + fj * P:
                                                    slot * DT * D + ht * D + (fj + 1) * P], rh))
                                bA.mm_group(mmsA, extra_waits=[t_lg_] + h2_load)
                                bB.mm_group(mmsB, extra_waits=[t_lu_] + h2_load)
                                t_sg = bA.read("scalar", lambda s, bA=bA: s.activation(
                                    sg2[:, :], bA.t[:, 0:512], AF.Silu))
                                t_ub = bB.read("scalar", lambda s, bB=bB: s.activation(
                                    ub2[:, :], bB.t[:, 0:512], AF.Identity))
                                pr.emit("vector", lambda v, fj=fj, tb=tb: v.tensor_mul(
                                    act[:, fj * D + tb * 512:fj * D + (tb + 1) * 512],
                                    sg2[:, :], ub2[:, :]),
                                    waits=[t_sg, t_ub], inc="vs")
                        t_act = pr.tok("vs")
                        for tt in range(DT):
                            c16 = th * 8 + tt
                            for hh in range(2):
                                bC = bk[2 + (tt * 2 + hh) % 2]
                                mms = []
                                for fj in range(DT):
                                    mms.append((bC.t[:, 0:512],
                                                ap(act, fj * D + tt * P, [[DT * D, P], [1, P]]),
                                                wD[:, fj * D + hh * 512:fj * D + (hh + 1) * 512]))
                                bC.mm_group(mms, extra_waits=[t_act, t_ld_])
                                if kind == "sh":
                                    csl = onef_col[:, 0:1]
                                else:
                                    csl = comb_all[:, c16 * NE + idx:c16 * NE + idx + 1]
                                bC.read("vector", lambda v, bC=bC, tt=tt, hh=hh, csl=csl: v.scalar_tensor_tensor(
                                    acc[:, tt * D + hh * 512:tt * D + (hh + 1) * 512],
                                    bC.t[:, 0:512], csl,
                                    acc[:, tt * D + hh * 512:tt * D + (hh + 1) * 512],
                                    op0=ALU.mult, op1=ALU.add),
                                    extra_waits=acc_guard)
                        wslot_free[slot] = [pr.tok("ts")]
                        wd_free = [pr.tok("ts")]
                    t_accdone = pr.tok("vs")
                    for cc in range(8):
                        c16 = th * 8 + cc
                        slot = cc % 2
                        t_xr = pr.emit("sync", lambda sy, c16=c16, slot=slot: sy.dma_start(
                            ap(xb, slot * D, [[2 * D, P], [1, D]]),
                            ap(d_x2s, c16 * P * D, [[D, P], [1, D]])),
                            waits=[t_accdone] + ([pr.tok("vs")] if cc >= 2 else []),
                            inc=f"dx{slot}", inc_by=16)
                        pr.emit("vector", lambda v, cc=cc: v.tensor_mul(
                            obuf[:, :], acc[:, cc * D:(cc + 1) * D], gb_bc[:, :]),
                            waits=[t_accdone, pr.tok("do")], inc="vs")
                        t_o2 = pr.emit("vector", lambda v, slot=slot: v.tensor_add(
                            obuf[:, :], obuf[:, :], xb[:, slot * D:(slot + 1) * D]),
                            waits=[t_xr], inc="vs")
                        out_toks.append(pr.emit("sync", lambda sy, c16=c16: sy.dma_start(
                            ap(d_out, c16 * P * D, [[D, P], [1, D]]), obuf[:, :]),
                            waits=[t_o2], inc="do", inc_by=16))

                pr.q["sync"].append(([out_toks[-1]], lambda sy: None, None))
                pr.lastw["sync"][out_toks[-1][0]] = out_toks[-1][1]

        else:
            pr.q["sync"].append(([t_phaseA[0]], lambda sy: None, None))

        block = octx.enter_context(nc.Block())

        @block.tensor
        def _(t):
            pr.replay("tensor", t)

        @block.vector
        def _(v):
            pr.replay("vector", v)

        @block.scalar
        def _(s):
            pr.replay("scalar", s)

        @block.gpsimd
        def _(g):
            pr.replay("gpsimd", g)

        @block.sync
        def _(sy):
            pr.replay("sync", sy)

    return nc, taps


def _mask(fn):
    r = np.arange(P)[:, None]
    c = np.arange(P)[None, :]
    return fn(r, c).astype(np.float32).astype(ml_dtypes.bfloat16)


def make_inputs(L0_attn_in_seq, L0_attn_theta_K, L0_attn_theta_V, L0_attn_theta_Q,
                L0_attn_W0, L0_attn_b0, L0_ffn_router_w, L0_ffn_gate_w, L0_ffn_up_w,
                L0_ffn_down_w, L0_ffn_shared_gate_w, L0_ffn_shared_up_w,
                L0_ffn_shared_down_w, L0_c, L0_wmod):
    x = np.ascontiguousarray(np.asarray(L0_attn_in_seq, dtype=np.float32))
    f32 = lambda a: np.ascontiguousarray(np.asarray(a, dtype=np.float32))
    base = {
        "thK": f32(np.asarray(L0_attn_theta_K).T),
        "thV": f32(np.asarray(L0_attn_theta_V).T),
        "thQ": f32(np.asarray(L0_attn_theta_Q).T),
        "w0T": f32(np.asarray(L0_attn_W0).T),
        "b0r": f32(np.asarray(L0_attn_b0).reshape(1, D)),
        "rw": f32(L0_ffn_router_w),
        "gw": f32(L0_ffn_gate_w), "uw": f32(L0_ffn_up_w), "dw": f32(L0_ffn_down_w),
        "sgw": f32(L0_ffn_shared_gate_w), "suw": f32(L0_ffn_shared_up_w),
        "sdw": f32(L0_ffn_shared_down_w),
        "wmod": f32(L0_wmod),
        "identb": np.eye(P, dtype=np.float32).astype(ml_dtypes.bfloat16),
        "mune": _mask(lambda r, c: np.where(r < c, -1.0, 0.0)),
        "mlne": _mask(lambda r, c: np.where(r > c, -1.0, 0.0)),
        "muinc": _mask(lambda r, c: np.where(r <= c, 1.0, 0.0)),
    }
    cmat = f32(L0_c)
    in_maps = []
    for k in range(8):
        b, h = k >> 1, k & 1
        gm = np.zeros((P, 2), dtype=np.float32)
        gm[:, 0] = -2.0 * LR / D * h
        gm[:, 1] = -2.0 * LR / D
        m = dict(base)
        m["xm"] = np.ascontiguousarray(x[b, h * TOWN:(h + 1) * TOWN])
        m["xp"] = np.ascontiguousarray(x[b, 0:TOWN])
        m["gmask"] = gm
        m["cvec"] = np.ascontiguousarray(cmat[b].reshape(D, 1))
        in_maps.append(m)
    return in_maps


def kernel(**inputs):
    in_maps = make_inputs(**inputs)
    nc, _ = build_program(debug=False)
    res = run_bass_kernel_spmd(nc, in_maps, core_ids=list(range(8)))
    out = np.empty((4, 4096, D), dtype=np.float32)
    for k in range(8):
        b, h = k >> 1, k & 1
        out[b, h * TOWN:(h + 1) * TOWN] = res.results[k]["out"]
    return out


# version 10
# speedup vs baseline: 4.7854x; 1.1450x over previous
import numpy as np
import ml_dtypes
from contextlib import ExitStack

import concourse.bass as bass
import concourse.mybir as mybir
from concourse.bass_utils import run_bass_kernel_spmd

F32 = mybir.dt.float32
BF16 = mybir.dt.bfloat16
U32 = mybir.dt.uint32
AF = mybir.ActivationFunctionType
ALU = mybir.AluOpType
AX = mybir.AxisListType

P = 128
D = 1024
DT = 8
SUP = 4
NSUP = 4
NCH = 16
TOWN = 2048
LR = 0.01
EPS = 1e-5
NE = 8
CAP = 768
TEW = CAP // 128
NSEG = NE * CAP
QUEUES = ("tensor", "vector", "scalar", "gpsimd", "sync")
QSEM = {"vector": "vs", "scalar": "ss", "gpsimd": "gs", "sync": "do", "tensor": "ts"}


class Prog:

    def __init__(self, nc, ctx):
        self.nc = nc
        self.ctx = ctx
        self.q = {k: [] for k in QUEUES}
        self.cnt = {}
        self.hand = {}
        self.lastw = {k: {} for k in QUEUES}

    def sem(self, name):
        if name not in self.hand:
            self.hand[name] = self.ctx.enter_context(self.nc.semaphore(name))
            self.cnt[name] = 0
        return self.hand[name]

    def tok(self, name):
        self.sem(name)
        return (name, self.cnt[name])

    def emit(self, queue, fn, waits=(), inc=None, inc_by=1):
        ws = []
        for w in waits:
            if w is None:
                continue
            s, v = w
            if v <= 0:
                continue
            if self.lastw[queue].get(s, 0) >= v:
                continue
            self.lastw[queue][s] = v
            self.sem(s)
            ws.append((s, v))
        tk = None
        it = None
        if inc is not None:
            self.sem(inc)
            self.cnt[inc] += inc_by
            tk = (inc, self.cnt[inc])
            it = (inc, inc_by)
        self.q[queue].append((ws, fn, it))
        return tk

    def replay(self, queue, eng):
        for ws, fn, it in self.q[queue]:
            for s, v in ws:
                eng.wait_ge(self.hand[s], v)
            ins = fn(eng)
            if it is not None:
                assert ins is not None, f"{queue}: inc {it} but fn returned None"
                ins.then_inc(self.hand[it[0]], it[1])


class Bank:

    def __init__(self, pr, name, dtype=F32):
        width = 512 if dtype == F32 else 1024
        self.pr = pr
        self.t = pr.ctx.enter_context(pr.nc.psum_tensor(name, [P, width], dtype))
        self.w = None
        self.readers = []

    def mm_group(self, mms, extra_waits=(), cont=False):
        pr = self.pr
        n = len(mms)
        waits = [w for w in list(extra_waits) + ([] if cont else self.readers) if w is not None]
        tk = None
        for i, (o, l, r) in enumerate(mms):
            tk = pr.emit(
                "tensor",
                (lambda t, o=o, l=l, r=r, st=(i == 0 and not cont), sp=(i == n - 1):
                 t.matmul(o, l, r, start=st, stop=sp)),
                waits=waits if i == 0 else (),
                inc="ts" if i == n - 1 else None)
        self.w = tk
        if not cont:
            self.readers = []
        return tk

    def read(self, queue, fn, extra_waits=()):
        tk = self.pr.emit(queue, fn, waits=[self.w] + list(extra_waits), inc=QSEM[queue])
        self.readers.append(tk)
        return tk


def ap(t, offset, dims):
    return bass.AP(t, offset, dims)


_BUILD_N = [0]


def build_program(debug=False, n_super=2 * NSUP, do_passb=True):
    _BUILD_N[0] += 1
    _tp = "tap_" if _BUILD_N[0] == 1 else f"tap{_BUILD_N[0]}_"
    nc = bass.Bass("TRN2", target_bir_lowering=False)

    d_xm = nc.dram_tensor("xm", [TOWN, D], F32, kind="ExternalInput")
    d_xp = nc.dram_tensor("xp", [TOWN, D], F32, kind="ExternalInput")
    d_gmask = nc.dram_tensor("gmask", [P, 2], F32, kind="ExternalInput")
    d_cvec = nc.dram_tensor("cvec", [D, 1], F32, kind="ExternalInput")
    d_wmod = nc.dram_tensor("wmod", [D, 6 * D], F32, kind="ExternalInput")
    d_thK = nc.dram_tensor("thK", [D, D], F32, kind="ExternalInput")
    d_thV = nc.dram_tensor("thV", [D, D], F32, kind="ExternalInput")
    d_thQ = nc.dram_tensor("thQ", [D, D], F32, kind="ExternalInput")
    d_w0T = nc.dram_tensor("w0T", [D, D], F32, kind="ExternalInput")
    d_b0 = nc.dram_tensor("b0r", [1, D], F32, kind="ExternalInput")
    d_rw = nc.dram_tensor("rw", [D, NE], F32, kind="ExternalInput")
    d_gw = nc.dram_tensor("gw", [NE, D, D], F32, kind="ExternalInput")
    d_uw = nc.dram_tensor("uw", [NE, D, D], F32, kind="ExternalInput")
    d_dw = nc.dram_tensor("dw", [NE, D, D], F32, kind="ExternalInput")
    d_sgw = nc.dram_tensor("sgw", [D, 2 * D], F32, kind="ExternalInput")
    d_suw = nc.dram_tensor("suw", [D, 2 * D], F32, kind="ExternalInput")
    d_sdw = nc.dram_tensor("sdw", [2 * D, D], F32, kind="ExternalInput")
    d_ident = nc.dram_tensor("identb", [P, P], BF16, kind="ExternalInput")
    d_mut1 = nc.dram_tensor("mut1", [P, P], BF16, kind="ExternalInput")
    d_ebase = nc.dram_tensor("ebase", [1, NE], F32, kind="ExternalInput")
    d_mune = nc.dram_tensor("mune", [P, P], BF16, kind="ExternalInput")
    d_mlne = nc.dram_tensor("mlne", [P, P], BF16, kind="ExternalInput")
    d_muinc = nc.dram_tensor("muinc", [P, P], BF16, kind="ExternalInput")

    d_out = nc.dram_tensor("out", [TOWN, D], F32, kind="ExternalOutput")

    d_x2s = nc.dram_tensor("x2s", [TOWN, D], F32)
    d_h2s = nc.dram_tensor("h2s", [NCH * P, D], BF16)
    d_modscr = nc.dram_tensor("modscr", [1, 6 * D], F32)
    d_thq_scr = nc.dram_tensor("thq_scr", [D, D], BF16)
    d_gw_scr = nc.dram_tensor("gw_scr", [NE, P, DT * D], BF16)
    d_uw_scr = nc.dram_tensor("uw_scr", [NE, P, DT * D], BF16)
    d_dw_scr = nc.dram_tensor("dw_scr", [NE, P, DT * D], BF16)
    d_sg_scr = nc.dram_tensor("sg_scr", [2, P, DT * D], BF16)
    d_su_scr = nc.dram_tensor("su_scr", [2, P, DT * D], BF16)
    d_sd_scr = nc.dram_tensor("sd_scr", [2, P, DT * D], BF16)
    d_hx = nc.dram_tensor("hx", [NSEG, D], BF16)
    d_yx = nc.dram_tensor("yx", [NSEG, D], F32)

    taps = {}

    with ExitStack() as octx:
        pr = Prog(nc, octx)
        E = octx.enter_context

        def tap(name, apv, waits):
            if not debug:
                return
            dt_ = nc.dram_tensor(_tp + name, list(apv.shape), apv.dtype, kind="ExternalOutput")
            taps[name] = True
            pr.emit("sync", lambda sy, a=apv, d2=dt_: sy.dma_start(d2[...], a),
                    waits=waits, inc="do", inc_by=16)

        s_ident = E(nc.sbuf_tensor("s_ident", [P, P], BF16))
        s_mune = E(nc.sbuf_tensor("s_mune", [P, P], BF16))
        s_mlne = E(nc.sbuf_tensor("s_mlne", [P, P], BF16))
        s_muinc = E(nc.sbuf_tensor("s_muinc", [P, P], BF16))
        ones_r1 = E(nc.sbuf_tensor("ones_r1", [1, P], BF16))
        ones_row = E(nc.sbuf_tensor("ones_row", [1, 512], BF16))
        ones_r1f = E(nc.sbuf_tensor("ones_r1f", [1, P], F32))
        ones_cl = E(nc.sbuf_tensor("ones_cl", [P, 1], BF16))
        onef_col = E(nc.sbuf_tensor("onef_col", [P, 1], F32))
        s_gmask = E(nc.sbuf_tensor("s_gmask", [P, 2], F32))
        gb_bc = E(nc.sbuf_tensor("gb_bc", [P, D], F32))
        ga_bc = E(nc.sbuf_tensor("ga_bc", [P, D], F32))
        sl2 = E(nc.sbuf_tensor("sl2", [P, 2 * NCH], U32))
        w2col = E(nc.sbuf_tensor("w2col", [P, 2 * NCH], F32))
        slotab = E(nc.sbuf_tensor("slotab", [P, NE], F32))
        maskf = E(nc.sbuf_tensor("maskf", [P, NE], F32))
        mk0 = E(nc.sbuf_tensor("mk0", [P, NE], F32))
        tmp8 = E(nc.sbuf_tensor("tmp8", [P, NE], F32))
        maskb = E(nc.sbuf_tensor("maskb", [P, NE], BF16))
        slcols = E(nc.sbuf_tensor("slcols", [P, 2], F32))
        absoffs = E(nc.sbuf_tensor("absoffs", [1, NE], F32))
        s_ebase = E(nc.sbuf_tensor("s_ebase", [1, NE], F32))
        s_mut1 = E(nc.sbuf_tensor("s_mut1", [P, P], BF16))
        h2cp = E(nc.sbuf_tensor("h2cp", [P, 2 * D], BF16))
        b_f32 = E(nc.sbuf_tensor("b_f32", [1, D], F32))
        b_bfA = E(nc.sbuf_tensor("b_bfA", [1, D], BF16))
        b_bfB = E(nc.sbuf_tensor("b_bfB", [1, D], BF16))
        xb = E(nc.sbuf_tensor("xb", [P, 2 * D], F32))
        lnt = E(nc.sbuf_tensor("lnt", [P, 16], F32))
        hbf = E(nc.sbuf_tensor("hbf", [P, D], BF16))
        lg = E(nc.sbuf_tensor("lg", [P, NE], F32))
        zex = E(nc.sbuf_tensor("zex", [P, NE], F32))
        sc8 = E(nc.sbuf_tensor("sc8", [P, NE], F32))
        top8 = E(nc.sbuf_tensor("top8", [P, 8], F32))
        rcol = E(nc.sbuf_tensor("rcol", [P, 4], F32))
        h2stg = E(nc.sbuf_tensor("h2stg", [P, D], BF16))
        ca1c = E(nc.sbuf_tensor("ca1c", [P, DT], F32))
        cb1c = E(nc.sbuf_tensor("cb1c", [P, DT], F32))
        sa_f = E(nc.sbuf_tensor("sa_f", [P, DT], F32))
        sb_f = E(nc.sbuf_tensor("sb_f", [P, DT], F32))
        rw_bf = E(nc.sbuf_tensor("rw_bf", [P, DT * NE], BF16))

        bk = [Bank(pr, f"bk{i}") for i in range(7)]
        bkT = Bank(pr, "bkT", BF16)

        cdb = E(nc.sbuf_tensor("cdb", [P, 4], F32))
        pr.emit("gpsimd", lambda g: g.memset(cdb[:, 0:1], 2 * LR / D), inc="gs")
        pr.emit("gpsimd", lambda g: g.memset(cdb[:, 1:2], 1.0), inc="gs")
        pr.emit("gpsimd", lambda g: g.memset(cdb[:, 2:3], 0.0), inc="gs")
        pr.emit("gpsimd", lambda g: g.memset(cdb[:, 3:4], -1.0), inc="gs")
        nc.const_aps.register(F32, 2 * LR / D, cdb[:, 0:1])
        nc.const_aps.register(F32, 1.0, cdb[:, 1:2])
        nc.const_aps.register(F32, 0.0, cdb[:, 2:3])
        nc.const_aps.register(F32, -1.0, cdb[:, 3:4])

        for t_, d_ in ((s_ident, d_ident), (s_mune, d_mune), (s_mlne, d_mlne), (s_muinc, d_muinc)):
            pr.emit("sync", lambda sy, t2=t_, d2=d_: sy.dma_start(t2[:, :], d2[:, :]),
                    inc="dini", inc_by=16)
        pr.emit("sync", lambda sy: sy.dma_start(s_gmask[:, :], d_gmask[:, :]), inc="dini", inc_by=16)
        pr.emit("sync", lambda sy: sy.dma_start(s_mut1[:, :], d_mut1[:, :]), inc="dini", inc_by=16)
        pr.emit("sync", lambda sy: sy.dma_start(s_ebase[:, :], d_ebase[:, :]), inc="dini", inc_by=16)
        pr.emit("sync", lambda sy: sy.dma_start(b_f32[:, :], d_b0[:, :]), inc="dini", inc_by=16)
        pr.emit("gpsimd", lambda g: g.memset(ones_r1[:, :], 1.0), inc="gs")
        pr.emit("gpsimd", lambda g: g.memset(ones_row[:, :], 1.0), inc="gs")
        pr.emit("gpsimd", lambda g: g.memset(ones_cl[:, :], 1.0), inc="gs")
        pr.emit("gpsimd", lambda g: g.memset(ones_r1f[:, :], 1.0), inc="gs")
        pr.emit("gpsimd", lambda g: g.memset(onef_col[:, :], 1.0), inc="gs")
        t_dl0 = pr.tok("dini")
        t_ms0 = pr.tok("gs")

        t_bb0 = pr.emit("scalar", lambda s: s.copy(b_bfA[0:1, :], b_f32[0:1, :]),
                        waits=[t_dl0], inc="ss")
        pr.emit("scalar", lambda s: s.copy(absoffs[0:1, :], s_ebase[0:1, :]),
                waits=[t_dl0], inc="ss")

        wprep_toks = {}


        with ExitStack() as actx:
            EA = actx.enter_context

            with ExitStack() as wctx:
                wmb = wctx.enter_context(nc.sbuf_tensor("wmb", [P, 2 * 6 * D], F32))
                silc = wctx.enter_context(nc.sbuf_tensor("silc", [P, DT], F32))
                modr = wctx.enter_context(nc.sbuf_tensor("modr", [1, 6 * D], F32))
                tmpr = wctx.enter_context(nc.sbuf_tensor("tmpr", [1, 3 * D], F32))

                t_cv = pr.emit("sync", lambda sy: sy.dma_start(
                    silc[:, :], ap(d_cvec, 0, [[1, P], [P, DT]]),
                    allow_slow_non_contiguous=True), inc="dcv", inc_by=16)
                t_sil = pr.emit("scalar", lambda s: s.activation(silc[:, :], silc[:, :], AF.Silu),
                                waits=[t_cv], inc="ss")
                prev_mm = [None] * DT
                for pas in range(2):
                    mm_toks = []
                    for dt_i in range(DT):
                        slot = dt_i % 2
                        wl = [mm_toks[dt_i - 2]] if dt_i >= 2 else [prev_mm[DT - 2 + dt_i]]
                        tkd = pr.emit("sync", lambda sy, dt_i=dt_i, slot=slot: sy.dma_start(
                            ap(wmb, slot * 6 * D, [[2 * 6 * D, P], [1, 6 * D]]),
                            ap(d_wmod, dt_i * P * 6 * D, [[6 * D, P], [1, 6 * D]])),
                            waits=wl, inc=f"dwm{slot}", inc_by=16)
                        tk = None
                        for jj in range(6):
                            jb = pas * 6 + jj
                            b2 = bk[jj]
                            o = b2.t[0:1, 0:512]
                            l = silc[:, dt_i:dt_i + 1]
                            r = wmb[:, slot * 6 * D + jb * 512:slot * 6 * D + (jb + 1) * 512]
                            wl2 = [tkd]
                            if dt_i == 0:
                                wl2 += [t_sil] + b2.readers
                            tk = pr.emit("tensor",
                                         (lambda t, o=o, l=l, r=r, st=(dt_i == 0), sp=(dt_i == DT - 1):
                                          t.matmul(o, l, r, start=st, stop=sp)),
                                         waits=wl2, inc="ts" if jj == 5 else None)
                        mm_toks.append(tk)
                        if dt_i == DT - 1:
                            for jj in range(6):
                                bk[jj].w = tk
                                bk[jj].readers = []
                    for jj in range(6):
                        jb = pas * 6 + jj
                        bk[jj].read("vector", lambda v, jj=jj, jb=jb: v.tensor_copy(
                            modr[0:1, jb * 512:(jb + 1) * 512], bk[jj].t[0:1, 0:512]))
                    prev_mm = mm_toks
                t_mod = pr.tok("vs")

                for idx in (2, 5):
                    g_sl = modr[0:1, idx * D:(idx + 1) * D]
                    a_sl = tmpr[0:1, 0:D]
                    bb_sl = tmpr[0:1, D:2 * D]
                    s_sl = tmpr[0:1, 2 * D:3 * D]
                    pr.emit("vector", lambda v, g=g_sl, a=a_sl: v.tensor_scalar_max(a, g, 0.25),
                            waits=[t_mod], inc="vs")
                    pr.emit("vector", lambda v, g=g_sl, b3=bb_sl: v.tensor_scalar_min(b3, g, -0.25), inc="vs")
                    tc0 = pr.emit("vector", lambda v, g=g_sl, s2=s_sl: v.tensor_single_scalar(s2, g, 0.0, op=ALU.is_ge), inc="vs")
                    tc1 = pr.emit("vector", lambda v, a=a_sl, b3=bb_sl: v.tensor_sub(a, a, b3),
                                  waits=[tc0], inc="vs")
                    tc2 = pr.emit("vector", lambda v, a=a_sl, s2=s_sl: v.tensor_mul(a, a, s2),
                                  waits=[tc1], inc="vs")
                    pr.emit("vector", lambda v, g=g_sl, a=a_sl, b3=bb_sl: v.tensor_add(g, b3, a),
                            waits=[tc2], inc="vs")
                t_clamp = pr.tok("vs")

                for row_i, dst in ((2, ga_bc), (5, gb_bc)):
                    for jh in range(2):
                        b2 = bk[6]
                        b2.mm_group(
                            [(b2.t[:, 0:512], ones_r1f[:, :],
                              modr[0:1, row_i * D + jh * 512:row_i * D + (jh + 1) * 512])],
                            extra_waits=[t_clamp, t_ms0])
                        b2.read("vector", lambda v, dst=dst, jh=jh, b2=b2: v.tensor_copy(
                            dst[:, jh * 512:(jh + 1) * 512], b2.t[:, 0:512]))
                t_ms = pr.emit("sync", lambda sy: sy.dma_start(d_modscr[:, :], modr[:, :]),
                               waits=[t_clamp], inc="dms", inc_by=16)
                t_l1 = pr.emit("sync", lambda sy: sy.dma_start(
                    ca1c[:, :], ap(d_modscr, 1 * D, [[1, P], [P, DT]]),
                    allow_slow_non_contiguous=True), waits=[t_ms], inc="dca", inc_by=16)
                t_l2 = pr.emit("sync", lambda sy: sy.dma_start(
                    cb1c[:, :], ap(d_modscr, 4 * D, [[1, P], [P, DT]]),
                    allow_slow_non_contiguous=True), waits=[t_ms], inc="dcb", inc_by=16)
                t_l3 = pr.emit("gpsimd", lambda g: g.dma_start(
                    sa_f[:, :], ap(d_modscr, 0 * D, [[1, P], [P, DT]]),
                    allow_slow_non_contiguous=True), waits=[t_ms], inc="dsa", inc_by=16)
                t_l4 = pr.emit("gpsimd", lambda g: g.dma_start(
                    sb_f[:, :], ap(d_modscr, 3 * D, [[1, P], [P, DT]]),
                    allow_slow_non_contiguous=True), waits=[t_ms], inc="dsb2", inc_by=16)
                t_c1 = pr.emit("vector", lambda v: v.tensor_scalar_add(ca1c[:, :], ca1c[:, :], 1.0),
                               waits=[t_l1], inc="vs")
                t_c2 = pr.emit("vector", lambda v: v.tensor_scalar_add(cb1c[:, :], cb1c[:, :], 1.0),
                               waits=[t_l2], inc="vs")
                t_wmod_ts = pr.tok("ts")
                t_wmod_vs = pr.tok("vs")

            thK = EA(nc.sbuf_tensor("s_thK", [P, DT * D], BF16))
            thVn = EA(nc.sbuf_tensor("s_thVn", [P, DT * D], BF16))
            thQp = EA(nc.sbuf_tensor("s_thQp", [P, DT * D], BF16))
            WT = EA(nc.sbuf_tensor("WT", [P, DT * D], F32))
            Wbf = EA(nc.sbuf_tensor("Wbf", [P, DT * D], BF16))
            hft = EA(nc.sbuf_tensor("hft", [P, DT * 512], BF16))
            Kft = EA(nc.sbuf_tensor("Kft", [P, DT * 512], BF16))
            Ktk = EA(nc.sbuf_tensor("Ktk", [P, SUP * D], BF16))
            Qft = EA(nc.sbuf_tensor("Qft", [P, DT * 512], BF16))
            Gsb = EA(nc.sbuf_tensor("Gsb", [P, SUP * D], BF16))
            Scr = EA(nc.sbuf_tensor("Scr", [P, 6 * P], BF16))
            Sqr = EA(nc.sbuf_tensor("Sqr", [P, 10 * P], BF16))
            Usb = EA(nc.sbuf_tensor("Usb", [P, 8 * P], BF16))
            usb_rd = {j: None for j in range(8)}
            Bp = EA(nc.sbuf_tensor("Bp", [P, 2 * SUP * P], BF16))
            Btp = EA(nc.sbuf_tensor("Btp", [P, 2 * SUP * P], BF16))
            Dsb = EA(nc.sbuf_tensor("Dsb", [P, SUP * P], BF16))
            Xsb = EA(nc.sbuf_tensor("Xsb", [P, SUP * P], BF16))
            RHSs = EA(nc.sbuf_tensor("RHSs", [P, D], BF16))
            x2b = EA(nc.sbuf_tensor("x2b", [P, 2 * D], F32))
            xeb = EA(nc.sbuf_tensor("xeb", [P, 2 * D], F32))

            for dt_i in range(DT):
                pr.emit("gpsimd", lambda g, dt_i=dt_i: g.dma_start(
                    ap(thK, dt_i * D, [[DT * D, P], [1, D]]),
                    ap(d_thK, dt_i * P * D, [[D, P], [1, D]])),
                    waits=[t_wmod_ts, t_wmod_vs], inc="dthk", inc_by=16)
            t_thk_all = pr.tok("dthk")
            thk_toks = [t_thk_all] * DT
            for dt_i in range(DT):
                pr.emit("gpsimd", lambda g, dt_i=dt_i: g.dma_start(
                    ap(thVn, dt_i * D, [[DT * D, P], [1, D]]),
                    ap(d_thV, dt_i * P * D, [[D, P], [1, D]])),
                    waits=[t_wmod_ts, t_wmod_vs], inc="dthv", inc_by=16)
            t_thv_all = pr.tok("dthv")
            thv_toks = [t_thv_all] * DT
            for dt_i in range(DT):
                pr.emit("gpsimd", lambda g, dt_i=dt_i: g.dma_start(
                    ap(thQp, dt_i * D, [[DT * D, P], [1, D]]),
                    ap(d_thQ, dt_i * P * D, [[D, P], [1, D]])),
                    waits=[t_wmod_ts, t_wmod_vs], inc="dthq", inc_by=16)
            t_thq_all = pr.tok("dthq")
            thq_toks = [t_thq_all] * DT

            for dt_i in range(DT):
                pr.emit("scalar", lambda s, dt_i=dt_i: s.activation(
                    thVn[:, dt_i * D:(dt_i + 1) * D], thVn[:, dt_i * D:(dt_i + 1) * D],
                    AF.Identity, scale=-1.0),
                    waits=[thv_toks[dt_i]], inc="ss")
            t_thprime = pr.tok("ss")
            t_qprime = t_thq_all

            t_rwl = pr.emit("gpsimd", lambda g: g.dma_start(
                rw_bf[:, :], ap(d_rw, 0, [[NE, P], [P * NE, DT], [1, NE]])),
                waits=[t_wmod_ts, t_wmod_vs], inc="drw", inc_by=16)


            def wprep_family(name, pairs):
                for s_ap, d_ap in pairs:
                    pr.emit("gpsimd", lambda g, s2=s_ap, d2=d_ap: g.dma_start(d2, s2),
                            inc="dp", inc_by=16)
                wprep_toks[name] = pr.tok("dp")

            def gu_pairs(src, dst):
                out = []
                for e2 in range(NE):
                    for ht in range(DT):
                        out.append((ap(src, e2 * D * D + ht * P * D, [[D, P], [1, D]]),
                                    ap(dst, e2 * P * DT * D + ht * D, [[DT * D, P], [1, D]])))
                return out
            wprep_family("gw", gu_pairs(d_gw, d_gw_scr))
            wprep_family("uw", gu_pairs(d_uw, d_uw_scr))
            wprep_family("dw", gu_pairs(d_dw, d_dw_scr))

            def sh_pairs(src, dst):
                out = []
                for fh in range(2):
                    for ht in range(DT):
                        out.append((ap(src, ht * P * 2 * D + fh * D, [[2 * D, P], [1, D]]),
                                    ap(dst, fh * P * DT * D + ht * D, [[DT * D, P], [1, D]])))
                return out
            wprep_family("sgw", sh_pairs(d_sgw, d_sg_scr))
            wprep_family("suw", sh_pairs(d_suw, d_su_scr))
            sd_pairs = []
            for fh in range(2):
                for fj in range(DT):
                    sd_pairs.append((ap(d_sdw, (fh * D + fj * P) * D, [[D, P], [1, D]]),
                                     ap(d_sd_scr, fh * P * DT * D + fj * D, [[DT * D, P], [1, D]])))
            wprep_family("sdw", sd_pairs)
            dp_all = pr.tok("dp")
            for k2 in ("gw", "uw", "dw", "sgw", "suw", "sdw"):
                wprep_toks[k2] = dp_all

            for dt_i in range(DT):
                pr.emit("sync", lambda sy, dt_i=dt_i: sy.dma_start(
                    ap(WT, dt_i * D, [[DT * D, P], [1, D]]),
                    ap(d_w0T, dt_i * P * D, [[D, P], [1, D]])),
                    waits=[t_wmod_ts, t_wmod_vs], inc="dw0t", inc_by=16)
            t_w0_all = pr.tok("dw0t")
            for dt_i in range(DT):
                pr.emit("scalar", lambda s, dt_i=dt_i: s.copy(
                    Wbf[:, dt_i * D:(dt_i + 1) * D], WT[:, dt_i * D:(dt_i + 1) * D]),
                    waits=[t_w0_all], inc="ss")
            t_w0 = pr.tok("ss")

            t_prologue = [pr.tok("ts"), pr.tok("vs"), pr.tok("ss"), pr.tok("gs"),
                          t_w0, t_bb0, t_thprime, t_qprime,
                          t_thk_all, t_thq_all, t_rwl, t_l3, t_l4]

            ln_done = {}
            x_dma = {}
            x2_spill_tok = None
            h2_guard = [None, None]
            h2cp_guard = [None, None]
            t_offprev = [None]
            last_cp = [None]
            last_hn = [None]
            last_tr = [None]

            def x_src(s_abs, ci):
                d_src = d_xp if s_abs < NSUP else d_xm
                c16 = (s_abs % NSUP) * SUP + ci
                return ap(d_src, c16 * P * D, [[D, P], [1, D]])

            def ensure_xdma(cl):
                if cl >= 2 * NSUP * SUP or cl in x_dma:
                    return
                slot = cl % 2
                wl = list(t_prologue) if cl == 0 else []
                if cl - 2 >= 0:
                    wl += list(ln_done[cl - 2])
                x_dma[cl] = pr.emit("sync", lambda sy, cl=cl, slot=slot: sy.dma_start(
                    ap(xb, slot * D, [[2 * D, P], [1, D]]), x_src(cl // SUP, cl % SUP)),
                    waits=wl, inc=f"dx{slot}", inc_by=16)

            def pair_id_s(c, cp):
                return c * (c - 1) // 2 + cp

            def pair_id_q(c, cp):
                return c * (c + 1) // 2 + cp

            def emit_ln(x_ap, wait0, hazard_guard):
                pr.emit("vector", lambda v, x3=x_ap: v.bn_stats(lnt[:, 0:6], x3[:, 0:512]),
                        waits=wait0 + [last_hn[0]], inc="vs")
                t_b2 = pr.emit("vector", lambda v, x3=x_ap: v.bn_stats(lnt[:, 6:12], x3[:, 512:1024]), inc="vs")
                t_ag = pr.emit("vector", lambda v: v.bn_aggr(lnt[:, 12:14], lnt[:, 0:12]),
                               waits=[t_b2], inc="vs")
                t_v1 = pr.emit("vector", lambda v: v.tensor_scalar_add(lnt[:, 14:15], lnt[:, 13:14], EPS),
                               waits=[t_ag], inc="vs")
                t_s1 = pr.emit("scalar", lambda s: s.activation(lnt[:, 14:15], lnt[:, 14:15], AF.Sqrt),
                               waits=[t_v1], inc="ss")
                t_rc = pr.emit("vector", lambda v: v.reciprocal(lnt[:, 15:16], lnt[:, 14:15]),
                               waits=[t_s1], inc="vs")
                t_v3 = pr.emit("vector", lambda v: v.scalar_tensor_tensor(
                    lnt[:, 14:15], lnt[:, 12:13], -1.0, lnt[:, 15:16], op0=ALU.mult, op1=ALU.mult),
                    waits=[t_rc], inc="vs")
                t_hn = pr.emit("scalar", lambda s, x3=x_ap: s.activation(
                    hbf[:, :], x3, AF.Identity, bias=lnt[:, 14:15], scale=lnt[:, 15:16]),
                    waits=[t_v3] + hazard_guard, inc="ss")
                last_hn[0] = t_hn
                return t_v3, t_hn

            def emit_transposes(t_hn, dst_fns, extra=()):
                tk = None
                for i in range(DT):
                    tk = pr.emit("tensor",
                                 (lambda t, i=i: t.transpose(
                                     bkT.t[:, i * P:(i + 1) * P], hbf[:, i * P:(i + 1) * P],
                                     s_ident[:, :])),
                                 waits=([t_hn] + bkT.readers) if i == 0 else (),
                                 inc="ts" if i == DT - 1 else None)
                bkT.w = tk
                bkT.readers = []
                last_tr[0] = tk
                tk = None
                for fi, fn in enumerate(dst_fns):
                    tk = bkT.read("scalar", fn, extra_waits=list(extra) if fi == 0 else ())
                return tk

            def emit_state_update(t_chain, t_outs, npar_buf):
                for dt_i in range(DT):
                    for jh in range(2):
                        b2 = bk[(dt_i * 2 + jh) % 2]
                        mms = []
                        for c in range(SUP):
                            mms.append((b2.t[:, 0:512],
                                        Ktk[:, c * D + dt_i * P:c * D + (dt_i + 1) * P],
                                        Gsb[:, c * D + jh * 512:c * D + (jh + 1) * 512]))
                        b2.mm_group(mms, extra_waits=[t_chain])
                        b2.read("vector", lambda v, b2=b2, dt_i=dt_i, jh=jh: v.tensor_add(
                            WT[:, dt_i * D + jh * 512:dt_i * D + (jh + 1) * 512],
                            WT[:, dt_i * D + jh * 512:dt_i * D + (jh + 1) * 512],
                            b2.t[:, 0:512]))
                t_wupd = pr.tok("vs")
                for dt_i in range(DT):
                    pr.emit("scalar", lambda s, dt_i=dt_i: s.copy(
                        Wbf[:, dt_i * D:(dt_i + 1) * D], WT[:, dt_i * D:(dt_i + 1) * D]),
                        waits=[t_wupd, t_outs], inc="ss")
                for jh in range(2):
                    b2 = bk[2 + jh]
                    mms = []
                    for c in range(SUP):
                        mms.append((b2.t[0:1, 0:512], ones_cl[:, :],
                                    Gsb[:, c * D + jh * 512:c * D + (jh + 1) * 512]))
                    b2.mm_group(mms, extra_waits=[t_chain])
                    b2.read("vector", lambda v, b2=b2, jh=jh: v.tensor_add(
                        b_f32[0:1, jh * 512:(jh + 1) * 512],
                        b_f32[0:1, jh * 512:(jh + 1) * 512], b2.t[0:1, 0:512]))
                t_bupd = pr.tok("vs")
                pr.emit("scalar", lambda s, nb=npar_buf: s.copy(nb[0:1, :], b_f32[0:1, :]),
                        waits=[t_bupd, t_outs], inc="ss")

            pending_state = None

            for s_abs in range(n_super):
                is_m = s_abs >= NSUP
                par_buf = b_bfA if s_abs % 2 == 0 else b_bfB
                npar_buf = b_bfB if s_abs % 2 == 0 else b_bfA
                gcol = 1 if is_m else 0
                t_guard_a = pr.tok("ts")
                if pending_state is not None:
                    emit_state_update(*pending_state)
                    pending_state = None
                t_super_ts0 = pr.tok("ts")

                for ci in range(SUP):
                    cl = s_abs * SUP + ci
                    ensure_xdma(cl)
                    ensure_xdma(cl + 1)
                    slot = cl % 2
                    xsl = xb[:, slot * D:(slot + 1) * D]
                    t_v3, t_hn = emit_ln(xsl, [x_dma[cl]], [last_tr[0], last_cp[0]])
                    ln_done[cl] = (t_v3, t_hn)
                    ensure_xdma(cl + 2)
                    emit_transposes(t_hn, [
                        (lambda s, ci=ci, dt=dt: s.activation(
                            hft[:, dt * 512 + ci * P:dt * 512 + ci * P + P],
                            bkT.t[:, dt * P:(dt + 1) * P], AF.Identity,
                            scale=ca1c[:, dt:dt + 1], bias=sa_f[:, dt:dt + 1]))
                        for dt in range(DT)],
                        extra=[t_guard_a])
                t_hft = pr.tok("ss")

                for et in range(DT):
                    b2 = bk[2 + et % 2]
                    mms = []
                    for dt_i in range(DT):
                        mms.append((b2.t[:, 0:512],
                                    thK[:, dt_i * D + et * P:dt_i * D + (et + 1) * P],
                                    hft[:, dt_i * 512:dt_i * 512 + 512]))
                    b2.mm_group(mms, extra_waits=[t_hft])
                    b2.read("scalar", lambda s, et=et, b2=b2: s.copy(
                        Kft[:, et * 512:(et + 1) * 512], b2.t[:, 0:512]),
                        extra_waits=[t_super_ts0])
                if is_m:
                    for et in range(DT):
                        b2 = bk[4 + et % 2]
                        mms = []
                        for dt_i in range(DT):
                            mms.append((b2.t[:, 0:512],
                                        thQp[:, dt_i * D + et * P:dt_i * D + (et + 1) * P],
                                        hft[:, dt_i * 512:dt_i * 512 + 512]))
                        b2.mm_group(mms, extra_waits=[t_hft])
                        b2.read("scalar", lambda s, et=et, b2=b2: s.copy(
                            Qft[:, et * 512:(et + 1) * 512], b2.t[:, 0:512]),
                            extra_waits=[t_super_ts0])
                t_proj = [pr.tok("ss"), pr.tok("vs")]

                for cp in range(SUP):
                    b2 = bk[cp % 2]
                    mms = []
                    for et in range(DT):
                        mms.append((b2.t[:, 0:512],
                                    Kft[:, et * 512 + cp * P:et * 512 + (cp + 1) * P],
                                    Kft[:, et * 512:et * 512 + 512]))
                    b2.mm_group(mms, extra_waits=t_proj)
                    usl = Usb[:, cp * P:(cp + 1) * P]
                    t_u = b2.read("scalar", lambda s, b2=b2, cp=cp, usl=usl: s.activation(
                        usl, b2.t[:, cp * P:(cp + 1) * P], AF.Identity,
                        scale=2 * LR / D, bias=2 * LR / D),
                        extra_waits=[usb_rd[cp]])
                    pr.emit("vector", lambda v, cp=cp, usl=usl: v.tensor_mul(
                        Bp[:, cp * P:(cp + 1) * P], usl, s_mune[:, :]),
                        waits=[t_u, t_super_ts0], inc="vs")
                    usb_rd[cp] = pr.emit("vector", lambda v, cp=cp, usl=usl: v.tensor_mul(
                        Btp[:, cp * P:(cp + 1) * P], usl, s_mlne[:, :]),
                        waits=[t_u], inc="vs")
                    for c in range(cp + 1, SUP):
                        pid = pair_id_s(c, cp)
                        b2.read("scalar", lambda s, b2=b2, pid=pid, c=c: s.activation(
                            Scr[:, pid * P:(pid + 1) * P], b2.t[:, c * P:(c + 1) * P],
                            AF.Identity, scale=1.0, bias=1.0),
                            extra_waits=[t_super_ts0])
                if is_m:
                    for cp in range(SUP):
                        b2 = bk[2 + cp % 2]
                        mms = []
                        for et in range(DT):
                            mms.append((b2.t[:, 0:512],
                                        Kft[:, et * 512 + cp * P:et * 512 + (cp + 1) * P],
                                        Qft[:, et * 512:et * 512 + 512]))
                        b2.mm_group(mms, extra_waits=t_proj)
                        pid = pair_id_q(cp, cp)
                        usl = Usb[:, (4 + cp) * P:(5 + cp) * P]
                        t_u = b2.read("scalar", lambda s, b2=b2, cp=cp, usl=usl: s.activation(
                            usl, b2.t[:, cp * P:(cp + 1) * P], AF.Identity, scale=1.0, bias=1.0),
                            extra_waits=[usb_rd[4 + cp]])
                        usb_rd[4 + cp] = pr.emit("vector", lambda v, pid=pid, usl=usl: v.tensor_mul(
                            Sqr[:, pid * P:(pid + 1) * P], usl, s_muinc[:, :]),
                            waits=[t_u, t_super_ts0], inc="vs")
                        for c in range(cp + 1, SUP):
                            pid = pair_id_q(c, cp)
                            b2.read("scalar", lambda s, b2=b2, pid=pid, c=c: s.activation(
                                Sqr[:, pid * P:(pid + 1) * P], b2.t[:, c * P:(c + 1) * P],
                                AF.Identity, scale=1.0, bias=1.0),
                                extra_waits=[t_super_ts0])
                t_pairs = [pr.tok("ss"), pr.tok("vs")]

                for c in range(SUP):
                    tk = None
                    for et in range(DT):
                        tk = pr.emit("tensor",
                                     (lambda t, c=c, et=et: t.transpose(
                                         bkT.t[:, et * P:(et + 1) * P],
                                         Kft[:, et * 512 + c * P:et * 512 + c * P + P],
                                         s_ident[:, :])),
                                     waits=(t_proj + bkT.readers) if et == 0 else (),
                                     inc="ts" if et == DT - 1 else None)
                    bkT.w = tk
                    bkT.readers = []
                    bkT.read("scalar", lambda s, c=c: s.copy(
                        Ktk[:, c * D:(c + 1) * D], bkT.t[:, :]))

                x_tok = {}
                for c in range(SUP):
                    x_tok[c] = pr.emit("vector", lambda v, c=c: v.tensor_add(
                        Xsb[:, c * P:(c + 1) * P], s_ident[:, :], Bp[:, c * P:(c + 1) * P]),
                        waits=t_pairs + [t_super_ts0], inc="vs")
                ds_free = {c: None for c in range(SUP)}
                for m in range(1, 7):
                    pslot = (m - 1) % 2
                    cslot = m % 2
                    last = (m == 6)
                    bt_tok = {}
                    if not last:
                        for c in range(SUP):
                            bsl = Bp[:, (pslot * SUP + c) * P:(pslot * SUP + c + 1) * P]
                            btsl = Btp[:, (pslot * SUP + c) * P:(pslot * SUP + c + 1) * P]
                            b2 = bk[c % 2]
                            b2.mm_group([(b2.t[:, 0:P], btsl, bsl)],
                                        extra_waits=[x_tok[c]] if m == 1 else [])
                            b2.read("scalar", lambda s, b2=b2, c=c, cslot=cslot: s.copy(
                                Bp[:, (cslot * SUP + c) * P:(cslot * SUP + c + 1) * P], b2.t[:, 0:P]))
                    for c in range(SUP):
                        bsl = Bp[:, (pslot * SUP + c) * P:(pslot * SUP + c + 1) * P]
                        btsl = Btp[:, (pslot * SUP + c) * P:(pslot * SUP + c + 1) * P]
                        b3 = bk[2 + c % 2]
                        b3.mm_group([(b3.t[:, 0:P], bsl, btsl)],
                                    extra_waits=[x_tok[c]] if m == 1 else [])
                        bt_tok[c] = b3.read("scalar", lambda s, b3=b3, c=c, cslot=cslot: s.copy(
                            Btp[:, (cslot * SUP + c) * P:(cslot * SUP + c + 1) * P], b3.t[:, 0:P]))
                    for c in range(SUP):
                        b4 = bk[4 + c % 2]
                        b4.mm_group([(b4.t[:, 0:P],
                                      Btp[:, (cslot * SUP + c) * P:(cslot * SUP + c + 1) * P],
                                      Xsb[:, c * P:(c + 1) * P])],
                                    extra_waits=[bt_tok[c], x_tok[c]])
                        t_d = b4.read("scalar", lambda s, b4=b4, c=c: s.copy(
                            Dsb[:, c * P:(c + 1) * P], b4.t[:, 0:P]),
                            extra_waits=[ds_free[c]])
                        x_tok[c] = pr.emit("vector", lambda v, c=c: v.tensor_add(
                            Xsb[:, c * P:(c + 1) * P], Xsb[:, c * P:(c + 1) * P],
                            Dsb[:, c * P:(c + 1) * P]),
                            waits=[t_d], inc="vs")
                        ds_free[c] = x_tok[c]
                t_newton = pr.tok("vs")

                g_tok = {}
                ep_pend = None

                def emit_epilogue(c, t_xe):
                    nonlocal x2_spill_tok, h2_guard
                    c16 = (s_abs % NSUP) * SUP + c
                    sl = (c % 2) * D
                    x2sl = x2b[:, sl:sl + D]
                    t_x2m = pr.emit("vector", lambda v, sl=sl: v.tensor_add(
                        x2b[:, sl:sl + D], x2b[:, sl:sl + D], xeb[:, sl:sl + D]),
                        waits=[t_xe], inc="vs")
                    x2_spill_tok = pr.emit("sync", lambda sy, c16=c16, x2sl=x2sl: sy.dma_start(
                        ap(d_x2s, c16 * P * D, [[D, P], [1, D]]), x2sl),
                        waits=[t_x2m], inc="dsp", inc_by=16)
                    t_v3, t_hn = emit_ln(x2sl, [t_x2m], [last_tr[0], last_cp[0]])
                    t_h2 = emit_transposes(t_hn, [
                        (lambda s, dt=dt: s.activation(
                            h2stg[:, dt * P:(dt + 1) * P],
                            bkT.t[:, dt * P:(dt + 1) * P], AF.Identity,
                            scale=cb1c[:, dt:dt + 1], bias=sb_f[:, dt:dt + 1]))
                        for dt in range(DT)],
                        extra=[h2_guard[0], h2_guard[1]])
                    h2_spill = pr.emit("sync", lambda sy, c16=c16: sy.dma_start(
                        ap(d_h2s, c16 * P * D, [[D, P], [1, D]]), h2stg[:, :]),
                        waits=[t_h2], inc="dh2s", inc_by=16)
                    b6 = bk[4]
                    mms = []
                    for dt_i in range(DT):
                        mms.append((b6.t[:, 0:NE],
                                    h2stg[:, dt_i * P:(dt_i + 1) * P],
                                    rw_bf[:, dt_i * NE:(dt_i + 1) * NE]))
                    b6.mm_group(mms, extra_waits=[t_h2])
                    h2_guard = [h2_spill, pr.tok("ts")]
                    b6.read("vector", lambda v, b6=b6: v.tensor_copy(lg[:, :], b6.t[:, 0:NE]))
                    t_m1 = pr.emit("vector", lambda v: v.reduce_max(rcol[:, 0:1], lg[:, :], axis=AX.X), inc="vs")
                    t_m2 = pr.emit("vector", lambda v: v.tensor_scalar_mul(rcol[:, 1:2], rcol[:, 0:1], -1.0),
                                   waits=[t_m1], inc="vs")
                    t_e1 = pr.emit("scalar", lambda s: s.activation(
                        zex[:, :], lg[:, :], AF.Exp, bias=rcol[:, 1:2]),
                        waits=[t_m2], inc="ss")
                    t_r1 = pr.emit("vector", lambda v: v.reduce_sum(rcol[:, 2:3], zex[:, :], axis=AX.X),
                                   waits=[t_e1], inc="vs")
                    t_r2 = pr.emit("vector", lambda v: v.reciprocal(rcol[:, 3:4], rcol[:, 2:3]),
                                   waits=[t_r1], inc="vs")
                    t_sc = pr.emit("scalar", lambda s: s.activation(
                        sc8[:, :], zex[:, :], AF.Identity, scale=rcol[:, 3:4]),
                        waits=[t_r2], inc="ss")
                    t_t8 = pr.emit("vector", lambda v: v.max(out=top8[:, :], in_=sc8[:, :]),
                                   waits=[t_sc], inc="vs")
                    pr.emit("vector", lambda v, c16=c16: v.tensor_copy(
                        w2col[:, 2 * c16:2 * c16 + 2], top8[:, 0:2]),
                        waits=[t_t8], inc="vs")
                    t_m0 = pr.emit("vector", lambda v: v.scalar_tensor_tensor(
                        mk0[:, :], sc8[:, :], top8[:, 0:1], sc8[:, :],
                        op0=ALU.is_ge, op1=ALU.bypass), inc="vs")
                    t_mk = pr.emit("vector", lambda v: v.scalar_tensor_tensor(
                        maskf[:, :], sc8[:, :], top8[:, 1:2], sc8[:, :],
                        op0=ALU.is_ge, op1=ALU.bypass), inc="vs")
                    t_mb = pr.emit("vector", lambda v: v.tensor_copy(maskb[:, :], maskf[:, :]),
                                   waits=[t_mk], inc="vs")
                    b6.mm_group([(b6.t[:, 0:NE], s_mut1[:, :], maskb[:, :]),
                                 (b6.t[:, 0:NE], ones_r1f[:, :], absoffs[0:1, :])],
                                extra_waits=[t_mb, t_offprev[0]])
                    b6.mm_group([(b6.t[0:1, NE:2 * NE], ones_cl[:, :], maskb[:, :])])
                    t_sl = b6.read("vector", lambda v, b6=b6: v.tensor_copy(
                        slotab[:, :], b6.t[:, 0:NE]))
                    t_offprev[0] = b6.read("vector", lambda v, b6=b6: v.tensor_add(
                        absoffs[0:1, :], absoffs[0:1, :], b6.t[0:1, NE:2 * NE]))
                    t_cl = pr.emit("vector", lambda v: v.tensor_scalar_min(
                        slotab[:, :], slotab[:, :], float(NSEG - 1)), waits=[t_sl], inc="vs")
                    t_s1 = pr.emit("vector", lambda v: v.tensor_sub(
                        tmp8[:, :], maskf[:, :], mk0[:, :]), waits=[t_mk, t_m0], inc="vs")
                    t_p0 = pr.emit("vector", lambda v: v.tensor_mul(
                        mk0[:, :], mk0[:, :], slotab[:, :]), waits=[t_cl], inc="vs")
                    t_p1 = pr.emit("vector", lambda v: v.tensor_mul(
                        tmp8[:, :], tmp8[:, :], slotab[:, :]), waits=[t_s1, t_cl], inc="vs")
                    t_r0s = pr.emit("vector", lambda v: v.reduce_sum(
                        slcols[:, 0:1], mk0[:, :], axis=AX.X), waits=[t_p0], inc="vs")
                    t_r1s = pr.emit("vector", lambda v: v.reduce_sum(
                        slcols[:, 1:2], tmp8[:, :], axis=AX.X), waits=[t_p1], inc="vs")
                    t_cast = pr.emit("vector", lambda v, c16=c16: v.tensor_copy(
                        sl2[:, 2 * c16:2 * c16 + 2], slcols[:, 0:2]),
                        waits=[t_r0s, t_r1s], inc="vs")
                    csl = c16 % 2
                    t_cp = pr.emit("scalar", lambda s, csl=csl: s.copy(
                        h2cp[:, csl * D:(csl + 1) * D], hbf[:, :]),
                        waits=[t_hn, h2cp_guard[csl]], inc="ss")
                    last_cp[0] = t_cp
                    for k2 in range(2):
                        h2cp_guard[csl] = pr.emit("gpsimd", lambda g, c16=c16, k2=k2, csl=csl:
                            g.indirect_dma_start(
                                ap(d_hx, 0, [[D, NSEG], [1, D]]),
                                bass.IndirectOffsetOnAxis(
                                    sl2[:, 2 * c16 + k2:2 * c16 + k2 + 1], 0),
                                h2cp[:, csl * D:(csl + 1) * D], None),
                            waits=[t_cp, t_cast], inc="dscx", inc_by=16)

                for c in range(SUP):
                    bl, bh = (bk[0], bk[1]) if c % 2 == 0 else (bk[2], bk[3])
                    if is_m:
                        t_xe = pr.emit("sync", lambda sy, s_abs=s_abs, c=c: sy.dma_start(
                            xeb[:, (c % 2) * D:(c % 2 + 1) * D], x_src(s_abs, c)),
                            waits=[pr.tok("vs")], inc="dxe", inc_by=16)
                    for jh, b2 in ((0, bl), (1, bh)):
                        mms = []
                        for et in range(DT):
                            mms.append((b2.t[:, 0:512],
                                        Kft[:, et * 512 + c * P:et * 512 + (c + 1) * P],
                                        Wbf[:, et * D + jh * 512:et * D + (jh + 1) * 512]))
                        for dt_i in range(DT):
                            mms.append((b2.t[:, 0:512],
                                        hft[:, dt_i * 512 + c * P:dt_i * 512 + (c + 1) * P],
                                        thVn[:, dt_i * D + jh * 512:dt_i * D + (jh + 1) * 512]))
                        mms.append((b2.t[:, 0:512], ones_r1[:, :],
                                    par_buf[0:1, jh * 512:(jh + 1) * 512]))
                        b2.mm_group(mms, extra_waits=[t_newton])
                    if is_m:
                        for jh, b2 in ((0, bk[5]), (1, bk[6])):
                            mms = []
                            for et in range(DT):
                                mms.append((b2.t[:, 0:512],
                                            Qft[:, et * 512 + c * P:et * 512 + (c + 1) * P],
                                            Wbf[:, et * D + jh * 512:et * D + (jh + 1) * 512]))
                            mms.append((b2.t[:, 0:512], ones_r1[:, :],
                                        par_buf[0:1, jh * 512:(jh + 1) * 512]))
                            b2.mm_group(mms, extra_waits=[t_newton])
                    for jh, b2 in ((0, bl), (1, bh)):
                        mms = []
                        for cp in range(c):
                            pid = pair_id_s(c, cp)
                            mms.append((b2.t[:, 0:512], Scr[:, pid * P:(pid + 1) * P],
                                        Gsb[:, cp * D + jh * 512:cp * D + (jh + 1) * 512]))
                        if mms:
                            b2.mm_group(mms, extra_waits=[g_tok[c - 1]], cont=True)
                        b2.read("scalar", lambda s, b2=b2, jh=jh: s.copy(
                            RHSs[0:P, jh * 512:(jh + 1) * 512], b2.t[:, 0:512]))
                    t_rhs = pr.tok("ss")
                    if is_m and ep_pend is not None:
                        emit_epilogue(*ep_pend)
                        ep_pend = None
                    bg = bk[4]
                    for jh in range(2):
                        bg.mm_group([(bg.t[:, 0:512], Xsb[:, c * P:(c + 1) * P],
                                      RHSs[:, jh * 512:(jh + 1) * 512])],
                                    extra_waits=[t_rhs])
                        g_tok[c] = bg.read("scalar", lambda s, b5=bg, c=c, jh=jh, gcol=gcol: s.activation(
                            Gsb[:, c * D + jh * 512:c * D + (jh + 1) * 512], b5.t[:, 0:512],
                            AF.Identity, scale=s_gmask[:, gcol:gcol + 1]),
                            extra_waits=[t_super_ts0])
                    if is_m:
                        for jh, b2 in ((0, bk[5]), (1, bk[6])):
                            mms = []
                            for cp in range(c + 1):
                                pid = pair_id_q(c, cp)
                                mms.append((b2.t[:, 0:512], Sqr[:, pid * P:(pid + 1) * P],
                                            Gsb[:, cp * D + jh * 512:cp * D + (jh + 1) * 512]))
                            b2.mm_group(mms, extra_waits=[g_tok[c]], cont=True)
                            b2.read("vector", lambda v, b2=b2, jh=jh, c=c: v.tensor_mul(
                                x2b[:, (c % 2) * D + jh * 512:(c % 2) * D + (jh + 1) * 512],
                                b2.t[:, 0:512],
                                ga_bc[:, jh * 512:(jh + 1) * 512]),
                                extra_waits=[pr.tok("ss"), x2_spill_tok])
                        ep_pend = (c, t_xe)
                if is_m and ep_pend is not None:
                    emit_epilogue(*ep_pend)
                    ep_pend = None
                t_chain = pr.tok("ss")

                t_outs = pr.tok("ts")
                pending_state = (t_chain, t_outs, npar_buf)

            if pending_state is not None:
                emit_state_update(*pending_state)
                pending_state = None

            t_phaseA = [pr.tok("ts"), pr.tok("vs"), pr.tok("ss"), pr.tok("dsp"), pr.tok("dh2s"),
                        pr.tok("dx0"), pr.tok("dx1"), pr.tok("dxe")]

            if debug:
                tap("WT", WT[:, :], t_phaseA)
                tap("Gsb", Gsb[:, :], t_phaseA)
                tap("bf32", b_f32[:, :], t_phaseA)
                tap("comb", comb_all[:, :], t_phaseA)
                tap("Kft", Kft[:, :], t_phaseA)
                tap("Ktk", Ktk[:, :], t_phaseA)
                tap("Qft", Qft[:, :], t_phaseA)
                tap("Xsb", Xsb[:, :], t_phaseA)
                tap("hft", hft[:, :], t_phaseA)
                tap("modcols_ca1", ca1c[:, :], t_phaseA)
                tap("h2s0", h2stg[:, :], t_phaseA)
                tap("ga_bc", ga_bc[:, 0:D], t_phaseA)
            if debug:
                t_phaseA = t_phaseA + [pr.tok("do")]

        if do_passb:
            with ExitStack() as bctx:
                EB = bctx.enter_context
                wG = EB(nc.sbuf_tensor("wG", [P, 2 * DT * D], BF16))
                wU = EB(nc.sbuf_tensor("wU", [P, 2 * DT * D], BF16))
                wD = EB(nc.sbuf_tensor("wD", [P, DT * D], BF16))
                sg2 = EB(nc.sbuf_tensor("sg2", [P, 512], BF16))
                ub2 = EB(nc.sbuf_tensor("ub2", [P, 512], BF16))
                obuf = EB(nc.sbuf_tensor("obuf", [P, D], F32))

                def pe_srcs(kind, idx):
                    if kind == "sh":
                        return (ap(d_sg_scr, idx * P * DT * D, [[DT * D, P], [1, DT * D]]),
                                ap(d_su_scr, idx * P * DT * D, [[DT * D, P], [1, DT * D]]),
                                ap(d_sd_scr, idx * P * DT * D, [[DT * D, P], [1, DT * D]]),
                                [wprep_toks["sgw"], wprep_toks["suw"], wprep_toks["sdw"]])
                    return (ap(d_gw_scr, idx * P * DT * D, [[DT * D, P], [1, DT * D]]),
                            ap(d_uw_scr, idx * P * DT * D, [[DT * D, P], [1, DT * D]]),
                            ap(d_dw_scr, idx * P * DT * D, [[DT * D, P], [1, DT * D]]),
                            [wprep_toks["gw"], wprep_toks["uw"], wprep_toks["dw"]])

                wslot_free = {0: list(t_phaseA), 1: list(t_phaseA)}
                wd_free = list(t_phaseA)
                out_toks = []

                t_scx_all = pr.tok("dscx")
                with ExitStack() as ectx:
                    EE = ectx.enter_context
                    hxb = EE(nc.sbuf_tensor("hxb", [P, 2 * TEW * D], BF16))
                    hfe = EE(nc.sbuf_tensor("hfe", [P, DT * CAP], BF16))
                    acts = EE(nc.sbuf_tensor("acts", [P, DT * CAP], BF16))
                    ysb = EE(nc.sbuf_tensor("ysb", [P, 2 * D], F32))
                    ysb_guard = {0: None, 1: None}
                    t_hx_free = {0: None, 1: None}
                    for e2 in range(NE):
                        slot = e2 % 2
                        gsrc, usrc, dsrc, wtoks = pe_srcs("ex", e2)
                        t_lg_ = pr.emit("sync", lambda sy, gsrc=gsrc, slot=slot: sy.dma_start(
                            ap(wG, slot * DT * D, [[2 * DT * D, P], [1, DT * D]]), gsrc),
                            waits=wtoks + wslot_free[slot], inc=f"dwg{slot}", inc_by=16)
                        t_lu_ = pr.emit("sync", lambda sy, usrc=usrc, slot=slot: sy.dma_start(
                            ap(wU, slot * DT * D, [[2 * DT * D, P], [1, DT * D]]), usrc),
                            waits=wtoks + wslot_free[slot], inc=f"dwu{slot}", inc_by=16)
                        t_ld_ = pr.emit("sync", lambda sy, dsrc=dsrc: sy.dma_start(wD[:, :], dsrc),
                                        waits=wtoks + wd_free, inc="dwd", inc_by=16)
                        hsl = (e2 % 2) * TEW * D
                        t_hx = pr.emit("scalar", lambda s, e2=e2, hsl=hsl: s.dma_start(
                            ap(hxb, hsl, [[2 * TEW * D, P], [D, TEW], [1, D]]),
                            ap(d_hx, e2 * CAP * D, [[D, P], [P * D, TEW], [1, D]])),
                            waits=[t_scx_all, t_hx_free[e2 % 2]], inc="dhx", inc_by=16)
                        for r in range(TEW):
                            tk = None
                            for i in range(DT):
                                tk = pr.emit("tensor",
                                             (lambda t, r=r, i=i, hsl=hsl: t.transpose(
                                                 bkT.t[:, i * P:(i + 1) * P],
                                                 hxb[:, hsl + r * D + i * P:hsl + r * D + (i + 1) * P],
                                                 s_ident[:, :])),
                                             waits=([t_hx] + bkT.readers) if i == 0 else (),
                                             inc="ts" if i == DT - 1 else None)
                            bkT.w = tk
                            bkT.readers = []
                            for i in range(DT):
                                bkT.read("scalar", lambda s, r=r, i=i: s.activation(
                                    hfe[:, i * CAP + r * P:i * CAP + (r + 1) * P],
                                    bkT.t[:, i * P:(i + 1) * P], AF.Identity,
                                    scale=cb1c[:, i:i + 1], bias=sb_f[:, i:i + 1]))
                        t_hfe = pr.tok("ss")
                        t_hx_free[e2 % 2] = pr.tok("ts")
                        for fj in range(DT):
                            for gi, (seg, nn) in enumerate(((0, 512), (512, 256))):
                                bA = bk[(fj * 2 + gi) % 2]
                                bB = bk[4 + (fj * 2 + gi) % 2]
                                mmsA, mmsB = [], []
                                for ht in range(DT):
                                    rh = hfe[:, ht * CAP + seg:ht * CAP + seg + nn]
                                    mmsA.append((bA.t[:, 0:nn],
                                                 wG[:, slot * DT * D + ht * D + fj * P:
                                                    slot * DT * D + ht * D + (fj + 1) * P], rh))
                                    mmsB.append((bB.t[:, 0:nn],
                                                 wU[:, slot * DT * D + ht * D + fj * P:
                                                    slot * DT * D + ht * D + (fj + 1) * P], rh))
                                bA.mm_group(mmsA, extra_waits=[t_lg_, t_hfe])
                                bB.mm_group(mmsB, extra_waits=[t_lu_, t_hfe])
                                t_sg = bA.read("scalar", lambda s, bA=bA, nn=nn: s.activation(
                                    sg2[:, 0:nn], bA.t[:, 0:nn], AF.Silu))
                                t_ub = bB.read("vector", lambda v, bB=bB, nn=nn: v.tensor_copy(
                                    ub2[:, 0:nn], bB.t[:, 0:nn]))
                                pr.emit("vector", lambda v, fj=fj, seg=seg, nn=nn: v.tensor_mul(
                                    acts[:, fj * CAP + seg:fj * CAP + seg + nn],
                                    sg2[:, 0:nn], ub2[:, 0:nn]),
                                    waits=[t_sg, t_ub], inc="vs")
                        t_act = pr.tok("vs")
                        for r in range(TEW):
                            for hh in range(2):
                                bC = bk[2 + (r * 2 + hh) % 2]
                                mms = []
                                for fj in range(DT):
                                    mms.append((bC.t[:, 0:512],
                                                acts[:, fj * CAP + r * P:fj * CAP + (r + 1) * P],
                                                wD[:, fj * D + hh * 512:fj * D + (hh + 1) * 512]))
                                bC.mm_group(mms, extra_waits=[t_act, t_ld_])
                                bC.read("vector", lambda v, bC=bC, r=r, hh=hh: v.tensor_copy(
                                    ysb[:, (r % 2) * D + hh * 512:(r % 2) * D + (hh + 1) * 512],
                                    bC.t[:, 0:512]),
                                    extra_waits=[ysb_guard[r % 2]])
                            t_ye = pr.tok("vs")
                            ysb_guard[r % 2] = pr.emit("gpsimd", lambda g, e2=e2, r=r: g.dma_start(
                                ap(d_yx, (e2 * CAP + r * P) * D, [[D, P], [1, D]]),
                                ysb[:, (r % 2) * D:(r % 2 + 1) * D]),
                                waits=[t_ye], inc="dyx", inc_by=16)
                        wslot_free[slot] = [pr.tok("ts")]
                        wd_free = [pr.tok("ts")]
                t_yx_all = pr.tok("dyx")

                acc = EB(nc.sbuf_tensor("acc", [P, (NCH // 2) * D], F32))
                h2b = EB(nc.sbuf_tensor("h2b", [P, DT * D], BF16))
                act = EB(nc.sbuf_tensor("act", [P, DT * D], BF16))
                ygb = EB(nc.sbuf_tensor("ygb", [P, 4 * D], F32))
                pes = [("sh", 0), ("sh", 1)]
                ygb_guard = {0: None, 1: None}
                for th in range(2):
                    t_ac = pr.emit("gpsimd", lambda g: g.memset(acc[:, :], 0.0),
                                   waits=t_phaseA + [pr.tok("vs"), pr.tok("ts"), t_yx_all],
                                   inc="gs")
                    acc_guard = [t_ac]
                    for ht in range(DT):
                        pr.emit("sync", lambda sy, ht=ht, th=th: sy.dma_start(
                            ap(h2b, ht * D, [[DT * D, P], [P, DT], [1, P]]),
                            ap(d_h2s, th * DT * P * D + P * ht + 0, [[D, P], [P * D, DT], [1, P]])),
                            waits=t_phaseA + [t_yx_all, pr.tok("ts"), pr.tok("vs"), pr.tok("ss")],
                            inc="dh2", inc_by=16)
                    h2_load = [pr.tok("dh2")]
                    for pi, (kind, idx) in enumerate(pes):
                        slot = pi % 2
                        gsrc, usrc, dsrc, wtoks = pe_srcs(kind, idx)
                        t_lg_ = pr.emit("sync", lambda sy, gsrc=gsrc, slot=slot: sy.dma_start(
                            ap(wG, slot * DT * D, [[2 * DT * D, P], [1, DT * D]]), gsrc),
                            waits=wtoks + wslot_free[slot], inc=f"dwg{slot}", inc_by=16)
                        t_lu_ = pr.emit("sync", lambda sy, usrc=usrc, slot=slot: sy.dma_start(
                            ap(wU, slot * DT * D, [[2 * DT * D, P], [1, DT * D]]), usrc),
                            waits=wtoks + wslot_free[slot], inc=f"dwu{slot}", inc_by=16)
                        t_ld_ = pr.emit("sync", lambda sy, dsrc=dsrc: sy.dma_start(wD[:, :], dsrc),
                                        waits=wtoks + wd_free, inc="dwd", inc_by=16)
                        for fj in range(DT):
                            for tb in range(2):
                                bA = bk[(fj * 2 + tb) % 2]
                                bB = bk[4 + (fj * 2 + tb) % 2]
                                mmsA, mmsB = [], []
                                for ht in range(DT):
                                    rh = h2b[:, ht * D + tb * 512:ht * D + (tb + 1) * 512]
                                    mmsA.append((bA.t[:, 0:512],
                                                 wG[:, slot * DT * D + ht * D + fj * P:
                                                    slot * DT * D + ht * D + (fj + 1) * P], rh))
                                    mmsB.append((bB.t[:, 0:512],
                                                 wU[:, slot * DT * D + ht * D + fj * P:
                                                    slot * DT * D + ht * D + (fj + 1) * P], rh))
                                bA.mm_group(mmsA, extra_waits=[t_lg_] + h2_load)
                                bB.mm_group(mmsB, extra_waits=[t_lu_] + h2_load)
                                t_sg = bA.read("scalar", lambda s, bA=bA: s.activation(
                                    sg2[:, :], bA.t[:, 0:512], AF.Silu))
                                t_ub = bB.read("scalar", lambda s, bB=bB: s.activation(
                                    ub2[:, :], bB.t[:, 0:512], AF.Identity))
                                pr.emit("vector", lambda v, fj=fj, tb=tb: v.tensor_mul(
                                    act[:, fj * D + tb * 512:fj * D + (tb + 1) * 512],
                                    sg2[:, :], ub2[:, :]),
                                    waits=[t_sg, t_ub], inc="vs")
                        t_act = pr.tok("vs")
                        for tt in range(DT):
                            c16 = th * 8 + tt
                            for hh in range(2):
                                bC = bk[2 + (tt * 2 + hh) % 2]
                                mms = []
                                for fj in range(DT):
                                    mms.append((bC.t[:, 0:512],
                                                ap(act, fj * D + tt * P, [[DT * D, P], [1, P]]),
                                                wD[:, fj * D + hh * 512:fj * D + (hh + 1) * 512]))
                                bC.mm_group(mms, extra_waits=[t_act, t_ld_])
                                csl = onef_col[:, 0:1]
                                bC.read("vector", lambda v, bC=bC, tt=tt, hh=hh, csl=csl: v.scalar_tensor_tensor(
                                    acc[:, tt * D + hh * 512:tt * D + (hh + 1) * 512],
                                    bC.t[:, 0:512], csl,
                                    acc[:, tt * D + hh * 512:tt * D + (hh + 1) * 512],
                                    op0=ALU.mult, op1=ALU.add),
                                    extra_waits=acc_guard)
                        wslot_free[slot] = [pr.tok("ts")]
                        wd_free = [pr.tok("ts")]
                    t_accdone = pr.tok("vs")
                    for cc in range(8):
                        c16 = th * 8 + cc
                        slot = cc % 2
                        gsl = cc % 2
                        t_yg = []
                        for k2 in range(2):
                            t_yg.append(pr.emit("gpsimd", lambda g, c16=c16, k2=k2, gsl=gsl:
                                g.indirect_dma_start(
                                    ygb[:, (gsl * 2 + k2) * D:(gsl * 2 + k2 + 1) * D], None,
                                    ap(d_yx, 0, [[D, NSEG], [1, D]]),
                                    bass.IndirectOffsetOnAxis(
                                        sl2[:, 2 * c16 + k2:2 * c16 + k2 + 1], 0)),
                                waits=[t_yx_all, ygb_guard[gsl]], inc="dyg", inc_by=16))
                        t_xr = pr.emit("sync", lambda sy, c16=c16, slot=slot: sy.dma_start(
                            ap(xb, slot * D, [[2 * D, P], [1, D]]),
                            ap(d_x2s, c16 * P * D, [[D, P], [1, D]])),
                            waits=[t_accdone] + ([pr.tok("vs")] if cc >= 2 else []),
                            inc=f"dx{slot}", inc_by=16)
                        t_cb = t_accdone
                        for k2 in range(2):
                            t_cb = pr.emit("vector", lambda v, cc=cc, c16=c16, k2=k2, gsl=gsl:
                                v.scalar_tensor_tensor(
                                    acc[:, cc * D:(cc + 1) * D],
                                    ygb[:, (gsl * 2 + k2) * D:(gsl * 2 + k2 + 1) * D],
                                    w2col[:, 2 * c16 + k2:2 * c16 + k2 + 1],
                                    acc[:, cc * D:(cc + 1) * D],
                                    op0=ALU.mult, op1=ALU.add),
                                waits=[t_accdone, t_yg[k2], t_cb], inc="vs")
                        ygb_guard[gsl] = t_cb
                        pr.emit("vector", lambda v, cc=cc: v.tensor_mul(
                            obuf[:, :], acc[:, cc * D:(cc + 1) * D], gb_bc[:, :]),
                            waits=[t_cb, pr.tok("do")], inc="vs")
                        t_o2 = pr.emit("vector", lambda v, slot=slot: v.tensor_add(
                            obuf[:, :], obuf[:, :], xb[:, slot * D:(slot + 1) * D]),
                            waits=[t_xr], inc="vs")
                        out_toks.append(pr.emit("sync", lambda sy, c16=c16: sy.dma_start(
                            ap(d_out, c16 * P * D, [[D, P], [1, D]]), obuf[:, :]),
                            waits=[t_o2], inc="do", inc_by=16))

                pr.q["sync"].append(([out_toks[-1]], lambda sy: None, None))
                pr.lastw["sync"][out_toks[-1][0]] = out_toks[-1][1]

        else:
            pr.q["sync"].append(([t_phaseA[0]], lambda sy: None, None))

        block = octx.enter_context(nc.Block())

        @block.tensor
        def _(t):
            pr.replay("tensor", t)

        @block.vector
        def _(v):
            pr.replay("vector", v)

        @block.scalar
        def _(s):
            pr.replay("scalar", s)

        @block.gpsimd
        def _(g):
            pr.replay("gpsimd", g)

        @block.sync
        def _(sy):
            pr.replay("sync", sy)

    return nc, taps


def _mask(fn):
    r = np.arange(P)[:, None]
    c = np.arange(P)[None, :]
    return fn(r, c).astype(np.float32).astype(ml_dtypes.bfloat16)


def make_inputs(L0_attn_in_seq, L0_attn_theta_K, L0_attn_theta_V, L0_attn_theta_Q,
                L0_attn_W0, L0_attn_b0, L0_ffn_router_w, L0_ffn_gate_w, L0_ffn_up_w,
                L0_ffn_down_w, L0_ffn_shared_gate_w, L0_ffn_shared_up_w,
                L0_ffn_shared_down_w, L0_c, L0_wmod):
    x = np.ascontiguousarray(np.asarray(L0_attn_in_seq, dtype=np.float32))
    f32 = lambda a: np.ascontiguousarray(np.asarray(a, dtype=np.float32))
    base = {
        "thK": f32(np.asarray(L0_attn_theta_K).T),
        "thV": f32(np.asarray(L0_attn_theta_V).T),
        "thQ": f32(np.asarray(L0_attn_theta_Q).T),
        "w0T": f32(np.asarray(L0_attn_W0).T),
        "b0r": f32(np.asarray(L0_attn_b0).reshape(1, D)),
        "rw": f32(L0_ffn_router_w),
        "gw": f32(L0_ffn_gate_w), "uw": f32(L0_ffn_up_w), "dw": f32(L0_ffn_down_w),
        "sgw": f32(L0_ffn_shared_gate_w), "suw": f32(L0_ffn_shared_up_w),
        "sdw": f32(L0_ffn_shared_down_w),
        "wmod": f32(L0_wmod),
        "identb": np.eye(P, dtype=np.float32).astype(ml_dtypes.bfloat16),
        "mune": _mask(lambda r, c: np.where(r < c, -1.0, 0.0)),
        "mlne": _mask(lambda r, c: np.where(r > c, -1.0, 0.0)),
        "muinc": _mask(lambda r, c: np.where(r <= c, 1.0, 0.0)),
        "mut1": _mask(lambda r, c: np.where(r < c, 1.0, 0.0)),
        "ebase": (np.arange(NE, dtype=np.float32) * CAP).reshape(1, NE),
    }
    cmat = f32(L0_c)
    in_maps = []
    for k in range(8):
        b, h = k >> 1, k & 1
        gm = np.zeros((P, 2), dtype=np.float32)
        gm[:, 0] = -2.0 * LR / D * h
        gm[:, 1] = -2.0 * LR / D
        m = dict(base)
        m["xm"] = np.ascontiguousarray(x[b, h * TOWN:(h + 1) * TOWN])
        m["xp"] = np.ascontiguousarray(x[b, 0:TOWN])
        m["gmask"] = gm
        m["cvec"] = np.ascontiguousarray(cmat[b].reshape(D, 1))
        in_maps.append(m)
    return in_maps


def kernel(**inputs):
    in_maps = make_inputs(**inputs)
    nc, _ = build_program(debug=False)
    res = run_bass_kernel_spmd(nc, in_maps, core_ids=list(range(8)))
    out = np.empty((4, 4096, D), dtype=np.float32)
    for k in range(8):
        b, h = k >> 1, k & 1
        out[b, h * TOWN:(h + 1) * TOWN] = res.results[k]["out"]
    return out


# version 12
# speedup vs baseline: 4.8037x; 1.0038x over previous
import numpy as np
import ml_dtypes
from contextlib import ExitStack

import concourse.bass as bass
import concourse.mybir as mybir
from concourse.bass_utils import run_bass_kernel_spmd

F32 = mybir.dt.float32
BF16 = mybir.dt.bfloat16
U32 = mybir.dt.uint32
AF = mybir.ActivationFunctionType
ALU = mybir.AluOpType
AX = mybir.AxisListType

P = 128
D = 1024
DT = 8
SUP = 4
NSUP = 4
NCH = 16
TOWN = 2048
LR = 0.01
EPS = 1e-5
NE = 8
CAP = 768
TEW = CAP // 128
NSEG = NE * CAP
QUEUES = ("tensor", "vector", "scalar", "gpsimd", "sync")
QSEM = {"vector": "vs", "scalar": "ss", "gpsimd": "gs", "sync": "do", "tensor": "ts"}


class Prog:

    def __init__(self, nc, ctx):
        self.nc = nc
        self.ctx = ctx
        self.q = {k: [] for k in QUEUES}
        self.cnt = {}
        self.hand = {}
        self.lastw = {k: {} for k in QUEUES}

    def sem(self, name):
        if name not in self.hand:
            self.hand[name] = self.ctx.enter_context(self.nc.semaphore(name))
            self.cnt[name] = 0
        return self.hand[name]

    def tok(self, name):
        self.sem(name)
        return (name, self.cnt[name])

    def emit(self, queue, fn, waits=(), inc=None, inc_by=1):
        ws = []
        for w in waits:
            if w is None:
                continue
            s, v = w
            if v <= 0:
                continue
            if self.lastw[queue].get(s, 0) >= v:
                continue
            self.lastw[queue][s] = v
            self.sem(s)
            ws.append((s, v))
        tk = None
        it = None
        if inc is not None:
            self.sem(inc)
            self.cnt[inc] += inc_by
            tk = (inc, self.cnt[inc])
            it = (inc, inc_by)
        self.q[queue].append((ws, fn, it))
        return tk

    def replay(self, queue, eng):
        for ws, fn, it in self.q[queue]:
            for s, v in ws:
                eng.wait_ge(self.hand[s], v)
            ins = fn(eng)
            if it is not None:
                assert ins is not None, f"{queue}: inc {it} but fn returned None"
                ins.then_inc(self.hand[it[0]], it[1])


class Bank:

    def __init__(self, pr, name, dtype=F32):
        width = 512 if dtype == F32 else 1024
        self.pr = pr
        self.t = pr.ctx.enter_context(pr.nc.psum_tensor(name, [P, width], dtype))
        self.w = None
        self.readers = []

    def mm_group(self, mms, extra_waits=(), cont=False, open_end=False):
        pr = self.pr
        n = len(mms)
        waits = [w for w in list(extra_waits) + ([] if cont else self.readers) if w is not None]
        tk = None
        for i, (o, l, r) in enumerate(mms):
            tk = pr.emit(
                "tensor",
                (lambda t, o=o, l=l, r=r, st=(i == 0 and not cont),
                 sp=(i == n - 1 and not open_end):
                 t.matmul(o, l, r, start=st, stop=sp)),
                waits=waits if i == 0 else (),
                inc="ts" if i == n - 1 else None)
        self.w = tk
        if not cont:
            self.readers = []
        return tk

    def read(self, queue, fn, extra_waits=()):
        tk = self.pr.emit(queue, fn, waits=[self.w] + list(extra_waits), inc=QSEM[queue])
        self.readers.append(tk)
        return tk


def ap(t, offset, dims):
    return bass.AP(t, offset, dims)


_BUILD_N = [0]


def build_program(debug=False, n_super=2 * NSUP, do_passb=True):
    _BUILD_N[0] += 1
    _tp = "tap_" if _BUILD_N[0] == 1 else f"tap{_BUILD_N[0]}_"
    nc = bass.Bass("TRN2", target_bir_lowering=False)

    d_xm = nc.dram_tensor("xm", [TOWN, D], F32, kind="ExternalInput")
    d_xp = nc.dram_tensor("xp", [TOWN, D], F32, kind="ExternalInput")
    d_gmask = nc.dram_tensor("gmask", [P, 2], F32, kind="ExternalInput")
    d_cvec = nc.dram_tensor("cvec", [D, 1], F32, kind="ExternalInput")
    d_wmod = nc.dram_tensor("wmod", [D, 6 * D], F32, kind="ExternalInput")
    d_thK = nc.dram_tensor("thK", [D, D], F32, kind="ExternalInput")
    d_thV = nc.dram_tensor("thV", [D, D], F32, kind="ExternalInput")
    d_thQ = nc.dram_tensor("thQ", [D, D], F32, kind="ExternalInput")
    d_w0T = nc.dram_tensor("w0T", [D, D], F32, kind="ExternalInput")
    d_b0 = nc.dram_tensor("b0r", [1, D], F32, kind="ExternalInput")
    d_rw = nc.dram_tensor("rw", [D, NE], F32, kind="ExternalInput")
    d_gw = nc.dram_tensor("gw", [NE, D, D], F32, kind="ExternalInput")
    d_uw = nc.dram_tensor("uw", [NE, D, D], F32, kind="ExternalInput")
    d_dw = nc.dram_tensor("dw", [NE, D, D], F32, kind="ExternalInput")
    d_sgw = nc.dram_tensor("sgw", [D, 2 * D], F32, kind="ExternalInput")
    d_suw = nc.dram_tensor("suw", [D, 2 * D], F32, kind="ExternalInput")
    d_sdw = nc.dram_tensor("sdw", [2 * D, D], F32, kind="ExternalInput")
    d_ident = nc.dram_tensor("identb", [P, P], BF16, kind="ExternalInput")
    d_mut1 = nc.dram_tensor("mut1", [P, P], BF16, kind="ExternalInput")
    d_ebase = nc.dram_tensor("ebase", [1, NE], F32, kind="ExternalInput")
    d_mune = nc.dram_tensor("mune", [P, P], BF16, kind="ExternalInput")
    d_mlne = nc.dram_tensor("mlne", [P, P], BF16, kind="ExternalInput")
    d_muinc = nc.dram_tensor("muinc", [P, P], BF16, kind="ExternalInput")

    d_out = nc.dram_tensor("out", [TOWN, D], F32, kind="ExternalOutput")

    d_x2s = nc.dram_tensor("x2s", [TOWN, D], F32)
    d_h2s = nc.dram_tensor("h2s", [NCH * P, D], BF16)
    d_modscr = nc.dram_tensor("modscr", [1, 6 * D], F32)
    d_thq_scr = nc.dram_tensor("thq_scr", [D, D], BF16)
    d_gw_scr = nc.dram_tensor("gw_scr", [NE, P, DT * D], BF16)
    d_uw_scr = nc.dram_tensor("uw_scr", [NE, P, DT * D], BF16)
    d_dw_scr = nc.dram_tensor("dw_scr", [NE, P, DT * D], BF16)
    d_sg_scr = nc.dram_tensor("sg_scr", [2, P, DT * D], BF16)
    d_su_scr = nc.dram_tensor("su_scr", [2, P, DT * D], BF16)
    d_sd_scr = nc.dram_tensor("sd_scr", [2, P, DT * D], BF16)
    d_hx = nc.dram_tensor("hx", [NSEG, D], BF16)
    d_yx = nc.dram_tensor("yx", [NSEG, D], F32)

    taps = {}

    with ExitStack() as octx:
        pr = Prog(nc, octx)
        E = octx.enter_context

        def tap(name, apv, waits):
            if not debug:
                return
            dt_ = nc.dram_tensor(_tp + name, list(apv.shape), apv.dtype, kind="ExternalOutput")
            taps[name] = True
            pr.emit("sync", lambda sy, a=apv, d2=dt_: sy.dma_start(d2[...], a),
                    waits=waits, inc="do", inc_by=16)

        s_ident = E(nc.sbuf_tensor("s_ident", [P, P], BF16))
        s_mune = E(nc.sbuf_tensor("s_mune", [P, P], BF16))
        s_mlne = E(nc.sbuf_tensor("s_mlne", [P, P], BF16))
        s_muinc = E(nc.sbuf_tensor("s_muinc", [P, P], BF16))
        ones_r1 = E(nc.sbuf_tensor("ones_r1", [1, P], BF16))
        ones_row = E(nc.sbuf_tensor("ones_row", [1, 512], BF16))
        ones_r1f = E(nc.sbuf_tensor("ones_r1f", [1, P], F32))
        ones_cl = E(nc.sbuf_tensor("ones_cl", [P, 1], BF16))
        onef_col = E(nc.sbuf_tensor("onef_col", [P, 1], F32))
        s_gmask = E(nc.sbuf_tensor("s_gmask", [P, 2], F32))
        gb_bc = E(nc.sbuf_tensor("gb_bc", [P, D], F32))
        ga_bc = E(nc.sbuf_tensor("ga_bc", [P, D], F32))
        sl2 = E(nc.sbuf_tensor("sl2", [P, 2 * NCH], U32))
        w2col = E(nc.sbuf_tensor("w2col", [P, 2 * NCH], F32))
        slotab = E(nc.sbuf_tensor("slotab", [P, NE], F32))
        maskf = E(nc.sbuf_tensor("maskf", [P, NE], F32))
        mk0 = E(nc.sbuf_tensor("mk0", [P, NE], F32))
        tmp8 = E(nc.sbuf_tensor("tmp8", [P, NE], F32))
        maskb = E(nc.sbuf_tensor("maskb", [P, NE], BF16))
        slcols = E(nc.sbuf_tensor("slcols", [P, 2], F32))
        absoffs = E(nc.sbuf_tensor("absoffs", [1, NE], F32))
        s_ebase = E(nc.sbuf_tensor("s_ebase", [1, NE], F32))
        s_mut1 = E(nc.sbuf_tensor("s_mut1", [P, P], BF16))
        h2cp = E(nc.sbuf_tensor("h2cp", [P, 2 * D], BF16))
        zhx = E(nc.sbuf_tensor("zhx", [P, D], BF16))
        b_f32 = E(nc.sbuf_tensor("b_f32", [1, D], F32))
        b_bfA = E(nc.sbuf_tensor("b_bfA", [1, D], BF16))
        b_bfB = E(nc.sbuf_tensor("b_bfB", [1, D], BF16))
        xb = E(nc.sbuf_tensor("xb", [P, 2 * D], F32))
        lnt = E(nc.sbuf_tensor("lnt", [P, 16], F32))
        hbf = E(nc.sbuf_tensor("hbf", [P, D], BF16))
        lg = E(nc.sbuf_tensor("lg", [P, NE], F32))
        zex = E(nc.sbuf_tensor("zex", [P, NE], F32))
        sc8 = E(nc.sbuf_tensor("sc8", [P, NE], F32))
        top8 = E(nc.sbuf_tensor("top8", [P, 8], F32))
        rcol = E(nc.sbuf_tensor("rcol", [P, 4], F32))
        h2stg = E(nc.sbuf_tensor("h2stg", [P, D], BF16))
        ca1c = E(nc.sbuf_tensor("ca1c", [P, DT], F32))
        cb1c = E(nc.sbuf_tensor("cb1c", [P, DT], F32))
        sa_f = E(nc.sbuf_tensor("sa_f", [P, DT], F32))
        sb_f = E(nc.sbuf_tensor("sb_f", [P, DT], F32))
        rw_bf = E(nc.sbuf_tensor("rw_bf", [P, DT * NE], BF16))

        bk = [Bank(pr, f"bk{i}") for i in range(7)]
        bkT = Bank(pr, "bkT", BF16)

        cdb = E(nc.sbuf_tensor("cdb", [P, 4], F32))
        pr.emit("gpsimd", lambda g: g.memset(cdb[:, 0:1], 2 * LR / D), inc="gs")
        pr.emit("gpsimd", lambda g: g.memset(cdb[:, 1:2], 1.0), inc="gs")
        pr.emit("gpsimd", lambda g: g.memset(cdb[:, 2:3], 0.0), inc="gs")
        pr.emit("gpsimd", lambda g: g.memset(cdb[:, 3:4], -1.0), inc="gs")
        nc.const_aps.register(F32, 2 * LR / D, cdb[:, 0:1])
        nc.const_aps.register(F32, 1.0, cdb[:, 1:2])
        nc.const_aps.register(F32, 0.0, cdb[:, 2:3])
        nc.const_aps.register(F32, -1.0, cdb[:, 3:4])

        for t_, d_ in ((s_ident, d_ident), (s_mune, d_mune), (s_mlne, d_mlne), (s_muinc, d_muinc)):
            pr.emit("sync", lambda sy, t2=t_, d2=d_: sy.dma_start(t2[:, :], d2[:, :]),
                    inc="dini", inc_by=16)
        pr.emit("sync", lambda sy: sy.dma_start(s_gmask[:, :], d_gmask[:, :]), inc="dini", inc_by=16)
        pr.emit("sync", lambda sy: sy.dma_start(s_mut1[:, :], d_mut1[:, :]), inc="dini", inc_by=16)
        pr.emit("sync", lambda sy: sy.dma_start(s_ebase[:, :], d_ebase[:, :]), inc="dini", inc_by=16)
        pr.emit("sync", lambda sy: sy.dma_start(b_f32[:, :], d_b0[:, :]), inc="dini", inc_by=16)
        pr.emit("gpsimd", lambda g: g.memset(ones_r1[:, :], 1.0), inc="gs")
        pr.emit("gpsimd", lambda g: g.memset(ones_row[:, :], 1.0), inc="gs")
        pr.emit("gpsimd", lambda g: g.memset(ones_cl[:, :], 1.0), inc="gs")
        pr.emit("gpsimd", lambda g: g.memset(ones_r1f[:, :], 1.0), inc="gs")
        pr.emit("gpsimd", lambda g: g.memset(onef_col[:, :], 1.0), inc="gs")
        t_zm = pr.emit("gpsimd", lambda g: g.memset(zhx[:, :], 0.0), inc="gs")
        for j in range(NSEG // P):
            pr.emit("sync", lambda sy, j=j: sy.dma_start(
                ap(d_hx, j * P * D, [[D, P], [1, D]]), zhx[:, :]),
                waits=[t_zm], inc="dzx", inc_by=16)
        t_zhx = pr.tok("dzx")
        t_dl0 = pr.tok("dini")
        t_ms0 = pr.tok("gs")

        t_bb0 = pr.emit("scalar", lambda s: s.copy(b_bfA[0:1, :], b_f32[0:1, :]),
                        waits=[t_dl0, t_ms0], inc="ss")
        pr.emit("scalar", lambda s: s.copy(absoffs[0:1, :], s_ebase[0:1, :]),
                waits=[t_dl0], inc="ss")

        wprep_toks = {}


        with ExitStack() as actx:
            EA = actx.enter_context

            with ExitStack() as wctx:
                wmb = wctx.enter_context(nc.sbuf_tensor("wmb", [P, 2 * 6 * D], F32))
                silc = wctx.enter_context(nc.sbuf_tensor("silc", [P, DT], F32))
                modr = wctx.enter_context(nc.sbuf_tensor("modr", [1, 6 * D], F32))
                tmpr = wctx.enter_context(nc.sbuf_tensor("tmpr", [1, 3 * D], F32))

                t_cv = pr.emit("sync", lambda sy: sy.dma_start(
                    silc[:, :], ap(d_cvec, 0, [[1, P], [P, DT]]),
                    allow_slow_non_contiguous=True), inc="dcv", inc_by=16)
                t_sil = pr.emit("scalar", lambda s: s.activation(silc[:, :], silc[:, :], AF.Silu),
                                waits=[t_cv], inc="ss")
                prev_mm = [None] * DT
                for pas in range(2):
                    mm_toks = []
                    for dt_i in range(DT):
                        slot = dt_i % 2
                        wl = [mm_toks[dt_i - 2]] if dt_i >= 2 else [prev_mm[DT - 2 + dt_i]]
                        tkd = pr.emit("sync", lambda sy, dt_i=dt_i, slot=slot: sy.dma_start(
                            ap(wmb, slot * 6 * D, [[2 * 6 * D, P], [1, 6 * D]]),
                            ap(d_wmod, dt_i * P * 6 * D, [[6 * D, P], [1, 6 * D]])),
                            waits=wl, inc=f"dwm{slot}", inc_by=16)
                        tk = None
                        for jj in range(6):
                            jb = pas * 6 + jj
                            b2 = bk[jj]
                            o = b2.t[0:1, 0:512]
                            l = silc[:, dt_i:dt_i + 1]
                            r = wmb[:, slot * 6 * D + jb * 512:slot * 6 * D + (jb + 1) * 512]
                            wl2 = [tkd]
                            if dt_i == 0:
                                wl2 += [t_sil] + b2.readers
                            tk = pr.emit("tensor",
                                         (lambda t, o=o, l=l, r=r, st=(dt_i == 0), sp=(dt_i == DT - 1):
                                          t.matmul(o, l, r, start=st, stop=sp)),
                                         waits=wl2, inc="ts" if jj == 5 else None)
                        mm_toks.append(tk)
                        if dt_i == DT - 1:
                            for jj in range(6):
                                bk[jj].w = tk
                                bk[jj].readers = []
                    for jj in range(6):
                        jb = pas * 6 + jj
                        bk[jj].read("vector", lambda v, jj=jj, jb=jb: v.tensor_copy(
                            modr[0:1, jb * 512:(jb + 1) * 512], bk[jj].t[0:1, 0:512]))
                    prev_mm = mm_toks
                t_mod = pr.tok("vs")

                t_gdone = t_mod
                for idx in (2, 5):
                    g_sl = modr[0:1, idx * D:(idx + 1) * D]
                    a_sl = tmpr[0:1, 0:D]
                    bb_sl = tmpr[0:1, D:2 * D]
                    s_sl = tmpr[0:1, 2 * D:3 * D]
                    pr.emit("vector", lambda v, g=g_sl, a=a_sl: v.tensor_scalar_max(a, g, 0.25),
                            waits=[t_mod, t_gdone], inc="vs")
                    pr.emit("vector", lambda v, g=g_sl, b3=bb_sl: v.tensor_scalar_min(b3, g, -0.25), inc="vs")
                    tc0 = pr.emit("vector", lambda v, g=g_sl, s2=s_sl: v.tensor_single_scalar(s2, g, 0.0, op=ALU.is_ge), inc="vs")
                    tc1 = pr.emit("vector", lambda v, a=a_sl, b3=bb_sl: v.tensor_sub(a, a, b3),
                                  waits=[tc0], inc="vs")
                    tc2 = pr.emit("vector", lambda v, a=a_sl, s2=s_sl: v.tensor_mul(a, a, s2),
                                  waits=[tc1], inc="vs")
                    t_gdone = pr.emit("vector", lambda v, g=g_sl, a=a_sl, b3=bb_sl: v.tensor_add(g, b3, a),
                                      waits=[tc2], inc="vs")
                t_clamp = pr.tok("vs")

                for row_i, dst in ((2, ga_bc), (5, gb_bc)):
                    for jh in range(2):
                        b2 = bk[6]
                        b2.mm_group(
                            [(b2.t[:, 0:512], ones_r1f[:, :],
                              modr[0:1, row_i * D + jh * 512:row_i * D + (jh + 1) * 512])],
                            extra_waits=[t_clamp, t_ms0])
                        b2.read("vector", lambda v, dst=dst, jh=jh, b2=b2: v.tensor_copy(
                            dst[:, jh * 512:(jh + 1) * 512], b2.t[:, 0:512]))
                t_ms = pr.emit("sync", lambda sy: sy.dma_start(d_modscr[:, :], modr[:, :]),
                               waits=[t_clamp], inc="dms", inc_by=16)
                t_l1 = pr.emit("sync", lambda sy: sy.dma_start(
                    ca1c[:, :], ap(d_modscr, 1 * D, [[1, P], [P, DT]]),
                    allow_slow_non_contiguous=True), waits=[t_ms], inc="dca", inc_by=16)
                t_l2 = pr.emit("sync", lambda sy: sy.dma_start(
                    cb1c[:, :], ap(d_modscr, 4 * D, [[1, P], [P, DT]]),
                    allow_slow_non_contiguous=True), waits=[t_ms], inc="dcb", inc_by=16)
                t_l3 = pr.emit("gpsimd", lambda g: g.dma_start(
                    sa_f[:, :], ap(d_modscr, 0 * D, [[1, P], [P, DT]]),
                    allow_slow_non_contiguous=True), waits=[t_ms], inc="dsa", inc_by=16)
                t_l4 = pr.emit("gpsimd", lambda g: g.dma_start(
                    sb_f[:, :], ap(d_modscr, 3 * D, [[1, P], [P, DT]]),
                    allow_slow_non_contiguous=True), waits=[t_ms], inc="dsb2", inc_by=16)
                t_c1 = pr.emit("vector", lambda v: v.tensor_scalar_add(ca1c[:, :], ca1c[:, :], 1.0),
                               waits=[t_l1], inc="vs")
                t_c2 = pr.emit("vector", lambda v: v.tensor_scalar_add(cb1c[:, :], cb1c[:, :], 1.0),
                               waits=[t_l2], inc="vs")
                t_wmod_ts = pr.tok("ts")
                t_wmod_vs = pr.tok("vs")

            thK = EA(nc.sbuf_tensor("s_thK", [P, DT * D], BF16))
            thVn = EA(nc.sbuf_tensor("s_thVn", [P, DT * D], BF16))
            thQp = EA(nc.sbuf_tensor("s_thQp", [P, DT * D], BF16))
            WT = EA(nc.sbuf_tensor("WT", [P, DT * D], F32))
            Wbf = EA(nc.sbuf_tensor("Wbf", [P, DT * D], BF16))
            hft = EA(nc.sbuf_tensor("hft", [P, DT * 512], BF16))
            Kft = EA(nc.sbuf_tensor("Kft", [P, DT * 512], BF16))
            Ktk = EA(nc.sbuf_tensor("Ktk", [P, SUP * D], BF16))
            Qft = EA(nc.sbuf_tensor("Qft", [P, DT * 512], BF16))
            Gsb = EA(nc.sbuf_tensor("Gsb", [P, SUP * D], BF16))
            Scr = EA(nc.sbuf_tensor("Scr", [P, 6 * P], BF16))
            Sqr = EA(nc.sbuf_tensor("Sqr", [P, 10 * P], BF16))
            Usb = EA(nc.sbuf_tensor("Usb", [P, 8 * P], BF16))
            usb_rd = {j: None for j in range(8)}
            Bp = EA(nc.sbuf_tensor("Bp", [P, 2 * SUP * P], BF16))
            Btp = EA(nc.sbuf_tensor("Btp", [P, 2 * SUP * P], BF16))
            Dsb = EA(nc.sbuf_tensor("Dsb", [P, SUP * P], BF16))
            Xsb = EA(nc.sbuf_tensor("Xsb", [P, SUP * P], BF16))
            RHSs = EA(nc.sbuf_tensor("RHSs", [P, D], BF16))
            x2b = EA(nc.sbuf_tensor("x2b", [P, 2 * D], F32))
            xeb = EA(nc.sbuf_tensor("xeb", [P, 2 * D], F32))

            for dt_i in range(DT):
                pr.emit("gpsimd", lambda g, dt_i=dt_i: g.dma_start(
                    ap(thK, dt_i * D, [[DT * D, P], [1, D]]),
                    ap(d_thK, dt_i * P * D, [[D, P], [1, D]])),
                    waits=[t_wmod_ts, t_wmod_vs], inc="dthk", inc_by=16)
            t_thk_all = pr.tok("dthk")
            thk_toks = [t_thk_all] * DT
            for dt_i in range(DT):
                pr.emit("gpsimd", lambda g, dt_i=dt_i: g.dma_start(
                    ap(thVn, dt_i * D, [[DT * D, P], [1, D]]),
                    ap(d_thV, dt_i * P * D, [[D, P], [1, D]])),
                    waits=[t_wmod_ts, t_wmod_vs], inc="dthv", inc_by=16)
            t_thv_all = pr.tok("dthv")
            thv_toks = [t_thv_all] * DT
            for dt_i in range(DT):
                pr.emit("gpsimd", lambda g, dt_i=dt_i: g.dma_start(
                    ap(thQp, dt_i * D, [[DT * D, P], [1, D]]),
                    ap(d_thQ, dt_i * P * D, [[D, P], [1, D]])),
                    waits=[t_wmod_ts, t_wmod_vs], inc="dthq", inc_by=16)
            t_thq_all = pr.tok("dthq")
            thq_toks = [t_thq_all] * DT

            for dt_i in range(DT):
                pr.emit("scalar", lambda s, dt_i=dt_i: s.activation(
                    thVn[:, dt_i * D:(dt_i + 1) * D], thVn[:, dt_i * D:(dt_i + 1) * D],
                    AF.Identity, scale=-1.0),
                    waits=[thv_toks[dt_i]], inc="ss")
            t_thprime = pr.tok("ss")
            t_qprime = t_thq_all

            t_rwl = pr.emit("gpsimd", lambda g: g.dma_start(
                rw_bf[:, :], ap(d_rw, 0, [[NE, P], [P * NE, DT], [1, NE]])),
                waits=[t_wmod_ts, t_wmod_vs], inc="drw", inc_by=16)


            def wprep_family(name, pairs):
                for s_ap, d_ap in pairs:
                    pr.emit("gpsimd", lambda g, s2=s_ap, d2=d_ap: g.dma_start(d2, s2),
                            inc="dp", inc_by=16)
                wprep_toks[name] = pr.tok("dp")

            def gu_pairs(src, dst):
                out = []
                for e2 in range(NE):
                    for ht in range(DT):
                        out.append((ap(src, e2 * D * D + ht * P * D, [[D, P], [1, D]]),
                                    ap(dst, e2 * P * DT * D + ht * D, [[DT * D, P], [1, D]])))
                return out
            wprep_family("gw", gu_pairs(d_gw, d_gw_scr))
            wprep_family("uw", gu_pairs(d_uw, d_uw_scr))
            wprep_family("dw", gu_pairs(d_dw, d_dw_scr))

            def sh_pairs(src, dst):
                out = []
                for fh in range(2):
                    for ht in range(DT):
                        out.append((ap(src, ht * P * 2 * D + fh * D, [[2 * D, P], [1, D]]),
                                    ap(dst, fh * P * DT * D + ht * D, [[DT * D, P], [1, D]])))
                return out
            wprep_family("sgw", sh_pairs(d_sgw, d_sg_scr))
            wprep_family("suw", sh_pairs(d_suw, d_su_scr))
            sd_pairs = []
            for fh in range(2):
                for fj in range(DT):
                    sd_pairs.append((ap(d_sdw, (fh * D + fj * P) * D, [[D, P], [1, D]]),
                                     ap(d_sd_scr, fh * P * DT * D + fj * D, [[DT * D, P], [1, D]])))
            wprep_family("sdw", sd_pairs)
            dp_all = pr.tok("dp")
            for k2 in ("gw", "uw", "dw", "sgw", "suw", "sdw"):
                wprep_toks[k2] = dp_all

            for dt_i in range(DT):
                pr.emit("sync", lambda sy, dt_i=dt_i: sy.dma_start(
                    ap(WT, dt_i * D, [[DT * D, P], [1, D]]),
                    ap(d_w0T, dt_i * P * D, [[D, P], [1, D]])),
                    waits=[t_wmod_ts, t_wmod_vs], inc="dw0t", inc_by=16)
            t_w0_all = pr.tok("dw0t")
            for dt_i in range(DT):
                pr.emit("scalar", lambda s, dt_i=dt_i: s.copy(
                    Wbf[:, dt_i * D:(dt_i + 1) * D], WT[:, dt_i * D:(dt_i + 1) * D]),
                    waits=[t_w0_all], inc="ss")
            t_w0 = pr.tok("ss")

            t_prologue = [pr.tok("ts"), pr.tok("vs"), pr.tok("ss"), pr.tok("gs"),
                          t_w0, t_bb0, t_thprime, t_qprime,
                          t_thk_all, t_thq_all, t_rwl, t_l3, t_l4]

            ln_done = {}
            x_dma = {}
            x2_spill_tok = None
            h2_guard = [None, None]
            h2cp_guard = [None, None]
            t_offprev = [None]
            last_cp = [None]
            last_hn = [None]
            last_tr = [None]

            def x_src(s_abs, ci):
                d_src = d_xp if s_abs < NSUP else d_xm
                c16 = (s_abs % NSUP) * SUP + ci
                return ap(d_src, c16 * P * D, [[D, P], [1, D]])

            def ensure_xdma(cl):
                if cl >= 2 * NSUP * SUP or cl in x_dma:
                    return
                slot = cl % 2
                wl = list(t_prologue) if cl == 0 else []
                if cl - 2 >= 0:
                    wl += list(ln_done[cl - 2])
                x_dma[cl] = pr.emit("sync", lambda sy, cl=cl, slot=slot: sy.dma_start(
                    ap(xb, slot * D, [[2 * D, P], [1, D]]), x_src(cl // SUP, cl % SUP)),
                    waits=wl, inc=f"dx{slot}", inc_by=16)

            def pair_id_s(c, cp):
                return c * (c - 1) // 2 + cp

            def pair_id_q(c, cp):
                return c * (c + 1) // 2 + cp

            def emit_ln(x_ap, wait0, hazard_guard):
                pr.emit("vector", lambda v, x3=x_ap: v.bn_stats(lnt[:, 0:6], x3[:, 0:512]),
                        waits=wait0 + [last_hn[0]], inc="vs")
                t_b2 = pr.emit("vector", lambda v, x3=x_ap: v.bn_stats(lnt[:, 6:12], x3[:, 512:1024]), inc="vs")
                t_ag = pr.emit("vector", lambda v: v.bn_aggr(lnt[:, 12:14], lnt[:, 0:12]),
                               waits=[t_b2], inc="vs")
                t_v1 = pr.emit("vector", lambda v: v.tensor_scalar_add(lnt[:, 14:15], lnt[:, 13:14], EPS),
                               waits=[t_ag], inc="vs")
                t_s1 = pr.emit("scalar", lambda s: s.activation(lnt[:, 14:15], lnt[:, 14:15], AF.Sqrt),
                               waits=[t_v1], inc="ss")
                t_rc = pr.emit("vector", lambda v: v.reciprocal(lnt[:, 15:16], lnt[:, 14:15]),
                               waits=[t_s1], inc="vs")
                t_v3 = pr.emit("vector", lambda v: v.scalar_tensor_tensor(
                    lnt[:, 14:15], lnt[:, 12:13], -1.0, lnt[:, 15:16], op0=ALU.mult, op1=ALU.mult),
                    waits=[t_rc], inc="vs")
                t_hn = pr.emit("scalar", lambda s, x3=x_ap: s.activation(
                    hbf[:, :], x3, AF.Identity, bias=lnt[:, 14:15], scale=lnt[:, 15:16]),
                    waits=[t_v3] + hazard_guard, inc="ss")
                last_hn[0] = t_hn
                return t_v3, t_hn

            def emit_transposes(t_hn, dst_fns, extra=()):
                tk = None
                for i in range(DT):
                    tk = pr.emit("tensor",
                                 (lambda t, i=i: t.transpose(
                                     bkT.t[:, i * P:(i + 1) * P], hbf[:, i * P:(i + 1) * P],
                                     s_ident[:, :])),
                                 waits=([t_hn] + bkT.readers) if i == 0 else (),
                                 inc="ts" if i == DT - 1 else None)
                bkT.w = tk
                bkT.readers = []
                last_tr[0] = tk
                tk = None
                for fi, fn in enumerate(dst_fns):
                    tk = bkT.read("scalar", fn, extra_waits=list(extra) if fi == 0 else ())
                return tk

            def emit_state_update(t_chain, t_outs, npar_buf):
                for dt_i in range(DT):
                    for jh in range(2):
                        b2 = bk[(dt_i * 2 + jh) % 2]
                        mms = []
                        for c in range(SUP):
                            mms.append((b2.t[:, 0:512],
                                        Ktk[:, c * D + dt_i * P:c * D + (dt_i + 1) * P],
                                        Gsb[:, c * D + jh * 512:c * D + (jh + 1) * 512]))
                        b2.mm_group(mms, extra_waits=[t_chain])
                        b2.read("vector", lambda v, b2=b2, dt_i=dt_i, jh=jh: v.tensor_add(
                            WT[:, dt_i * D + jh * 512:dt_i * D + (jh + 1) * 512],
                            WT[:, dt_i * D + jh * 512:dt_i * D + (jh + 1) * 512],
                            b2.t[:, 0:512]))
                t_wupd = pr.tok("vs")
                for dt_i in range(DT):
                    pr.emit("scalar", lambda s, dt_i=dt_i: s.copy(
                        Wbf[:, dt_i * D:(dt_i + 1) * D], WT[:, dt_i * D:(dt_i + 1) * D]),
                        waits=[t_wupd, t_outs], inc="ss")
                for jh in range(2):
                    b2 = bk[2 + jh]
                    mms = []
                    for c in range(SUP):
                        mms.append((b2.t[0:1, 0:512], ones_cl[:, :],
                                    Gsb[:, c * D + jh * 512:c * D + (jh + 1) * 512]))
                    b2.mm_group(mms, extra_waits=[t_chain])
                    b2.read("vector", lambda v, b2=b2, jh=jh: v.tensor_add(
                        b_f32[0:1, jh * 512:(jh + 1) * 512],
                        b_f32[0:1, jh * 512:(jh + 1) * 512], b2.t[0:1, 0:512]))
                t_bupd = pr.tok("vs")
                pr.emit("scalar", lambda s, nb=npar_buf: s.copy(nb[0:1, :], b_f32[0:1, :]),
                        waits=[t_bupd, t_outs], inc="ss")

            pending_state = None

            for s_abs in range(n_super):
                is_m = s_abs >= NSUP
                par_buf = b_bfA if s_abs % 2 == 0 else b_bfB
                npar_buf = b_bfB if s_abs % 2 == 0 else b_bfA
                gcol = 1 if is_m else 0
                t_guard_a = pr.tok("ts")
                if pending_state is not None:
                    emit_state_update(*pending_state)
                    pending_state = None
                t_super_ts0 = pr.tok("ts")

                for ci in range(SUP):
                    cl = s_abs * SUP + ci
                    ensure_xdma(cl)
                    ensure_xdma(cl + 1)
                    slot = cl % 2
                    xsl = xb[:, slot * D:(slot + 1) * D]
                    t_v3, t_hn = emit_ln(xsl, [x_dma[cl]], [last_tr[0], last_cp[0]])
                    ln_done[cl] = (t_v3, t_hn)
                    ensure_xdma(cl + 2)
                    emit_transposes(t_hn, [
                        (lambda s, ci=ci, dt=dt: s.activation(
                            hft[:, dt * 512 + ci * P:dt * 512 + ci * P + P],
                            bkT.t[:, dt * P:(dt + 1) * P], AF.Identity,
                            scale=ca1c[:, dt:dt + 1], bias=sa_f[:, dt:dt + 1]))
                        for dt in range(DT)],
                        extra=[t_guard_a])
                t_hft = pr.tok("ss")

                for et in range(DT):
                    b2 = bk[2 + et % 2]
                    mms = []
                    for dt_i in range(DT):
                        mms.append((b2.t[:, 0:512],
                                    thK[:, dt_i * D + et * P:dt_i * D + (et + 1) * P],
                                    hft[:, dt_i * 512:dt_i * 512 + 512]))
                    b2.mm_group(mms, extra_waits=[t_hft])
                    b2.read("scalar", lambda s, et=et, b2=b2: s.copy(
                        Kft[:, et * 512:(et + 1) * 512], b2.t[:, 0:512]),
                        extra_waits=[t_super_ts0])
                if is_m:
                    for et in range(DT):
                        b2 = bk[4 + et % 2]
                        mms = []
                        for dt_i in range(DT):
                            mms.append((b2.t[:, 0:512],
                                        thQp[:, dt_i * D + et * P:dt_i * D + (et + 1) * P],
                                        hft[:, dt_i * 512:dt_i * 512 + 512]))
                        b2.mm_group(mms, extra_waits=[t_hft])
                        b2.read("scalar", lambda s, et=et, b2=b2: s.copy(
                            Qft[:, et * 512:(et + 1) * 512], b2.t[:, 0:512]),
                            extra_waits=[t_super_ts0])
                t_proj = [pr.tok("ss"), pr.tok("vs")]

                for cp in range(SUP):
                    b2 = bk[cp % 2]
                    mms = []
                    for et in range(DT):
                        mms.append((b2.t[:, 0:512],
                                    Kft[:, et * 512 + cp * P:et * 512 + (cp + 1) * P],
                                    Kft[:, et * 512:et * 512 + 512]))
                    b2.mm_group(mms, extra_waits=t_proj)
                    usl = Usb[:, cp * P:(cp + 1) * P]
                    t_u = b2.read("scalar", lambda s, b2=b2, cp=cp, usl=usl: s.activation(
                        usl, b2.t[:, cp * P:(cp + 1) * P], AF.Identity,
                        scale=2 * LR / D, bias=2 * LR / D),
                        extra_waits=[usb_rd[cp]])
                    pr.emit("vector", lambda v, cp=cp, usl=usl: v.tensor_mul(
                        Bp[:, cp * P:(cp + 1) * P], usl, s_mune[:, :]),
                        waits=[t_u, t_super_ts0], inc="vs")
                    usb_rd[cp] = pr.emit("vector", lambda v, cp=cp, usl=usl: v.tensor_mul(
                        Btp[:, cp * P:(cp + 1) * P], usl, s_mlne[:, :]),
                        waits=[t_u], inc="vs")
                    for c in range(cp + 1, SUP):
                        pid = pair_id_s(c, cp)
                        b2.read("scalar", lambda s, b2=b2, pid=pid, c=c: s.activation(
                            Scr[:, pid * P:(pid + 1) * P], b2.t[:, c * P:(c + 1) * P],
                            AF.Identity, scale=1.0, bias=1.0),
                            extra_waits=[t_super_ts0])
                if is_m:
                    for cp in range(SUP):
                        b2 = bk[2 + cp % 2]
                        mms = []
                        for et in range(DT):
                            mms.append((b2.t[:, 0:512],
                                        Kft[:, et * 512 + cp * P:et * 512 + (cp + 1) * P],
                                        Qft[:, et * 512:et * 512 + 512]))
                        b2.mm_group(mms, extra_waits=t_proj)
                        pid = pair_id_q(cp, cp)
                        usl = Usb[:, (4 + cp) * P:(5 + cp) * P]
                        t_u = b2.read("scalar", lambda s, b2=b2, cp=cp, usl=usl: s.activation(
                            usl, b2.t[:, cp * P:(cp + 1) * P], AF.Identity, scale=1.0, bias=1.0),
                            extra_waits=[usb_rd[4 + cp]])
                        usb_rd[4 + cp] = pr.emit("vector", lambda v, pid=pid, usl=usl: v.tensor_mul(
                            Sqr[:, pid * P:(pid + 1) * P], usl, s_muinc[:, :]),
                            waits=[t_u, t_super_ts0], inc="vs")
                        for c in range(cp + 1, SUP):
                            pid = pair_id_q(c, cp)
                            b2.read("scalar", lambda s, b2=b2, pid=pid, c=c: s.activation(
                                Sqr[:, pid * P:(pid + 1) * P], b2.t[:, c * P:(c + 1) * P],
                                AF.Identity, scale=1.0, bias=1.0),
                                extra_waits=[t_super_ts0])
                t_pairs = [pr.tok("ss"), pr.tok("vs")]

                for c in range(SUP):
                    tk = None
                    for et in range(DT):
                        tk = pr.emit("tensor",
                                     (lambda t, c=c, et=et: t.transpose(
                                         bkT.t[:, et * P:(et + 1) * P],
                                         Kft[:, et * 512 + c * P:et * 512 + c * P + P],
                                         s_ident[:, :])),
                                     waits=(t_proj + bkT.readers) if et == 0 else (),
                                     inc="ts" if et == DT - 1 else None)
                    bkT.w = tk
                    bkT.readers = []
                    bkT.read("scalar", lambda s, c=c: s.copy(
                        Ktk[:, c * D:(c + 1) * D], bkT.t[:, :]))

                x_tok = {}
                for c in range(SUP):
                    x_tok[c] = pr.emit("vector", lambda v, c=c: v.tensor_add(
                        Xsb[:, c * P:(c + 1) * P], s_ident[:, :], Bp[:, c * P:(c + 1) * P]),
                        waits=t_pairs + [t_super_ts0], inc="vs")
                ds_free = {c: None for c in range(SUP)}
                for m in range(1, 7):
                    pslot = (m - 1) % 2
                    cslot = m % 2
                    last = (m == 6)
                    bt_tok = {}
                    if not last:
                        for c in range(SUP):
                            bsl = Bp[:, (pslot * SUP + c) * P:(pslot * SUP + c + 1) * P]
                            btsl = Btp[:, (pslot * SUP + c) * P:(pslot * SUP + c + 1) * P]
                            b2 = bk[c % 2]
                            b2.mm_group([(b2.t[:, 0:P], btsl, bsl)],
                                        extra_waits=[x_tok[c]] if m == 1 else [])
                            b2.read("scalar", lambda s, b2=b2, c=c, cslot=cslot: s.copy(
                                Bp[:, (cslot * SUP + c) * P:(cslot * SUP + c + 1) * P], b2.t[:, 0:P]))
                    for c in range(SUP):
                        bsl = Bp[:, (pslot * SUP + c) * P:(pslot * SUP + c + 1) * P]
                        btsl = Btp[:, (pslot * SUP + c) * P:(pslot * SUP + c + 1) * P]
                        b3 = bk[2 + c % 2]
                        b3.mm_group([(b3.t[:, 0:P], bsl, btsl)],
                                    extra_waits=[x_tok[c]] if m == 1 else [])
                        bt_tok[c] = b3.read("scalar", lambda s, b3=b3, c=c, cslot=cslot: s.copy(
                            Btp[:, (cslot * SUP + c) * P:(cslot * SUP + c + 1) * P], b3.t[:, 0:P]))
                    for c in range(SUP):
                        b4 = bk[4 + c % 2]
                        b4.mm_group([(b4.t[:, 0:P],
                                      Btp[:, (cslot * SUP + c) * P:(cslot * SUP + c + 1) * P],
                                      Xsb[:, c * P:(c + 1) * P])],
                                    extra_waits=[bt_tok[c], x_tok[c]])
                        t_d = b4.read("scalar", lambda s, b4=b4, c=c: s.copy(
                            Dsb[:, c * P:(c + 1) * P], b4.t[:, 0:P]),
                            extra_waits=[ds_free[c]])
                        x_tok[c] = pr.emit("vector", lambda v, c=c: v.tensor_add(
                            Xsb[:, c * P:(c + 1) * P], Xsb[:, c * P:(c + 1) * P],
                            Dsb[:, c * P:(c + 1) * P]),
                            waits=[t_d], inc="vs")
                        ds_free[c] = x_tok[c]
                t_newton = pr.tok("vs")

                g_tok = {}
                ep_pend = None

                def emit_epilogue(c, t_xe, t_oap=()):
                    nonlocal x2_spill_tok, h2_guard
                    c16 = (s_abs % NSUP) * SUP + c
                    sl = (c % 2) * D
                    x2sl = x2b[:, sl:sl + D]
                    t_x2m = pr.emit("vector", lambda v, sl=sl: v.tensor_add(
                        x2b[:, sl:sl + D], x2b[:, sl:sl + D], xeb[:, sl:sl + D]),
                        waits=[t_xe] + list(t_oap), inc="vs")
                    x2_spill_tok = pr.emit("sync", lambda sy, c16=c16, x2sl=x2sl: sy.dma_start(
                        ap(d_x2s, c16 * P * D, [[D, P], [1, D]]), x2sl),
                        waits=[t_x2m], inc="dsp", inc_by=16)
                    t_v3, t_hn = emit_ln(x2sl, [t_x2m], [last_tr[0], last_cp[0]])
                    t_h2 = emit_transposes(t_hn, [
                        (lambda s, dt=dt: s.activation(
                            h2stg[:, dt * P:(dt + 1) * P],
                            bkT.t[:, dt * P:(dt + 1) * P], AF.Identity,
                            scale=cb1c[:, dt:dt + 1], bias=sb_f[:, dt:dt + 1]))
                        for dt in range(DT)],
                        extra=[h2_guard[0], h2_guard[1]])
                    h2_spill = pr.emit("sync", lambda sy, c16=c16: sy.dma_start(
                        ap(d_h2s, c16 * P * D, [[D, P], [1, D]]), h2stg[:, :]),
                        waits=[t_h2], inc="dh2s", inc_by=16)
                    b6 = bk[4]
                    mms = []
                    for dt_i in range(DT):
                        mms.append((b6.t[:, 0:NE],
                                    h2stg[:, dt_i * P:(dt_i + 1) * P],
                                    rw_bf[:, dt_i * NE:(dt_i + 1) * NE]))
                    b6.mm_group(mms, extra_waits=[t_h2])
                    h2_guard = [h2_spill, pr.tok("ts")]
                    t_lg = b6.read("vector", lambda v, b6=b6: v.tensor_copy(lg[:, :], b6.t[:, 0:NE]))
                    t_m1 = pr.emit("vector", lambda v: v.reduce_max(rcol[:, 0:1], lg[:, :], axis=AX.X),
                                   waits=[t_lg], inc="vs")
                    t_m2 = pr.emit("vector", lambda v: v.tensor_scalar_mul(rcol[:, 1:2], rcol[:, 0:1], -1.0),
                                   waits=[t_m1], inc="vs")
                    t_e1 = pr.emit("scalar", lambda s: s.activation(
                        zex[:, :], lg[:, :], AF.Exp, bias=rcol[:, 1:2]),
                        waits=[t_m2], inc="ss")
                    t_r1 = pr.emit("vector", lambda v: v.reduce_sum(rcol[:, 2:3], zex[:, :], axis=AX.X),
                                   waits=[t_e1], inc="vs")
                    t_r2 = pr.emit("vector", lambda v: v.reciprocal(rcol[:, 3:4], rcol[:, 2:3]),
                                   waits=[t_r1], inc="vs")
                    t_sc = pr.emit("scalar", lambda s: s.activation(
                        sc8[:, :], zex[:, :], AF.Identity, scale=rcol[:, 3:4]),
                        waits=[t_r2], inc="ss")
                    t_t8 = pr.emit("vector", lambda v: v.max(out=top8[:, :], in_=sc8[:, :]),
                                   waits=[t_sc], inc="vs")
                    pr.emit("vector", lambda v, c16=c16: v.tensor_copy(
                        w2col[:, 2 * c16:2 * c16 + 2], top8[:, 0:2]),
                        waits=[t_t8], inc="vs")
                    t_m0 = pr.emit("vector", lambda v: v.scalar_tensor_tensor(
                        mk0[:, :], sc8[:, :], top8[:, 0:1], sc8[:, :],
                        op0=ALU.is_ge, op1=ALU.bypass), inc="vs")
                    t_mk = pr.emit("vector", lambda v: v.scalar_tensor_tensor(
                        maskf[:, :], sc8[:, :], top8[:, 1:2], sc8[:, :],
                        op0=ALU.is_ge, op1=ALU.bypass), inc="vs")
                    t_mb = pr.emit("vector", lambda v: v.tensor_copy(maskb[:, :], maskf[:, :]),
                                   waits=[t_mk], inc="vs")
                    b6.mm_group([(b6.t[:, 0:NE], s_mut1[:, :], maskb[:, :]),
                                 (b6.t[:, 0:NE], ones_r1f[:, :], absoffs[0:1, :])],
                                extra_waits=[t_mb, t_offprev[0]])
                    b6.mm_group([(b6.t[0:1, NE:2 * NE], ones_cl[:, :], maskb[:, :])])
                    t_sl = b6.read("vector", lambda v, b6=b6: v.tensor_copy(
                        slotab[:, :], b6.t[:, 0:NE]))
                    t_offprev[0] = b6.read("vector", lambda v, b6=b6: v.tensor_add(
                        absoffs[0:1, :], absoffs[0:1, :], b6.t[0:1, NE:2 * NE]))
                    t_cl = pr.emit("vector", lambda v: v.tensor_scalar_min(
                        slotab[:, :], slotab[:, :], float(NSEG - 1)), waits=[t_sl], inc="vs")
                    t_s1 = pr.emit("vector", lambda v: v.tensor_sub(
                        tmp8[:, :], maskf[:, :], mk0[:, :]), waits=[t_mk, t_m0], inc="vs")
                    t_p0 = pr.emit("vector", lambda v: v.tensor_mul(
                        mk0[:, :], mk0[:, :], slotab[:, :]), waits=[t_cl], inc="vs")
                    t_p1 = pr.emit("vector", lambda v: v.tensor_mul(
                        tmp8[:, :], tmp8[:, :], slotab[:, :]), waits=[t_s1, t_cl], inc="vs")
                    t_r0s = pr.emit("vector", lambda v: v.reduce_sum(
                        slcols[:, 0:1], mk0[:, :], axis=AX.X), waits=[t_p0], inc="vs")
                    t_r1s = pr.emit("vector", lambda v: v.reduce_sum(
                        slcols[:, 1:2], tmp8[:, :], axis=AX.X), waits=[t_p1], inc="vs")
                    t_cast = pr.emit("vector", lambda v, c16=c16: v.tensor_copy(
                        sl2[:, 2 * c16:2 * c16 + 2], slcols[:, 0:2]),
                        waits=[t_r0s, t_r1s], inc="vs")
                    csl = c16 % 2
                    t_cp = pr.emit("scalar", lambda s, csl=csl: s.copy(
                        h2cp[:, csl * D:(csl + 1) * D], hbf[:, :]),
                        waits=[t_hn, h2cp_guard[csl]], inc="ss")
                    last_cp[0] = t_cp
                    for k2 in range(2):
                        h2cp_guard[csl] = pr.emit("gpsimd", lambda g, c16=c16, k2=k2, csl=csl:
                            g.indirect_dma_start(
                                ap(d_hx, 0, [[D, NSEG], [1, D]]),
                                bass.IndirectOffsetOnAxis(
                                    sl2[:, 2 * c16 + k2:2 * c16 + k2 + 1], 0),
                                h2cp[:, csl * D:(csl + 1) * D], None),
                            waits=[t_cp, t_cast, t_zhx], inc=f"dscx{csl}", inc_by=16)

                for c in range(SUP):
                    bl, bh = (bk[0], bk[1]) if c % 2 == 0 else (bk[2], bk[3])
                    if is_m:
                        t_xe = pr.emit("sync", lambda sy, s_abs=s_abs, c=c: sy.dma_start(
                            xeb[:, (c % 2) * D:(c % 2 + 1) * D], x_src(s_abs, c)),
                            waits=[pr.tok("vs")], inc=f"dxe{c % 2}", inc_by=16)
                    for jh, b2 in ((0, bl), (1, bh)):
                        mms = []
                        for et in range(DT):
                            mms.append((b2.t[:, 0:512],
                                        Kft[:, et * 512 + c * P:et * 512 + (c + 1) * P],
                                        Wbf[:, et * D + jh * 512:et * D + (jh + 1) * 512]))
                        for dt_i in range(DT):
                            mms.append((b2.t[:, 0:512],
                                        hft[:, dt_i * 512 + c * P:dt_i * 512 + (c + 1) * P],
                                        thVn[:, dt_i * D + jh * 512:dt_i * D + (jh + 1) * 512]))
                        mms.append((b2.t[:, 0:512], ones_r1[:, :],
                                    par_buf[0:1, jh * 512:(jh + 1) * 512]))
                        b2.mm_group(mms, extra_waits=[t_newton], open_end=(c > 0))
                    if is_m:
                        for jh, b2 in ((0, bk[5]), (1, bk[6])):
                            mms = []
                            for et in range(DT):
                                mms.append((b2.t[:, 0:512],
                                            Qft[:, et * 512 + c * P:et * 512 + (c + 1) * P],
                                            Wbf[:, et * D + jh * 512:et * D + (jh + 1) * 512]))
                            mms.append((b2.t[:, 0:512], ones_r1[:, :],
                                        par_buf[0:1, jh * 512:(jh + 1) * 512]))
                            b2.mm_group(mms, extra_waits=[t_newton], open_end=True)
                    for jh, b2 in ((0, bl), (1, bh)):
                        mms = []
                        for cp in range(c):
                            pid = pair_id_s(c, cp)
                            mms.append((b2.t[:, 0:512], Scr[:, pid * P:(pid + 1) * P],
                                        Gsb[:, cp * D + jh * 512:cp * D + (jh + 1) * 512]))
                        if mms:
                            b2.mm_group(mms, extra_waits=[g_tok[c - 1]], cont=True)
                        b2.read("scalar", lambda s, b2=b2, jh=jh: s.copy(
                            RHSs[0:P, jh * 512:(jh + 1) * 512], b2.t[:, 0:512]))
                    t_rhs = pr.tok("ss")
                    if is_m and ep_pend is not None:
                        emit_epilogue(*ep_pend)
                        ep_pend = None
                    bg = bk[4]
                    for jh in range(2):
                        bg.mm_group([(bg.t[:, 0:512], Xsb[:, c * P:(c + 1) * P],
                                      RHSs[:, jh * 512:(jh + 1) * 512])],
                                    extra_waits=[t_rhs])
                        g_tok[c] = bg.read("scalar", lambda s, b5=bg, c=c, jh=jh, gcol=gcol: s.activation(
                            Gsb[:, c * D + jh * 512:c * D + (jh + 1) * 512], b5.t[:, 0:512],
                            AF.Identity, scale=s_gmask[:, gcol:gcol + 1]),
                            extra_waits=[t_super_ts0])
                    if is_m:
                        t_oap = []
                        for jh, b2 in ((0, bk[5]), (1, bk[6])):
                            mms = []
                            for cp in range(c + 1):
                                pid = pair_id_q(c, cp)
                                mms.append((b2.t[:, 0:512], Sqr[:, pid * P:(pid + 1) * P],
                                            Gsb[:, cp * D + jh * 512:cp * D + (jh + 1) * 512]))
                            b2.mm_group(mms, extra_waits=[g_tok[c]], cont=True)
                            t_oap.append(b2.read("vector", lambda v, b2=b2, jh=jh, c=c: v.tensor_mul(
                                x2b[:, (c % 2) * D + jh * 512:(c % 2) * D + (jh + 1) * 512],
                                b2.t[:, 0:512],
                                ga_bc[:, jh * 512:(jh + 1) * 512]),
                                extra_waits=[pr.tok("ss"), x2_spill_tok]))
                        ep_pend = (c, t_xe, t_oap)
                if is_m and ep_pend is not None:
                    emit_epilogue(*ep_pend)
                    ep_pend = None
                t_chain = pr.tok("ss")

                t_outs = pr.tok("ts")
                pending_state = (t_chain, t_outs, npar_buf)

            if pending_state is not None:
                emit_state_update(*pending_state)
                pending_state = None

            t_phaseA = [pr.tok("ts"), pr.tok("vs"), pr.tok("ss"), pr.tok("dsp"), pr.tok("dh2s"),
                        pr.tok("dx0"), pr.tok("dx1"), pr.tok("dxe0"), pr.tok("dxe1")]

            if debug:
                tap("WT", WT[:, :], t_phaseA)
                tap("Gsb", Gsb[:, :], t_phaseA)
                tap("bf32", b_f32[:, :], t_phaseA)
                tap("comb", comb_all[:, :], t_phaseA)
                tap("Kft", Kft[:, :], t_phaseA)
                tap("Ktk", Ktk[:, :], t_phaseA)
                tap("Qft", Qft[:, :], t_phaseA)
                tap("Xsb", Xsb[:, :], t_phaseA)
                tap("hft", hft[:, :], t_phaseA)
                tap("modcols_ca1", ca1c[:, :], t_phaseA)
                tap("h2s0", h2stg[:, :], t_phaseA)
                tap("ga_bc", ga_bc[:, 0:D], t_phaseA)
            if debug:
                t_phaseA = t_phaseA + [pr.tok("do")]

        if do_passb:
            with ExitStack() as bctx:
                EB = bctx.enter_context
                wG = EB(nc.sbuf_tensor("wG", [P, 2 * DT * D], BF16))
                wU = EB(nc.sbuf_tensor("wU", [P, 2 * DT * D], BF16))
                wD = EB(nc.sbuf_tensor("wD", [P, DT * D], BF16))
                sg2 = EB(nc.sbuf_tensor("sg2", [P, 512], BF16))
                ub2 = EB(nc.sbuf_tensor("ub2", [P, 512], BF16))
                obuf = EB(nc.sbuf_tensor("obuf", [P, D], F32))

                def pe_srcs(kind, idx):
                    if kind == "sh":
                        return (ap(d_sg_scr, idx * P * DT * D, [[DT * D, P], [1, DT * D]]),
                                ap(d_su_scr, idx * P * DT * D, [[DT * D, P], [1, DT * D]]),
                                ap(d_sd_scr, idx * P * DT * D, [[DT * D, P], [1, DT * D]]),
                                [wprep_toks["sgw"], wprep_toks["suw"], wprep_toks["sdw"]])
                    return (ap(d_gw_scr, idx * P * DT * D, [[DT * D, P], [1, DT * D]]),
                            ap(d_uw_scr, idx * P * DT * D, [[DT * D, P], [1, DT * D]]),
                            ap(d_dw_scr, idx * P * DT * D, [[DT * D, P], [1, DT * D]]),
                            [wprep_toks["gw"], wprep_toks["uw"], wprep_toks["dw"]])

                wslot_free = {0: list(t_phaseA), 1: list(t_phaseA)}
                wd_free = list(t_phaseA)
                out_toks = []

                t_scx_all = [pr.tok("dscx0"), pr.tok("dscx1")]
                with ExitStack() as ectx:
                    EE = ectx.enter_context
                    hxb = EE(nc.sbuf_tensor("hxb", [P, 2 * TEW * D], BF16))
                    hfe = EE(nc.sbuf_tensor("hfe", [P, DT * CAP], BF16))
                    acts = EE(nc.sbuf_tensor("acts", [P, DT * CAP], BF16))
                    ysb = EE(nc.sbuf_tensor("ysb", [P, 2 * D], F32))
                    ysb_guard = {0: None, 1: None}
                    t_gumul = [None]
                    t_hx_free = {0: list(t_phaseA), 1: list(t_phaseA)}
                    for e2 in range(NE):
                        slot = e2 % 2
                        gsrc, usrc, dsrc, wtoks = pe_srcs("ex", e2)
                        t_lg_ = pr.emit("sync", lambda sy, gsrc=gsrc, slot=slot: sy.dma_start(
                            ap(wG, slot * DT * D, [[2 * DT * D, P], [1, DT * D]]), gsrc),
                            waits=wtoks + wslot_free[slot], inc=f"dwg{slot}", inc_by=16)
                        t_lu_ = pr.emit("sync", lambda sy, usrc=usrc, slot=slot: sy.dma_start(
                            ap(wU, slot * DT * D, [[2 * DT * D, P], [1, DT * D]]), usrc),
                            waits=wtoks + wslot_free[slot], inc=f"dwu{slot}", inc_by=16)
                        t_ld_ = pr.emit("sync", lambda sy, dsrc=dsrc: sy.dma_start(wD[:, :], dsrc),
                                        waits=wtoks + wd_free, inc="dwd", inc_by=16)
                        hsl = (e2 % 2) * TEW * D
                        t_hx = pr.emit("scalar", lambda s, e2=e2, hsl=hsl: s.dma_start(
                            ap(hxb, hsl, [[2 * TEW * D, P], [D, TEW], [1, D]]),
                            ap(d_hx, e2 * CAP * D, [[D, P], [P * D, TEW], [1, D]])),
                            waits=t_scx_all + t_hx_free[e2 % 2], inc=f"dhx{e2 % 2}", inc_by=16)
                        for r in range(TEW):
                            tk = None
                            for i in range(DT):
                                tk = pr.emit("tensor",
                                             (lambda t, r=r, i=i, hsl=hsl: t.transpose(
                                                 bkT.t[:, i * P:(i + 1) * P],
                                                 hxb[:, hsl + r * D + i * P:hsl + r * D + (i + 1) * P],
                                                 s_ident[:, :])),
                                             waits=([t_hx] + bkT.readers) if i == 0 else (),
                                             inc="ts" if i == DT - 1 else None)
                            bkT.w = tk
                            bkT.readers = []
                            for i in range(DT):
                                bkT.read("scalar", lambda s, r=r, i=i: s.activation(
                                    hfe[:, i * CAP + r * P:i * CAP + (r + 1) * P],
                                    bkT.t[:, i * P:(i + 1) * P], AF.Identity,
                                    scale=cb1c[:, i:i + 1], bias=sb_f[:, i:i + 1]))
                        t_hfe = pr.tok("ss")
                        t_hx_free[e2 % 2] = [pr.tok("ts")]
                        for fj in range(DT):
                            for gi, (seg, nn) in enumerate(((0, 512), (512, 256))):
                                bA = bk[(fj * 2 + gi) % 2]
                                bB = bk[4 + (fj * 2 + gi) % 2]
                                mmsA, mmsB = [], []
                                for ht in range(DT):
                                    rh = hfe[:, ht * CAP + seg:ht * CAP + seg + nn]
                                    mmsA.append((bA.t[:, 0:nn],
                                                 wG[:, slot * DT * D + ht * D + fj * P:
                                                    slot * DT * D + ht * D + (fj + 1) * P], rh))
                                    mmsB.append((bB.t[:, 0:nn],
                                                 wU[:, slot * DT * D + ht * D + fj * P:
                                                    slot * DT * D + ht * D + (fj + 1) * P], rh))
                                bA.mm_group(mmsA, extra_waits=[t_lg_, t_hfe])
                                bB.mm_group(mmsB, extra_waits=[t_lu_, t_hfe])
                                t_sg = bA.read("scalar", lambda s, bA=bA, nn=nn: s.activation(
                                    sg2[:, 0:nn], bA.t[:, 0:nn], AF.Silu),
                                    extra_waits=[t_gumul[0]])
                                t_ub = bB.read("vector", lambda v, bB=bB, nn=nn: v.tensor_copy(
                                    ub2[:, 0:nn], bB.t[:, 0:nn]),
                                    extra_waits=[t_gumul[0]])
                                t_gumul[0] = pr.emit("vector", lambda v, fj=fj, seg=seg, nn=nn: v.tensor_mul(
                                    acts[:, fj * CAP + seg:fj * CAP + seg + nn],
                                    sg2[:, 0:nn], ub2[:, 0:nn]),
                                    waits=[t_sg, t_ub], inc="vs")
                        t_act = pr.tok("vs")
                        for r in range(TEW):
                            for hh in range(2):
                                bC = bk[2 + (r * 2 + hh) % 2]
                                mms = []
                                for fj in range(DT):
                                    mms.append((bC.t[:, 0:512],
                                                acts[:, fj * CAP + r * P:fj * CAP + (r + 1) * P],
                                                wD[:, fj * D + hh * 512:fj * D + (hh + 1) * 512]))
                                bC.mm_group(mms, extra_waits=[t_act, t_ld_])
                                bC.read("vector", lambda v, bC=bC, r=r, hh=hh: v.tensor_copy(
                                    ysb[:, (r % 2) * D + hh * 512:(r % 2) * D + (hh + 1) * 512],
                                    bC.t[:, 0:512]),
                                    extra_waits=[ysb_guard[r % 2]])
                            t_ye = pr.tok("vs")
                            ysb_guard[r % 2] = pr.emit("gpsimd", lambda g, e2=e2, r=r: g.dma_start(
                                ap(d_yx, (e2 * CAP + r * P) * D, [[D, P], [1, D]]),
                                ysb[:, (r % 2) * D:(r % 2 + 1) * D]),
                                waits=[t_ye], inc=f"dyx{r % 2}", inc_by=16)
                        wslot_free[slot] = [pr.tok("ts")]
                        wd_free = [pr.tok("ts")]
                t_yx_all = [pr.tok("dyx0"), pr.tok("dyx1")]

                acc = EB(nc.sbuf_tensor("acc", [P, (NCH // 2) * D], F32))
                h2b = EB(nc.sbuf_tensor("h2b", [P, DT * D], BF16))
                act = EB(nc.sbuf_tensor("act", [P, DT * D], BF16))
                ygb = EB(nc.sbuf_tensor("ygb", [P, 4 * D], F32))
                pes = [("sh", 0), ("sh", 1)]
                ygb_guard = {0: None, 1: None}
                for th in range(2):
                    t_ac = pr.emit("gpsimd", lambda g: g.memset(acc[:, :], 0.0),
                                   waits=t_phaseA + [pr.tok("vs"), pr.tok("ts")] + t_yx_all,
                                   inc="gs")
                    acc_guard = [t_ac]
                    for ht in range(DT):
                        pr.emit("sync", lambda sy, ht=ht, th=th: sy.dma_start(
                            ap(h2b, ht * D, [[DT * D, P], [P, DT], [1, P]]),
                            ap(d_h2s, th * DT * P * D + P * ht + 0, [[D, P], [P * D, DT], [1, P]])),
                            waits=t_phaseA + t_yx_all + [pr.tok("ts"), pr.tok("vs"), pr.tok("ss")],
                            inc="dh2", inc_by=16)
                    h2_load = [pr.tok("dh2")]
                    for pi, (kind, idx) in enumerate(pes):
                        slot = pi % 2
                        gsrc, usrc, dsrc, wtoks = pe_srcs(kind, idx)
                        t_lg_ = pr.emit("sync", lambda sy, gsrc=gsrc, slot=slot: sy.dma_start(
                            ap(wG, slot * DT * D, [[2 * DT * D, P], [1, DT * D]]), gsrc),
                            waits=wtoks + wslot_free[slot], inc=f"dwg{slot}", inc_by=16)
                        t_lu_ = pr.emit("sync", lambda sy, usrc=usrc, slot=slot: sy.dma_start(
                            ap(wU, slot * DT * D, [[2 * DT * D, P], [1, DT * D]]), usrc),
                            waits=wtoks + wslot_free[slot], inc=f"dwu{slot}", inc_by=16)
                        t_ld_ = pr.emit("sync", lambda sy, dsrc=dsrc: sy.dma_start(wD[:, :], dsrc),
                                        waits=wtoks + wd_free, inc="dwd", inc_by=16)
                        for fj in range(DT):
                            for tb in range(2):
                                bA = bk[(fj * 2 + tb) % 2]
                                bB = bk[4 + (fj * 2 + tb) % 2]
                                mmsA, mmsB = [], []
                                for ht in range(DT):
                                    rh = h2b[:, ht * D + tb * 512:ht * D + (tb + 1) * 512]
                                    mmsA.append((bA.t[:, 0:512],
                                                 wG[:, slot * DT * D + ht * D + fj * P:
                                                    slot * DT * D + ht * D + (fj + 1) * P], rh))
                                    mmsB.append((bB.t[:, 0:512],
                                                 wU[:, slot * DT * D + ht * D + fj * P:
                                                    slot * DT * D + ht * D + (fj + 1) * P], rh))
                                bA.mm_group(mmsA, extra_waits=[t_lg_] + h2_load)
                                bB.mm_group(mmsB, extra_waits=[t_lu_] + h2_load)
                                t_sg = bA.read("scalar", lambda s, bA=bA: s.activation(
                                    sg2[:, :], bA.t[:, 0:512], AF.Silu),
                                    extra_waits=[t_gumul[0]])
                                t_ub = bB.read("scalar", lambda s, bB=bB: s.activation(
                                    ub2[:, :], bB.t[:, 0:512], AF.Identity),
                                    extra_waits=[t_gumul[0]])
                                t_gumul[0] = pr.emit("vector", lambda v, fj=fj, tb=tb: v.tensor_mul(
                                    act[:, fj * D + tb * 512:fj * D + (tb + 1) * 512],
                                    sg2[:, :], ub2[:, :]),
                                    waits=[t_sg, t_ub], inc="vs")
                        t_act = pr.tok("vs")
                        for tt in range(DT):
                            c16 = th * 8 + tt
                            for hh in range(2):
                                bC = bk[2 + (tt * 2 + hh) % 2]
                                mms = []
                                for fj in range(DT):
                                    mms.append((bC.t[:, 0:512],
                                                ap(act, fj * D + tt * P, [[DT * D, P], [1, P]]),
                                                wD[:, fj * D + hh * 512:fj * D + (hh + 1) * 512]))
                                bC.mm_group(mms, extra_waits=[t_act, t_ld_])
                                csl = onef_col[:, 0:1]
                                bC.read("vector", lambda v, bC=bC, tt=tt, hh=hh, csl=csl: v.scalar_tensor_tensor(
                                    acc[:, tt * D + hh * 512:tt * D + (hh + 1) * 512],
                                    bC.t[:, 0:512], csl,
                                    acc[:, tt * D + hh * 512:tt * D + (hh + 1) * 512],
                                    op0=ALU.mult, op1=ALU.add),
                                    extra_waits=acc_guard)
                        wslot_free[slot] = [pr.tok("ts")]
                        wd_free = [pr.tok("ts")]
                    t_accdone = pr.tok("vs")
                    for cc in range(8):
                        c16 = th * 8 + cc
                        slot = cc % 2
                        gsl = cc % 2
                        t_yg = []
                        for k2 in range(2):
                            t_yg.append(pr.emit("gpsimd", lambda g, c16=c16, k2=k2, gsl=gsl:
                                g.indirect_dma_start(
                                    ygb[:, (gsl * 2 + k2) * D:(gsl * 2 + k2 + 1) * D], None,
                                    ap(d_yx, 0, [[D, NSEG], [1, D]]),
                                    bass.IndirectOffsetOnAxis(
                                        sl2[:, 2 * c16 + k2:2 * c16 + k2 + 1], 0)),
                                waits=t_yx_all + [ygb_guard[gsl]], inc=f"dyg{gsl}{k2}", inc_by=16))
                        t_xr = pr.emit("sync", lambda sy, c16=c16, slot=slot: sy.dma_start(
                            ap(xb, slot * D, [[2 * D, P], [1, D]]),
                            ap(d_x2s, c16 * P * D, [[D, P], [1, D]])),
                            waits=[t_accdone] + ([pr.tok("vs")] if cc >= 2 else []),
                            inc=f"dx{slot}", inc_by=16)
                        t_cb = t_accdone
                        for k2 in range(2):
                            t_cb = pr.emit("vector", lambda v, cc=cc, c16=c16, k2=k2, gsl=gsl:
                                v.scalar_tensor_tensor(
                                    acc[:, cc * D:(cc + 1) * D],
                                    ygb[:, (gsl * 2 + k2) * D:(gsl * 2 + k2 + 1) * D],
                                    w2col[:, 2 * c16 + k2:2 * c16 + k2 + 1],
                                    acc[:, cc * D:(cc + 1) * D],
                                    op0=ALU.mult, op1=ALU.add),
                                waits=[t_accdone, t_yg[k2], t_cb], inc="vs")
                        ygb_guard[gsl] = t_cb
                        t_ob = pr.emit("vector", lambda v, cc=cc: v.tensor_mul(
                            obuf[:, :], acc[:, cc * D:(cc + 1) * D], gb_bc[:, :]),
                            waits=[t_cb, pr.tok("do")], inc="vs")
                        t_o2 = pr.emit("vector", lambda v, slot=slot: v.tensor_add(
                            obuf[:, :], obuf[:, :], xb[:, slot * D:(slot + 1) * D]),
                            waits=[t_xr, t_ob], inc="vs")
                        out_toks.append(pr.emit("sync", lambda sy, c16=c16: sy.dma_start(
                            ap(d_out, c16 * P * D, [[D, P], [1, D]]), obuf[:, :]),
                            waits=[t_o2], inc="do", inc_by=16))

                pr.q["sync"].append(([out_toks[-1]], lambda sy: None, None))
                pr.lastw["sync"][out_toks[-1][0]] = out_toks[-1][1]

        else:
            pr.q["sync"].append(([t_phaseA[0]], lambda sy: None, None))

        block = octx.enter_context(nc.Block())

        @block.tensor
        def _(t):
            pr.replay("tensor", t)

        @block.vector
        def _(v):
            pr.replay("vector", v)

        @block.scalar
        def _(s):
            pr.replay("scalar", s)

        @block.gpsimd
        def _(g):
            pr.replay("gpsimd", g)

        @block.sync
        def _(sy):
            pr.replay("sync", sy)

    return nc, taps


def _mask(fn):
    r = np.arange(P)[:, None]
    c = np.arange(P)[None, :]
    return fn(r, c).astype(np.float32).astype(ml_dtypes.bfloat16)


def make_inputs(L0_attn_in_seq, L0_attn_theta_K, L0_attn_theta_V, L0_attn_theta_Q,
                L0_attn_W0, L0_attn_b0, L0_ffn_router_w, L0_ffn_gate_w, L0_ffn_up_w,
                L0_ffn_down_w, L0_ffn_shared_gate_w, L0_ffn_shared_up_w,
                L0_ffn_shared_down_w, L0_c, L0_wmod):
    x = np.ascontiguousarray(np.asarray(L0_attn_in_seq, dtype=np.float32))
    f32 = lambda a: np.ascontiguousarray(np.asarray(a, dtype=np.float32))
    base = {
        "thK": f32(np.asarray(L0_attn_theta_K).T),
        "thV": f32(np.asarray(L0_attn_theta_V).T),
        "thQ": f32(np.asarray(L0_attn_theta_Q).T),
        "w0T": f32(np.asarray(L0_attn_W0).T),
        "b0r": f32(np.asarray(L0_attn_b0).reshape(1, D)),
        "rw": f32(L0_ffn_router_w),
        "gw": f32(L0_ffn_gate_w), "uw": f32(L0_ffn_up_w), "dw": f32(L0_ffn_down_w),
        "sgw": f32(L0_ffn_shared_gate_w), "suw": f32(L0_ffn_shared_up_w),
        "sdw": f32(L0_ffn_shared_down_w),
        "wmod": f32(L0_wmod),
        "identb": np.eye(P, dtype=np.float32).astype(ml_dtypes.bfloat16),
        "mune": _mask(lambda r, c: np.where(r < c, -1.0, 0.0)),
        "mlne": _mask(lambda r, c: np.where(r > c, -1.0, 0.0)),
        "muinc": _mask(lambda r, c: np.where(r <= c, 1.0, 0.0)),
        "mut1": _mask(lambda r, c: np.where(r < c, 1.0, 0.0)),
        "ebase": (np.arange(NE, dtype=np.float32) * CAP).reshape(1, NE),
    }
    cmat = f32(L0_c)
    in_maps = []
    for k in range(8):
        b, h = k >> 1, k & 1
        gm = np.zeros((P, 2), dtype=np.float32)
        gm[:, 0] = -2.0 * LR / D * h
        gm[:, 1] = -2.0 * LR / D
        m = dict(base)
        m["xm"] = np.ascontiguousarray(x[b, h * TOWN:(h + 1) * TOWN])
        m["xp"] = np.ascontiguousarray(x[b, 0:TOWN])
        m["gmask"] = gm
        m["cvec"] = np.ascontiguousarray(cmat[b].reshape(D, 1))
        in_maps.append(m)
    return in_maps


def kernel(**inputs):
    in_maps = make_inputs(**inputs)
    nc, _ = build_program(debug=False)
    res = run_bass_kernel_spmd(nc, in_maps, core_ids=list(range(8)))
    out = np.empty((4, 4096, D), dtype=np.float32)
    for k in range(8):
        b, h = k >> 1, k & 1
        out[b, h * TOWN:(h + 1) * TOWN] = res.results[k]["out"]
    return out
